# version 19
# baseline (speedup 1.0000x reference)
import numpy as np
import ml_dtypes
import concourse.bass as bass
import concourse.mybir as mybir
from concourse.bass_utils import run_bass_kernel_spmd
from contextlib import ExitStack

F32 = mybir.dt.float32
BF16 = mybir.dt.bfloat16
AF = mybir.ActivationFunctionType
ALU = mybir.AluOpType
AX = mybir.AxisListType

NCORES = 8
SEQS = 4
S = 2048
D = 1024
NCH = 16
TOK = SEQS * S
CAP = 20480.0
LN_EPS = 1e-5
BIG = 1.0e30
MAGIC = 1.5 * 2.0 ** 23

UPTO = 6
DEVSTOP = 0
FUSED = True

ENG2BLK = {"pe": "tensor", "act": "scalar", "dve": "vector", "pool": "gpsimd", "sp": "sync"}


class KB:
    def __init__(self, nc):
        self.nc = nc
        self.ops = {e: [] for e in ENG2BLK}
        self.cnt = {}
        self.sems = {}

    LANES = ("xc0", "xc1", "px", "lw0", "lw1", "qw0", "qw1", "kq",
             "xst0", "xst1", "xlnd", "ddl", "msc", "mscp", "pw", "pr", "cc", "tw")

    def make_sems(self, ctx):
        for name in ("pe", "act", "dve", "pool", "sp") + KB.LANES:
            self.sems[name] = ctx.enter_context(self.nc.semaphore("sem_" + name))
            self.cnt[name] = 0

    def op(self, eng, fn, waits=(), hard=(), sig=None, dma=None):
        incs = []
        tok = None
        if eng in ("dve", "act", "pool") and dma is None:
            hard = tuple(hard) + ((eng, self.cnt[eng]),)
            if sig is None:
                sig = eng
        if sig is not None:
            self.cnt[sig] += 1
            incs.append((sig, 1))
            tok = (sig, self.cnt[sig])
        if dma is not None:
            self.cnt[dma] += 16
            incs.append((dma, 16))
            tok = (dma, self.cnt[dma])
        ws = tuple(w for w in waits if w is not None)
        hs = tuple(h for h in hard if h is not None)
        for t in ws + hs:
            assert t[1] <= self.cnt[t[0]], f"wait on future token {t} (cnt={self.cnt[t[0]]})"
        self.ops[eng].append((ws, hs, fn, incs))
        return tok

    def wait(self, eng, waits):
        ws = tuple(w for w in waits if w is not None)
        for t in ws:
            assert t[1] <= self.cnt[t[0]], f"wait on future token {t}"
        self.ops[eng].append((ws, (), None, []))

    def emit(self):
        with self.nc.Block() as blk:
            for eng in ENG2BLK:
                ops = self.ops[eng]
                own = eng

                def body(e, ops=ops, own=own):
                    waited = {}
                    own_cnt = 0
                    for ws, hs, fn, incs in ops:
                        need = {}
                        for s, c in ws:
                            if s == own and c <= own_cnt:
                                continue
                            if need.get(s, 0) < c:
                                need[s] = c
                        for s, c in hs:
                            if need.get(s, 0) < c:
                                need[s] = c
                        for s, c in need.items():
                            if waited.get(s, 0) < c:
                                e.wait_ge(self.sems[s], c)
                                waited[s] = c
                        if fn is not None:
                            instr = fn(e)
                            for s, n in incs:
                                instr.then_inc(self.sems[s], n)
                            own_cnt += sum(1 for s, n in incs if s == own)

                getattr(blk, ENG2BLK[eng])(body)


class NS:
    pass


W_NAMES = [
    ("L0_attn_Wq", [D, D]), ("L0_attn_Wk", [D, D]), ("L0_attn_Wv", [D, D]),
    ("L0_attn_Wo", [D, D]), ("L0_attn_gamma", [D]), ("L0_attn_beta", [D]),
    ("L0_ffn_kernel", [D, D]), ("L0_ffn_bias", [D]), ("L0_ffn_a", [D, 16]),
    ("L0_ffn_b", [16, D]), ("L1_attn_in_proj_w", [3 * D, D]),
    ("L1_attn_in_proj_b", [3 * D]), ("L1_attn_out_proj_w", [D, D]),
    ("L1_attn_out_proj_b", [D]), ("L1_ffn_gate_w", [8, D]),
    ("L1_ffn_we_gate", [8, D, D]), ("L1_ffn_we_up", [8, D, D]),
    ("L1_ffn_we_down", [8, D, D]), ("L1_ffn_ws_gate", [2 * D, D]),
    ("L1_ffn_ws_up", [2 * D, D]), ("L1_ffn_ws_down", [D, 2 * D]),
]

SC_NAMES = [
    ("q", [D, D]), ("k", [D, D]), ("v", [D, D]), ("o", [D, D]),
    ("kq", [D, D]), ("ip", [3 * D, D]), ("op", [D, D]),
    ("weg", [8, D, D]), ("weu", [8, D, D]), ("wed", [8, D, D]),
    ("wsg", [2 * D, D]), ("wsu", [2 * D, D]), ("wsd", [D, 2 * D]),
    ("gw", [16, D]),
    ("bv", [16, 4 * D]),
]


def declare_io(nc, phase):
    io = NS()
    din = lambda name, shape, dt=F32: nc.dram_tensor(name, shape, dt, kind="ExternalInput").ap()
    dout = lambda name, shape, dt=F32: nc.dram_tensor(name, shape, dt, kind="ExternalOutput").ap()
    dint = lambda name, shape, dt=BF16: nc.dram_tensor(name, shape, dt).ap()

    io.w = {nm: din(nm, shp) for nm, shp in W_NAMES}
    io.u128 = din("u128", [128, 128], BF16)
    io.cmask = din("cmask", [128, 128], BF16)
    io.rankmask = din("rankmask", [8, 1])
    io.rowsel = din("rowsel", [8, 1])

    last = (phase == "all") or (phase == "p1" and UPTO <= 3) or \
           (phase == "p2" and UPTO <= 5) or (phase == "p3")
    io.x_in = din("x", [TOK, D]) if phase in ("all", "p1") else din("xs_in", [TOK, D])
    io.out = dout("out", [TOK, D])
    io.xs = dint("xs", [TOK, D], F32) if (phase == "all" or last) else dout("xs_out", [TOK, D], F32)
    if phase == "all":
        io.xlnd = dint("xlnd", [TOK, D], BF16)
        io.xlnd_in = io.xlnd
    else:
        io.xlnd = io_xlnd_out = (dint("xlnd", [TOK, D], BF16) if last
                                 else dout("xlnd_out", [TOK, D], BF16))
        io.xlnd_in = din("xlnd_in", [TOK, D], BF16) if phase in ("p2", "p3") else io.xlnd
    if phase == "all":
        io.gd_wfull = dint("gd_wfull", [128, 512], F32)
        io.gd_pref = dint("gd_pref", [128, 512], F32)
        io.gd_trow = dint("gd_trow", [1, 512], F32)
        io.ccrow = None
        io.gdi_wfull = io.gd_wfull
        io.gdi_pref = io.gd_pref
        io.gdi_trow = io.gd_trow
        io.cc = None
    else:
        mkout = phase in ("p1", "p2") and not last
        f = dout if mkout else dint
        io.gd_wfull = f("gdo_wfull" if mkout else "gd_wfull", [128, 512], F32)
        io.gd_pref = f("gdo_pref" if mkout else "gd_pref", [128, 512], F32)
        io.gd_trow = f("gdo_trow" if mkout else "gd_trow", [1, 512], F32)
        io.ccrow = dout("ccrow", [1, 8], F32) if mkout else None
        if phase in ("p2", "p3"):
            io.gdi_wfull = din("gdi_wfull", [128, 512], F32)
            io.gdi_pref = din("gdi_pref", [128, 512], F32)
            io.gdi_trow = din("gdi_trow", [1, 512], F32)
            io.cc = din("cc", [8, 8], F32)
        else:
            io.gdi_wfull = io.gdi_pref = io.gdi_trow = io.cc = None

    io.sc = {nm: dint("sc_" + nm, shp, BF16) for nm, shp in SC_NAMES}
    io.tw = {}
    for nm, shp in [("ipq", [128, 8192]), ("ipk", [128, 8192]), ("ipv", [128, 8192]),
                    ("op", [128, 8192]), ("weg", [8, 128, 8192]), ("weu", [8, 128, 8192]),
                    ("wed", [8, 128, 8192]), ("wsg", [2, 128, 8192]),
                    ("wsu", [2, 128, 8192]), ("wsd", [2, 128, 8192])]:
        io.tw[nm] = dint("tw_" + nm, shp, BF16)
    io.dd = dint("dd", [D, S], BF16)
    io.bounce = dint("bounce", [1, 1024], F32)
    io.bounce2 = dint("bounce2", [1024, 1], F32)
    io.t64d = dint("t64d", [64, 8], F32)
    io.offd = dint("offd", [1, 512], F32)
    io.ccin_d = dint("ccin_d", [8, 8], F32)
    io.ccout_d = dint("ccout_d", [8, 8], F32)
    return io


def alloc_sbuf(nc, ctx):
    sb = NS()
    t = lambda n, s, d=BF16: ctx.enter_context(nc.sbuf_tensor(n, s, d))
    sb.PX = t("PX", [128, 16384])
    sb.PQ = t("PQ", [128, 16384])
    sb.PK = t("PK", [128, 16384])
    sb.PV = t("PV", [128, 16640])
    sb.PO = t("PO", [128, 16384])
    sb.POf = sb.PO[:].bitcast(F32)
    sb.xc = t("xc", [128, 2048], F32)
    sb.dtb = t("dtb", [128, 2048], BF16)
    sb.st = t("stg", [128, 4096], BF16)
    sb.tt = sb.st[:, 0:2048].bitcast(F32)
    sb.bn6 = t("bn6", [128, 12], F32)
    sb.stats = t("stats", [128, 32], F32)
    sb.mu16 = t("mu16", [128, 16], F32)
    sb.rstd16 = t("rstd16", [128, 16], F32)
    sb.sc1 = t("sc1", [128, 64], F32)
    sb.packA = t("packA", [128, 2048], F32)
    sb.row = sb.packA[0:1, 0:1024]
    sb.Trow = sb.packA[0:1, 1024:1536]
    sb.OFFROW = sb.packA[0:1, 1536:2048]
    sb.vb_row = t("vb_row", [1, 1024], BF16)
    sb.biasf = t("biasf", [128, 32], F32)
    sb.rowb = t("rowbufb", [1, 512], BF16)
    sb.u128 = t("u128c", [128, 128], BF16)
    sb.cmask = t("cmaskc", [128, 128], BF16)
    sb.rankmask = t("rankmaskc", [8, 1], F32)
    sb.rowsel = t("rowselc", [8, 1], F32)
    sb.ones_row_b = t("ones_row_b", [1, 128], BF16)
    sb.ones512_b = t("ones512_b", [1, 512], BF16)
    sb.ones_row_f = t("ones_row_f", [1, 128], F32)
    sb.ones_col_b = t("ones_col_b", [128, 1], BF16)
    sb.gamma_rep = t("gamma_rep", [128, D], BF16)
    sb.beta_rep = t("beta_rep", [128, D], BF16)
    sb.gwT = t("gwT", [128, 128], BF16)
    sb.quant = t("quant", [128, 8], F32)
    sb.rrep = t("rrep", [128, 2], F32)
    sb.wfull = t("wfull", [128, 512], F32)
    sb.pref = t("pref", [128, 512], F32)
    sb.comb = t("comb", [128, 512], F32)
    sb.CC = t("CCt", [8, 8], F32)
    sb.g8 = t("g8", [128, 64], F32)
    sb.selb = t("selb", [128, 32], BF16)
    sb.cb = t("cb", [128, 4], F32)
    return sb


def alloc_psum(nc, ctx):
    ps = NS()
    ps.A = ctx.enter_context(nc.psum_tensor("PSA", [128, 2048], F32))
    ps.B = ctx.enter_context(nc.psum_tensor("PSB", [128, 2048], F32))
    return ps



def stage_preprocess(kb, io, sb, ps):
    row1 = lambda ap: ap.rearrange("(o n) -> o n", o=1)
    for dst, src in [(sb.u128, io.u128), (sb.cmask, io.cmask),
                     (sb.rankmask, io.rankmask), (sb.rowsel, io.rowsel)]:
        kb.op("sp", lambda e, d=dst, s=src: e.dma_start(d[:], s), dma="msc")
    t_mp = kb.op("pool", lambda e: e.dma_start(
        sb.vb_row[:], io.w["L1_attn_in_proj_b"].rearrange("(o n) -> o n", o=1)[0:1, 2 * D:3 * D]),
        dma="mscp")
    t_g1 = kb.op("sp", lambda e: e.dma_start(sb.packA[0:1, 0:D], row1(io.w["L0_attn_gamma"])), dma="msc")
    kb.op("dve", lambda e: e.memset(sb.ones_row_b[:], 1.0), sig="dve")
    kb.op("dve", lambda e: e.memset(sb.ones512_b[:], 1.0), sig="dve")
    kb.op("dve", lambda e: e.memset(sb.ones_row_f[:], 1.0), sig="dve")
    t_ones = kb.op("dve", lambda e: e.memset(sb.ones_col_b[:], 1.0), sig="dve")
    tg_ev = []
    for j in range(2):
        tm = kb.op("pe", lambda e, j=j: e.matmul(
            ps.A[:, 512 * j:512 * (j + 1)], sb.ones_row_f[:],
            sb.packA[0:1, 512 * j:512 * (j + 1)], start=True, stop=True),
            waits=(t_g1, t_ones), sig="pe")
        tg_ev.append(kb.op("dve", lambda e, j=j: e.tensor_copy(
            sb.gamma_rep[:, 512 * j:512 * (j + 1)], ps.A[:, 512 * j:512 * (j + 1)]),
            waits=(tm,), sig="dve"))
    t_b1 = kb.op("sp", lambda e: e.dma_start(sb.packA[0:1, 0:D], row1(io.w["L0_attn_beta"])),
                 waits=(tg_ev[0], tg_ev[1]), dma="msc")
    for j in range(2):
        tm = kb.op("pe", lambda e, j=j: e.matmul(
            ps.A[:, 1024 + 512 * j:1024 + 512 * (j + 1)], sb.ones_row_f[:],
            sb.packA[0:1, 512 * j:512 * (j + 1)], start=True, stop=True),
            waits=(t_b1,), sig="pe")
        kb.op("dve", lambda e, j=j: e.tensor_copy(
            sb.beta_rep[:, 512 * j:512 * (j + 1)], ps.A[:, 1024 + 512 * j:1024 + 512 * (j + 1)]),
            waits=(tm,), sig="dve")

    cast_jobs = []

    def cast_copy(w_ap, sc_ap):
        rows = w_ap.shape[-2]
        for r0 in range(0, rows, 128):
            cast_jobs.append((w_ap, sc_ap, r0))

    for nm, src in [("q", io.w["L0_attn_Wq"]), ("k", io.w["L0_attn_Wk"]),
                    ("v", io.w["L0_attn_Wv"]), ("o", io.w["L0_attn_Wo"]),
                    ("op", io.w["L1_attn_out_proj_w"]), ("ip", io.w["L1_attn_in_proj_w"]),
                    ("wsg", io.w["L1_ffn_ws_gate"]), ("wsu", io.w["L1_ffn_ws_up"]),
                    ("wsd", io.w["L1_ffn_ws_down"])]:
        cast_copy(src, io.sc[nm])
    for nm, src in [("weg", io.w["L1_ffn_we_gate"]), ("weu", io.w["L1_ffn_we_up"]),
                    ("wed", io.w["L1_ffn_we_down"])]:
        for ee in range(8):
            cast_copy(src[ee], io.sc[nm][ee])
    t_r = ("dve", kb.cnt["dve"])
    for b0 in range(0, len(cast_jobs), 8):
        batch = cast_jobs[b0:b0 + 8]
        t_w = None
        for i, (w_ap, sc_ap, r0) in enumerate(batch):
            cols = w_ap.shape[-1]
            t_w = kb.op("pool", lambda e, w_ap=w_ap, r0=r0, c0=2048 * i, cols=cols: e.dma_start(
                sb.PO[:, c0:c0 + cols], w_ap[r0:r0 + 128, :]), waits=(t_r,), dma="pw")
        for i, (w_ap, sc_ap, r0) in enumerate(batch):
            cols = w_ap.shape[-1]
            t_r = kb.op("sp", lambda e, sc_ap=sc_ap, r0=r0, c0=2048 * i, cols=cols: e.dma_start(
                sc_ap[r0:r0 + 128, :], sb.PO[:, c0:c0 + cols]), waits=(t_w,), dma="pr")
    tgwz = kb.op("dve", lambda e: e.memset(sb.PO[0:16, 0:D], 0.0), waits=(t_r,), sig="dve")
    tgwc = kb.op("pool", lambda e: e.dma_start(sb.PO[0:8, 0:D], io.w["L1_ffn_gate_w"]),
                 waits=(tgwz, t_r, t_mp), dma="mscp")
    tgws = kb.op("sp", lambda e: e.dma_start(io.sc["gw"], sb.PO[0:16, 0:D]),
                 waits=(tgwc,), dma="msc")
    tgwT = None
    for dc in range(8):
        tgwT = kb.op("sp", lambda e, dc=dc: e.dma_start(
            sb.gwT[:, 16 * dc:16 * (dc + 1)], io.sc["gw"][:, 128 * dc:128 * (dc + 1)],
            transpose=True), waits=(tgws,), dma="msc")

    POf = sb.POf
    tload = None
    for dc in range(8):
        tload = kb.op("sp", lambda e, dc=dc: e.dma_start(
            POf[:, 1024 * dc:1024 * (dc + 1)], io.w["L0_ffn_kernel"][128 * dc:128 * (dc + 1), :]),
            dma="pr")
    tred = []
    for dc in range(8):
        tred.append(kb.op("dve", lambda e, dc=dc: e.reduce_max(
            sb.quant[:, dc:dc + 1], POf[:, 1024 * dc:1024 * (dc + 1)],
            axis=AX.X, apply_absolute_value=True), waits=(tload,), sig="dve"))
    tm = kb.op("dve", lambda e: e.reduce_max(sb.sc1[:, 0:1], sb.quant[:, 0:8], axis=AX.X),
               hard=(tred[-1],), sig="dve")
    tb = kb.op("pool", lambda e: e.dma_start(io.bounce2[0:128, :], sb.sc1[:, 0:1]),
               waits=(tm, t_mp), dma="mscp")
    tb2 = kb.op("sp", lambda e: e.dma_start(
        sb.packA[0:1, 0:128], io.bounce2.rearrange("(o p) n -> o (p n)", o=1)[0:1, 0:128]),
        waits=(tb,), dma="msc")
    tmax = kb.op("dve", lambda e: e.reduce_max(sb.packA[0:1, 128:129], sb.packA[0:1, 0:128], axis=AX.X),
                 waits=(tb2,), sig="dve")
    tsc = kb.op("dve", lambda e: e.tensor_scalar(sb.packA[0:1, 129:130], sb.packA[0:1, 128:129],
                                                 float(1.0 / 7.0), None, ALU.mult),
                hard=(tmax,), sig="dve")
    trc = kb.op("dve", lambda e: e.reciprocal(sb.packA[0:1, 130:131], sb.packA[0:1, 129:130]),
                hard=(tsc,), sig="dve")
    tb3 = kb.op("pool", lambda e: e.dma_start(io.bounce[0:1, 0:2], sb.packA[0:1, 129:131]),
                waits=(trc,), dma="mscp")
    tb4 = kb.op("sp", lambda e: e.dma_start(sb.packA[0:1, 132:134], io.bounce[0:1, 0:2]),
                waits=(tb3,), dma="msc")
    tmr = kb.op("pe", lambda e: e.matmul(ps.B[:, 0:2], sb.ones_row_f[:], sb.packA[0:1, 132:134],
                                         start=True, stop=True), waits=(tb4,), sig="pe")
    trr = kb.op("dve", lambda e: e.tensor_copy(sb.rrep[:, 0:2], ps.B[:, 0:2]),
                waits=(tmr,), sig="dve")
    t_qst = []
    for dc in range(8):
        stg = sb.st[:, 0:1024] if dc % 2 == 0 else sb.st[:, 1024:2048]
        kb.op("dve", lambda e, dc=dc: e.tensor_scalar(
            POf[:, 1024 * dc:1024 * (dc + 1)], POf[:, 1024 * dc:1024 * (dc + 1)],
            sb.rrep[:, 1:2], MAGIC, ALU.mult, ALU.add), hard=(trr,), sig="dve")
        t2 = kb.op("dve", lambda e, dc=dc: e.tensor_scalar(
            POf[:, 1024 * dc:1024 * (dc + 1)], POf[:, 1024 * dc:1024 * (dc + 1)],
            -MAGIC, None, ALU.add), sig="dve")
        t3 = kb.op("dve", lambda e, dc=dc, stg=stg: e.tensor_scalar(
            stg, POf[:, 1024 * dc:1024 * (dc + 1)], sb.rrep[:, 0:1], None, ALU.mult),
            waits=(t_qst[dc - 2] if dc >= 2 else None,), hard=(t2,), sig="dve")
        t_qst.append(kb.op("pool", lambda e, dc=dc, stg=stg: e.dma_start(
            io.sc["kq"][128 * dc:128 * (dc + 1), :], stg),
            waits=(t3, t_qst[dc - 1] if dc >= 1 else None), dma="pw"))
    tbz = kb.op("dve", lambda e: e.memset(sb.PO[0:16, 4096:8192], 0.0), waits=(t_r,), sig="dve")
    tb0 = kb.op("pool", lambda e: e.dma_start(
        sb.PO[0:1, 4096:5120], row1(io.w["L0_ffn_bias"])), waits=(tbz, t_mp), dma="mscp")
    tb1x = kb.op("pool", lambda e: e.dma_start(
        sb.PO[0:1, 5120:7168],
        io.w["L1_attn_in_proj_b"].rearrange("(o n) -> o n", o=1)[0:1, 0:2 * D]),
        waits=(tb0,), dma="mscp")
    tb2x = kb.op("pool", lambda e: e.dma_start(
        sb.PO[0:1, 7168:8192], row1(io.w["L1_attn_out_proj_b"])), waits=(tb1x,), dma="mscp")
    tbs = kb.op("sp", lambda e: e.dma_start(io.sc["bv"], sb.PO[0:16, 4096:8192]),
                waits=(tb0, tb1x, tb2x), dma="msc")
    tbt = None
    for ch in range(32):
        tbt = kb.op("sp", lambda e, ch=ch: e.dma_start(
            sb.st[:, 16 * ch:16 * (ch + 1)], io.sc["bv"][:, 128 * ch:128 * (ch + 1)],
            transpose=True), waits=(tbs,), dma="msc")
    kb.op("dve", lambda e: e.tensor_copy(
        sb.biasf[:], sb.st[:].rearrange("p (c r) -> p c r", r=16)[:, 0:32, 0:1]
        .rearrange("p c r -> p (c r)")), waits=(tbt,), sig="dve")

    kb.t_pr_all = ("pr", kb.cnt["pr"])
    return t_qst[-1], ("pr", kb.cnt["pr"])


def image_jobs(io):
    def b_io(ooff):
        return [(1024 * dc + 512 * oh, ooff + 512 * oh, 512, 128 * dc)
                for dc in range(8) for oh in range(2)]
    jobs = []
    for nm in ("weg", "weu"):
        for ee in range(8):
            jobs.append((io.tw[nm][ee],
                         [(1024 * bc, 0, 1024, 128 * bc) for bc in range(8)],
                         io.sc[nm][ee]))
    for ee in range(8):
        jobs.append((io.tw["wed"][ee],
                     [(1024 * ic + 512 * hh, 512 * hh, 512, 128 * ic)
                      for ic in range(8) for hh in range(2)], io.sc["wed"][ee]))
    for grp in range(2):
        jobs.append((io.tw["wsg"][grp],
                     [(1024 * bc, 1024 * grp, 1024, 128 * bc) for bc in range(8)], io.sc["wsg"]))
        jobs.append((io.tw["wsu"][grp],
                     [(1024 * bc, 1024 * grp, 1024, 128 * bc) for bc in range(8)], io.sc["wsu"]))
        jobs.append((io.tw["wsd"][grp],
                     [(1024 * ic + 512 * hh, 512 * hh, 512, 128 * (8 * grp + ic))
                      for ic in range(8) for hh in range(2)], io.sc["wsd"]))
    for nm, ooff in (("ipq", 0), ("ipk", D), ("ipv", 2 * D)):
        jobs.append((io.tw[nm], b_io(ooff), io.sc["ip"]))
    jobs.append((io.tw["op"], b_io(0), io.sc["op"]))
    return jobs


def stage_images(kb, io, sb, jobs, after_tok):
    slots = (sb.PV[:, 0:8192], sb.PV[:, 8448:16640])
    t_store = [kb.tw_done, kb.tw_done]
    for i, (dst, blocks, src) in enumerate(jobs):
        stg = slots[i % 2]
        t_tr = None
        for (doff, r0, rn, c0) in blocks:
            t_tr = kb.op("act", lambda e, stg=stg, doff=doff, r0=r0, rn=rn, c0=c0, src=src: e.dma_start(
                stg[:, doff:doff + rn], src[r0:r0 + rn, c0:c0 + 128], transpose=True),
                waits=(t_store[i % 2], kb.t_pr_all, after_tok), dma="tw")
        t_store[i % 2] = kb.op("act", lambda e, dst=dst, stg=stg: e.dma_start(dst, stg),
                               waits=(t_tr,), dma="tw")
    kb.tw_done = t_store[(len(jobs) - 1) % 2]
    kb.tw_done = (kb.tw_done[0], max(kb.cnt["tw"], kb.tw_done[1]))
    return kb.tw_done



def pass_residual_ln(kb, io, sb, ps, seq, mode, want_ln, x_src, x_dst,
                     scale_delta=1.0, px_free=None, chunks=None, y_tok=None,
                     dd_tok=None, x_tok=None, pq_tok=None):
    if chunks is None:
        chunks = list(range(NCH))
    else:
        chunks = list(chunks)
    t_aff = None
    last_store = None
    st_tok = {}
    prev_v = {}
    batch_tok = [pq_tok]
    x_tok = tuple(x_tok) if isinstance(x_tok, (tuple, list)) else (x_tok,)
    for c in chunks:
        xcs = sb.xc[:, 1024 * (c % 2):1024 * (c % 2 + 1)]
        row0 = seq * S + 128 * c
        lane = "xc%d" % (c % 2)
        pv = prev_v.get(c % 2, ())
        tld = kb.op("sp", lambda e, xcs=xcs, row0=row0, x_src=x_src: e.dma_start(
            xcs, x_src[row0:row0 + 128, :]), waits=pv + x_tok, dma=lane)
        tv = tld
        if mode == "delta":
            dtb = sb.dtb[:, 1024 * (c % 2):1024 * (c % 2 + 1)]
            tdt = kb.op("sp", lambda e, dtb=dtb, c=c: e.dma_start(
                dtb, io.dd[:, 128 * c:128 * (c + 1)], transpose=True),
                waits=pv + (dd_tok,), dma=lane)
            tv = kb.op("dve", lambda e, xcs=xcs, dtb=dtb: e.scalar_tensor_tensor(
                xcs, dtb, scale_delta, xcs, ALU.mult, ALU.add), waits=(tld, tdt), sig="dve")
        elif mode == "y":
            ysl = sb.POf[:, 1024 * (c % 8):1024 * (c % 8 + 1)]
            tv = kb.op("dve", lambda e, xcs=xcs, ysl=ysl: e.scalar_tensor_tensor(
                xcs, ysl, 0.5, xcs, ALU.mult, ALU.add), waits=(tld, y_tok), sig="dve")
        elif mode == "l0inner":
            dtb = sb.dtb[:, 1024 * (c % 2):1024 * (c % 2 + 1)]
            tdt = kb.op("sp", lambda e, dtb=dtb, c=c: e.dma_start(
                dtb, io.dd[:, 128 * c:128 * (c + 1)], transpose=True),
                waits=pv + (dd_tok,), dma=lane)
            tt = sb.tt[:, 0:1024]
            kb.op("dve", lambda e, xcs=xcs, tt=tt, c=c: e.tensor_scalar(
                tt, xcs, sb.stats[:, 2 * c:2 * c + 1], sb.rstd16[:, c:c + 1],
                ALU.subtract, ALU.mult), waits=(tld,), sig="dve")
            kb.op("dve", lambda e, tt=tt, dtb=dtb: e.tensor_tensor(
                tt, tt, dtb, ALU.add), waits=(tdt,), sig="dve")
            kb.op("dve", lambda e, tt=tt: e.bn_stats(sb.bn6[:, 0:6], tt[:, 0:512]), sig="dve")
            tb1 = kb.op("dve", lambda e, tt=tt: e.bn_stats(sb.bn6[:, 6:12], tt[:, 512:1024]), sig="dve")
            tb2 = kb.op("dve", lambda e: e.bn_aggr(sb.sc1[:, 2:4], sb.bn6[:]),
                        hard=(tb1,), sig="dve")
            tv1 = kb.op("dve", lambda e: e.tensor_scalar(
                sb.sc1[:, 4:5], sb.sc1[:, 3:4], LN_EPS, None, ALU.add), hard=(tb2,), sig="dve")
            tl1 = kb.op("act", lambda e: e.activation(sb.sc1[:, 5:6], sb.sc1[:, 4:5], AF.Ln,
                                                      bias=0.0, scale=1.0), waits=(tv1,), sig="act")
            tl2 = kb.op("act", lambda e: e.activation(sb.sc1[:, 6:7], sb.sc1[:, 5:6], AF.Exp,
                                                      bias=0.0, scale=-0.5), hard=(tl1,), sig="act")
            kb.op("dve", lambda e, tt=tt: e.tensor_scalar(
                tt, tt, sb.sc1[:, 2:3], sb.sc1[:, 6:7], ALU.subtract, ALU.mult),
                hard=(tl2, tb2), sig="dve")
            kb.op("dve", lambda e, tt=tt: e.tensor_tensor(tt, tt, sb.gamma_rep[:], ALU.mult),
                  sig="dve")
            kb.op("dve", lambda e, tt=tt: e.tensor_tensor(tt, tt, sb.beta_rep[:], ALU.add),
                  sig="dve")
            tv = kb.op("dve", lambda e, xcs=xcs, tt=tt: e.tensor_tensor(xcs, xcs, tt, ALU.add),
                       sig="dve")
        last_store = kb.op("pool", lambda e, xcs=xcs, row0=row0, x_dst=x_dst: e.dma_start(
            x_dst[row0:row0 + 128, :], xcs), waits=(tv,), dma="xst%d" % (c % 2))
        st_tok[c % 2] = last_store
        if want_ln:
            kb.op("dve", lambda e, xcs=xcs: e.bn_stats(sb.bn6[:, 0:6], xcs[:, 0:512]),
                  waits=(tv,), sig="dve")
            tb1 = kb.op("dve", lambda e, xcs=xcs: e.bn_stats(sb.bn6[:, 6:12], xcs[:, 512:1024]),
                  sig="dve")
            tb2 = kb.op("dve", lambda e, c=c: e.bn_aggr(sb.stats[:, 2 * c:2 * c + 2], sb.bn6[:]),
                        hard=(tb1,), sig="dve")
            tv1 = kb.op("dve", lambda e, c=c: e.tensor_scalar(
                sb.sc1[:, 8:9], sb.stats[:, 2 * c + 1:2 * c + 2], LN_EPS, None, ALU.add),
                hard=(tb2,), sig="dve")
            tl1 = kb.op("act", lambda e: e.activation(
                sb.sc1[:, 9:10], sb.sc1[:, 8:9], AF.Ln, bias=0.0, scale=1.0),
                waits=(tv1,), sig="act")
            tl2 = kb.op("act", lambda e, c=c: e.activation(
                sb.rstd16[:, c:c + 1], sb.sc1[:, 9:10], AF.Exp, bias=0.0, scale=-0.5),
                hard=(tl1,), sig="act")
            t_aff = kb.op("dve", lambda e, xcs=xcs, c=c: e.tensor_scalar(
                sb.PQ[:, 1024 * (c % 8):1024 * (c % 8 + 1)], xcs,
                sb.stats[:, 2 * c:2 * c + 1], sb.rstd16[:, c:c + 1],
                ALU.subtract, ALU.mult),
                waits=(px_free, batch_tok[0]),
                hard=(tl2, tb2), sig="dve")
            prev_v[c % 2] = (last_store, t_aff)
            if want_ln and c % 8 == 7:
                c0b = c - 7
                batch_tok[0] = kb.op("pool", lambda e, c0b=c0b, seq=seq: e.dma_start(
                    io.xlnd[seq * S + 128 * c0b:seq * S + 128 * (c0b + 8), :]
                    .rearrange("(c p) n -> p c n", p=128),
                    sb.PQ[:, 0:8192].rearrange("p (c n) -> p c n", n=1024)),
                    waits=(t_aff,), dma="xlnd")
        else:
            prev_v[c % 2] = (last_store, tv)
    ret_store = (st_tok.get(0), st_tok.get(1))
    if not want_ln:
        return ret_store, None
    return ret_store, batch_tok[0]


def load_px(kb, io, sb, seq, t_ready, px_free=None, src=None):
    t = None
    xl = src if src is not None else io.xlnd
    for dc in range(8):
        t = kb.op("sp", lambda e, dc=dc, xl=xl, seq=seq: e.dma_start(
            sb.PX[:, 2048 * dc:2048 * (dc + 1)],
            xl[seq * S:(seq + 1) * S, 128 * dc:128 * (dc + 1)], transpose=True),
            waits=(t_ready, px_free), dma="px")
    return t



def fm_proj(kb, sb, ps, dst, reg, wtok, t_px, bias_col=None,
            transposed=False, war0=None, evac_eng="dve"):
    ev = {}
    t_last = None
    for fc in range(8):
        for slab in range(4):
            pa = ps.A[:, 512 * slab:512 * (slab + 1)]
            for dc in range(8):
                if transposed:
                    off = reg + 1024 * dc + 512 * (fc // 4) + 128 * (fc % 4)
                else:
                    off = reg + 1024 * dc + 128 * fc
                lhsT = sb.PO[:, off:off + 128]
                w0 = (wtok, ev.get(slab), t_px, war0) if dc == 0 else ()
                tmm = kb.op("pe", lambda e, pa=pa, lhsT=lhsT, dc=dc, slab=slab: e.matmul(
                    pa, lhsT, sb.PX[:, 2048 * dc + 512 * slab:2048 * dc + 512 * (slab + 1)],
                    start=(dc == 0), stop=(dc == 7)),
                    waits=w0, sig="pe" if dc == 7 else None)
            if bias_col is not None:
                ev[slab] = kb.op("dve", lambda e, fc=fc, slab=slab, pa=pa, dst=dst: e.tensor_scalar(
                    dst[:, 2048 * fc + 512 * slab:2048 * fc + 512 * (slab + 1)], pa,
                    sb.biasf[:, bias_col + fc:bias_col + fc + 1], None, ALU.add),
                    waits=(tmm,), sig="dve")
            elif evac_eng == "dve":
                ev[slab] = kb.op("dve", lambda e, fc=fc, slab=slab, pa=pa, dst=dst: e.tensor_copy(
                    dst[:, 2048 * fc + 512 * slab:2048 * fc + 512 * (slab + 1)], pa),
                    waits=(tmm,), sig="dve")
            else:
                ev[slab] = kb.op("act", lambda e, fc=fc, slab=slab, pa=pa, dst=dst: e.activation(
                    dst[:, 2048 * fc + 512 * slab:2048 * fc + 512 * (slab + 1)], pa,
                    AF.Copy, bias=0.0, scale=1.0), waits=(tmm,), sig="act")
            t_last = ev[slab]
    return t_last



def stage_ffn0(kb, io, sb, ps, t_px, wfree=None):
    t_w = None
    for dc in range(8):
        t_w = kb.op("sp", lambda e, dc=dc: e.dma_start(
            sb.PK[:, 1024 * dc:1024 * (dc + 1)], io.sc["kq"][128 * dc:128 * (dc + 1), :]),
            waits=(wfree,), dma="kq")
    aT = sb.PK[:, 9216:9344]
    bT = sb.PK[0:16, 8192:9216]
    kb.op("pool", lambda e, aT=aT: e.dma_start(
        aT.rearrange("p (c r) -> p c r", r=16),
        io.w["L0_ffn_a"].rearrange("(c p) r -> p c r", p=128)), waits=(wfree,), dma="pw")
    t_ab2 = kb.op("pool", lambda e, bT=bT: e.dma_start(bT, io.w["L0_ffn_b"]),
                  waits=(wfree,), dma="pw")
    tl1 = []
    for slab in range(4):
        t = None
        for dc in range(8):
            t = kb.op("pe", lambda e, slab=slab, dc=dc, aT=aT: e.matmul(
                ps.B[0:16, 512 * slab:512 * (slab + 1)],
                aT[:, 16 * dc:16 * (dc + 1)],
                sb.PX[:, 2048 * dc + 512 * slab:2048 * dc + 512 * (slab + 1)],
                start=(dc == 0), stop=(dc == 7)),
                waits=(t_px, t_ab2) if dc == 0 else (), sig="pe" if dc == 7 else None)
        tl1.append(t)
    tl1b = []
    for slab in range(4):
        tl1b.append(kb.op("dve", lambda e, slab=slab: e.tensor_copy(
            sb.st[0:16, 512 * slab:512 * (slab + 1)], ps.B[0:16, 512 * slab:512 * (slab + 1)]),
            waits=(tl1[slab],), sig="dve"))
    ev = {}
    t_dd = None
    for fc in range(8):
        for slab in range(4):
            pa = ps.A[:, 512 * slab:512 * (slab + 1)]
            for dc in range(8):
                kb.op("pe", lambda e, fc=fc, slab=slab, dc=dc, pa=pa: e.matmul(
                    pa, sb.PK[:, 1024 * dc + 128 * fc:1024 * dc + 128 * (fc + 1)],
                    sb.PX[:, 2048 * dc + 512 * slab:2048 * dc + 512 * (slab + 1)],
                    start=(dc == 0), stop=False),
                    waits=(t_w, ev.get(slab)) if dc == 0 else ())
            tmm = kb.op("pe", lambda e, fc=fc, slab=slab, pa=pa, bT=bT: e.matmul(
                pa, bT[:, 128 * fc:128 * (fc + 1)],
                sb.st[0:16, 512 * slab:512 * (slab + 1)], start=False, stop=True),
                waits=(tl1b[slab], t_ab2), sig="pe")
            ev[slab] = kb.op("dve", lambda e, slab=slab, fc=fc, pa=pa: e.tensor_scalar(
                sb.st[:, 2048 + 512 * slab:2048 + 512 * (slab + 1)], pa,
                sb.biasf[:, fc:fc + 1], 0.5, ALU.add, ALU.mult),
                waits=(tmm,), sig="dve")
        tsd = kb.op("pool", lambda e, fc=fc: e.dma_start(
            io.dd[128 * fc:128 * (fc + 1), :], sb.st[:, 2048:4096]),
            waits=(ev[0], ev[1], ev[2], ev[3]), dma="ddl")
        for slab in range(4):
            ev[slab] = tsd
        t_dd = tsd
    return t_dd, ("pe", kb.cnt["pe"])



def stage_attn(kb, io, sb, ps, seq, layer, t_px, wfree=None, pv_tok=None):
    L1 = (layer == 1)
    regA, regB = 0, 8192
    SC = 0.125

    def load_direct(scname, reg, wtok=None):
        lane = "qw0" if reg == 0 else "qw1"
        t = None
        for dc in range(8):
            t = kb.op("sp", lambda e, dc=dc, scname=scname, reg=reg: e.dma_start(
                sb.PO[:, reg + 1024 * dc:reg + 1024 * (dc + 1)],
                io.sc[scname][128 * dc:128 * (dc + 1), :]), waits=(wtok,), dma=lane)
        return t

    def load_T(twname, reg, wtok=None):
        lane = "qw0" if reg == 0 else "qw1"
        return kb.op("sp", lambda e, twname=twname, reg=reg: e.dma_start(
            sb.PO[:, reg:reg + 8192], io.tw[twname]),
            waits=(wtok, kb.tw_done), dma=lane)

    if L1:
        wq = load_T("ipq", regA, wtok=wfree)
        tq = fm_proj(kb, sb, ps, sb.PQ, regA, wq, t_px, bias_col=8, transposed=True)
        wk = load_T("ipk", regB)
        tk = fm_proj(kb, sb, ps, sb.PK, regB, wk, t_px, bias_col=16, transposed=True)
    else:
        wq = load_direct("q", regA, wtok=wfree)
        tq = fm_proj(kb, sb, ps, sb.PQ, regA, wq, t_px)
        wk = load_direct("k", regB)
        tk = fm_proj(kb, sb, ps, sb.PK, regB, wk, t_px, evac_eng="act")

    tvm = kb.op("dve", lambda e: e.memset(sb.PV[:], 1.0), waits=(pv_tok,), sig="dve")
    if L1:
        wv = load_T("ipv", regA, wtok=tq)
    else:
        wv = load_direct("v", regA, wtok=tq)
    ev = {}
    t_v = None
    for tc in range(NCH):
        for half in range(2):
            bank = (tc * 2 + half) % 4
            pb = ps.B[:, 512 * bank:512 * (bank + 1)]
            for dc in range(8):
                lhsT = sb.PX[:, 2048 * dc + 128 * tc:2048 * dc + 128 * (tc + 1)]
                w0 = (wv, ev.get(bank)) if dc == 0 else ()
                kb.op("pe", lambda e, pb=pb, lhsT=lhsT, dc=dc, half=half: e.matmul(
                    pb, lhsT,
                    sb.PO[:, regA + 1024 * dc + 512 * half:regA + 1024 * dc + 512 * (half + 1)],
                    start=(dc == 0), stop=(dc == 7 and not L1)),
                    waits=w0, sig="pe" if (dc == 7 and not L1) else None)
            if L1:
                tmm = kb.op("pe", lambda e, pb=pb, half=half: e.matmul(
                    pb, sb.ones_row_b[:],
                    sb.vb_row[0:1, 512 * half:512 * (half + 1)],
                    start=False, stop=True), sig="pe")
            else:
                tmm = ("pe", kb.cnt["pe"])
            ev_t = None
            for hh in range(8):
                h = half * 8 + hh
                ev_t = kb.op("dve", lambda e, pb=pb, tc=tc, h=h, hh=hh: e.tensor_copy(
                    sb.PV[:, 1040 * tc + 65 * h:1040 * tc + 65 * h + 64],
                    pb[:, 64 * hh:64 * (hh + 1)]), waits=(tmm, tvm), sig="dve")
            ev[bank] = ev_t
            t_v = ev_t

    t_cb_prev = None
    sbuf_free = {}
    t_px_writes = []
    t_cb = None
    t_opx = {}
    pend_fin = [None]

    def flush_fin():
        f = pend_fin[0]
        pend_fin[0] = None
        if f:
            f()
    hq = 0
    for h in range(16):
        fc, pb0 = h // 2, 64 * (h % 2)
        for qh in range(2):
            par = hq % 2
            ob = ps.B[0:65, 1024 * par:1024 * par + 1024]
            kcs = list(range(NCH)) if not L1 else list(range(8 * qh + 8))
            pend_pv = []
            t_pv = None

            def emit_pv(first_overall, last_overall):
                nonlocal t_pv
                kc, te, tz, pt = pend_pv.pop(0)
                for j in range(2):
                    last = (j == 1 and last_overall)
                    t_pv = kb.op("pe", lambda e, kc=kc, h=h, j=j, pt=pt, fo=(first_overall and j == 0 or first_overall and j == 1), first=first_overall, last2=last_overall, ob=ob: e.matmul(
                        ob[:, 512 * j:512 * (j + 1)],
                        sb.PV[:, 1040 * kc + 65 * h:1040 * kc + 65 * (h + 1)],
                        pt[:, 512 * j:512 * (j + 1)], start=first, stop=last2),
                        waits=(te, tz, t_v, t_opx.get(par) if first_overall else None),
                        sig="pe" if last else None)
                sbuf_free[("pt", kc % (2 if L1 else 3))] = ("pe", kb.cnt["pe"])

            for i, kc in enumerate(kcs):
                if i == 4:
                    flush_fin()
                sbi = kc % 2
                pA = ps.A[:, 1024 * sbi:1024 * (sbi + 1)]
                dprime = kc - 8 * qh if L1 else -1
                qlo = 128 * dprime if (L1 and dprime >= 0) else 0
                tsm = None
                for j in range(2):
                    c0, c1 = max(512 * j, qlo), 512 * (j + 1)
                    if c0 >= c1:
                        continue
                    tsm = kb.op("pe", lambda e, pA=pA, fc=fc, pb0=pb0, kc=kc, qh=qh, c0=c0, c1=c1: e.matmul(
                        pA[:, c0:c1],
                        sb.PK[pb0:pb0 + 64, 2048 * fc + 128 * kc:2048 * fc + 128 * (kc + 1)],
                        sb.PQ[pb0:pb0 + 64, 2048 * fc + 1024 * qh + c0:2048 * fc + 1024 * qh + c1],
                        start=True, stop=True, tile_position=(pb0, 0)),
                        waits=(tq, tk, sbuf_free.get(sbi),
                               t_cb if ((not L1) and qh == 0 and kc == 1) else None),
                        sig="pe")
                if (not L1) and qh == 0 and kc == 0:
                    tr = kb.op("dve", lambda e, pA=pA: e.reduce_max(
                        sb.sc1[0:1, 16:17], pA[0:1, 0:1024], axis=AX.X, negate=True),
                        waits=(tsm,), sig="dve")
                    tr2 = kb.op("dve", lambda e: e.tensor_scalar(
                        sb.sc1[0:1, 17:18], sb.sc1[0:1, 16:17], SC, None, ALU.mult),
                        hard=(tr,), sig="dve")
                    tr3 = kb.op("dve", lambda e: e.tensor_copy(sb.rowb[0:1, 0:1], sb.sc1[0:1, 17:18]),
                                hard=(tr2,), sig="dve")
                    trm = kb.op("pe", lambda e: e.matmul(
                        ps.A[:, 2047:2048], sb.ones_row_b[:], sb.rowb[0:1, 0:1],
                        start=True, stop=True), waits=(tr3, sbuf_free.get(1)), sig="pe")
                    t_cb = kb.op("dve", lambda e: e.tensor_copy(sb.cb[:, 0:1], ps.A[:, 2047:2048]),
                                 waits=(trm,), hard=(t_cb_prev,), sig="dve")
                if L1 and dprime >= 0:
                    tsm = kb.op("dve", lambda e, pA=pA, qlo=qlo: e.tensor_tensor(
                        pA[:, qlo:qlo + 128], pA[:, qlo:qlo + 128], sb.cmask[:], ALU.add),
                        waits=(tsm,), sig="dve")
                sb3 = kc % (2 if L1 else 3)
                pt = sb.PO[:, 1024 * sb3:1024 * (sb3 + 1)]
                tz = None
                if qlo > 0:
                    tz = kb.op("pool", lambda e, pt=pt, qlo=qlo: e.memset(pt[:, 0:qlo], 0.0),
                               waits=(sbuf_free.get(("pt", sb3)) or t_v,), sig="pool")
                if L1:
                    te = kb.op("act", lambda e, pA=pA, pt=pt, qlo=qlo: e.activation(
                        pt[:, qlo:1024], pA[:, qlo:1024], AF.Exp, bias=0.0, scale=SC),
                        waits=(tsm, sbuf_free.get(("pt", sb3))), sig="act")
                else:
                    te = kb.op("act", lambda e, pA=pA, pt=pt: e.activation(
                        pt[:], pA[:], AF.Exp, bias=sb.cb[:, 0:1], scale=SC),
                        waits=(tsm, sbuf_free.get(("pt", sb3))), hard=(t_cb,), sig="act")
                t_cb_prev = te
                sbuf_free[sbi] = te
                pend_pv.append((kc, te, tz, pt))
                if L1:
                    emit_pv(i == 0, i == len(kcs) - 1)
                elif len(pend_pv) > 2:
                    emit_pv(i == 2, False)
            while pend_pv:
                emit_pv(len(kcs) <= 2 and len(pend_pv) == len(kcs), len(pend_pv) == 1)
            def fin(ob=ob, par=par, fc=fc, pb0=pb0, qh=qh, t_pv=t_pv):
                stv = sb.st[64:128, 1024 * par:1024 * (par + 1)]
                t0 = kb.op("dve", lambda e, ob=ob: e.tensor_copy(
                    sb.row, ob[64:65, 0:1024]), waits=(t_pv,), sig="dve")
                tex = kb.op("dve", lambda e: e.reciprocal(sb.row, sb.row),
                            hard=(t0,), sig="dve")
                tev = kb.op("act", lambda e, ob=ob, stv=stv: e.activation(
                    stv, ob[0:64, 0:1024], AF.Copy, bias=0.0, scale=1.0),
                    waits=(t_pv, t_opx.get(par)), sig="act")
                trep = []
                for j in range(2):
                    trep.append(kb.op("pe", lambda e, j=j, ob=ob: e.matmul(
                        ob[0:64, 512 * j:512 * (j + 1)],
                        sb.ones_row_f[0:1, 0:64], sb.packA[0:1, 512 * j:512 * (j + 1)],
                        start=True, stop=True), waits=(tex, tev), sig="pe"))
                t_ox = kb.op("dve", lambda e, fc=fc, pb0=pb0, qh=qh, ob=ob, stv=stv: e.tensor_tensor(
                    sb.PX[pb0:pb0 + 64, 2048 * fc + 1024 * qh:2048 * fc + 1024 * (qh + 1)],
                    stv, ob[0:64, 0:1024], ALU.mult),
                    waits=(tev, trep[0], trep[1]), sig="dve")
                t_px_writes.append(t_ox)
                t_opx[par] = t_ox
            pend_fin[0] = fin
            hq += 1

    flush_fin()
    if L1:
        wo = load_T("op", regB, wtok=tk)
    else:
        wo = load_direct("o", regB, wtok=tk)
    t_all_px = t_px_writes[-1]
    ev2 = {}
    t_dd = None
    for fc in range(8):
        for slab in range(4):
            pa = ps.A[:, 512 * slab:512 * (slab + 1)]
            for dc in range(8):
                if L1:
                    off = regB + 1024 * dc + 512 * (fc // 4) + 128 * (fc % 4)
                else:
                    off = regB + 1024 * dc + 128 * fc
                lhsT = sb.PO[:, off:off + 128]
                w0 = (wo, ev2.get(slab), t_all_px, sbuf_free.get(0), sbuf_free.get(1)) \
                    if dc == 0 else ()
                tmm = kb.op("pe", lambda e, pa=pa, lhsT=lhsT, dc=dc, slab=slab: e.matmul(
                    pa, lhsT, sb.PX[:, 2048 * dc + 512 * slab:2048 * dc + 512 * (slab + 1)],
                    start=(dc == 0), stop=(dc == 7)),
                    waits=w0, sig="pe" if dc == 7 else None)
            if L1:
                ev2[slab] = kb.op("dve", lambda e, slab=slab, fc=fc, pa=pa: e.tensor_scalar(
                    sb.st[:, 2048 + 512 * slab:2048 + 512 * (slab + 1)], pa,
                    sb.biasf[:, 24 + fc:24 + fc + 1], None, ALU.add),
                    waits=(tmm,), sig="dve")
            else:
                ev2[slab] = kb.op("dve", lambda e, slab=slab, pa=pa: e.tensor_copy(
                    sb.st[:, 2048 + 512 * slab:2048 + 512 * (slab + 1)], pa),
                    waits=(tmm,), sig="dve")
        tsd = kb.op("pool", lambda e, fc=fc: e.dma_start(
            io.dd[128 * fc:128 * (fc + 1), :], sb.st[:, 2048:4096]),
            waits=(ev2[0], ev2[1], ev2[2], ev2[3]), dma="ddl")
        for slab in range(4):
            ev2[slab] = tsd
        t_dd = tsd
    return t_dd, ("pe", kb.cnt["pe"])



def stage_gate(kb, io, sb, ps, seq, t_px):
    t_prev = {0: None, 1: None, 2: None, 3: None}
    selb_prev = {0: None, 1: None, 2: None, 3: None}
    te1_prev = [None]
    for tc in range(NCH):
        slot = seq * NCH + tc
        par = tc % 4
        pg = ps.B[:, 1024 + 16 * par:1032 + 16 * par]
        tg = None
        for dc in range(8):
            tg = kb.op("pe", lambda e, dc=dc, tc=tc, pg=pg: e.matmul(
                pg, sb.PX[:, 2048 * dc + 128 * tc:2048 * dc + 128 * (tc + 1)],
                sb.gwT[:, 16 * dc:16 * dc + 8], start=(dc == 0), stop=(dc == 7)),
                waits=(t_px, t_prev[par]) if dc == 0 else (), sig="pe" if dc == 7 else None)
        g = sb.g8
        tm0 = kb.op("dve", lambda e, pg=pg: e.reduce_max(g[:, 8:9], pg, axis=AX.X, negate=True),
                    waits=(tg,), hard=(te1_prev[0],), sig="dve")
        te1 = kb.op("act", lambda e, pg=pg: e.activation(g[:, 16:24], pg, AF.Exp,
                                                         bias=g[:, 8:9], scale=1.0),
                    waits=(tg,), hard=(tm0,), sig="act")
        te1_prev[0] = te1
        ti1 = kb.op("dve", lambda e, pg=pg: e.tensor_scalar(g[:, 40:48], pg, g[:, 8:9],
                                                            0.0, ALU.add, ALU.is_ge),
                    hard=(tm0,), sig="dve")
        tk0 = kb.op("dve", lambda e, pg=pg: e.scalar_tensor_tensor(g[:, 48:56], g[:, 40:48], -BIG,
                                                                   pg, ALU.mult, ALU.add),
                    hard=(ti1,), sig="dve")
        tm2 = kb.op("dve", lambda e: e.reduce_max(g[:, 12:13], g[:, 48:56], axis=AX.X),
                    hard=(tk0,), sig="dve")
        tsel = kb.op("dve", lambda e, pg=pg: e.tensor_scalar(g[:, 56:64], pg, g[:, 12:13],
                                                             None, ALU.is_ge), hard=(tm2,), sig="dve")
        tt0 = kb.op("dve", lambda e: e.tensor_tensor(g[:, 24:32], g[:, 56:64], g[:, 16:24],
                                                     ALU.mult), waits=(te1,), hard=(tsel,), sig="dve")
        ts2 = kb.op("dve", lambda e: e.reduce_sum(g[:, 13:14], g[:, 24:32], axis=AX.X),
                    hard=(tt0,), sig="dve")
        tr2 = kb.op("dve", lambda e: e.reciprocal(g[:, 15:16], g[:, 13:14]), hard=(ts2,), sig="dve")
        twf = kb.op("dve", lambda e, slot=slot: e.tensor_scalar(
            sb.wfull[:, 8 * (slot % 64):8 * (slot % 64) + 8], g[:, 24:32], g[:, 15:16],
            None, ALU.mult), hard=(tr2,), sig="dve")
        sel = sb.selb[:, 8 * par:8 * (par + 1)]
        tsb = kb.op("dve", lambda e, sel=sel: e.tensor_copy(sel, g[:, 56:64]),
                    waits=(selb_prev[par],), hard=(tsel,), sig="dve")
        pp = ps.B[:, 1536 + 16 * par:1544 + 16 * par]
        tpm = kb.op("pe", lambda e, pp=pp, sel=sel: e.matmul(pp, sb.u128[:], sel,
                                                             start=True, stop=True),
                    waits=(tsb, t_prev[par]), sig="pe")
        tpe = kb.op("dve", lambda e, slot=slot, pp=pp: e.tensor_copy(
            sb.pref[:, 8 * (slot % 64):8 * (slot % 64) + 8], pp), waits=(tpm,), sig="dve")
        pt2 = ps.B[0:1, 1600 + 16 * par:1608 + 16 * par]
        ttm = kb.op("pe", lambda e, pt2=pt2, sel=sel: e.matmul(pt2, sb.ones_col_b[:], sel,
                                                               start=True, stop=True),
                    waits=(tsb,), sig="pe")
        tte = kb.op("dve", lambda e, slot=slot, pt2=pt2: e.tensor_copy(
            sb.Trow[0:1, 8 * (slot % 64):8 * (slot % 64) + 8], pt2), waits=(ttm,), sig="dve")
        t_prev[par] = tte
        selb_prev[par] = ("pe", kb.cnt["pe"])
    return tte, ("pe", kb.cnt["pe"])


def dump_gate(kb, io, sb, t_gate):
    t1 = kb.op("pool", lambda e: e.dma_start(io.gd_wfull, sb.wfull[:]), waits=(t_gate,), dma="ddl")
    t2 = kb.op("pool", lambda e: e.dma_start(io.gd_pref, sb.pref[:]), waits=(t_gate,), dma="ddl")
    t3 = kb.op("pool", lambda e: e.dma_start(io.gd_trow, sb.Trow), waits=(t_gate,), dma="ddl")
    return t3


def stage_counts_out(kb, io, sb, ps, t_gate, fused):
    tG = kb.op("dve", lambda e: e.tensor_reduce(
        sb.packA[0:1, 0:8], sb.Trow.rearrange("o (s e) -> o e s", e=8), AX.X, ALU.add),
        waits=(t_gate,), sig="dve")
    if not fused:
        return kb.op("pool", lambda e: e.dma_start(io.ccrow, sb.packA[0:1, 0:8]),
                     waits=(tG,), dma="mscp")
    tmr = kb.op("pe", lambda e: e.matmul(ps.B[0:8, 1024:1032], sb.ones_row_f[0:1, 0:8],
                                         sb.packA[0:1, 0:8], start=True, stop=True),
                waits=(tG,), sig="pe")
    tcc = kb.op("dve", lambda e: e.tensor_scalar(sb.CC[:], ps.B[0:8, 1024:1032],
                                                 sb.rowsel[:], None, ALU.mult),
                waits=(tmr,), sig="dve")
    tst = kb.op("pool", lambda e: e.dma_start(io.ccin_d, sb.CC[:]), waits=(tcc,), dma="mscp")
    tcol = kb.op("pool", lambda e: e.collective_compute(
        "AllReduce", ALU.add, replica_groups=[list(range(NCORES))],
        ins=[io.ccin_d.opt()], outs=[io.ccout_d.opt()]),
        waits=(tst,), sig="cc")
    return tcol


def stage_offsets(kb, io, sb, ps, t_cc, fused):
    cc_src = io.ccout_d if fused else io.cc
    tld = kb.op("sp", lambda e, cc_src=cc_src: e.dma_start(sb.CC[:], cc_src),
                waits=(t_cc,), dma="msc")
    tt1 = kb.op("pool", lambda e: e.dma_start(
        io.t64d.rearrange("p n -> (p n)").rearrange("(o x) -> o x", o=1), sb.Trow),
        waits=(t_cc,), dma="mscp")
    tt2 = kb.op("sp", lambda e: e.dma_start(sb.sc1[0:64, 16:24], io.t64d), waits=(tt1,), dma="msc")
    tt3 = kb.op("dve", lambda e: e.tensor_copy(sb.st[0:64, 0:8], sb.sc1[0:64, 16:24]),
                waits=(tt2,), sig="dve")
    tof = kb.op("pe", lambda e: e.matmul(ps.B[0:64, 1024:1032], sb.u128[0:64, 0:64],
                                         sb.st[0:64, 0:8], start=True, stop=True),
                waits=(tt3,), sig="pe")
    tofe = kb.op("dve", lambda e: e.tensor_copy(sb.sc1[0:64, 24:32], ps.B[0:64, 1024:1032]),
                 waits=(tof,), sig="dve")
    tco = kb.op("pe", lambda e: e.matmul(ps.B[0:1, 1536:1544], sb.rankmask[:], sb.CC[:],
                                         start=True, stop=True), waits=(tld,), sig="pe")
    tcoe = kb.op("dve", lambda e: e.tensor_copy(sb.packA[0:1, 8:16], ps.B[0:1, 1536:1544]),
                 waits=(tco,), sig="dve")
    tcor = kb.op("pe", lambda e: e.matmul(ps.B[0:64, 1032:1040], sb.ones_row_f[0:1, 0:64],
                                          sb.packA[0:1, 8:16], start=True, stop=True),
                 waits=(tcoe,), sig="pe")
    toff = kb.op("dve", lambda e: e.tensor_tensor(sb.sc1[0:64, 24:32], sb.sc1[0:64, 24:32],
                                                  ps.B[0:64, 1032:1040], ALU.add),
                 waits=(tcor,), hard=(tofe,), sig="dve")
    tst = kb.op("pool", lambda e: e.dma_start(
        io.offd.rearrange("o (p n) -> (o p) n", n=8)[0:64, :], sb.sc1[0:64, 24:32]),
        waits=(toff,), dma="mscp")
    trd = kb.op("sp", lambda e: e.dma_start(sb.OFFROW, io.offd), waits=(tst,), dma="msc")
    return trd


def stage_comb(kb, io, sb, ps, seq, t_off, t_wp_ready):
    t_last = None
    t_prev = None
    for tc in range(NCH):
        slot = seq * NCH + tc
        po = ps.B[:, 1032 + 8 * (tc % 2):1040 + 8 * (tc % 2)]
        trm = kb.op("pe", lambda e, slot=slot, po=po: e.matmul(
            po, sb.ones_row_f[:], sb.OFFROW[0:1, 8 * slot:8 * slot + 8],
            start=True, stop=True), waits=(t_off, t_prev), sig="pe")
        t1 = kb.op("dve", lambda e, slot=slot, po=po: e.tensor_tensor(
            sb.g8[:, 0:8], sb.pref[:, 8 * (slot % 64):8 * (slot % 64) + 8], po, ALU.add),
            waits=(trm, t_wp_ready), sig="dve")
        t2 = kb.op("dve", lambda e: e.tensor_scalar(sb.g8[:, 8:16], sb.g8[:, 0:8], CAP,
                                                    None, ALU.is_lt), hard=(t1,), sig="dve")
        t_last = kb.op("dve", lambda e, slot=slot: e.tensor_tensor(
            sb.comb[:, 8 * (slot % 64):8 * (slot % 64) + 8],
            sb.g8[:, 8:16], sb.wfull[:, 8 * (slot % 64):8 * (slot % 64) + 8], ALU.mult),
            hard=(t2,), sig="dve")
        t_prev = t_last
    return t_last



def stage_moe(kb, io, sb, ps, seq, t_px, t_comb, x_src, x_dst, want_ln, px_free_next,
              x_tok=None):
    t_store = None
    t_aff_all = []
    mcount = [0]
    t_matdone = {}
    y_done = None
    war = [("dve", kb.cnt["dve"])]

    def wreg():
        return 8192 * (mcount[0] % 2)

    def wload_img(img_ap):
        m = mcount[0]
        reg = 8192 * (m % 2)
        lane = "lw%d" % (m % 2)
        wtok = t_matdone.get(m - 2, px_free_next)
        t = kb.op("sp", lambda e, img_ap=img_ap, reg=reg: e.dma_start(
            sb.PK[:, reg:reg + 8192], img_ap), waits=(wtok, kb.tw_done), dma=lane)
        mcount[0] += 1
        return reg, t

    for half in range(2):
        toks = 1024 * half

        def gu_phase(wg_reg, wg_toks, n_ic, ic0, psum_off, dst_fn, extra_wait=None):
            ev = {}
            for i in range(n_ic):
                ic = ic0 + i
                P = ps.A if ic % 2 == 0 else ps.B
                for half2 in range(2):
                    pg = P[:, psum_off + 512 * half2:psum_off + 512 * (half2 + 1)]
                    for bc in range(8):
                        lhsT = sb.PK[:, wg_reg + 1024 * bc + 128 * (ic % 8):
                                     wg_reg + 1024 * bc + 128 * (ic % 8 + 1)]
                        w0 = (wg_toks, ev.get((ic % 2, half2)), t_px, extra_wait) \
                            if bc == 0 else ()
                        kb.op("pe", lambda e, pg=pg, lhsT=lhsT, bc=bc, half2=half2, toks=toks: e.matmul(
                            pg, lhsT,
                            sb.PX[:, 2048 * bc + toks + 512 * half2:2048 * bc + toks + 512 * (half2 + 1)],
                            start=(bc == 0), stop=(bc == 7)),
                            waits=w0, sig="pe" if bc == 7 else None)
                    tgm = ("pe", kb.cnt["pe"])
                    ev[(ic % 2, half2)] = dst_fn(ic, half2, pg, tgm)
            return ev

        def silu_dst(ic, half2, pg, tgm):
            return kb.op("act", lambda e, pg=pg, ic=ic, half2=half2: e.activation(
                sb.PQ[:, 8192 + 1024 * (ic % 8) + 512 * half2:8192 + 1024 * (ic % 8) + 512 * (half2 + 1)],
                pg, AF.Silu, bias=0.0, scale=1.0), waits=(tgm, kb.tw_done), sig="act")

        def hs_dst(ic, half2, pg, tgm, tsil):
            return kb.op("dve", lambda e, pg=pg, ic=ic, half2=half2: e.tensor_tensor(
                sb.PV[:, 1024 * ic + 512 * half2:1024 * ic + 512 * (half2 + 1)],
                sb.PQ[:, 8192 + 1024 * (ic % 8) + 512 * half2:8192 + 1024 * (ic % 8) + 512 * (half2 + 1)],
                pg, ALU.mult), waits=(tgm, tsil, kb.tw_done), sig="dve")

        def down_phase(wd_reg, wd_toks, n_ic, comb_e, t_hs_last, nblk1024):
            ev_d = {}
            t_last = None
            for tc in range(8):
                slot = seq * NCH + half * 8 + tc
                pos = tc % 4
                P = ps.A if pos % 2 == 0 else ps.B
                coff = 1024 * (pos // 2)
                for hh in range(2):
                    pd = P[:, coff + 512 * hh:coff + 512 * (hh + 1)]
                    for i in range(n_ic):
                        lw = wd_toks if not isinstance(wd_toks, list) else wd_toks[i]
                        lhsT = sb.PV[:, 1024 * i + 128 * tc:1024 * i + 128 * (tc + 1)]
                        w0 = (lw, ev_d.get((pos, hh)), t_hs_last) if i == 0 else (lw,)
                        roff = wd_reg(i) if callable(wd_reg) else wd_reg
                        kb.op("pe", lambda e, pd=pd, lhsT=lhsT, i=i, hh=hh, roff=roff: e.matmul(
                            pd, lhsT,
                            sb.PK[:, roff + 1024 * (i % 8) + 512 * hh:roff + 1024 * (i % 8) + 512 * (hh + 1)],
                            start=(i == 0), stop=(i == n_ic - 1)),
                            waits=w0, sig="pe" if i == n_ic - 1 else None)
                    tdm = ("pe", kb.cnt["pe"])
                    ysl = sb.POf[:, 1024 * tc + 512 * hh:1024 * tc + 512 * (hh + 1)]
                    if comb_e is not None and comb_e >= 0:
                        ccol = sb.comb[:, 8 * (slot % 64) + comb_e:8 * (slot % 64) + comb_e + 1]
                        tde = kb.op("dve", lambda e, pd=pd, ysl=ysl, ccol=ccol: e.scalar_tensor_tensor(
                            ysl, pd, ccol, ysl, ALU.mult, ALU.add), waits=(tdm,), hard=(t_comb,), sig="dve")
                    elif comb_e == -1:
                        ccol = sb.comb[:, 8 * (slot % 64):8 * (slot % 64) + 1]
                        yw = y_done if isinstance(y_done, tuple) and len(y_done) == 2 \
                            and not isinstance(y_done[0], str) else (y_done,)
                        tde = kb.op("dve", lambda e, pd=pd, ysl=ysl, ccol=ccol: e.tensor_scalar(
                            ysl, pd, ccol, None, ALU.mult),
                            waits=(tdm,) + tuple(yw), hard=(t_comb,), sig="dve")
                    else:
                        tde = kb.op("dve", lambda e, pd=pd, ysl=ysl: e.tensor_tensor(
                            ysl, ysl, pd, ALU.add), waits=(tdm,), sig="dve")
                    ev_d[(pos, hh)] = tde
                    t_last = tde
            return t_last

        for ee in range(8):
            gm = mcount[0]
            reg_g, wg = wload_img(io.tw["weg"][ee])
            ev_g = gu_phase(reg_g, wg, 8, 0, 0, silu_dst, extra_wait=war[0])
            t_matdone[gm] = ("pe", kb.cnt["pe"])
            um = mcount[0]
            reg_u, wu = wload_img(io.tw["weu"][ee])
            evh = gu_phase(reg_u, wu, 8, 0, 1024,
                           lambda ic, half2, pg, tgm: hs_dst(ic, half2, pg, tgm,
                                                             ev_g[(ic % 2, half2)]),
                           extra_wait=war[0])
            t_hs_all = ("dve", kb.cnt["dve"])
            t_matdone[um] = ("pe", kb.cnt["pe"])
            dm = mcount[0]
            reg_d, wd = wload_img(io.tw["wed"][ee])
            down_phase(reg_d, wd, 8, (-1 if ee == 0 else ee), t_hs_all, False)
            t_matdone[dm] = ("pe", kb.cnt["pe"])
            war[0] = ("dve", kb.cnt["dve"])

        t_hs_sh = []
        greg = {}
        for grp in range(2):
            gm = mcount[0]
            reg_g, wg = wload_img(io.tw["wsg"][grp])
            ev_g = gu_phase(reg_g, wg, 8, 8 * grp, 0, silu_dst, extra_wait=war[0])
            t_matdone[gm] = ("pe", kb.cnt["pe"])
            um = mcount[0]
            reg_u, wu = wload_img(io.tw["wsu"][grp])
            evh = gu_phase(reg_u, wu, 8, 8 * grp, 1024,
                           lambda ic, half2, pg, tgm: hs_dst(ic, half2, pg, tgm,
                                                             ev_g[(ic % 2, half2)]),
                           extra_wait=war[0])
            t_hs_sh.append(("dve", kb.cnt["dve"]))
            t_matdone[um] = ("pe", kb.cnt["pe"])
        dregs = []
        wd_last = None
        for grp in range(2):
            dm = mcount[0]
            reg_d, wd = wload_img(io.tw["wsd"][grp])
            dregs.append(reg_d)
            wd_last = wd
            if grp == 0:
                dm0 = dm
        t_down = down_phase(lambda i: dregs[i // 8], wd_last, 16, None,
                            t_hs_sh[-1], False)
        war[0] = ("dve", kb.cnt["dve"])
        t_matdone[dm0] = ("pe", kb.cnt["pe"])
        t_matdone[mcount[0] - 1] = ("pe", kb.cnt["pe"])
        y_done2 = t_down

        t_store, t_xln_h = pass_residual_ln(
            kb, io, sb, ps, seq, "y", want_ln, x_src, x_dst,
            chunks=range(half * 8, half * 8 + 8), y_tok=y_done2,
            px_free=px_free_next if half == 0 else None, x_tok=x_tok,
            pq_tok=t_aff_all[-1] if t_aff_all else None)
        if t_xln_h is not None:
            t_aff_all.append(t_xln_h)
        y_done = t_store
    t_xln = t_aff_all[-1] if t_aff_all else None
    return t_store, t_xln, ("pe", kb.cnt["pe"])



def build_program(phase):
    nc = bass.Bass("TRN2", target_bir_lowering=False)
    io = declare_io(nc, phase)
    with ExitStack() as ctx:
        kb = KB(nc)
        kb.make_sems(ctx)
        sb = alloc_sbuf(nc, ctx)
        ps = alloc_psum(nc, ctx)
        t_pw, t_pr = stage_preprocess(kb, io, sb, ps)
        pre_done = (("pe", kb.cnt["pe"]), ("dve", kb.cnt["dve"]), t_pw, t_pr)
        _phase_body(kb, io, sb, ps, phase, pre_done)
        kb.emit()
    return nc


def _phase_body(kb, io, sb, ps, phase, pre_done):
    fused = (phase == "all")

    def is_final(sl):
        return sl == UPTO

    def final_wait():
        kb.wait("sp", [(lane, kb.cnt[lane]) for lane in
                       ("xst0", "xst1", "xlnd", "ddl", "msc", "mscp", "pw", "pr", "cc")])

    px_free = pre_done[0]
    xtk = None
    jobs = image_jobs(io)
    kb.tw_done = None
    if True:
        stage_images(kb, io, sb, jobs, None)
    q = (len(jobs) + 3) // 4
    jgroups = [jobs[i * q:(i + 1) * q] for i in range(4)]

    if DEVSTOP == 1:
        final_wait()
        return

    if phase in ("p1", "all"):
        t_gate = None
        first = True
        for seq in range(SEQS):
            xtk_in = xtk
            tx, t_xln = pass_residual_ln(kb, io, sb, ps, seq, "init", True,
                                         io.x_in, io.xs,
                                         px_free=pre_done[1] if first else px_free,
                                         x_tok=xtk)
            xtk = tx
            if DEVSTOP == 2:
                final_wait()
                return
            t_px = load_px(kb, io, sb, seq, t_xln, px_free=px_free)
            t_dd, px_free = stage_ffn0(kb, io, sb, ps, t_px,
                                       wfree=pre_done[2] if first else px_free)
            first = False
            if DEVSTOP == 3:
                final_wait()
                return
            if UPTO == 1:
                tx, _ = pass_residual_ln(kb, io, sb, ps, seq, "delta", False,
                                         io.xs, io.out, dd_tok=t_dd, x_tok=xtk)
                xtk = tx
                continue
            tx, t_xln = pass_residual_ln(kb, io, sb, ps, seq, "delta", True,
                                         io.xs, io.xs, px_free=px_free, dd_tok=t_dd,
                                         x_tok=xtk)
            xtk = tx
            t_px = load_px(kb, io, sb, seq, t_xln, px_free=px_free)
            t_dd, px_free = stage_attn(kb, io, sb, ps, seq, 0, t_px, wfree=px_free,
                                       pv_tok=kb.tw_done)
            if UPTO == 2:
                tx, _ = pass_residual_ln(kb, io, sb, ps, seq, "l0inner", False,
                                         io.xs, io.out, dd_tok=t_dd, x_tok=xtk)
                xtk = tx
                continue
            tx, t_xln = pass_residual_ln(kb, io, sb, ps, seq, "l0inner", True,
                                         io.xs, io.xs, px_free=px_free, dd_tok=t_dd,
                                         x_tok=xtk)
            xtk = tx
            t_px = load_px(kb, io, sb, seq, t_xln, px_free=px_free)
            t_dd, px_free = stage_ffn0(kb, io, sb, ps, t_px, wfree=px_free)
            if UPTO == 3:
                tx, _ = pass_residual_ln(kb, io, sb, ps, seq, "delta", False,
                                         io.xs, io.out, dd_tok=t_dd, x_tok=xtk)
                xtk = tx
                continue
            tx, t_xln = pass_residual_ln(kb, io, sb, ps, seq, "delta", True,
                                         io.xs, io.xs, px_free=px_free, dd_tok=t_dd,
                                         x_tok=xtk)
            xtk = tx
            t_px = load_px(kb, io, sb, seq, t_xln, px_free=px_free)
            t_gate, px_free = stage_gate(kb, io, sb, ps, seq, t_px)
        if UPTO <= 3:
            final_wait()
            return
        t_gd = dump_gate(kb, io, sb, t_gate)
        t_cc = stage_counts_out(kb, io, sb, ps, t_gate, fused)
        if phase == "p1":
            final_wait()
            return
        t_p1_st = ("xlnd", kb.cnt["xlnd"])

    if phase in ("p2", "all"):
        if phase == "p2":
            t1 = kb.op("sp", lambda e: e.dma_start(sb.wfull[:], io.gdi_wfull), dma="msc")
            t2 = kb.op("sp", lambda e: e.dma_start(sb.pref[:], io.gdi_pref), dma="msc")
            t3 = kb.op("sp", lambda e: e.dma_start(sb.Trow, io.gdi_trow), dma="msc")
            t_wp = t3
            t_cc = t3
            xlsrc = io.xlnd_in
            xln_tok = None
            px_free = pre_done[0]
            xtk = None
        else:
            t_wp = None
            xlsrc = io.xlnd
            xln_tok = t_p1_st
        t_off = stage_offsets(kb, io, sb, ps, t_cc, fused)
        t_gate = None
        for seq in range(SEQS):
            t_comb = stage_comb(kb, io, sb, ps, seq, t_off, t_wp)
            t_px = load_px(kb, io, sb, seq, xln_tok if fused else t_wp,
                           px_free=px_free, src=xlsrc)
            xsrc0 = io.xs if fused else io.x_in
            if is_final(4):
                tx, _, px_free = stage_moe(kb, io, sb, ps, seq, t_px, t_comb,
                                           xsrc0, io.out, False, px_free, x_tok=xtk)
                xtk = tx
                continue
            tx, t_xln, px_free = stage_moe(kb, io, sb, ps, seq, t_px, t_comb,
                                           xsrc0, io.xs, True, px_free, x_tok=xtk)
            xtk = tx
            t_px = load_px(kb, io, sb, seq, t_xln, px_free=px_free)
            t_dd, px_free = stage_attn(kb, io, sb, ps, seq, 1, t_px, wfree=px_free, pv_tok=kb.tw_done)
            if is_final(5):
                tx, _ = pass_residual_ln(kb, io, sb, ps, seq, "delta", False,
                                         io.xs, io.out, dd_tok=t_dd, x_tok=xtk)
                xtk = tx
                continue
            tx, t_xln = pass_residual_ln(kb, io, sb, ps, seq, "delta", True,
                                         io.xs, io.xs, px_free=px_free, dd_tok=t_dd,
                                         x_tok=xtk)
            xtk = tx
            t_px = load_px(kb, io, sb, seq, t_xln, px_free=px_free)
            t_gate, px_free = stage_gate(kb, io, sb, ps, seq, t_px)
        if UPTO <= 5:
            final_wait()
            return
        t_gd = dump_gate(kb, io, sb, t_gate)
        t_cc = stage_counts_out(kb, io, sb, ps, t_gate, fused)
        if phase == "p2":
            final_wait()
            return
        t_p2_st = ("xlnd", kb.cnt["xlnd"])

    if phase == "p3":
        t1 = kb.op("sp", lambda e: e.dma_start(sb.wfull[:], io.gdi_wfull), dma="msc")
        t2 = kb.op("sp", lambda e: e.dma_start(sb.pref[:], io.gdi_pref), dma="msc")
        t3 = kb.op("sp", lambda e: e.dma_start(sb.Trow, io.gdi_trow), dma="msc")
        t_wp = t3
        t_cc = t3
        xlsrc = io.xlnd_in
        xsrc = io.x_in
        xln_tok = None
        px_free = pre_done[0]
        xtk = None
    else:
        t_wp = None
        xlsrc = io.xlnd
        xsrc = io.xs
        xln_tok = t_p2_st
    t_off = stage_offsets(kb, io, sb, ps, t_cc, fused)
    for seq in range(SEQS):
        t_comb = stage_comb(kb, io, sb, ps, seq, t_off, t_wp)
        t_px = load_px(kb, io, sb, seq, xln_tok if fused else t_wp,
                       px_free=px_free, src=xlsrc)
        tx, _, px_free = stage_moe(kb, io, sb, ps, seq, t_px, t_comb,
                                   xsrc, io.out, False, px_free, x_tok=xtk)
        xtk = tx
    final_wait()



def host_consts(core):
    k = np.arange(128)[:, None]
    m = np.arange(128)[None, :]
    return {
        "u128": (k < m).astype(ml_dtypes.bfloat16),
        "cmask": np.where(k <= m, 0.0, -BIG).astype(ml_dtypes.bfloat16),
        "rankmask": (np.arange(8) < core).astype(np.float32)[:, None],
        "rowsel": (np.arange(8) == core).astype(np.float32)[:, None],
    }


def kernel(**inputs):
    x_full = np.ascontiguousarray(np.asarray(inputs["L0_attn_h"], dtype=np.float32))
    shards = [np.ascontiguousarray(x_full[4 * c:4 * (c + 1)].reshape(TOK, D))
              for c in range(NCORES)]
    weights = {nm: np.ascontiguousarray(np.asarray(inputs[nm], dtype=np.float32))
               for nm, _ in W_NAMES if nm != "L0_attn_h"}
    weights.pop("L0_attn_h", None)

    def maps(extra_per_core):
        out = []
        for c in range(NCORES):
            m = dict(weights)
            m.update(host_consts(c))
            m.update(extra_per_core[c])
            out.append(m)
        return out

    if FUSED:
        nc = build_program("all")
        in_maps = maps([{"x": shards[c]} for c in range(NCORES)])
        res = run_bass_kernel_spmd(nc, in_maps, core_ids=list(range(NCORES)))
        outs = [np.asarray(r["out"]) for r in res.results]
    else:
        nc1 = build_program("p1")
        in_maps = maps([{"x": shards[c]} for c in range(NCORES)])
        r1 = run_bass_kernel_spmd(nc1, in_maps, core_ids=list(range(NCORES)))
        if UPTO <= 3:
            outs = [np.asarray(r["out"]) for r in r1.results]
        else:
            cc = np.concatenate([np.asarray(r["ccrow"]) for r in r1.results], axis=0)
            nc2 = build_program("p2")
            in_maps = maps([{
                "xs_in": np.asarray(r1.results[c]["xs_out"]),
                "xlnd_in": np.asarray(r1.results[c]["xlnd_out"]),
                "gdi_wfull": np.asarray(r1.results[c]["gdo_wfull"]),
                "gdi_pref": np.asarray(r1.results[c]["gdo_pref"]),
                "gdi_trow": np.asarray(r1.results[c]["gdo_trow"]),
                "cc": cc,
            } for c in range(NCORES)])
            r2 = run_bass_kernel_spmd(nc2, in_maps, core_ids=list(range(NCORES)))
            if UPTO <= 5:
                outs = [np.asarray(r["out"]) for r in r2.results]
            else:
                cc2 = np.concatenate([np.asarray(r["ccrow"]) for r in r2.results], axis=0)
                nc3 = build_program("p3")
                in_maps = maps([{
                    "xs_in": np.asarray(r2.results[c]["xs_out"]),
                    "xlnd_in": np.asarray(r2.results[c]["xlnd_out"]),
                    "gdi_wfull": np.asarray(r2.results[c]["gdo_wfull"]),
                    "gdi_pref": np.asarray(r2.results[c]["gdo_pref"]),
                    "gdi_trow": np.asarray(r2.results[c]["gdo_trow"]),
                    "cc": cc2,
                } for c in range(NCORES)])
                r3 = run_bass_kernel_spmd(nc3, in_maps, core_ids=list(range(NCORES)))
                outs = [np.asarray(r["out"]) for r in r3.results]

    full = np.concatenate([o.reshape(4, S, D) for o in outs], axis=0)
    return full.astype(np.float32)


# version 21
# speedup vs baseline: 1.0851x; 1.0851x over previous
import numpy as np
import ml_dtypes
import concourse.bass as bass
import concourse.mybir as mybir
from concourse.bass_utils import run_bass_kernel_spmd
from contextlib import ExitStack

F32 = mybir.dt.float32
BF16 = mybir.dt.bfloat16
AF = mybir.ActivationFunctionType
ALU = mybir.AluOpType
AX = mybir.AxisListType

NCORES = 8
SEQS = 4
S = 2048
D = 1024
NCH = 16
TOK = SEQS * S
CAP = 20480.0
LN_EPS = 1e-5
BIG = 1.0e30
MAGIC = 1.5 * 2.0 ** 23

UPTO = 6
DEVSTOP = 0
FUSED = True

ENG2BLK = {"pe": "tensor", "act": "scalar", "dve": "vector", "pool": "gpsimd", "sp": "sync"}


class KB:
    def __init__(self, nc):
        self.nc = nc
        self.ops = {e: [] for e in ENG2BLK}
        self.cnt = {}
        self.sems = {}

    LANES = ("xc0", "xc1", "px", "lw0", "lw1", "qw0", "qw1", "kq",
             "xst0", "xst1", "xlnd", "ddl", "msc", "mscp", "pw", "pr", "cc", "tw")

    def make_sems(self, ctx):
        for name in ("pe", "act", "dve", "pool", "sp") + KB.LANES:
            self.sems[name] = ctx.enter_context(self.nc.semaphore("sem_" + name))
            self.cnt[name] = 0

    def op(self, eng, fn, waits=(), hard=(), sig=None, dma=None):
        incs = []
        tok = None
        if eng in ("dve", "act", "pool") and dma is None:
            hard = tuple(hard) + ((eng, self.cnt[eng]),)
            if sig is None:
                sig = eng
        if sig is not None:
            self.cnt[sig] += 1
            incs.append((sig, 1))
            tok = (sig, self.cnt[sig])
        if dma is not None:
            self.cnt[dma] += 16
            incs.append((dma, 16))
            tok = (dma, self.cnt[dma])
        ws = tuple(w for w in waits if w is not None)
        hs = tuple(h for h in hard if h is not None)
        for t in ws + hs:
            assert t[1] <= self.cnt[t[0]], f"wait on future token {t} (cnt={self.cnt[t[0]]})"
        self.ops[eng].append((ws, hs, fn, incs))
        return tok

    def wait(self, eng, waits):
        ws = tuple(w for w in waits if w is not None)
        for t in ws:
            assert t[1] <= self.cnt[t[0]], f"wait on future token {t}"
        self.ops[eng].append((ws, (), None, []))

    def emit(self):
        with self.nc.Block() as blk:
            for eng in ENG2BLK:
                ops = self.ops[eng]
                own = eng

                def body(e, ops=ops, own=own):
                    waited = {}
                    own_cnt = 0
                    for ws, hs, fn, incs in ops:
                        need = {}
                        for s, c in ws:
                            if s == own and c <= own_cnt:
                                continue
                            if need.get(s, 0) < c:
                                need[s] = c
                        for s, c in hs:
                            if need.get(s, 0) < c:
                                need[s] = c
                        for s, c in need.items():
                            if waited.get(s, 0) < c:
                                e.wait_ge(self.sems[s], c)
                                waited[s] = c
                        if fn is not None:
                            instr = fn(e)
                            for s, n in incs:
                                instr.then_inc(self.sems[s], n)
                            own_cnt += sum(1 for s, n in incs if s == own)

                getattr(blk, ENG2BLK[eng])(body)


class NS:
    pass


W_NAMES = [
    ("L0_attn_Wq", [D, D]), ("L0_attn_Wk", [D, D]), ("L0_attn_Wv", [D, D]),
    ("L0_attn_Wo", [D, D]), ("L0_attn_gamma", [D]), ("L0_attn_beta", [D]),
    ("L0_ffn_kernel", [D, D]), ("L0_ffn_bias", [D]), ("L0_ffn_a", [D, 16]),
    ("L0_ffn_b", [16, D]), ("L1_attn_in_proj_w", [3 * D, D]),
    ("L1_attn_in_proj_b", [3 * D]), ("L1_attn_out_proj_w", [D, D]),
    ("L1_attn_out_proj_b", [D]), ("L1_ffn_gate_w", [8, D]),
    ("L1_ffn_we_gate", [8, D, D]), ("L1_ffn_we_up", [8, D, D]),
    ("L1_ffn_we_down", [8, D, D]), ("L1_ffn_ws_gate", [2 * D, D]),
    ("L1_ffn_ws_up", [2 * D, D]), ("L1_ffn_ws_down", [D, 2 * D]),
]

SC_NAMES = [
    ("q", [D, D]), ("k", [D, D]), ("v", [D, D]), ("o", [D, D]),
    ("kq", [D, D]), ("ip", [3 * D, D]), ("op", [D, D]),
    ("weg", [8, D, D]), ("weu", [8, D, D]), ("wed", [8, D, D]),
    ("wsg", [2 * D, D]), ("wsu", [2 * D, D]), ("wsd", [D, 2 * D]),
    ("gw", [16, D]),
    ("bv", [16, 4 * D]),
]


def declare_io(nc, phase):
    io = NS()
    din = lambda name, shape, dt=F32: nc.dram_tensor(name, shape, dt, kind="ExternalInput").ap()
    dout = lambda name, shape, dt=F32: nc.dram_tensor(name, shape, dt, kind="ExternalOutput").ap()
    dint = lambda name, shape, dt=BF16: nc.dram_tensor(name, shape, dt).ap()

    io.w = {nm: din(nm, shp) for nm, shp in W_NAMES}
    io.u128 = din("u128", [128, 128], BF16)
    io.cmask = din("cmask", [128, 128], BF16)
    io.rankmask = din("rankmask", [8, 1])
    io.rowsel = din("rowsel", [8, 1])

    last = (phase == "all") or (phase == "p1" and UPTO <= 3) or \
           (phase == "p2" and UPTO <= 5) or (phase == "p3")
    io.x_in = din("x", [TOK, D]) if phase in ("all", "p1") else din("xs_in", [TOK, D])
    io.out = dout("out", [TOK, D])
    io.xs = dint("xs", [TOK, D], F32) if (phase == "all" or last) else dout("xs_out", [TOK, D], F32)
    if phase == "all":
        io.xlnd = dint("xlnd", [TOK, D], BF16)
        io.xlnd_in = io.xlnd
    else:
        io.xlnd = io_xlnd_out = (dint("xlnd", [TOK, D], BF16) if last
                                 else dout("xlnd_out", [TOK, D], BF16))
        io.xlnd_in = din("xlnd_in", [TOK, D], BF16) if phase in ("p2", "p3") else io.xlnd
    if phase == "all":
        io.gd_wfull = dint("gd_wfull", [128, 512], F32)
        io.gd_pref = dint("gd_pref", [128, 512], F32)
        io.gd_trow = dint("gd_trow", [1, 512], F32)
        io.ccrow = None
        io.gdi_wfull = io.gd_wfull
        io.gdi_pref = io.gd_pref
        io.gdi_trow = io.gd_trow
        io.cc = None
    else:
        mkout = phase in ("p1", "p2") and not last
        f = dout if mkout else dint
        io.gd_wfull = f("gdo_wfull" if mkout else "gd_wfull", [128, 512], F32)
        io.gd_pref = f("gdo_pref" if mkout else "gd_pref", [128, 512], F32)
        io.gd_trow = f("gdo_trow" if mkout else "gd_trow", [1, 512], F32)
        io.ccrow = dout("ccrow", [1, 8], F32) if mkout else None
        if phase in ("p2", "p3"):
            io.gdi_wfull = din("gdi_wfull", [128, 512], F32)
            io.gdi_pref = din("gdi_pref", [128, 512], F32)
            io.gdi_trow = din("gdi_trow", [1, 512], F32)
            io.cc = din("cc", [8, 8], F32)
        else:
            io.gdi_wfull = io.gdi_pref = io.gdi_trow = io.cc = None

    io.sc = {nm: dint("sc_" + nm, shp, BF16) for nm, shp in SC_NAMES}
    io.tw = {}
    for nm, shp in [("ipq", [128, 8192]), ("ipk", [128, 8192]), ("ipv", [128, 8192]),
                    ("op", [128, 8192]), ("weg", [8, 128, 8192]), ("weu", [8, 128, 8192]),
                    ("wed", [8, 128, 8192]), ("wsg", [2, 128, 8192]),
                    ("wsu", [2, 128, 8192]), ("wsd", [2, 128, 8192])]:
        io.tw[nm] = dint("tw_" + nm, shp, BF16)
    io.dd = dint("dd", [D, S], BF16)
    io.bounce = dint("bounce", [1, 1024], F32)
    io.bounce2 = dint("bounce2", [1024, 1], F32)
    io.t64d = dint("t64d", [64, 8], F32)
    io.offd = dint("offd", [1, 512], F32)
    io.ccin_d = dint("ccin_d", [8, 8], F32)
    io.ccout_d = dint("ccout_d", [8, 8], F32)
    return io


def alloc_sbuf(nc, ctx):
    sb = NS()
    t = lambda n, s, d=BF16: ctx.enter_context(nc.sbuf_tensor(n, s, d))
    sb.PX = t("PX", [128, 16384])
    sb.PQ = t("PQ", [128, 16384])
    sb.PK = t("PK", [128, 16384])
    sb.PV = t("PV", [128, 16640])
    sb.PO = t("PO", [128, 16384])
    sb.POf = sb.PO[:].bitcast(F32)
    sb.xc = t("xc", [128, 2048], F32)
    sb.dtb = t("dtb", [128, 2048], BF16)
    sb.st = t("stg", [128, 4096], BF16)
    sb.tt = sb.st[:, 0:2048].bitcast(F32)
    sb.bn6 = t("bn6", [128, 12], F32)
    sb.stats = t("stats", [128, 32], F32)
    sb.mu16 = t("mu16", [128, 16], F32)
    sb.rstd16 = t("rstd16", [128, 16], F32)
    sb.sc1 = t("sc1", [128, 64], F32)
    sb.packA = t("packA", [128, 2048], F32)
    sb.row = sb.packA[0:1, 0:1024]
    sb.Trow = sb.packA[0:1, 1024:1536]
    sb.OFFROW = sb.packA[0:1, 1536:2048]
    sb.vb_row = t("vb_row", [1, 1024], BF16)
    sb.biasf = t("biasf", [128, 32], F32)
    sb.rowb = t("rowbufb", [1, 512], BF16)
    sb.u128 = t("u128c", [128, 128], BF16)
    sb.cmask = t("cmaskc", [128, 128], BF16)
    sb.rankmask = t("rankmaskc", [8, 1], F32)
    sb.rowsel = t("rowselc", [8, 1], F32)
    sb.ones_row_b = t("ones_row_b", [1, 128], BF16)
    sb.ones512_b = t("ones512_b", [1, 512], BF16)
    sb.ones_row_f = t("ones_row_f", [1, 128], F32)
    sb.ones_col_b = t("ones_col_b", [128, 1], BF16)
    sb.gamma_rep = t("gamma_rep", [128, D], BF16)
    sb.beta_rep = t("beta_rep", [128, D], BF16)
    sb.gwT = t("gwT", [128, 128], BF16)
    sb.quant = t("quant", [128, 8], F32)
    sb.rrep = t("rrep", [128, 2], F32)
    sb.wfull = t("wfull", [128, 512], F32)
    sb.pref = t("pref", [128, 512], F32)
    sb.comb = t("comb", [128, 512], F32)
    sb.CC = t("CCt", [8, 8], F32)
    sb.g8 = t("g8", [128, 64], F32)
    sb.selb = t("selb", [128, 32], BF16)
    sb.cb = t("cb", [128, 4], F32)
    return sb


def alloc_psum(nc, ctx):
    ps = NS()
    ps.A = ctx.enter_context(nc.psum_tensor("PSA", [128, 2048], F32))
    ps.B = ctx.enter_context(nc.psum_tensor("PSB", [128, 2048], F32))
    return ps



def stage_preprocess(kb, io, sb, ps):
    row1 = lambda ap: ap.rearrange("(o n) -> o n", o=1)
    for dst, src in [(sb.u128, io.u128), (sb.cmask, io.cmask),
                     (sb.rankmask, io.rankmask), (sb.rowsel, io.rowsel)]:
        kb.op("sp", lambda e, d=dst, s=src: e.dma_start(d[:], s), dma="msc")
    t_mp = kb.op("pool", lambda e: e.dma_start(
        sb.vb_row[:], io.w["L1_attn_in_proj_b"].rearrange("(o n) -> o n", o=1)[0:1, 2 * D:3 * D]),
        dma="mscp")
    t_g1 = kb.op("sp", lambda e: e.dma_start(sb.packA[0:1, 0:D], row1(io.w["L0_attn_gamma"])), dma="msc")
    kb.op("dve", lambda e: e.memset(sb.ones_row_b[:], 1.0), sig="dve")
    kb.op("dve", lambda e: e.memset(sb.ones512_b[:], 1.0), sig="dve")
    kb.op("dve", lambda e: e.memset(sb.ones_row_f[:], 1.0), sig="dve")
    t_ones = kb.op("dve", lambda e: e.memset(sb.ones_col_b[:], 1.0), sig="dve")
    tg_ev = []
    for j in range(2):
        tm = kb.op("pe", lambda e, j=j: e.matmul(
            ps.A[:, 512 * j:512 * (j + 1)], sb.ones_row_f[:],
            sb.packA[0:1, 512 * j:512 * (j + 1)], start=True, stop=True),
            waits=(t_g1, t_ones), sig="pe")
        tg_ev.append(kb.op("dve", lambda e, j=j: e.tensor_copy(
            sb.gamma_rep[:, 512 * j:512 * (j + 1)], ps.A[:, 512 * j:512 * (j + 1)]),
            waits=(tm,), sig="dve"))
    t_b1 = kb.op("sp", lambda e: e.dma_start(sb.packA[0:1, 0:D], row1(io.w["L0_attn_beta"])),
                 waits=(tg_ev[0], tg_ev[1]), dma="msc")
    for j in range(2):
        tm = kb.op("pe", lambda e, j=j: e.matmul(
            ps.A[:, 1024 + 512 * j:1024 + 512 * (j + 1)], sb.ones_row_f[:],
            sb.packA[0:1, 512 * j:512 * (j + 1)], start=True, stop=True),
            waits=(t_b1,), sig="pe")
        kb.op("dve", lambda e, j=j: e.tensor_copy(
            sb.beta_rep[:, 512 * j:512 * (j + 1)], ps.A[:, 1024 + 512 * j:1024 + 512 * (j + 1)]),
            waits=(tm,), sig="dve")

    cast_jobs = []

    def cast_copy(w_ap, sc_ap):
        rows = w_ap.shape[-2]
        for r0 in range(0, rows, 128):
            cast_jobs.append((w_ap, sc_ap, r0))

    for nm, src in [("q", io.w["L0_attn_Wq"]), ("k", io.w["L0_attn_Wk"]),
                    ("v", io.w["L0_attn_Wv"]), ("o", io.w["L0_attn_Wo"]),
                    ("op", io.w["L1_attn_out_proj_w"]), ("ip", io.w["L1_attn_in_proj_w"]),
                    ("wsg", io.w["L1_ffn_ws_gate"]), ("wsu", io.w["L1_ffn_ws_up"]),
                    ("wsd", io.w["L1_ffn_ws_down"])]:
        cast_copy(src, io.sc[nm])
    for nm, src in [("weg", io.w["L1_ffn_we_gate"]), ("weu", io.w["L1_ffn_we_up"]),
                    ("wed", io.w["L1_ffn_we_down"])]:
        for ee in range(8):
            cast_copy(src[ee], io.sc[nm][ee])
    t_r = ("dve", kb.cnt["dve"])
    for b0 in range(0, len(cast_jobs), 8):
        batch = cast_jobs[b0:b0 + 8]
        t_w = None
        for i, (w_ap, sc_ap, r0) in enumerate(batch):
            cols = w_ap.shape[-1]
            t_w = kb.op("pool", lambda e, w_ap=w_ap, r0=r0, c0=2048 * i, cols=cols: e.dma_start(
                sb.PO[:, c0:c0 + cols], w_ap[r0:r0 + 128, :]), waits=(t_r,), dma="pw")
        for i, (w_ap, sc_ap, r0) in enumerate(batch):
            cols = w_ap.shape[-1]
            t_r = kb.op("sp", lambda e, sc_ap=sc_ap, r0=r0, c0=2048 * i, cols=cols: e.dma_start(
                sc_ap[r0:r0 + 128, :], sb.PO[:, c0:c0 + cols]), waits=(t_w,), dma="pr")
    tgwz = kb.op("dve", lambda e: e.memset(sb.PO[0:16, 0:D], 0.0), waits=(t_r,), sig="dve")
    tgwc = kb.op("pool", lambda e: e.dma_start(sb.PO[0:8, 0:D], io.w["L1_ffn_gate_w"]),
                 waits=(tgwz, t_r, t_mp), dma="mscp")
    tgws = kb.op("sp", lambda e: e.dma_start(io.sc["gw"], sb.PO[0:16, 0:D]),
                 waits=(tgwc,), dma="msc")
    tgwT = None
    for dc in range(8):
        tgwT = kb.op("sp", lambda e, dc=dc: e.dma_start(
            sb.gwT[:, 16 * dc:16 * (dc + 1)], io.sc["gw"][:, 128 * dc:128 * (dc + 1)],
            transpose=True), waits=(tgws,), dma="msc")

    POf = sb.POf
    tload = None
    for dc in range(8):
        tload = kb.op("sp", lambda e, dc=dc: e.dma_start(
            POf[:, 1024 * dc:1024 * (dc + 1)], io.w["L0_ffn_kernel"][128 * dc:128 * (dc + 1), :]),
            dma="pr")
    tred = []
    for dc in range(8):
        tred.append(kb.op("dve", lambda e, dc=dc: e.reduce_max(
            sb.quant[:, dc:dc + 1], POf[:, 1024 * dc:1024 * (dc + 1)],
            axis=AX.X, apply_absolute_value=True), waits=(tload,), sig="dve"))
    tm = kb.op("dve", lambda e: e.reduce_max(sb.sc1[:, 0:1], sb.quant[:, 0:8], axis=AX.X),
               hard=(tred[-1],), sig="dve")
    tb = kb.op("pool", lambda e: e.dma_start(io.bounce2[0:128, :], sb.sc1[:, 0:1]),
               waits=(tm, t_mp), dma="mscp")
    tb2 = kb.op("sp", lambda e: e.dma_start(
        sb.packA[0:1, 0:128], io.bounce2.rearrange("(o p) n -> o (p n)", o=1)[0:1, 0:128]),
        waits=(tb,), dma="msc")
    tmax = kb.op("dve", lambda e: e.reduce_max(sb.packA[0:1, 128:129], sb.packA[0:1, 0:128], axis=AX.X),
                 waits=(tb2,), sig="dve")
    tsc = kb.op("dve", lambda e: e.tensor_scalar(sb.packA[0:1, 129:130], sb.packA[0:1, 128:129],
                                                 float(1.0 / 7.0), None, ALU.mult),
                hard=(tmax,), sig="dve")
    trc = kb.op("dve", lambda e: e.reciprocal(sb.packA[0:1, 130:131], sb.packA[0:1, 129:130]),
                hard=(tsc,), sig="dve")
    tb3 = kb.op("pool", lambda e: e.dma_start(io.bounce[0:1, 0:2], sb.packA[0:1, 129:131]),
                waits=(trc,), dma="mscp")
    tb4 = kb.op("sp", lambda e: e.dma_start(sb.packA[0:1, 132:134], io.bounce[0:1, 0:2]),
                waits=(tb3,), dma="msc")
    tmr = kb.op("pe", lambda e: e.matmul(ps.B[:, 0:2], sb.ones_row_f[:], sb.packA[0:1, 132:134],
                                         start=True, stop=True), waits=(tb4,), sig="pe")
    trr = kb.op("dve", lambda e: e.tensor_copy(sb.rrep[:, 0:2], ps.B[:, 0:2]),
                waits=(tmr,), sig="dve")
    t_qst = []
    for dc in range(8):
        stg = sb.st[:, 0:1024] if dc % 2 == 0 else sb.st[:, 1024:2048]
        kb.op("dve", lambda e, dc=dc: e.tensor_scalar(
            POf[:, 1024 * dc:1024 * (dc + 1)], POf[:, 1024 * dc:1024 * (dc + 1)],
            sb.rrep[:, 1:2], MAGIC, ALU.mult, ALU.add), hard=(trr,), sig="dve")
        t2 = kb.op("dve", lambda e, dc=dc: e.tensor_scalar(
            POf[:, 1024 * dc:1024 * (dc + 1)], POf[:, 1024 * dc:1024 * (dc + 1)],
            -MAGIC, None, ALU.add), sig="dve")
        t3 = kb.op("dve", lambda e, dc=dc, stg=stg: e.tensor_scalar(
            stg, POf[:, 1024 * dc:1024 * (dc + 1)], sb.rrep[:, 0:1], None, ALU.mult),
            waits=(t_qst[dc - 2] if dc >= 2 else None,), hard=(t2,), sig="dve")
        t_qst.append(kb.op("pool", lambda e, dc=dc, stg=stg: e.dma_start(
            io.sc["kq"][128 * dc:128 * (dc + 1), :], stg),
            waits=(t3, t_qst[dc - 1] if dc >= 1 else None), dma="pw"))
    tbz = kb.op("dve", lambda e: e.memset(sb.PO[0:16, 4096:8192], 0.0), waits=(t_r,), sig="dve")
    tb0 = kb.op("pool", lambda e: e.dma_start(
        sb.PO[0:1, 4096:5120], row1(io.w["L0_ffn_bias"])), waits=(tbz, t_mp), dma="mscp")
    tb1x = kb.op("pool", lambda e: e.dma_start(
        sb.PO[0:1, 5120:7168],
        io.w["L1_attn_in_proj_b"].rearrange("(o n) -> o n", o=1)[0:1, 0:2 * D]),
        waits=(tb0,), dma="mscp")
    tb2x = kb.op("pool", lambda e: e.dma_start(
        sb.PO[0:1, 7168:8192], row1(io.w["L1_attn_out_proj_b"])), waits=(tb1x,), dma="mscp")
    tbs = kb.op("sp", lambda e: e.dma_start(io.sc["bv"], sb.PO[0:16, 4096:8192]),
                waits=(tb0, tb1x, tb2x), dma="msc")
    tbt = None
    for ch in range(32):
        tbt = kb.op("sp", lambda e, ch=ch: e.dma_start(
            sb.st[:, 16 * ch:16 * (ch + 1)], io.sc["bv"][:, 128 * ch:128 * (ch + 1)],
            transpose=True), waits=(tbs,), dma="msc")
    kb.op("dve", lambda e: e.tensor_copy(
        sb.biasf[:], sb.st[:].rearrange("p (c r) -> p c r", r=16)[:, 0:32, 0:1]
        .rearrange("p c r -> p (c r)")), waits=(tbt,), sig="dve")

    kb.t_pr_all = ("pr", kb.cnt["pr"])
    return t_qst[-1], ("pr", kb.cnt["pr"])


def image_jobs(io):
    def b_io(ooff):
        return [(1024 * dc + 512 * oh, ooff + 512 * oh, 512, 128 * dc)
                for dc in range(8) for oh in range(2)]
    jobs = []
    for nm in ("weg", "weu"):
        for ee in range(8):
            jobs.append((io.tw[nm][ee],
                         [(1024 * bc, 0, 1024, 128 * bc) for bc in range(8)],
                         io.sc[nm][ee]))
    for ee in range(8):
        jobs.append((io.tw["wed"][ee],
                     [(1024 * ic + 512 * hh, 512 * hh, 512, 128 * ic)
                      for ic in range(8) for hh in range(2)], io.sc["wed"][ee]))
    for grp in range(2):
        jobs.append((io.tw["wsg"][grp],
                     [(1024 * bc, 1024 * grp, 1024, 128 * bc) for bc in range(8)], io.sc["wsg"]))
        jobs.append((io.tw["wsu"][grp],
                     [(1024 * bc, 1024 * grp, 1024, 128 * bc) for bc in range(8)], io.sc["wsu"]))
        jobs.append((io.tw["wsd"][grp],
                     [(1024 * ic + 512 * hh, 512 * hh, 512, 128 * (8 * grp + ic))
                      for ic in range(8) for hh in range(2)], io.sc["wsd"]))
    for nm, ooff in (("ipq", 0), ("ipk", D), ("ipv", 2 * D)):
        jobs.append((io.tw[nm], b_io(ooff), io.sc["ip"]))
    jobs.append((io.tw["op"], b_io(0), io.sc["op"]))
    return jobs


def stage_images(kb, io, sb, jobs, after_tok):
    slots = (sb.PV[:, 0:8192], sb.PV[:, 8448:16640])
    t_store = [kb.tw_done, kb.tw_done]
    for i, (dst, blocks, src) in enumerate(jobs):
        stg = slots[i % 2]
        t_tr = None
        for (doff, r0, rn, c0) in blocks:
            t_tr = kb.op("act", lambda e, stg=stg, doff=doff, r0=r0, rn=rn, c0=c0, src=src: e.dma_start(
                stg[:, doff:doff + rn], src[r0:r0 + rn, c0:c0 + 128], transpose=True),
                waits=(t_store[i % 2], kb.t_pr_all, after_tok), dma="tw")
        t_store[i % 2] = kb.op("act", lambda e, dst=dst, stg=stg: e.dma_start(dst, stg),
                               waits=(t_tr,), dma="tw")
    kb.tw_done = t_store[(len(jobs) - 1) % 2]
    kb.tw_done = (kb.tw_done[0], max(kb.cnt["tw"], kb.tw_done[1]))
    return kb.tw_done



def pass_residual_ln(kb, io, sb, ps, seq, mode, want_ln, x_src, x_dst,
                     scale_delta=1.0, px_free=None, chunks=None, y_tok=None,
                     dd_tok=None, x_tok=None, pq_tok=None):
    if chunks is None:
        chunks = list(range(NCH))
    else:
        chunks = list(chunks)
    t_aff = None
    last_store = None
    st_tok = {}
    prev_v = {}
    batch_tok = [pq_tok]
    x_tok = tuple(x_tok) if isinstance(x_tok, (tuple, list)) else (x_tok,)
    for c in chunks:
        xcs = sb.xc[:, 1024 * (c % 2):1024 * (c % 2 + 1)]
        row0 = seq * S + 128 * c
        lane = "xc%d" % (c % 2)
        pv = prev_v.get(c % 2, ())
        tld = kb.op("sp", lambda e, xcs=xcs, row0=row0, x_src=x_src: e.dma_start(
            xcs, x_src[row0:row0 + 128, :]), waits=pv + x_tok, dma=lane)
        tv = tld
        if mode == "delta":
            dtb = sb.dtb[:, 1024 * (c % 2):1024 * (c % 2 + 1)]
            tdt = kb.op("sp", lambda e, dtb=dtb, c=c: e.dma_start(
                dtb, io.dd[:, 128 * c:128 * (c + 1)], transpose=True),
                waits=pv + (dd_tok,), dma=lane)
            tv = kb.op("dve", lambda e, xcs=xcs, dtb=dtb: e.scalar_tensor_tensor(
                xcs, dtb, scale_delta, xcs, ALU.mult, ALU.add), waits=(tld, tdt), sig="dve")
        elif mode == "y":
            ysl = sb.POf[:, 1024 * (c % 8):1024 * (c % 8 + 1)]
            tv = kb.op("dve", lambda e, xcs=xcs, ysl=ysl: e.scalar_tensor_tensor(
                xcs, ysl, 0.5, xcs, ALU.mult, ALU.add), waits=(tld, y_tok), sig="dve")
        elif mode == "l0inner":
            dtb = sb.dtb[:, 1024 * (c % 2):1024 * (c % 2 + 1)]
            tdt = kb.op("sp", lambda e, dtb=dtb, c=c: e.dma_start(
                dtb, io.dd[:, 128 * c:128 * (c + 1)], transpose=True),
                waits=pv + (dd_tok,), dma=lane)
            tt = sb.tt[:, 0:1024]
            kb.op("dve", lambda e, xcs=xcs, tt=tt, c=c: e.tensor_scalar(
                tt, xcs, sb.stats[:, 2 * c:2 * c + 1], sb.rstd16[:, c:c + 1],
                ALU.subtract, ALU.mult), waits=(tld,), sig="dve")
            kb.op("dve", lambda e, tt=tt, dtb=dtb: e.tensor_tensor(
                tt, tt, dtb, ALU.add), waits=(tdt,), sig="dve")
            kb.op("dve", lambda e, tt=tt: e.bn_stats(sb.bn6[:, 0:6], tt[:, 0:512]), sig="dve")
            tb1 = kb.op("dve", lambda e, tt=tt: e.bn_stats(sb.bn6[:, 6:12], tt[:, 512:1024]), sig="dve")
            tb2 = kb.op("dve", lambda e: e.bn_aggr(sb.sc1[:, 2:4], sb.bn6[:]),
                        hard=(tb1,), sig="dve")
            tv1 = kb.op("dve", lambda e: e.tensor_scalar(
                sb.sc1[:, 4:5], sb.sc1[:, 3:4], LN_EPS, None, ALU.add), hard=(tb2,), sig="dve")
            tl1 = kb.op("act", lambda e: e.activation(sb.sc1[:, 5:6], sb.sc1[:, 4:5], AF.Ln,
                                                      bias=0.0, scale=1.0), waits=(tv1,), sig="act")
            tl2 = kb.op("act", lambda e: e.activation(sb.sc1[:, 6:7], sb.sc1[:, 5:6], AF.Exp,
                                                      bias=0.0, scale=-0.5), hard=(tl1,), sig="act")
            kb.op("dve", lambda e, tt=tt: e.tensor_scalar(
                tt, tt, sb.sc1[:, 2:3], sb.sc1[:, 6:7], ALU.subtract, ALU.mult),
                hard=(tl2, tb2), sig="dve")
            kb.op("dve", lambda e, tt=tt: e.tensor_tensor(tt, tt, sb.gamma_rep[:], ALU.mult),
                  sig="dve")
            kb.op("dve", lambda e, tt=tt: e.tensor_tensor(tt, tt, sb.beta_rep[:], ALU.add),
                  sig="dve")
            tv = kb.op("dve", lambda e, xcs=xcs, tt=tt: e.tensor_tensor(xcs, xcs, tt, ALU.add),
                       sig="dve")
        last_store = kb.op("pool", lambda e, xcs=xcs, row0=row0, x_dst=x_dst: e.dma_start(
            x_dst[row0:row0 + 128, :], xcs), waits=(tv,), dma="xst%d" % (c % 2))
        st_tok[c % 2] = last_store
        if want_ln:
            kb.op("dve", lambda e, xcs=xcs: e.bn_stats(sb.bn6[:, 0:6], xcs[:, 0:512]),
                  waits=(tv,), sig="dve")
            tb1 = kb.op("dve", lambda e, xcs=xcs: e.bn_stats(sb.bn6[:, 6:12], xcs[:, 512:1024]),
                  sig="dve")
            tb2 = kb.op("dve", lambda e, c=c: e.bn_aggr(sb.stats[:, 2 * c:2 * c + 2], sb.bn6[:]),
                        hard=(tb1,), sig="dve")
            tv1 = kb.op("dve", lambda e, c=c: e.tensor_scalar(
                sb.sc1[:, 8:9], sb.stats[:, 2 * c + 1:2 * c + 2], LN_EPS, None, ALU.add),
                hard=(tb2,), sig="dve")
            tl1 = kb.op("act", lambda e: e.activation(
                sb.sc1[:, 9:10], sb.sc1[:, 8:9], AF.Ln, bias=0.0, scale=1.0),
                waits=(tv1,), sig="act")
            tl2 = kb.op("act", lambda e, c=c: e.activation(
                sb.rstd16[:, c:c + 1], sb.sc1[:, 9:10], AF.Exp, bias=0.0, scale=-0.5),
                hard=(tl1,), sig="act")
            t_aff = kb.op("dve", lambda e, xcs=xcs, c=c: e.tensor_scalar(
                sb.PQ[:, 1024 * (c % 8):1024 * (c % 8 + 1)], xcs,
                sb.stats[:, 2 * c:2 * c + 1], sb.rstd16[:, c:c + 1],
                ALU.subtract, ALU.mult),
                waits=(px_free, batch_tok[0]),
                hard=(tl2, tb2), sig="dve")
            prev_v[c % 2] = (last_store, t_aff)
            if want_ln and c % 8 == 7:
                c0b = c - 7
                batch_tok[0] = kb.op("pool", lambda e, c0b=c0b, seq=seq: e.dma_start(
                    io.xlnd[seq * S + 128 * c0b:seq * S + 128 * (c0b + 8), :]
                    .rearrange("(c p) n -> p c n", p=128),
                    sb.PQ[:, 0:8192].rearrange("p (c n) -> p c n", n=1024)),
                    waits=(t_aff,), dma="xlnd")
        else:
            prev_v[c % 2] = (last_store, tv)
    ret_store = (st_tok.get(0), st_tok.get(1))
    if not want_ln:
        return ret_store, None
    return ret_store, batch_tok[0]


def load_px(kb, io, sb, seq, t_ready, px_free=None, src=None):
    t = None
    xl = src if src is not None else io.xlnd
    for dc in range(8):
        t = kb.op("sp", lambda e, dc=dc, xl=xl, seq=seq: e.dma_start(
            sb.PX[:, 2048 * dc:2048 * (dc + 1)],
            xl[seq * S:(seq + 1) * S, 128 * dc:128 * (dc + 1)], transpose=True),
            waits=(t_ready, px_free), dma="px")
    return t



def fm_proj(kb, sb, ps, dst, reg, wtok, t_px, bias_col=None,
            transposed=False, war0=None, evac_eng="dve"):
    ev = {}
    t_last = None
    for fc in range(8):
        for slab in range(4):
            pa = ps.A[:, 512 * slab:512 * (slab + 1)]
            for dc in range(8):
                if transposed:
                    off = reg + 1024 * dc + 512 * (fc // 4) + 128 * (fc % 4)
                else:
                    off = reg + 1024 * dc + 128 * fc
                lhsT = sb.PO[:, off:off + 128]
                w0 = (wtok, ev.get(slab), t_px, war0) if dc == 0 else ()
                tmm = kb.op("pe", lambda e, pa=pa, lhsT=lhsT, dc=dc, slab=slab: e.matmul(
                    pa, lhsT, sb.PX[:, 2048 * dc + 512 * slab:2048 * dc + 512 * (slab + 1)],
                    start=(dc == 0), stop=(dc == 7)),
                    waits=w0, sig="pe" if dc == 7 else None)
            if bias_col is not None:
                ev[slab] = kb.op("dve", lambda e, fc=fc, slab=slab, pa=pa, dst=dst: e.tensor_scalar(
                    dst[:, 2048 * fc + 512 * slab:2048 * fc + 512 * (slab + 1)], pa,
                    sb.biasf[:, bias_col + fc:bias_col + fc + 1], None, ALU.add),
                    waits=(tmm,), sig="dve")
            elif evac_eng == "dve":
                ev[slab] = kb.op("dve", lambda e, fc=fc, slab=slab, pa=pa, dst=dst: e.tensor_copy(
                    dst[:, 2048 * fc + 512 * slab:2048 * fc + 512 * (slab + 1)], pa),
                    waits=(tmm,), sig="dve")
            else:
                ev[slab] = kb.op("act", lambda e, fc=fc, slab=slab, pa=pa, dst=dst: e.activation(
                    dst[:, 2048 * fc + 512 * slab:2048 * fc + 512 * (slab + 1)], pa,
                    AF.Copy, bias=0.0, scale=1.0), waits=(tmm,), sig="act")
            t_last = ev[slab]
    return t_last



def stage_ffn0(kb, io, sb, ps, t_px, wfree=None):
    t_w = None
    for dc in range(8):
        t_w = kb.op("sp", lambda e, dc=dc: e.dma_start(
            sb.PK[:, 1024 * dc:1024 * (dc + 1)], io.sc["kq"][128 * dc:128 * (dc + 1), :]),
            waits=(wfree,), dma="kq")
    aT = sb.PK[:, 9216:9344]
    bT = sb.PK[0:16, 8192:9216]
    kb.op("pool", lambda e, aT=aT: e.dma_start(
        aT.rearrange("p (c r) -> p c r", r=16),
        io.w["L0_ffn_a"].rearrange("(c p) r -> p c r", p=128)), waits=(wfree,), dma="pw")
    t_ab2 = kb.op("pool", lambda e, bT=bT: e.dma_start(bT, io.w["L0_ffn_b"]),
                  waits=(wfree,), dma="pw")
    tl1 = []
    for slab in range(4):
        t = None
        for dc in range(8):
            t = kb.op("pe", lambda e, slab=slab, dc=dc, aT=aT: e.matmul(
                ps.B[0:16, 512 * slab:512 * (slab + 1)],
                aT[:, 16 * dc:16 * (dc + 1)],
                sb.PX[:, 2048 * dc + 512 * slab:2048 * dc + 512 * (slab + 1)],
                start=(dc == 0), stop=(dc == 7)),
                waits=(t_px, t_ab2) if dc == 0 else (), sig="pe" if dc == 7 else None)
        tl1.append(t)
    tl1b = []
    for slab in range(4):
        tl1b.append(kb.op("dve", lambda e, slab=slab: e.tensor_copy(
            sb.st[0:16, 512 * slab:512 * (slab + 1)], ps.B[0:16, 512 * slab:512 * (slab + 1)]),
            waits=(tl1[slab],), sig="dve"))
    ev = {}
    t_dd = None
    for fc in range(8):
        for slab in range(4):
            pa = ps.A[:, 512 * slab:512 * (slab + 1)]
            for dc in range(8):
                kb.op("pe", lambda e, fc=fc, slab=slab, dc=dc, pa=pa: e.matmul(
                    pa, sb.PK[:, 1024 * dc + 128 * fc:1024 * dc + 128 * (fc + 1)],
                    sb.PX[:, 2048 * dc + 512 * slab:2048 * dc + 512 * (slab + 1)],
                    start=(dc == 0), stop=False),
                    waits=(t_w, ev.get(slab)) if dc == 0 else ())
            tmm = kb.op("pe", lambda e, fc=fc, slab=slab, pa=pa, bT=bT: e.matmul(
                pa, bT[:, 128 * fc:128 * (fc + 1)],
                sb.st[0:16, 512 * slab:512 * (slab + 1)], start=False, stop=True),
                waits=(tl1b[slab], t_ab2), sig="pe")
            ev[slab] = kb.op("dve", lambda e, slab=slab, fc=fc, pa=pa: e.tensor_scalar(
                sb.st[:, 2048 + 512 * slab:2048 + 512 * (slab + 1)], pa,
                sb.biasf[:, fc:fc + 1], 0.5, ALU.add, ALU.mult),
                waits=(tmm,), sig="dve")
        tsd = kb.op("pool", lambda e, fc=fc: e.dma_start(
            io.dd[128 * fc:128 * (fc + 1), :], sb.st[:, 2048:4096]),
            waits=(ev[0], ev[1], ev[2], ev[3]), dma="ddl")
        for slab in range(4):
            ev[slab] = tsd
        t_dd = tsd
    return t_dd, ("pe", kb.cnt["pe"])



def stage_attn(kb, io, sb, ps, seq, layer, t_px, wfree=None, pv_tok=None):
    L1 = (layer == 1)
    regA, regB = 0, 8192
    SC = 0.125

    def load_direct(scname, reg, wtok=None):
        lane = "qw0" if reg == 0 else "qw1"
        t = None
        for dc in range(8):
            t = kb.op("sp", lambda e, dc=dc, scname=scname, reg=reg: e.dma_start(
                sb.PO[:, reg + 1024 * dc:reg + 1024 * (dc + 1)],
                io.sc[scname][128 * dc:128 * (dc + 1), :]), waits=(wtok,), dma=lane)
        return t

    def load_T(twname, reg, wtok=None):
        lane = "qw0" if reg == 0 else "qw1"
        return kb.op("sp", lambda e, twname=twname, reg=reg: e.dma_start(
            sb.PO[:, reg:reg + 8192], io.tw[twname]),
            waits=(wtok, kb.tw_done), dma=lane)

    if L1:
        wq = load_T("ipq", regA, wtok=wfree)
        tq = fm_proj(kb, sb, ps, sb.PQ, regA, wq, t_px, bias_col=8, transposed=True)
        wk = load_T("ipk", regB)
        tk = fm_proj(kb, sb, ps, sb.PK, regB, wk, t_px, bias_col=16, transposed=True)
    else:
        wq = load_direct("q", regA, wtok=wfree)
        tq = fm_proj(kb, sb, ps, sb.PQ, regA, wq, t_px)
        wk = load_direct("k", regB)
        tk = fm_proj(kb, sb, ps, sb.PK, regB, wk, t_px, evac_eng="act")

    tvm = kb.op("dve", lambda e: e.memset(sb.PV[:], 1.0), waits=(pv_tok,), sig="dve")
    if L1:
        wv = load_T("ipv", regA, wtok=tq)
    else:
        wv = load_direct("v", regA, wtok=tq)
    ev = {}
    t_v = None
    for tc in range(NCH):
        for half in range(2):
            bank = (tc * 2 + half) % 4
            pb = ps.B[:, 512 * bank:512 * (bank + 1)]
            for dc in range(8):
                lhsT = sb.PX[:, 2048 * dc + 128 * tc:2048 * dc + 128 * (tc + 1)]
                w0 = (wv, ev.get(bank)) if dc == 0 else ()
                kb.op("pe", lambda e, pb=pb, lhsT=lhsT, dc=dc, half=half: e.matmul(
                    pb, lhsT,
                    sb.PO[:, regA + 1024 * dc + 512 * half:regA + 1024 * dc + 512 * (half + 1)],
                    start=(dc == 0), stop=(dc == 7 and not L1)),
                    waits=w0, sig="pe" if (dc == 7 and not L1) else None)
            if L1:
                tmm = kb.op("pe", lambda e, pb=pb, half=half: e.matmul(
                    pb, sb.ones_row_b[:],
                    sb.vb_row[0:1, 512 * half:512 * (half + 1)],
                    start=False, stop=True), sig="pe")
            else:
                tmm = ("pe", kb.cnt["pe"])
            ev_t = None
            for hh in range(8):
                h = half * 8 + hh
                ev_t = kb.op("dve", lambda e, pb=pb, tc=tc, h=h, hh=hh: e.tensor_copy(
                    sb.PV[:, 1040 * tc + 65 * h:1040 * tc + 65 * h + 64],
                    pb[:, 64 * hh:64 * (hh + 1)]), waits=(tmm, tvm), sig="dve")
            ev[bank] = ev_t
            t_v = ev_t

    t_cb_prev = None
    sbuf_free = {}
    t_px_writes = []
    t_cb = None
    t_opx = {}
    pend_fin = [None]

    def flush_fin():
        f = pend_fin[0]
        pend_fin[0] = None
        if f:
            f()
    hq = 0
    for h in range(16):
        fc, pb0 = h // 2, 64 * (h % 2)
        for qh in range(2):
            par = hq % 2
            ob = ps.B[0:65, 1024 * par:1024 * par + 1024]
            kcs = list(range(NCH)) if not L1 else list(range(8 * qh + 8))
            pend_pv = []
            t_pv = None

            def emit_pv(first_overall, last_overall):
                nonlocal t_pv
                kc, te, tz, pt = pend_pv.pop(0)
                for j in range(2):
                    last = (j == 1 and last_overall)
                    t_pv = kb.op("pe", lambda e, kc=kc, h=h, j=j, pt=pt, fo=(first_overall and j == 0 or first_overall and j == 1), first=first_overall, last2=last_overall, ob=ob: e.matmul(
                        ob[:, 512 * j:512 * (j + 1)],
                        sb.PV[:, 1040 * kc + 65 * h:1040 * kc + 65 * (h + 1)],
                        pt[:, 512 * j:512 * (j + 1)], start=first, stop=last2),
                        waits=(te, tz, t_v, t_opx.get(par) if first_overall else None),
                        sig="pe" if last else None)
                sbuf_free[("pt", kc % (2 if L1 else 3))] = ("pe", kb.cnt["pe"])

            for i, kc in enumerate(kcs):
                if i == 6:
                    flush_fin()
                sbi = kc % 2
                pA = ps.A[:, 1024 * sbi:1024 * (sbi + 1)]
                dprime = kc - 8 * qh if L1 else -1
                qlo = 128 * dprime if (L1 and dprime >= 0) else 0
                tsm = None
                for j in range(2):
                    c0, c1 = max(512 * j, qlo), 512 * (j + 1)
                    if c0 >= c1:
                        continue
                    tsm = kb.op("pe", lambda e, pA=pA, fc=fc, pb0=pb0, kc=kc, qh=qh, c0=c0, c1=c1: e.matmul(
                        pA[:, c0:c1],
                        sb.PK[pb0:pb0 + 64, 2048 * fc + 128 * kc:2048 * fc + 128 * (kc + 1)],
                        sb.PQ[pb0:pb0 + 64, 2048 * fc + 1024 * qh + c0:2048 * fc + 1024 * qh + c1],
                        start=True, stop=True, tile_position=(pb0, 0)),
                        waits=(tq, tk, sbuf_free.get(sbi),
                               t_cb if ((not L1) and qh == 0 and kc == 1) else None),
                        sig="pe")
                if (not L1) and qh == 0 and kc == 0:
                    tr = kb.op("dve", lambda e, pA=pA: e.reduce_max(
                        sb.sc1[0:1, 16:17], pA[0:1, 0:1024], axis=AX.X, negate=True),
                        waits=(tsm,), sig="dve")
                    tr2 = kb.op("dve", lambda e: e.tensor_scalar(
                        sb.sc1[0:1, 17:18], sb.sc1[0:1, 16:17], SC, None, ALU.mult),
                        hard=(tr,), sig="dve")
                    tr3 = kb.op("dve", lambda e: e.tensor_copy(sb.rowb[0:1, 0:1], sb.sc1[0:1, 17:18]),
                                hard=(tr2,), sig="dve")
                    trm = kb.op("pe", lambda e: e.matmul(
                        ps.A[:, 2047:2048], sb.ones_row_b[:], sb.rowb[0:1, 0:1],
                        start=True, stop=True), waits=(tr3, sbuf_free.get(1)), sig="pe")
                    t_cb = kb.op("dve", lambda e: e.tensor_copy(sb.cb[:, 0:1], ps.A[:, 2047:2048]),
                                 waits=(trm,), hard=(t_cb_prev,), sig="dve")
                if L1 and dprime >= 0:
                    tsm = kb.op("dve", lambda e, pA=pA, qlo=qlo: e.tensor_tensor(
                        pA[:, qlo:qlo + 128], pA[:, qlo:qlo + 128], sb.cmask[:], ALU.add),
                        waits=(tsm,), sig="dve")
                sb3 = kc % (2 if L1 else 3)
                pt = sb.PO[:, 1024 * sb3:1024 * (sb3 + 1)]
                tz = None
                if qlo > 0:
                    tz = kb.op("pool", lambda e, pt=pt, qlo=qlo: e.memset(pt[:, 0:qlo], 0.0),
                               waits=(sbuf_free.get(("pt", sb3)) or t_v,), sig="pool")
                if L1:
                    te = kb.op("act", lambda e, pA=pA, pt=pt, qlo=qlo: e.activation(
                        pt[:, qlo:1024], pA[:, qlo:1024], AF.Exp, bias=0.0, scale=SC),
                        waits=(tsm, sbuf_free.get(("pt", sb3))), sig="act")
                else:
                    te = kb.op("act", lambda e, pA=pA, pt=pt: e.activation(
                        pt[:], pA[:], AF.Exp, bias=sb.cb[:, 0:1], scale=SC),
                        waits=(tsm, sbuf_free.get(("pt", sb3))), hard=(t_cb,), sig="act")
                t_cb_prev = te
                sbuf_free[sbi] = te
                pend_pv.append((kc, te, tz, pt))
                if L1:
                    emit_pv(i == 0, i == len(kcs) - 1)
                elif len(pend_pv) > 2:
                    emit_pv(i == 2, False)
            while pend_pv:
                emit_pv(len(pend_pv) == len(kcs) and len(kcs) <= 2,
                        len(pend_pv) == 1)
            def fin(ob=ob, par=par, fc=fc, pb0=pb0, qh=qh, t_pv=t_pv):
                stv = sb.st[64:128, 1024 * par:1024 * (par + 1)]
                t0 = kb.op("dve", lambda e, ob=ob: e.tensor_copy(
                    sb.row, ob[64:65, 0:1024]), waits=(t_pv,), sig="dve")
                tex = kb.op("dve", lambda e: e.reciprocal(sb.row, sb.row),
                            hard=(t0,), sig="dve")
                tev = kb.op("act", lambda e, ob=ob, stv=stv: e.activation(
                    stv, ob[0:64, 0:1024], AF.Copy, bias=0.0, scale=1.0),
                    waits=(t_pv, t_opx.get(par)), sig="act")
                trep = []
                for j in range(2):
                    trep.append(kb.op("pe", lambda e, j=j, ob=ob: e.matmul(
                        ob[0:64, 512 * j:512 * (j + 1)],
                        sb.ones_row_f[0:1, 0:64], sb.packA[0:1, 512 * j:512 * (j + 1)],
                        start=True, stop=True), waits=(tex, tev), sig="pe"))
                t_ox = kb.op("dve", lambda e, fc=fc, pb0=pb0, qh=qh, ob=ob, stv=stv: e.tensor_tensor(
                    sb.PX[pb0:pb0 + 64, 2048 * fc + 1024 * qh:2048 * fc + 1024 * (qh + 1)],
                    stv, ob[0:64, 0:1024], ALU.mult),
                    waits=(tev, trep[0], trep[1]), sig="dve")
                t_px_writes.append(t_ox)
                t_opx[par] = t_ox
            pend_fin[0] = fin
            hq += 1

    flush_fin()
    if L1:
        wo = load_T("op", regB, wtok=tk)
    else:
        wo = load_direct("o", regB, wtok=tk)
    t_all_px = t_px_writes[-1]
    ev2 = {}
    t_dd = None
    for fc in range(8):
        for slab in range(4):
            pa = ps.A[:, 512 * slab:512 * (slab + 1)]
            for dc in range(8):
                if L1:
                    off = regB + 1024 * dc + 512 * (fc // 4) + 128 * (fc % 4)
                else:
                    off = regB + 1024 * dc + 128 * fc
                lhsT = sb.PO[:, off:off + 128]
                w0 = (wo, ev2.get(slab), t_all_px, sbuf_free.get(0), sbuf_free.get(1)) \
                    if dc == 0 else ()
                tmm = kb.op("pe", lambda e, pa=pa, lhsT=lhsT, dc=dc, slab=slab: e.matmul(
                    pa, lhsT, sb.PX[:, 2048 * dc + 512 * slab:2048 * dc + 512 * (slab + 1)],
                    start=(dc == 0), stop=(dc == 7)),
                    waits=w0, sig="pe" if dc == 7 else None)
            if L1:
                ev2[slab] = kb.op("dve", lambda e, slab=slab, fc=fc, pa=pa: e.tensor_scalar(
                    sb.st[:, 2048 + 512 * slab:2048 + 512 * (slab + 1)], pa,
                    sb.biasf[:, 24 + fc:24 + fc + 1], None, ALU.add),
                    waits=(tmm,), sig="dve")
            else:
                ev2[slab] = kb.op("dve", lambda e, slab=slab, pa=pa: e.tensor_copy(
                    sb.st[:, 2048 + 512 * slab:2048 + 512 * (slab + 1)], pa),
                    waits=(tmm,), sig="dve")
        tsd = kb.op("pool", lambda e, fc=fc: e.dma_start(
            io.dd[128 * fc:128 * (fc + 1), :], sb.st[:, 2048:4096]),
            waits=(ev2[0], ev2[1], ev2[2], ev2[3]), dma="ddl")
        for slab in range(4):
            ev2[slab] = tsd
        t_dd = tsd
    return t_dd, ("pe", kb.cnt["pe"])



def stage_gate(kb, io, sb, ps, seq, t_px):
    t_prev = {0: None, 1: None, 2: None, 3: None}
    selb_prev = {0: None, 1: None, 2: None, 3: None}
    te1_prev = [None]
    for tc in range(NCH):
        slot = seq * NCH + tc
        par = tc % 4
        pg = ps.B[:, 1024 + 16 * par:1032 + 16 * par]
        tg = None
        for dc in range(8):
            tg = kb.op("pe", lambda e, dc=dc, tc=tc, pg=pg: e.matmul(
                pg, sb.PX[:, 2048 * dc + 128 * tc:2048 * dc + 128 * (tc + 1)],
                sb.gwT[:, 16 * dc:16 * dc + 8], start=(dc == 0), stop=(dc == 7)),
                waits=(t_px, t_prev[par]) if dc == 0 else (), sig="pe" if dc == 7 else None)
        g = sb.g8
        tm0 = kb.op("dve", lambda e, pg=pg: e.reduce_max(g[:, 8:9], pg, axis=AX.X, negate=True),
                    waits=(tg,), hard=(te1_prev[0],), sig="dve")
        te1 = kb.op("act", lambda e, pg=pg: e.activation(g[:, 16:24], pg, AF.Exp,
                                                         bias=g[:, 8:9], scale=1.0),
                    waits=(tg,), hard=(tm0,), sig="act")
        te1_prev[0] = te1
        ti1 = kb.op("dve", lambda e, pg=pg: e.tensor_scalar(g[:, 40:48], pg, g[:, 8:9],
                                                            0.0, ALU.add, ALU.is_ge),
                    hard=(tm0,), sig="dve")
        tk0 = kb.op("dve", lambda e, pg=pg: e.scalar_tensor_tensor(g[:, 48:56], g[:, 40:48], -BIG,
                                                                   pg, ALU.mult, ALU.add),
                    hard=(ti1,), sig="dve")
        tm2 = kb.op("dve", lambda e: e.reduce_max(g[:, 12:13], g[:, 48:56], axis=AX.X),
                    hard=(tk0,), sig="dve")
        tsel = kb.op("dve", lambda e, pg=pg: e.tensor_scalar(g[:, 56:64], pg, g[:, 12:13],
                                                             None, ALU.is_ge), hard=(tm2,), sig="dve")
        tt0 = kb.op("dve", lambda e: e.tensor_tensor(g[:, 24:32], g[:, 56:64], g[:, 16:24],
                                                     ALU.mult), waits=(te1,), hard=(tsel,), sig="dve")
        ts2 = kb.op("dve", lambda e: e.reduce_sum(g[:, 13:14], g[:, 24:32], axis=AX.X),
                    hard=(tt0,), sig="dve")
        tr2 = kb.op("dve", lambda e: e.reciprocal(g[:, 15:16], g[:, 13:14]), hard=(ts2,), sig="dve")
        twf = kb.op("dve", lambda e, slot=slot: e.tensor_scalar(
            sb.wfull[:, 8 * (slot % 64):8 * (slot % 64) + 8], g[:, 24:32], g[:, 15:16],
            None, ALU.mult), hard=(tr2,), sig="dve")
        sel = sb.selb[:, 8 * par:8 * (par + 1)]
        tsb = kb.op("dve", lambda e, sel=sel: e.tensor_copy(sel, g[:, 56:64]),
                    waits=(selb_prev[par],), hard=(tsel,), sig="dve")
        pp = ps.B[:, 1536 + 16 * par:1544 + 16 * par]
        tpm = kb.op("pe", lambda e, pp=pp, sel=sel: e.matmul(pp, sb.u128[:], sel,
                                                             start=True, stop=True),
                    waits=(tsb, t_prev[par]), sig="pe")
        tpe = kb.op("dve", lambda e, slot=slot, pp=pp: e.tensor_copy(
            sb.pref[:, 8 * (slot % 64):8 * (slot % 64) + 8], pp), waits=(tpm,), sig="dve")
        pt2 = ps.B[0:1, 1600 + 16 * par:1608 + 16 * par]
        ttm = kb.op("pe", lambda e, pt2=pt2, sel=sel: e.matmul(pt2, sb.ones_col_b[:], sel,
                                                               start=True, stop=True),
                    waits=(tsb,), sig="pe")
        tte = kb.op("dve", lambda e, slot=slot, pt2=pt2: e.tensor_copy(
            sb.Trow[0:1, 8 * (slot % 64):8 * (slot % 64) + 8], pt2), waits=(ttm,), sig="dve")
        t_prev[par] = tte
        selb_prev[par] = ("pe", kb.cnt["pe"])
    return tte, ("pe", kb.cnt["pe"])


def dump_gate(kb, io, sb, t_gate):
    t1 = kb.op("pool", lambda e: e.dma_start(io.gd_wfull, sb.wfull[:]), waits=(t_gate,), dma="ddl")
    t2 = kb.op("pool", lambda e: e.dma_start(io.gd_pref, sb.pref[:]), waits=(t_gate,), dma="ddl")
    t3 = kb.op("pool", lambda e: e.dma_start(io.gd_trow, sb.Trow), waits=(t_gate,), dma="ddl")
    return t3


def stage_counts_out(kb, io, sb, ps, t_gate, fused):
    tG = kb.op("dve", lambda e: e.tensor_reduce(
        sb.packA[0:1, 0:8], sb.Trow.rearrange("o (s e) -> o e s", e=8), AX.X, ALU.add),
        waits=(t_gate,), sig="dve")
    if not fused:
        return kb.op("pool", lambda e: e.dma_start(io.ccrow, sb.packA[0:1, 0:8]),
                     waits=(tG,), dma="mscp")
    tmr = kb.op("pe", lambda e: e.matmul(ps.B[0:8, 1024:1032], sb.ones_row_f[0:1, 0:8],
                                         sb.packA[0:1, 0:8], start=True, stop=True),
                waits=(tG,), sig="pe")
    tcc = kb.op("dve", lambda e: e.tensor_scalar(sb.CC[:], ps.B[0:8, 1024:1032],
                                                 sb.rowsel[:], None, ALU.mult),
                waits=(tmr,), sig="dve")
    tst = kb.op("pool", lambda e: e.dma_start(io.ccin_d, sb.CC[:]), waits=(tcc,), dma="mscp")
    tcol = kb.op("pool", lambda e: e.collective_compute(
        "AllReduce", ALU.add, replica_groups=[list(range(NCORES))],
        ins=[io.ccin_d.opt()], outs=[io.ccout_d.opt()]),
        waits=(tst,), sig="cc")
    return tcol


def stage_offsets(kb, io, sb, ps, t_cc, fused):
    cc_src = io.ccout_d if fused else io.cc
    tld = kb.op("sp", lambda e, cc_src=cc_src: e.dma_start(sb.CC[:], cc_src),
                waits=(t_cc,), dma="msc")
    tt1 = kb.op("pool", lambda e: e.dma_start(
        io.t64d.rearrange("p n -> (p n)").rearrange("(o x) -> o x", o=1), sb.Trow),
        waits=(t_cc,), dma="mscp")
    tt2 = kb.op("sp", lambda e: e.dma_start(sb.sc1[0:64, 16:24], io.t64d), waits=(tt1,), dma="msc")
    tt3 = kb.op("dve", lambda e: e.tensor_copy(sb.st[0:64, 0:8], sb.sc1[0:64, 16:24]),
                waits=(tt2,), sig="dve")
    tof = kb.op("pe", lambda e: e.matmul(ps.B[0:64, 1024:1032], sb.u128[0:64, 0:64],
                                         sb.st[0:64, 0:8], start=True, stop=True),
                waits=(tt3,), sig="pe")
    tofe = kb.op("dve", lambda e: e.tensor_copy(sb.sc1[0:64, 24:32], ps.B[0:64, 1024:1032]),
                 waits=(tof,), sig="dve")
    tco = kb.op("pe", lambda e: e.matmul(ps.B[0:1, 1536:1544], sb.rankmask[:], sb.CC[:],
                                         start=True, stop=True), waits=(tld,), sig="pe")
    tcoe = kb.op("dve", lambda e: e.tensor_copy(sb.packA[0:1, 8:16], ps.B[0:1, 1536:1544]),
                 waits=(tco,), sig="dve")
    tcor = kb.op("pe", lambda e: e.matmul(ps.B[0:64, 1032:1040], sb.ones_row_f[0:1, 0:64],
                                          sb.packA[0:1, 8:16], start=True, stop=True),
                 waits=(tcoe,), sig="pe")
    toff = kb.op("dve", lambda e: e.tensor_tensor(sb.sc1[0:64, 24:32], sb.sc1[0:64, 24:32],
                                                  ps.B[0:64, 1032:1040], ALU.add),
                 waits=(tcor,), hard=(tofe,), sig="dve")
    tst = kb.op("pool", lambda e: e.dma_start(
        io.offd.rearrange("o (p n) -> (o p) n", n=8)[0:64, :], sb.sc1[0:64, 24:32]),
        waits=(toff,), dma="mscp")
    trd = kb.op("sp", lambda e: e.dma_start(sb.OFFROW, io.offd), waits=(tst,), dma="msc")
    return trd


def stage_comb(kb, io, sb, ps, seq, t_off, t_wp_ready):
    t_last = None
    t_prev = None
    for tc in range(NCH):
        slot = seq * NCH + tc
        po = ps.B[:, 1032 + 8 * (tc % 2):1040 + 8 * (tc % 2)]
        trm = kb.op("pe", lambda e, slot=slot, po=po: e.matmul(
            po, sb.ones_row_f[:], sb.OFFROW[0:1, 8 * slot:8 * slot + 8],
            start=True, stop=True), waits=(t_off, t_prev), sig="pe")
        t1 = kb.op("dve", lambda e, slot=slot, po=po: e.tensor_tensor(
            sb.g8[:, 0:8], sb.pref[:, 8 * (slot % 64):8 * (slot % 64) + 8], po, ALU.add),
            waits=(trm, t_wp_ready), sig="dve")
        t2 = kb.op("dve", lambda e: e.tensor_scalar(sb.g8[:, 8:16], sb.g8[:, 0:8], CAP,
                                                    None, ALU.is_lt), hard=(t1,), sig="dve")
        t_last = kb.op("dve", lambda e, slot=slot: e.tensor_tensor(
            sb.comb[:, 8 * (slot % 64):8 * (slot % 64) + 8],
            sb.g8[:, 8:16], sb.wfull[:, 8 * (slot % 64):8 * (slot % 64) + 8], ALU.mult),
            hard=(t2,), sig="dve")
        t_prev = t_last
    return t_last



def stage_moe(kb, io, sb, ps, seq, t_px, t_comb, x_src, x_dst, want_ln, px_free_next,
              x_tok=None):
    t_store = None
    t_aff_all = []
    mcount = [0]
    t_matdone = {}
    y_done = None
    war = [("dve", kb.cnt["dve"])]

    def wreg():
        return 8192 * (mcount[0] % 2)

    def wload_img(img_ap):
        m = mcount[0]
        reg = 8192 * (m % 2)
        lane = "lw%d" % (m % 2)
        wtok = t_matdone.get(m - 2, px_free_next)
        t = kb.op("sp", lambda e, img_ap=img_ap, reg=reg: e.dma_start(
            sb.PK[:, reg:reg + 8192], img_ap), waits=(wtok, kb.tw_done), dma=lane)
        mcount[0] += 1
        return reg, t

    for half in range(2):
        toks = 1024 * half

        def gu_phase(wg_reg, wg_toks, n_ic, ic0, psum_off, dst_fn, extra_wait=None):
            ev = {}
            for i in range(n_ic):
                ic = ic0 + i
                P = ps.A if ic % 2 == 0 else ps.B
                for half2 in range(2):
                    pg = P[:, psum_off + 512 * half2:psum_off + 512 * (half2 + 1)]
                    for bc in range(8):
                        lhsT = sb.PK[:, wg_reg + 1024 * bc + 128 * (ic % 8):
                                     wg_reg + 1024 * bc + 128 * (ic % 8 + 1)]
                        w0 = (wg_toks, ev.get((ic % 2, half2)), t_px, extra_wait) \
                            if bc == 0 else ()
                        kb.op("pe", lambda e, pg=pg, lhsT=lhsT, bc=bc, half2=half2, toks=toks: e.matmul(
                            pg, lhsT,
                            sb.PX[:, 2048 * bc + toks + 512 * half2:2048 * bc + toks + 512 * (half2 + 1)],
                            start=(bc == 0), stop=(bc == 7)),
                            waits=w0, sig="pe" if bc == 7 else None)
                    tgm = ("pe", kb.cnt["pe"])
                    ev[(ic % 2, half2)] = dst_fn(ic, half2, pg, tgm)
            return ev

        def silu_dst(ic, half2, pg, tgm):
            return kb.op("act", lambda e, pg=pg, ic=ic, half2=half2: e.activation(
                sb.PQ[:, 8192 + 1024 * (ic % 8) + 512 * half2:8192 + 1024 * (ic % 8) + 512 * (half2 + 1)],
                pg, AF.Silu, bias=0.0, scale=1.0), waits=(tgm, kb.tw_done), sig="act")

        def hs_dst(ic, half2, pg, tgm, tsil):
            return kb.op("dve", lambda e, pg=pg, ic=ic, half2=half2: e.tensor_tensor(
                sb.PV[:, 1024 * ic + 512 * half2:1024 * ic + 512 * (half2 + 1)],
                sb.PQ[:, 8192 + 1024 * (ic % 8) + 512 * half2:8192 + 1024 * (ic % 8) + 512 * (half2 + 1)],
                pg, ALU.mult), waits=(tgm, tsil, kb.tw_done), sig="dve")

        def down_phase(wd_reg, wd_toks, n_ic, comb_e, t_hs_last, nblk1024):
            ev_d = {}
            t_last = None
            for tc in range(8):
                slot = seq * NCH + half * 8 + tc
                pos = tc % 4
                P = ps.A if pos % 2 == 0 else ps.B
                coff = 1024 * (pos // 2)
                for hh in range(2):
                    pd = P[:, coff + 512 * hh:coff + 512 * (hh + 1)]
                    for i in range(n_ic):
                        lw = wd_toks if not isinstance(wd_toks, list) else wd_toks[i]
                        lhsT = sb.PV[:, 1024 * i + 128 * tc:1024 * i + 128 * (tc + 1)]
                        w0 = (lw, ev_d.get((pos, hh)), t_hs_last) if i == 0 else (lw,)
                        roff = wd_reg(i) if callable(wd_reg) else wd_reg
                        kb.op("pe", lambda e, pd=pd, lhsT=lhsT, i=i, hh=hh, roff=roff: e.matmul(
                            pd, lhsT,
                            sb.PK[:, roff + 1024 * (i % 8) + 512 * hh:roff + 1024 * (i % 8) + 512 * (hh + 1)],
                            start=(i == 0), stop=(i == n_ic - 1)),
                            waits=w0, sig="pe" if i == n_ic - 1 else None)
                    tdm = ("pe", kb.cnt["pe"])
                    ysl = sb.POf[:, 1024 * tc + 512 * hh:1024 * tc + 512 * (hh + 1)]
                    if comb_e is not None and comb_e >= 0:
                        ccol = sb.comb[:, 8 * (slot % 64) + comb_e:8 * (slot % 64) + comb_e + 1]
                        tde = kb.op("dve", lambda e, pd=pd, ysl=ysl, ccol=ccol: e.scalar_tensor_tensor(
                            ysl, pd, ccol, ysl, ALU.mult, ALU.add), waits=(tdm,), hard=(t_comb,), sig="dve")
                    elif comb_e == -1:
                        ccol = sb.comb[:, 8 * (slot % 64):8 * (slot % 64) + 1]
                        yw = y_done if isinstance(y_done, tuple) and len(y_done) == 2 \
                            and not isinstance(y_done[0], str) else (y_done,)
                        tde = kb.op("dve", lambda e, pd=pd, ysl=ysl, ccol=ccol: e.tensor_scalar(
                            ysl, pd, ccol, None, ALU.mult),
                            waits=(tdm,) + tuple(yw), hard=(t_comb,), sig="dve")
                    else:
                        tde = kb.op("dve", lambda e, pd=pd, ysl=ysl: e.tensor_tensor(
                            ysl, ysl, pd, ALU.add), waits=(tdm,), sig="dve")
                    ev_d[(pos, hh)] = tde
                    t_last = tde
            return t_last

        for ee in range(8):
            gm = mcount[0]
            reg_g, wg = wload_img(io.tw["weg"][ee])
            ev_g = gu_phase(reg_g, wg, 8, 0, 0, silu_dst, extra_wait=war[0])
            t_matdone[gm] = ("pe", kb.cnt["pe"])
            um = mcount[0]
            reg_u, wu = wload_img(io.tw["weu"][ee])
            evh = gu_phase(reg_u, wu, 8, 0, 1024,
                           lambda ic, half2, pg, tgm: hs_dst(ic, half2, pg, tgm,
                                                             ev_g[(ic % 2, half2)]),
                           extra_wait=war[0])
            t_hs_all = ("dve", kb.cnt["dve"])
            t_matdone[um] = ("pe", kb.cnt["pe"])
            dm = mcount[0]
            reg_d, wd = wload_img(io.tw["wed"][ee])
            down_phase(reg_d, wd, 8, (-1 if ee == 0 else ee), t_hs_all, False)
            t_matdone[dm] = ("pe", kb.cnt["pe"])
            war[0] = ("dve", kb.cnt["dve"])

        t_hs_sh = []
        greg = {}
        for grp in range(2):
            gm = mcount[0]
            reg_g, wg = wload_img(io.tw["wsg"][grp])
            ev_g = gu_phase(reg_g, wg, 8, 8 * grp, 0, silu_dst, extra_wait=war[0])
            t_matdone[gm] = ("pe", kb.cnt["pe"])
            um = mcount[0]
            reg_u, wu = wload_img(io.tw["wsu"][grp])
            evh = gu_phase(reg_u, wu, 8, 8 * grp, 1024,
                           lambda ic, half2, pg, tgm: hs_dst(ic, half2, pg, tgm,
                                                             ev_g[(ic % 2, half2)]),
                           extra_wait=war[0])
            t_hs_sh.append(("dve", kb.cnt["dve"]))
            t_matdone[um] = ("pe", kb.cnt["pe"])
        dregs = []
        wd_last = None
        for grp in range(2):
            dm = mcount[0]
            reg_d, wd = wload_img(io.tw["wsd"][grp])
            dregs.append(reg_d)
            wd_last = wd
            if grp == 0:
                dm0 = dm
        t_down = down_phase(lambda i: dregs[i // 8], wd_last, 16, None,
                            t_hs_sh[-1], False)
        war[0] = ("dve", kb.cnt["dve"])
        t_matdone[dm0] = ("pe", kb.cnt["pe"])
        t_matdone[mcount[0] - 1] = ("pe", kb.cnt["pe"])
        y_done2 = t_down

        t_store, t_xln_h = pass_residual_ln(
            kb, io, sb, ps, seq, "y", want_ln, x_src, x_dst,
            chunks=range(half * 8, half * 8 + 8), y_tok=y_done2,
            px_free=px_free_next if half == 0 else None, x_tok=x_tok,
            pq_tok=t_aff_all[-1] if t_aff_all else None)
        if t_xln_h is not None:
            t_aff_all.append(t_xln_h)
        y_done = t_store
    t_xln = t_aff_all[-1] if t_aff_all else None
    return t_store, t_xln, ("pe", kb.cnt["pe"])



def build_program(phase):
    nc = bass.Bass("TRN2", target_bir_lowering=False)
    io = declare_io(nc, phase)
    with ExitStack() as ctx:
        kb = KB(nc)
        kb.make_sems(ctx)
        sb = alloc_sbuf(nc, ctx)
        ps = alloc_psum(nc, ctx)
        t_pw, t_pr = stage_preprocess(kb, io, sb, ps)
        pre_done = (("pe", kb.cnt["pe"]), ("dve", kb.cnt["dve"]), t_pw, t_pr)
        _phase_body(kb, io, sb, ps, phase, pre_done)
        kb.emit()
    return nc


def _phase_body(kb, io, sb, ps, phase, pre_done):
    fused = (phase == "all")

    def is_final(sl):
        return sl == UPTO

    def final_wait():
        kb.wait("sp", [(lane, kb.cnt[lane]) for lane in
                       ("xst0", "xst1", "xlnd", "ddl", "msc", "mscp", "pw", "pr", "cc")])

    px_free = pre_done[0]
    xtk = None
    jobs = image_jobs(io)
    kb.tw_done = None
    if True:
        stage_images(kb, io, sb, jobs, None)
    q = (len(jobs) + 3) // 4
    jgroups = [jobs[i * q:(i + 1) * q] for i in range(4)]

    if DEVSTOP == 1:
        final_wait()
        return

    if phase in ("p1", "all"):
        t_gate = None
        first = True
        for seq in range(SEQS):
            xtk_in = xtk
            tx, t_xln = pass_residual_ln(kb, io, sb, ps, seq, "init", True,
                                         io.x_in, io.xs,
                                         px_free=pre_done[1] if first else px_free,
                                         x_tok=xtk)
            xtk = tx
            if DEVSTOP == 2:
                final_wait()
                return
            t_px = load_px(kb, io, sb, seq, t_xln, px_free=px_free)
            t_dd, px_free = stage_ffn0(kb, io, sb, ps, t_px,
                                       wfree=pre_done[2] if first else px_free)
            first = False
            if DEVSTOP == 3:
                final_wait()
                return
            if UPTO == 1:
                tx, _ = pass_residual_ln(kb, io, sb, ps, seq, "delta", False,
                                         io.xs, io.out, dd_tok=t_dd, x_tok=xtk)
                xtk = tx
                continue
            tx, t_xln = pass_residual_ln(kb, io, sb, ps, seq, "delta", True,
                                         io.xs, io.xs, px_free=px_free, dd_tok=t_dd,
                                         x_tok=xtk)
            xtk = tx
            t_px = load_px(kb, io, sb, seq, t_xln, px_free=px_free)
            t_dd, px_free = stage_attn(kb, io, sb, ps, seq, 0, t_px, wfree=px_free,
                                       pv_tok=kb.tw_done)
            if UPTO == 2:
                tx, _ = pass_residual_ln(kb, io, sb, ps, seq, "l0inner", False,
                                         io.xs, io.out, dd_tok=t_dd, x_tok=xtk)
                xtk = tx
                continue
            tx, t_xln = pass_residual_ln(kb, io, sb, ps, seq, "l0inner", True,
                                         io.xs, io.xs, px_free=px_free, dd_tok=t_dd,
                                         x_tok=xtk)
            xtk = tx
            t_px = load_px(kb, io, sb, seq, t_xln, px_free=px_free)
            t_dd, px_free = stage_ffn0(kb, io, sb, ps, t_px, wfree=px_free)
            if UPTO == 3:
                tx, _ = pass_residual_ln(kb, io, sb, ps, seq, "delta", False,
                                         io.xs, io.out, dd_tok=t_dd, x_tok=xtk)
                xtk = tx
                continue
            tx, t_xln = pass_residual_ln(kb, io, sb, ps, seq, "delta", True,
                                         io.xs, io.xs, px_free=px_free, dd_tok=t_dd,
                                         x_tok=xtk)
            xtk = tx
            t_px = load_px(kb, io, sb, seq, t_xln, px_free=px_free)
            t_gate, px_free = stage_gate(kb, io, sb, ps, seq, t_px)
        if UPTO <= 3:
            final_wait()
            return
        t_gd = dump_gate(kb, io, sb, t_gate)
        t_cc = stage_counts_out(kb, io, sb, ps, t_gate, fused)
        if phase == "p1":
            final_wait()
            return
        t_p1_st = ("xlnd", kb.cnt["xlnd"])

    if phase in ("p2", "all"):
        if phase == "p2":
            t1 = kb.op("sp", lambda e: e.dma_start(sb.wfull[:], io.gdi_wfull), dma="msc")
            t2 = kb.op("sp", lambda e: e.dma_start(sb.pref[:], io.gdi_pref), dma="msc")
            t3 = kb.op("sp", lambda e: e.dma_start(sb.Trow, io.gdi_trow), dma="msc")
            t_wp = t3
            t_cc = t3
            xlsrc = io.xlnd_in
            xln_tok = None
            px_free = pre_done[0]
            xtk = None
        else:
            t_wp = None
            xlsrc = io.xlnd
            xln_tok = t_p1_st
        t_off = stage_offsets(kb, io, sb, ps, t_cc, fused)
        t_gate = None
        for seq in range(SEQS):
            t_comb = stage_comb(kb, io, sb, ps, seq, t_off, t_wp)
            t_px = load_px(kb, io, sb, seq, xln_tok if fused else t_wp,
                           px_free=px_free, src=xlsrc)
            xsrc0 = io.xs if fused else io.x_in
            if is_final(4):
                tx, _, px_free = stage_moe(kb, io, sb, ps, seq, t_px, t_comb,
                                           xsrc0, io.out, False, px_free, x_tok=xtk)
                xtk = tx
                continue
            tx, t_xln, px_free = stage_moe(kb, io, sb, ps, seq, t_px, t_comb,
                                           xsrc0, io.xs, True, px_free, x_tok=xtk)
            xtk = tx
            t_px = load_px(kb, io, sb, seq, t_xln, px_free=px_free)
            t_dd, px_free = stage_attn(kb, io, sb, ps, seq, 1, t_px, wfree=px_free, pv_tok=kb.tw_done)
            if is_final(5):
                tx, _ = pass_residual_ln(kb, io, sb, ps, seq, "delta", False,
                                         io.xs, io.out, dd_tok=t_dd, x_tok=xtk)
                xtk = tx
                continue
            tx, t_xln = pass_residual_ln(kb, io, sb, ps, seq, "delta", True,
                                         io.xs, io.xs, px_free=px_free, dd_tok=t_dd,
                                         x_tok=xtk)
            xtk = tx
            t_px = load_px(kb, io, sb, seq, t_xln, px_free=px_free)
            t_gate, px_free = stage_gate(kb, io, sb, ps, seq, t_px)
        if UPTO <= 5:
            final_wait()
            return
        t_gd = dump_gate(kb, io, sb, t_gate)
        t_cc = stage_counts_out(kb, io, sb, ps, t_gate, fused)
        if phase == "p2":
            final_wait()
            return
        t_p2_st = ("xlnd", kb.cnt["xlnd"])

    if phase == "p3":
        t1 = kb.op("sp", lambda e: e.dma_start(sb.wfull[:], io.gdi_wfull), dma="msc")
        t2 = kb.op("sp", lambda e: e.dma_start(sb.pref[:], io.gdi_pref), dma="msc")
        t3 = kb.op("sp", lambda e: e.dma_start(sb.Trow, io.gdi_trow), dma="msc")
        t_wp = t3
        t_cc = t3
        xlsrc = io.xlnd_in
        xsrc = io.x_in
        xln_tok = None
        px_free = pre_done[0]
        xtk = None
    else:
        t_wp = None
        xlsrc = io.xlnd
        xsrc = io.xs
        xln_tok = t_p2_st
    t_off = stage_offsets(kb, io, sb, ps, t_cc, fused)
    for seq in range(SEQS):
        t_comb = stage_comb(kb, io, sb, ps, seq, t_off, t_wp)
        t_px = load_px(kb, io, sb, seq, xln_tok if fused else t_wp,
                       px_free=px_free, src=xlsrc)
        tx, _, px_free = stage_moe(kb, io, sb, ps, seq, t_px, t_comb,
                                   xsrc, io.out, False, px_free, x_tok=xtk)
        xtk = tx
    final_wait()



def host_consts(core):
    k = np.arange(128)[:, None]
    m = np.arange(128)[None, :]
    return {
        "u128": (k < m).astype(ml_dtypes.bfloat16),
        "cmask": np.where(k <= m, 0.0, -BIG).astype(ml_dtypes.bfloat16),
        "rankmask": (np.arange(8) < core).astype(np.float32)[:, None],
        "rowsel": (np.arange(8) == core).astype(np.float32)[:, None],
    }


def kernel(**inputs):
    x_full = np.ascontiguousarray(np.asarray(inputs["L0_attn_h"], dtype=np.float32))
    shards = [np.ascontiguousarray(x_full[4 * c:4 * (c + 1)].reshape(TOK, D))
              for c in range(NCORES)]
    weights = {nm: np.ascontiguousarray(np.asarray(inputs[nm], dtype=np.float32))
               for nm, _ in W_NAMES if nm != "L0_attn_h"}
    weights.pop("L0_attn_h", None)

    def maps(extra_per_core):
        out = []
        for c in range(NCORES):
            m = dict(weights)
            m.update(host_consts(c))
            m.update(extra_per_core[c])
            out.append(m)
        return out

    if FUSED:
        nc = build_program("all")
        in_maps = maps([{"x": shards[c]} for c in range(NCORES)])
        res = run_bass_kernel_spmd(nc, in_maps, core_ids=list(range(NCORES)))
        outs = [np.asarray(r["out"]) for r in res.results]
    else:
        nc1 = build_program("p1")
        in_maps = maps([{"x": shards[c]} for c in range(NCORES)])
        r1 = run_bass_kernel_spmd(nc1, in_maps, core_ids=list(range(NCORES)))
        if UPTO <= 3:
            outs = [np.asarray(r["out"]) for r in r1.results]
        else:
            cc = np.concatenate([np.asarray(r["ccrow"]) for r in r1.results], axis=0)
            nc2 = build_program("p2")
            in_maps = maps([{
                "xs_in": np.asarray(r1.results[c]["xs_out"]),
                "xlnd_in": np.asarray(r1.results[c]["xlnd_out"]),
                "gdi_wfull": np.asarray(r1.results[c]["gdo_wfull"]),
                "gdi_pref": np.asarray(r1.results[c]["gdo_pref"]),
                "gdi_trow": np.asarray(r1.results[c]["gdo_trow"]),
                "cc": cc,
            } for c in range(NCORES)])
            r2 = run_bass_kernel_spmd(nc2, in_maps, core_ids=list(range(NCORES)))
            if UPTO <= 5:
                outs = [np.asarray(r["out"]) for r in r2.results]
            else:
                cc2 = np.concatenate([np.asarray(r["ccrow"]) for r in r2.results], axis=0)
                nc3 = build_program("p3")
                in_maps = maps([{
                    "xs_in": np.asarray(r2.results[c]["xs_out"]),
                    "xlnd_in": np.asarray(r2.results[c]["xlnd_out"]),
                    "gdi_wfull": np.asarray(r2.results[c]["gdo_wfull"]),
                    "gdi_pref": np.asarray(r2.results[c]["gdo_pref"]),
                    "gdi_trow": np.asarray(r2.results[c]["gdo_trow"]),
                    "cc": cc2,
                } for c in range(NCORES)])
                r3 = run_bass_kernel_spmd(nc3, in_maps, core_ids=list(range(NCORES)))
                outs = [np.asarray(r["out"]) for r in r3.results]

    full = np.concatenate([o.reshape(4, S, D) for o in outs], axis=0)
    return full.astype(np.float32)


# version 22
# speedup vs baseline: 1.0897x; 1.0042x over previous
import numpy as np
import ml_dtypes
import concourse.bass as bass
import concourse.mybir as mybir
from concourse.bass_utils import run_bass_kernel_spmd
from contextlib import ExitStack

F32 = mybir.dt.float32
BF16 = mybir.dt.bfloat16
AF = mybir.ActivationFunctionType
ALU = mybir.AluOpType
AX = mybir.AxisListType

NCORES = 8
SEQS = 4
S = 2048
D = 1024
NCH = 16
TOK = SEQS * S
CAP = 20480.0
LN_EPS = 1e-5
BIG = 1.0e30
MAGIC = 1.5 * 2.0 ** 23

UPTO = 6
DEVSTOP = 0
FUSED = True

ENG2BLK = {"pe": "tensor", "act": "scalar", "dve": "vector", "pool": "gpsimd", "sp": "sync"}


class KB:
    def __init__(self, nc):
        self.nc = nc
        self.ops = {e: [] for e in ENG2BLK}
        self.cnt = {}
        self.sems = {}

    LANES = ("xc0", "xc1", "px", "lw0", "lw1", "qw0", "qw1", "kq",
             "xst0", "xst1", "xlnd", "ddl", "msc", "mscp", "pw", "pr", "cc", "tw")

    def make_sems(self, ctx):
        for name in ("pe", "act", "dve", "pool", "sp") + KB.LANES:
            self.sems[name] = ctx.enter_context(self.nc.semaphore("sem_" + name))
            self.cnt[name] = 0

    def op(self, eng, fn, waits=(), hard=(), sig=None, dma=None):
        incs = []
        tok = None
        if eng in ("dve", "act", "pool") and dma is None:
            hard = tuple(hard) + ((eng, self.cnt[eng]),)
            if sig is None:
                sig = eng
        if sig is not None:
            self.cnt[sig] += 1
            incs.append((sig, 1))
            tok = (sig, self.cnt[sig])
        if dma is not None:
            self.cnt[dma] += 16
            incs.append((dma, 16))
            tok = (dma, self.cnt[dma])
        ws = tuple(w for w in waits if w is not None)
        hs = tuple(h for h in hard if h is not None)
        for t in ws + hs:
            assert t[1] <= self.cnt[t[0]], f"wait on future token {t} (cnt={self.cnt[t[0]]})"
        self.ops[eng].append((ws, hs, fn, incs))
        return tok

    def wait(self, eng, waits):
        ws = tuple(w for w in waits if w is not None)
        for t in ws:
            assert t[1] <= self.cnt[t[0]], f"wait on future token {t}"
        self.ops[eng].append((ws, (), None, []))

    def emit(self):
        with self.nc.Block() as blk:
            for eng in ENG2BLK:
                ops = self.ops[eng]
                own = eng

                def body(e, ops=ops, own=own):
                    waited = {}
                    own_cnt = 0
                    for ws, hs, fn, incs in ops:
                        need = {}
                        for s, c in ws:
                            if s == own and c <= own_cnt:
                                continue
                            if need.get(s, 0) < c:
                                need[s] = c
                        for s, c in hs:
                            if need.get(s, 0) < c:
                                need[s] = c
                        for s, c in need.items():
                            if waited.get(s, 0) < c:
                                e.wait_ge(self.sems[s], c)
                                waited[s] = c
                        if fn is not None:
                            instr = fn(e)
                            for s, n in incs:
                                instr.then_inc(self.sems[s], n)
                            own_cnt += sum(1 for s, n in incs if s == own)

                getattr(blk, ENG2BLK[eng])(body)


class NS:
    pass


W_NAMES = [
    ("L0_attn_Wq", [D, D]), ("L0_attn_Wk", [D, D]), ("L0_attn_Wv", [D, D]),
    ("L0_attn_Wo", [D, D]), ("L0_attn_gamma", [D]), ("L0_attn_beta", [D]),
    ("L0_ffn_kernel", [D, D]), ("L0_ffn_bias", [D]), ("L0_ffn_a", [D, 16]),
    ("L0_ffn_b", [16, D]), ("L1_attn_in_proj_w", [3 * D, D]),
    ("L1_attn_in_proj_b", [3 * D]), ("L1_attn_out_proj_w", [D, D]),
    ("L1_attn_out_proj_b", [D]), ("L1_ffn_gate_w", [8, D]),
    ("L1_ffn_we_gate", [8, D, D]), ("L1_ffn_we_up", [8, D, D]),
    ("L1_ffn_we_down", [8, D, D]), ("L1_ffn_ws_gate", [2 * D, D]),
    ("L1_ffn_ws_up", [2 * D, D]), ("L1_ffn_ws_down", [D, 2 * D]),
]

SC_NAMES = [
    ("q", [D, D]), ("k", [D, D]), ("v", [D, D]), ("o", [D, D]),
    ("kq", [D, D]), ("ip", [3 * D, D]), ("op", [D, D]),
    ("weg", [8, D, D]), ("weu", [8, D, D]), ("wed", [8, D, D]),
    ("wsg", [2 * D, D]), ("wsu", [2 * D, D]), ("wsd", [D, 2 * D]),
    ("gw", [16, D]),
    ("bv", [16, 4 * D]),
]


def declare_io(nc, phase):
    io = NS()
    din = lambda name, shape, dt=F32: nc.dram_tensor(name, shape, dt, kind="ExternalInput").ap()
    dout = lambda name, shape, dt=F32: nc.dram_tensor(name, shape, dt, kind="ExternalOutput").ap()
    dint = lambda name, shape, dt=BF16: nc.dram_tensor(name, shape, dt).ap()

    io.w = {nm: din(nm, shp) for nm, shp in W_NAMES}
    io.u128 = din("u128", [128, 128], BF16)
    io.cmask = din("cmask", [128, 128], BF16)
    io.rankmask = din("rankmask", [8, 1])
    io.rowsel = din("rowsel", [8, 1])

    last = (phase == "all") or (phase == "p1" and UPTO <= 3) or \
           (phase == "p2" and UPTO <= 5) or (phase == "p3")
    io.x_in = din("x", [TOK, D]) if phase in ("all", "p1") else din("xs_in", [TOK, D])
    io.out = dout("out", [TOK, D])
    io.xs = dint("xs", [TOK, D], F32) if (phase == "all" or last) else dout("xs_out", [TOK, D], F32)
    if phase == "all":
        io.xlnd = dint("xlnd", [TOK, D], BF16)
        io.xlnd_in = io.xlnd
    else:
        io.xlnd = io_xlnd_out = (dint("xlnd", [TOK, D], BF16) if last
                                 else dout("xlnd_out", [TOK, D], BF16))
        io.xlnd_in = din("xlnd_in", [TOK, D], BF16) if phase in ("p2", "p3") else io.xlnd
    if phase == "all":
        io.gd_wfull = dint("gd_wfull", [128, 512], F32)
        io.gd_pref = dint("gd_pref", [128, 512], F32)
        io.gd_trow = dint("gd_trow", [1, 512], F32)
        io.ccrow = None
        io.gdi_wfull = io.gd_wfull
        io.gdi_pref = io.gd_pref
        io.gdi_trow = io.gd_trow
        io.cc = None
    else:
        mkout = phase in ("p1", "p2") and not last
        f = dout if mkout else dint
        io.gd_wfull = f("gdo_wfull" if mkout else "gd_wfull", [128, 512], F32)
        io.gd_pref = f("gdo_pref" if mkout else "gd_pref", [128, 512], F32)
        io.gd_trow = f("gdo_trow" if mkout else "gd_trow", [1, 512], F32)
        io.ccrow = dout("ccrow", [1, 8], F32) if mkout else None
        if phase in ("p2", "p3"):
            io.gdi_wfull = din("gdi_wfull", [128, 512], F32)
            io.gdi_pref = din("gdi_pref", [128, 512], F32)
            io.gdi_trow = din("gdi_trow", [1, 512], F32)
            io.cc = din("cc", [8, 8], F32)
        else:
            io.gdi_wfull = io.gdi_pref = io.gdi_trow = io.cc = None

    io.sc = {nm: dint("sc_" + nm, shp, BF16) for nm, shp in SC_NAMES}
    io.tw = {}
    for nm, shp in [("ipq", [128, 8192]), ("ipk", [128, 8192]), ("ipv", [128, 8192]),
                    ("op", [128, 8192]), ("weg", [8, 128, 8192]), ("weu", [8, 128, 8192]),
                    ("wed", [8, 128, 8192]), ("wsg", [2, 128, 8192]),
                    ("wsu", [2, 128, 8192]), ("wsd", [2, 128, 8192])]:
        io.tw[nm] = dint("tw_" + nm, shp, BF16)
    io.dd = dint("dd", [D, S], BF16)
    io.bounce = dint("bounce", [1, 1024], F32)
    io.bounce2 = dint("bounce2", [1024, 1], F32)
    io.t64d = dint("t64d", [64, 8], F32)
    io.offd = dint("offd", [1, 512], F32)
    io.ccin_d = dint("ccin_d", [8, 8], F32)
    io.ccout_d = dint("ccout_d", [8, 8], F32)
    return io


def alloc_sbuf(nc, ctx):
    sb = NS()
    t = lambda n, s, d=BF16: ctx.enter_context(nc.sbuf_tensor(n, s, d))
    sb.PX = t("PX", [128, 16384])
    sb.PQ = t("PQ", [128, 16384])
    sb.PK = t("PK", [128, 16384])
    sb.PV = t("PV", [128, 16640])
    sb.PO = t("PO", [128, 16384])
    sb.POf = sb.PO[:].bitcast(F32)
    sb.xc = t("xc", [128, 2048], F32)
    sb.dtb = t("dtb", [128, 2048], BF16)
    sb.st = t("stg", [128, 4096], BF16)
    sb.tt = sb.st[:, 0:2048].bitcast(F32)
    sb.bn6 = t("bn6", [128, 12], F32)
    sb.stats = t("stats", [128, 32], F32)
    sb.mu16 = t("mu16", [128, 16], F32)
    sb.rstd16 = t("rstd16", [128, 16], F32)
    sb.sc1 = t("sc1", [128, 64], F32)
    sb.packA = t("packA", [128, 2048], F32)
    sb.row = sb.packA[0:1, 0:1024]
    sb.Trow = sb.packA[0:1, 1024:1536]
    sb.OFFROW = sb.packA[0:1, 1536:2048]
    sb.vb_row = t("vb_row", [1, 1024], BF16)
    sb.biasf = t("biasf", [128, 32], F32)
    sb.rowb = t("rowbufb", [1, 512], BF16)
    sb.u128 = t("u128c", [128, 128], BF16)
    sb.cmask = t("cmaskc", [128, 128], BF16)
    sb.rankmask = t("rankmaskc", [8, 1], F32)
    sb.rowsel = t("rowselc", [8, 1], F32)
    sb.ones_row_b = t("ones_row_b", [1, 128], BF16)
    sb.ones512_b = t("ones512_b", [1, 512], BF16)
    sb.ones_row_f = t("ones_row_f", [1, 128], F32)
    sb.ones_col_b = t("ones_col_b", [128, 1], BF16)
    sb.gamma_rep = t("gamma_rep", [128, D], BF16)
    sb.beta_rep = t("beta_rep", [128, D], BF16)
    sb.gwT = t("gwT", [128, 128], BF16)
    sb.quant = t("quant", [128, 8], F32)
    sb.rrep = t("rrep", [128, 2], F32)
    sb.wfull = t("wfull", [128, 512], F32)
    sb.pref = t("pref", [128, 512], F32)
    sb.comb = t("comb", [128, 512], F32)
    sb.CC = t("CCt", [8, 8], F32)
    sb.g8 = t("g8", [128, 64], F32)
    sb.selb = t("selb", [128, 32], BF16)
    sb.cb = t("cb", [128, 4], F32)
    return sb


def alloc_psum(nc, ctx):
    ps = NS()
    ps.A = ctx.enter_context(nc.psum_tensor("PSA", [128, 2048], F32))
    ps.B = ctx.enter_context(nc.psum_tensor("PSB", [128, 2048], F32))
    return ps



def stage_preprocess(kb, io, sb, ps):
    row1 = lambda ap: ap.rearrange("(o n) -> o n", o=1)
    for dst, src in [(sb.u128, io.u128), (sb.cmask, io.cmask),
                     (sb.rankmask, io.rankmask), (sb.rowsel, io.rowsel)]:
        kb.op("sp", lambda e, d=dst, s=src: e.dma_start(d[:], s), dma="msc")
    t_mp = kb.op("pool", lambda e: e.dma_start(
        sb.vb_row[:], io.w["L1_attn_in_proj_b"].rearrange("(o n) -> o n", o=1)[0:1, 2 * D:3 * D]),
        dma="mscp")
    t_g1 = kb.op("sp", lambda e: e.dma_start(sb.packA[0:1, 0:D], row1(io.w["L0_attn_gamma"])), dma="msc")
    kb.op("dve", lambda e: e.memset(sb.ones_row_b[:], 1.0), sig="dve")
    kb.op("dve", lambda e: e.memset(sb.ones512_b[:], 1.0), sig="dve")
    kb.op("dve", lambda e: e.memset(sb.ones_row_f[:], 1.0), sig="dve")
    t_ones = kb.op("dve", lambda e: e.memset(sb.ones_col_b[:], 1.0), sig="dve")
    tg_ev = []
    for j in range(2):
        tm = kb.op("pe", lambda e, j=j: e.matmul(
            ps.A[:, 512 * j:512 * (j + 1)], sb.ones_row_f[:],
            sb.packA[0:1, 512 * j:512 * (j + 1)], start=True, stop=True),
            waits=(t_g1, t_ones), sig="pe")
        tg_ev.append(kb.op("dve", lambda e, j=j: e.tensor_copy(
            sb.gamma_rep[:, 512 * j:512 * (j + 1)], ps.A[:, 512 * j:512 * (j + 1)]),
            waits=(tm,), sig="dve"))
    t_b1 = kb.op("sp", lambda e: e.dma_start(sb.packA[0:1, 0:D], row1(io.w["L0_attn_beta"])),
                 waits=(tg_ev[0], tg_ev[1]), dma="msc")
    for j in range(2):
        tm = kb.op("pe", lambda e, j=j: e.matmul(
            ps.A[:, 1024 + 512 * j:1024 + 512 * (j + 1)], sb.ones_row_f[:],
            sb.packA[0:1, 512 * j:512 * (j + 1)], start=True, stop=True),
            waits=(t_b1,), sig="pe")
        kb.op("dve", lambda e, j=j: e.tensor_copy(
            sb.beta_rep[:, 512 * j:512 * (j + 1)], ps.A[:, 1024 + 512 * j:1024 + 512 * (j + 1)]),
            waits=(tm,), sig="dve")

    cast_jobs = []

    def cast_copy(w_ap, sc_ap):
        rows = w_ap.shape[-2]
        for r0 in range(0, rows, 128):
            cast_jobs.append((w_ap, sc_ap, r0))

    for nm, src in [("q", io.w["L0_attn_Wq"]), ("k", io.w["L0_attn_Wk"]),
                    ("v", io.w["L0_attn_Wv"]), ("o", io.w["L0_attn_Wo"]),
                    ("op", io.w["L1_attn_out_proj_w"]), ("ip", io.w["L1_attn_in_proj_w"]),
                    ("wsg", io.w["L1_ffn_ws_gate"]), ("wsu", io.w["L1_ffn_ws_up"]),
                    ("wsd", io.w["L1_ffn_ws_down"])]:
        cast_copy(src, io.sc[nm])
    for nm, src in [("weg", io.w["L1_ffn_we_gate"]), ("weu", io.w["L1_ffn_we_up"]),
                    ("wed", io.w["L1_ffn_we_down"])]:
        for ee in range(8):
            cast_copy(src[ee], io.sc[nm][ee])
    t_r = ("dve", kb.cnt["dve"])
    for b0 in range(0, len(cast_jobs), 8):
        batch = cast_jobs[b0:b0 + 8]
        t_w = None
        for i, (w_ap, sc_ap, r0) in enumerate(batch):
            cols = w_ap.shape[-1]
            t_w = kb.op("pool", lambda e, w_ap=w_ap, r0=r0, c0=2048 * i, cols=cols: e.dma_start(
                sb.PO[:, c0:c0 + cols], w_ap[r0:r0 + 128, :]), waits=(t_r,), dma="pw")
        for i, (w_ap, sc_ap, r0) in enumerate(batch):
            cols = w_ap.shape[-1]
            t_r = kb.op("sp", lambda e, sc_ap=sc_ap, r0=r0, c0=2048 * i, cols=cols: e.dma_start(
                sc_ap[r0:r0 + 128, :], sb.PO[:, c0:c0 + cols]), waits=(t_w,), dma="pr")
    tgwz = kb.op("dve", lambda e: e.memset(sb.PO[0:16, 0:D], 0.0), waits=(t_r,), sig="dve")
    tgwc = kb.op("pool", lambda e: e.dma_start(sb.PO[0:8, 0:D], io.w["L1_ffn_gate_w"]),
                 waits=(tgwz, t_r, t_mp), dma="mscp")
    tgws = kb.op("sp", lambda e: e.dma_start(io.sc["gw"], sb.PO[0:16, 0:D]),
                 waits=(tgwc,), dma="msc")
    tgwT = None
    for dc in range(8):
        tgwT = kb.op("sp", lambda e, dc=dc: e.dma_start(
            sb.gwT[:, 16 * dc:16 * (dc + 1)], io.sc["gw"][:, 128 * dc:128 * (dc + 1)],
            transpose=True), waits=(tgws,), dma="msc")

    POf = sb.POf
    tload = None
    for dc in range(8):
        tload = kb.op("sp", lambda e, dc=dc: e.dma_start(
            POf[:, 1024 * dc:1024 * (dc + 1)], io.w["L0_ffn_kernel"][128 * dc:128 * (dc + 1), :]),
            dma="pr")
    tred = []
    for dc in range(8):
        tred.append(kb.op("dve", lambda e, dc=dc: e.reduce_max(
            sb.quant[:, dc:dc + 1], POf[:, 1024 * dc:1024 * (dc + 1)],
            axis=AX.X, apply_absolute_value=True), waits=(tload,), sig="dve"))
    tm = kb.op("dve", lambda e: e.reduce_max(sb.sc1[:, 0:1], sb.quant[:, 0:8], axis=AX.X),
               hard=(tred[-1],), sig="dve")
    tb = kb.op("pool", lambda e: e.dma_start(io.bounce2[0:128, :], sb.sc1[:, 0:1]),
               waits=(tm, t_mp), dma="mscp")
    tb2 = kb.op("sp", lambda e: e.dma_start(
        sb.packA[0:1, 0:128], io.bounce2.rearrange("(o p) n -> o (p n)", o=1)[0:1, 0:128]),
        waits=(tb,), dma="msc")
    tmax = kb.op("dve", lambda e: e.reduce_max(sb.packA[0:1, 128:129], sb.packA[0:1, 0:128], axis=AX.X),
                 waits=(tb2,), sig="dve")
    tsc = kb.op("dve", lambda e: e.tensor_scalar(sb.packA[0:1, 129:130], sb.packA[0:1, 128:129],
                                                 float(1.0 / 7.0), None, ALU.mult),
                hard=(tmax,), sig="dve")
    trc = kb.op("dve", lambda e: e.reciprocal(sb.packA[0:1, 130:131], sb.packA[0:1, 129:130]),
                hard=(tsc,), sig="dve")
    tb3 = kb.op("pool", lambda e: e.dma_start(io.bounce[0:1, 0:2], sb.packA[0:1, 129:131]),
                waits=(trc,), dma="mscp")
    tb4 = kb.op("sp", lambda e: e.dma_start(sb.packA[0:1, 132:134], io.bounce[0:1, 0:2]),
                waits=(tb3,), dma="msc")
    tmr = kb.op("pe", lambda e: e.matmul(ps.B[:, 0:2], sb.ones_row_f[:], sb.packA[0:1, 132:134],
                                         start=True, stop=True), waits=(tb4,), sig="pe")
    trr = kb.op("dve", lambda e: e.tensor_copy(sb.rrep[:, 0:2], ps.B[:, 0:2]),
                waits=(tmr,), sig="dve")
    t_qst = []
    for dc in range(8):
        stg = sb.st[:, 0:1024] if dc % 2 == 0 else sb.st[:, 1024:2048]
        kb.op("dve", lambda e, dc=dc: e.tensor_scalar(
            POf[:, 1024 * dc:1024 * (dc + 1)], POf[:, 1024 * dc:1024 * (dc + 1)],
            sb.rrep[:, 1:2], MAGIC, ALU.mult, ALU.add), hard=(trr,), sig="dve")
        t2 = kb.op("dve", lambda e, dc=dc: e.tensor_scalar(
            POf[:, 1024 * dc:1024 * (dc + 1)], POf[:, 1024 * dc:1024 * (dc + 1)],
            -MAGIC, None, ALU.add), sig="dve")
        t3 = kb.op("dve", lambda e, dc=dc, stg=stg: e.tensor_scalar(
            stg, POf[:, 1024 * dc:1024 * (dc + 1)], sb.rrep[:, 0:1], None, ALU.mult),
            waits=(t_qst[dc - 2] if dc >= 2 else None,), hard=(t2,), sig="dve")
        t_qst.append(kb.op("pool", lambda e, dc=dc, stg=stg: e.dma_start(
            io.sc["kq"][128 * dc:128 * (dc + 1), :], stg),
            waits=(t3, t_qst[dc - 1] if dc >= 1 else None), dma="pw"))
    tbz = kb.op("dve", lambda e: e.memset(sb.PO[0:16, 4096:8192], 0.0), waits=(t_r,), sig="dve")
    tb0 = kb.op("pool", lambda e: e.dma_start(
        sb.PO[0:1, 4096:5120], row1(io.w["L0_ffn_bias"])), waits=(tbz, t_mp), dma="mscp")
    tb1x = kb.op("pool", lambda e: e.dma_start(
        sb.PO[0:1, 5120:7168],
        io.w["L1_attn_in_proj_b"].rearrange("(o n) -> o n", o=1)[0:1, 0:2 * D]),
        waits=(tb0,), dma="mscp")
    tb2x = kb.op("pool", lambda e: e.dma_start(
        sb.PO[0:1, 7168:8192], row1(io.w["L1_attn_out_proj_b"])), waits=(tb1x,), dma="mscp")
    tbs = kb.op("sp", lambda e: e.dma_start(io.sc["bv"], sb.PO[0:16, 4096:8192]),
                waits=(tb0, tb1x, tb2x), dma="msc")
    tbt = None
    for ch in range(32):
        tbt = kb.op("sp", lambda e, ch=ch: e.dma_start(
            sb.st[:, 16 * ch:16 * (ch + 1)], io.sc["bv"][:, 128 * ch:128 * (ch + 1)],
            transpose=True), waits=(tbs,), dma="msc")
    kb.op("dve", lambda e: e.tensor_copy(
        sb.biasf[:], sb.st[:].rearrange("p (c r) -> p c r", r=16)[:, 0:32, 0:1]
        .rearrange("p c r -> p (c r)")), waits=(tbt,), sig="dve")

    kb.t_pr_all = ("pr", kb.cnt["pr"])
    return t_qst[-1], ("pr", kb.cnt["pr"])


def image_jobs(io):
    def b_io(ooff):
        return [(1024 * dc + 512 * oh, ooff + 512 * oh, 512, 128 * dc)
                for dc in range(8) for oh in range(2)]
    jobs = []
    for nm in ("weg", "weu"):
        for ee in range(8):
            jobs.append((io.tw[nm][ee],
                         [(1024 * bc, 0, 1024, 128 * bc) for bc in range(8)],
                         io.sc[nm][ee]))
    for ee in range(8):
        jobs.append((io.tw["wed"][ee],
                     [(1024 * ic + 512 * hh, 512 * hh, 512, 128 * ic)
                      for ic in range(8) for hh in range(2)], io.sc["wed"][ee]))
    for grp in range(2):
        jobs.append((io.tw["wsg"][grp],
                     [(1024 * bc, 1024 * grp, 1024, 128 * bc) for bc in range(8)], io.sc["wsg"]))
        jobs.append((io.tw["wsu"][grp],
                     [(1024 * bc, 1024 * grp, 1024, 128 * bc) for bc in range(8)], io.sc["wsu"]))
        jobs.append((io.tw["wsd"][grp],
                     [(1024 * ic + 512 * hh, 512 * hh, 512, 128 * (8 * grp + ic))
                      for ic in range(8) for hh in range(2)], io.sc["wsd"]))
    for nm, ooff in (("ipq", 0), ("ipk", D), ("ipv", 2 * D)):
        jobs.append((io.tw[nm], b_io(ooff), io.sc["ip"]))
    jobs.append((io.tw["op"], b_io(0), io.sc["op"]))
    return jobs


def stage_images(kb, io, sb, jobs, after_tok):
    slots = (sb.PV[:, 0:8192], sb.PV[:, 8448:16640])
    t_store = [kb.tw_done, kb.tw_done]
    for i, (dst, blocks, src) in enumerate(jobs):
        stg = slots[i % 2]
        t_tr = None
        for (doff, r0, rn, c0) in blocks:
            t_tr = kb.op("act", lambda e, stg=stg, doff=doff, r0=r0, rn=rn, c0=c0, src=src: e.dma_start(
                stg[:, doff:doff + rn], src[r0:r0 + rn, c0:c0 + 128], transpose=True),
                waits=(t_store[i % 2], kb.t_pr_all, after_tok), dma="tw")
        t_store[i % 2] = kb.op("act", lambda e, dst=dst, stg=stg: e.dma_start(dst, stg),
                               waits=(t_tr,), dma="tw")
    kb.tw_done = t_store[(len(jobs) - 1) % 2]
    kb.tw_done = (kb.tw_done[0], max(kb.cnt["tw"], kb.tw_done[1]))
    return kb.tw_done



def pass_residual_ln(kb, io, sb, ps, seq, mode, want_ln, x_src, x_dst,
                     scale_delta=1.0, px_free=None, chunks=None, y_tok=None,
                     dd_tok=None, x_tok=None, pq_tok=None):
    if chunks is None:
        chunks = list(range(NCH))
    else:
        chunks = list(chunks)
    t_aff = None
    last_store = None
    st_tok = {}
    prev_v = {}
    batch_tok = [pq_tok]
    x_tok = tuple(x_tok) if isinstance(x_tok, (tuple, list)) else (x_tok,)
    for c in chunks:
        xcs = sb.xc[:, 1024 * (c % 2):1024 * (c % 2 + 1)]
        row0 = seq * S + 128 * c
        lane = "xc%d" % (c % 2)
        pv = prev_v.get(c % 2, ())
        tld = kb.op("sp", lambda e, xcs=xcs, row0=row0, x_src=x_src: e.dma_start(
            xcs, x_src[row0:row0 + 128, :]), waits=pv + x_tok, dma=lane)
        tv = tld
        if mode == "delta":
            dtb = sb.dtb[:, 1024 * (c % 2):1024 * (c % 2 + 1)]
            tdt = kb.op("sp", lambda e, dtb=dtb, c=c: e.dma_start(
                dtb, io.dd[:, 128 * c:128 * (c + 1)], transpose=True),
                waits=pv + (dd_tok,), dma=lane)
            tv = kb.op("dve", lambda e, xcs=xcs, dtb=dtb: e.scalar_tensor_tensor(
                xcs, dtb, scale_delta, xcs, ALU.mult, ALU.add), waits=(tld, tdt), sig="dve")
        elif mode == "y":
            ysl = sb.POf[:, 1024 * (c % 8):1024 * (c % 8 + 1)]
            tv = kb.op("dve", lambda e, xcs=xcs, ysl=ysl: e.scalar_tensor_tensor(
                xcs, ysl, 0.5, xcs, ALU.mult, ALU.add), waits=(tld, y_tok), sig="dve")
        elif mode == "l0inner":
            dtb = sb.dtb[:, 1024 * (c % 2):1024 * (c % 2 + 1)]
            tdt = kb.op("sp", lambda e, dtb=dtb, c=c: e.dma_start(
                dtb, io.dd[:, 128 * c:128 * (c + 1)], transpose=True),
                waits=pv + (dd_tok,), dma=lane)
            tt = sb.tt[:, 0:1024]
            kb.op("dve", lambda e, xcs=xcs, tt=tt, c=c: e.tensor_scalar(
                tt, xcs, sb.stats[:, 2 * c:2 * c + 1], sb.rstd16[:, c:c + 1],
                ALU.subtract, ALU.mult), waits=(tld,), sig="dve")
            kb.op("dve", lambda e, tt=tt, dtb=dtb: e.tensor_tensor(
                tt, tt, dtb, ALU.add), waits=(tdt,), sig="dve")
            kb.op("dve", lambda e, tt=tt: e.bn_stats(sb.bn6[:, 0:6], tt[:, 0:512]), sig="dve")
            tb1 = kb.op("dve", lambda e, tt=tt: e.bn_stats(sb.bn6[:, 6:12], tt[:, 512:1024]), sig="dve")
            tb2 = kb.op("dve", lambda e: e.bn_aggr(sb.sc1[:, 2:4], sb.bn6[:]),
                        hard=(tb1,), sig="dve")
            tv1 = kb.op("dve", lambda e: e.tensor_scalar(
                sb.sc1[:, 4:5], sb.sc1[:, 3:4], LN_EPS, None, ALU.add), hard=(tb2,), sig="dve")
            tl1 = kb.op("act", lambda e: e.activation(sb.sc1[:, 5:6], sb.sc1[:, 4:5], AF.Ln,
                                                      bias=0.0, scale=1.0), waits=(tv1,), sig="act")
            tl2 = kb.op("act", lambda e: e.activation(sb.sc1[:, 6:7], sb.sc1[:, 5:6], AF.Exp,
                                                      bias=0.0, scale=-0.5), hard=(tl1,), sig="act")
            kb.op("dve", lambda e, tt=tt: e.tensor_scalar(
                tt, tt, sb.sc1[:, 2:3], sb.sc1[:, 6:7], ALU.subtract, ALU.mult),
                hard=(tl2, tb2), sig="dve")
            kb.op("dve", lambda e, tt=tt: e.tensor_tensor(tt, tt, sb.gamma_rep[:], ALU.mult),
                  sig="dve")
            kb.op("dve", lambda e, tt=tt: e.tensor_tensor(tt, tt, sb.beta_rep[:], ALU.add),
                  sig="dve")
            tv = kb.op("dve", lambda e, xcs=xcs, tt=tt: e.tensor_tensor(xcs, xcs, tt, ALU.add),
                       sig="dve")
        last_store = kb.op("pool", lambda e, xcs=xcs, row0=row0, x_dst=x_dst: e.dma_start(
            x_dst[row0:row0 + 128, :], xcs), waits=(tv,), dma="xst%d" % (c % 2))
        st_tok[c % 2] = last_store
        if want_ln:
            kb.op("dve", lambda e, xcs=xcs: e.bn_stats(sb.bn6[:, 0:6], xcs[:, 0:512]),
                  waits=(tv,), sig="dve")
            tb1 = kb.op("dve", lambda e, xcs=xcs: e.bn_stats(sb.bn6[:, 6:12], xcs[:, 512:1024]),
                  sig="dve")
            tb2 = kb.op("dve", lambda e, c=c: e.bn_aggr(sb.stats[:, 2 * c:2 * c + 2], sb.bn6[:]),
                        hard=(tb1,), sig="dve")
            tv1 = kb.op("dve", lambda e, c=c: e.tensor_scalar(
                sb.sc1[:, 8:9], sb.stats[:, 2 * c + 1:2 * c + 2], LN_EPS, None, ALU.add),
                hard=(tb2,), sig="dve")
            tl1 = kb.op("act", lambda e: e.activation(
                sb.sc1[:, 9:10], sb.sc1[:, 8:9], AF.Ln, bias=0.0, scale=1.0),
                waits=(tv1,), sig="act")
            tl2 = kb.op("act", lambda e, c=c: e.activation(
                sb.rstd16[:, c:c + 1], sb.sc1[:, 9:10], AF.Exp, bias=0.0, scale=-0.5),
                hard=(tl1,), sig="act")
            t_aff = kb.op("dve", lambda e, xcs=xcs, c=c: e.tensor_scalar(
                sb.PQ[:, 1024 * (c % 8):1024 * (c % 8 + 1)], xcs,
                sb.stats[:, 2 * c:2 * c + 1], sb.rstd16[:, c:c + 1],
                ALU.subtract, ALU.mult),
                waits=(px_free, batch_tok[0]),
                hard=(tl2, tb2), sig="dve")
            prev_v[c % 2] = (last_store, t_aff)
            if want_ln and c % 8 == 7:
                c0b = c - 7
                batch_tok[0] = kb.op("pool", lambda e, c0b=c0b, seq=seq: e.dma_start(
                    io.xlnd[seq * S + 128 * c0b:seq * S + 128 * (c0b + 8), :]
                    .rearrange("(c p) n -> p c n", p=128),
                    sb.PQ[:, 0:8192].rearrange("p (c n) -> p c n", n=1024)),
                    waits=(t_aff,), dma="xlnd")
        else:
            prev_v[c % 2] = (last_store, tv)
    ret_store = (st_tok.get(0), st_tok.get(1))
    if not want_ln:
        return ret_store, None
    return ret_store, batch_tok[0]


def load_px(kb, io, sb, seq, t_ready, px_free=None, src=None):
    t = None
    xl = src if src is not None else io.xlnd
    for dc in range(8):
        t = kb.op("sp", lambda e, dc=dc, xl=xl, seq=seq: e.dma_start(
            sb.PX[:, 2048 * dc:2048 * (dc + 1)],
            xl[seq * S:(seq + 1) * S, 128 * dc:128 * (dc + 1)], transpose=True),
            waits=(t_ready, px_free), dma="px")
    return t



def fm_proj(kb, sb, ps, dst, reg, wtok, t_px, bias_col=None,
            transposed=False, war0=None, evac_eng="dve"):
    ev = {}
    t_last = None
    for fc in range(8):
        for slab in range(4):
            pa = ps.A[:, 512 * slab:512 * (slab + 1)]
            for dc in range(8):
                if transposed:
                    off = reg + 1024 * dc + 512 * (fc // 4) + 128 * (fc % 4)
                else:
                    off = reg + 1024 * dc + 128 * fc
                lhsT = sb.PO[:, off:off + 128]
                w0 = (wtok, ev.get(slab), t_px, war0) if dc == 0 else ()
                tmm = kb.op("pe", lambda e, pa=pa, lhsT=lhsT, dc=dc, slab=slab: e.matmul(
                    pa, lhsT, sb.PX[:, 2048 * dc + 512 * slab:2048 * dc + 512 * (slab + 1)],
                    start=(dc == 0), stop=(dc == 7)),
                    waits=w0, sig="pe" if dc == 7 else None)
            if bias_col is not None:
                ev[slab] = kb.op("dve", lambda e, fc=fc, slab=slab, pa=pa, dst=dst: e.tensor_scalar(
                    dst[:, 2048 * fc + 512 * slab:2048 * fc + 512 * (slab + 1)], pa,
                    sb.biasf[:, bias_col + fc:bias_col + fc + 1], None, ALU.add),
                    waits=(tmm,), sig="dve")
            elif evac_eng == "dve":
                ev[slab] = kb.op("dve", lambda e, fc=fc, slab=slab, pa=pa, dst=dst: e.tensor_copy(
                    dst[:, 2048 * fc + 512 * slab:2048 * fc + 512 * (slab + 1)], pa),
                    waits=(tmm,), sig="dve")
            else:
                ev[slab] = kb.op("act", lambda e, fc=fc, slab=slab, pa=pa, dst=dst: e.activation(
                    dst[:, 2048 * fc + 512 * slab:2048 * fc + 512 * (slab + 1)], pa,
                    AF.Copy, bias=0.0, scale=1.0), waits=(tmm,), sig="act")
            t_last = ev[slab]
    return t_last



def stage_ffn0(kb, io, sb, ps, t_px, wfree=None):
    t_w = None
    for dc in range(8):
        t_w = kb.op("sp", lambda e, dc=dc: e.dma_start(
            sb.PK[:, 1024 * dc:1024 * (dc + 1)], io.sc["kq"][128 * dc:128 * (dc + 1), :]),
            waits=(wfree,), dma="kq")
    aT = sb.PK[:, 9216:9344]
    bT = sb.PK[0:16, 8192:9216]
    kb.op("pool", lambda e, aT=aT: e.dma_start(
        aT.rearrange("p (c r) -> p c r", r=16),
        io.w["L0_ffn_a"].rearrange("(c p) r -> p c r", p=128)), waits=(wfree,), dma="pw")
    t_ab2 = kb.op("pool", lambda e, bT=bT: e.dma_start(bT, io.w["L0_ffn_b"]),
                  waits=(wfree,), dma="pw")
    tl1 = []
    for slab in range(4):
        t = None
        for dc in range(8):
            t = kb.op("pe", lambda e, slab=slab, dc=dc, aT=aT: e.matmul(
                ps.B[0:16, 512 * slab:512 * (slab + 1)],
                aT[:, 16 * dc:16 * (dc + 1)],
                sb.PX[:, 2048 * dc + 512 * slab:2048 * dc + 512 * (slab + 1)],
                start=(dc == 0), stop=(dc == 7)),
                waits=(t_px, t_ab2) if dc == 0 else (), sig="pe" if dc == 7 else None)
        tl1.append(t)
    tl1b = []
    for slab in range(4):
        tl1b.append(kb.op("dve", lambda e, slab=slab: e.tensor_copy(
            sb.st[0:16, 512 * slab:512 * (slab + 1)], ps.B[0:16, 512 * slab:512 * (slab + 1)]),
            waits=(tl1[slab],), sig="dve"))
    ev = {}
    t_dd = None
    for fc in range(8):
        for slab in range(4):
            pa = ps.A[:, 512 * slab:512 * (slab + 1)]
            for dc in range(8):
                kb.op("pe", lambda e, fc=fc, slab=slab, dc=dc, pa=pa: e.matmul(
                    pa, sb.PK[:, 1024 * dc + 128 * fc:1024 * dc + 128 * (fc + 1)],
                    sb.PX[:, 2048 * dc + 512 * slab:2048 * dc + 512 * (slab + 1)],
                    start=(dc == 0), stop=False),
                    waits=(t_w, ev.get(slab)) if dc == 0 else ())
            tmm = kb.op("pe", lambda e, fc=fc, slab=slab, pa=pa, bT=bT: e.matmul(
                pa, bT[:, 128 * fc:128 * (fc + 1)],
                sb.st[0:16, 512 * slab:512 * (slab + 1)], start=False, stop=True),
                waits=(tl1b[slab], t_ab2), sig="pe")
            ev[slab] = kb.op("dve", lambda e, slab=slab, fc=fc, pa=pa: e.tensor_scalar(
                sb.st[:, 2048 + 512 * slab:2048 + 512 * (slab + 1)], pa,
                sb.biasf[:, fc:fc + 1], 0.5, ALU.add, ALU.mult),
                waits=(tmm,), sig="dve")
        tsd = kb.op("pool", lambda e, fc=fc: e.dma_start(
            io.dd[128 * fc:128 * (fc + 1), :], sb.st[:, 2048:4096]),
            waits=(ev[0], ev[1], ev[2], ev[3]), dma="ddl")
        for slab in range(4):
            ev[slab] = tsd
        t_dd = tsd
    return t_dd, ("pe", kb.cnt["pe"])



def stage_attn(kb, io, sb, ps, seq, layer, t_px, wfree=None, pv_tok=None):
    L1 = (layer == 1)
    regA, regB = 0, 8192
    SC = 0.125

    def load_direct(scname, reg, wtok=None):
        lane = "qw0" if reg == 0 else "qw1"
        t = None
        for dc in range(8):
            t = kb.op("sp", lambda e, dc=dc, scname=scname, reg=reg: e.dma_start(
                sb.PO[:, reg + 1024 * dc:reg + 1024 * (dc + 1)],
                io.sc[scname][128 * dc:128 * (dc + 1), :]), waits=(wtok,), dma=lane)
        return t

    def load_T(twname, reg, wtok=None):
        lane = "qw0" if reg == 0 else "qw1"
        return kb.op("sp", lambda e, twname=twname, reg=reg: e.dma_start(
            sb.PO[:, reg:reg + 8192], io.tw[twname]),
            waits=(wtok, kb.tw_done), dma=lane)

    if L1:
        wq = load_T("ipq", regA, wtok=wfree)
        tq = fm_proj(kb, sb, ps, sb.PQ, regA, wq, t_px, bias_col=8, transposed=True)
        wk = load_T("ipk", regB)
        tk = fm_proj(kb, sb, ps, sb.PK, regB, wk, t_px, bias_col=16, transposed=True)
    else:
        wq = load_direct("q", regA, wtok=wfree)
        tq = fm_proj(kb, sb, ps, sb.PQ, regA, wq, t_px)
        wk = load_direct("k", regB)
        tk = fm_proj(kb, sb, ps, sb.PK, regB, wk, t_px, evac_eng="act")

    tvm = kb.op("dve", lambda e: e.memset(sb.PV[:], 1.0), waits=(pv_tok,), sig="dve")
    if L1:
        wv = load_T("ipv", regA, wtok=tq)
    else:
        wv = load_direct("v", regA, wtok=tq)
    ev = {}
    t_v = None
    for tc in range(NCH):
        for half in range(2):
            bank = (tc * 2 + half) % 4
            pb = ps.B[:, 512 * bank:512 * (bank + 1)]
            for dc in range(8):
                lhsT = sb.PX[:, 2048 * dc + 128 * tc:2048 * dc + 128 * (tc + 1)]
                w0 = (wv, ev.get(bank)) if dc == 0 else ()
                kb.op("pe", lambda e, pb=pb, lhsT=lhsT, dc=dc, half=half: e.matmul(
                    pb, lhsT,
                    sb.PO[:, regA + 1024 * dc + 512 * half:regA + 1024 * dc + 512 * (half + 1)],
                    start=(dc == 0), stop=(dc == 7 and not L1)),
                    waits=w0, sig="pe" if (dc == 7 and not L1) else None)
            if L1:
                tmm = kb.op("pe", lambda e, pb=pb, half=half: e.matmul(
                    pb, sb.ones_row_b[:],
                    sb.vb_row[0:1, 512 * half:512 * (half + 1)],
                    start=False, stop=True), sig="pe")
            else:
                tmm = ("pe", kb.cnt["pe"])
            ev_t = None
            for hh in range(8):
                h = half * 8 + hh
                ev_t = kb.op("dve", lambda e, pb=pb, tc=tc, h=h, hh=hh: e.tensor_copy(
                    sb.PV[:, 1040 * tc + 65 * h:1040 * tc + 65 * h + 64],
                    pb[:, 64 * hh:64 * (hh + 1)]), waits=(tmm, tvm), sig="dve")
            ev[bank] = ev_t
            t_v = ev_t

    t_cb_prev = None
    sbuf_free = {}
    t_px_writes = []
    t_cb = None
    t_opx = {}
    pend_fin = [None]

    def flush_fin():
        f = pend_fin[0]
        pend_fin[0] = None
        if f:
            f()
    hq = 0
    for h in range(16):
        fc, pb0 = h // 2, 64 * (h % 2)
        for qh in range(2):
            par = hq % 2
            ob = ps.B[0:65, 1024 * par:1024 * par + 1024]
            kcs = list(range(NCH)) if not L1 else list(range(8 * qh + 8))
            pend_pv = []
            t_pv = None

            def emit_pv(first_overall, last_overall):
                nonlocal t_pv
                kc, te, tz, pt = pend_pv.pop(0)
                for j in range(2):
                    last = (j == 1 and last_overall)
                    t_pv = kb.op("pe", lambda e, kc=kc, h=h, j=j, pt=pt, fo=(first_overall and j == 0 or first_overall and j == 1), first=first_overall, last2=last_overall, ob=ob: e.matmul(
                        ob[:, 512 * j:512 * (j + 1)],
                        sb.PV[:, 1040 * kc + 65 * h:1040 * kc + 65 * (h + 1)],
                        pt[:, 512 * j:512 * (j + 1)], start=first, stop=last2),
                        waits=(te, tz, t_v, t_opx.get(par) if first_overall else None),
                        sig="pe" if last else None)
                sbuf_free[("pt", kc % (2 if L1 else 3))] = ("pe", kb.cnt["pe"])

            for i, kc in enumerate(kcs):
                if i == 8 - 2 * L1:
                    flush_fin()
                sbi = kc % 2
                pA = ps.A[:, 1024 * sbi:1024 * (sbi + 1)]
                dprime = kc - 8 * qh if L1 else -1
                qlo = 128 * dprime if (L1 and dprime >= 0) else 0
                tsm = None
                for j in range(2):
                    c0, c1 = max(512 * j, qlo), 512 * (j + 1)
                    if c0 >= c1:
                        continue
                    tsm = kb.op("pe", lambda e, pA=pA, fc=fc, pb0=pb0, kc=kc, qh=qh, c0=c0, c1=c1: e.matmul(
                        pA[:, c0:c1],
                        sb.PK[pb0:pb0 + 64, 2048 * fc + 128 * kc:2048 * fc + 128 * (kc + 1)],
                        sb.PQ[pb0:pb0 + 64, 2048 * fc + 1024 * qh + c0:2048 * fc + 1024 * qh + c1],
                        start=True, stop=True, tile_position=(pb0, 0)),
                        waits=(tq, tk, sbuf_free.get(sbi),
                               t_cb if ((not L1) and qh == 0 and kc == 1) else None),
                        sig="pe")
                if (not L1) and qh == 0 and kc == 0:
                    tr = kb.op("dve", lambda e, pA=pA: e.reduce_max(
                        sb.sc1[0:1, 16:17], pA[0:1, 0:1024], axis=AX.X, negate=True),
                        waits=(tsm,), sig="dve")
                    tr2 = kb.op("dve", lambda e: e.tensor_scalar(
                        sb.sc1[0:1, 17:18], sb.sc1[0:1, 16:17], SC, None, ALU.mult),
                        hard=(tr,), sig="dve")
                    tr3 = kb.op("dve", lambda e: e.tensor_copy(sb.rowb[0:1, 0:1], sb.sc1[0:1, 17:18]),
                                hard=(tr2,), sig="dve")
                    trm = kb.op("pe", lambda e: e.matmul(
                        ps.A[:, 2047:2048], sb.ones_row_b[:], sb.rowb[0:1, 0:1],
                        start=True, stop=True), waits=(tr3, sbuf_free.get(1)), sig="pe")
                    t_cb = kb.op("dve", lambda e: e.tensor_copy(sb.cb[:, 0:1], ps.A[:, 2047:2048]),
                                 waits=(trm,), hard=(t_cb_prev,), sig="dve")
                if L1 and dprime >= 0:
                    tsm = kb.op("dve", lambda e, pA=pA, qlo=qlo: e.tensor_tensor(
                        pA[:, qlo:qlo + 128], pA[:, qlo:qlo + 128], sb.cmask[:], ALU.add),
                        waits=(tsm,), sig="dve")
                sb3 = kc % (2 if L1 else 3)
                pt = sb.PO[:, 1024 * sb3:1024 * (sb3 + 1)]
                tz = None
                if qlo > 0:
                    tz = kb.op("pool", lambda e, pt=pt, qlo=qlo: e.memset(pt[:, 0:qlo], 0.0),
                               waits=(sbuf_free.get(("pt", sb3)) or t_v,), sig="pool")
                if L1:
                    te = kb.op("act", lambda e, pA=pA, pt=pt, qlo=qlo: e.activation(
                        pt[:, qlo:1024], pA[:, qlo:1024], AF.Exp, bias=0.0, scale=SC),
                        waits=(tsm, sbuf_free.get(("pt", sb3))), sig="act")
                else:
                    te = kb.op("act", lambda e, pA=pA, pt=pt: e.activation(
                        pt[:], pA[:], AF.Exp, bias=sb.cb[:, 0:1], scale=SC),
                        waits=(tsm, sbuf_free.get(("pt", sb3))), hard=(t_cb,), sig="act")
                t_cb_prev = te
                sbuf_free[sbi] = te
                pend_pv.append((kc, te, tz, pt))
                if L1:
                    emit_pv(i == 0, i == len(kcs) - 1)
                elif len(pend_pv) > 2:
                    emit_pv(i == 2, False)
            while pend_pv:
                emit_pv(len(pend_pv) == len(kcs) and len(kcs) <= 2,
                        len(pend_pv) == 1)
            def fin(ob=ob, par=par, fc=fc, pb0=pb0, qh=qh, t_pv=t_pv):
                stv = sb.st[64:128, 1024 * par:1024 * (par + 1)]
                t0 = kb.op("dve", lambda e, ob=ob: e.tensor_copy(
                    sb.row, ob[64:65, 0:1024]), waits=(t_pv,), sig="dve")
                tex = kb.op("dve", lambda e: e.reciprocal(sb.row, sb.row),
                            hard=(t0,), sig="dve")
                tev = kb.op("act", lambda e, ob=ob, stv=stv: e.activation(
                    stv, ob[0:64, 0:1024], AF.Copy, bias=0.0, scale=1.0),
                    waits=(t_pv, t_opx.get(par)), sig="act")
                trep = []
                for j in range(2):
                    trep.append(kb.op("pe", lambda e, j=j, ob=ob: e.matmul(
                        ob[0:64, 512 * j:512 * (j + 1)],
                        sb.ones_row_f[0:1, 0:64], sb.packA[0:1, 512 * j:512 * (j + 1)],
                        start=True, stop=True), waits=(tex, tev), sig="pe"))
                t_ox = kb.op("dve", lambda e, fc=fc, pb0=pb0, qh=qh, ob=ob, stv=stv: e.tensor_tensor(
                    sb.PX[pb0:pb0 + 64, 2048 * fc + 1024 * qh:2048 * fc + 1024 * (qh + 1)],
                    stv, ob[0:64, 0:1024], ALU.mult),
                    waits=(tev, trep[0], trep[1]), sig="dve")
                t_px_writes.append(t_ox)
                t_opx[par] = t_ox
            pend_fin[0] = fin
            hq += 1

    flush_fin()
    if L1:
        wo = load_T("op", regB, wtok=tk)
    else:
        wo = load_direct("o", regB, wtok=tk)
    t_all_px = t_px_writes[-1]
    ev2 = {}
    t_dd = None
    for fc in range(8):
        for slab in range(4):
            pa = ps.A[:, 512 * slab:512 * (slab + 1)]
            for dc in range(8):
                if L1:
                    off = regB + 1024 * dc + 512 * (fc // 4) + 128 * (fc % 4)
                else:
                    off = regB + 1024 * dc + 128 * fc
                lhsT = sb.PO[:, off:off + 128]
                w0 = (wo, ev2.get(slab), t_all_px, sbuf_free.get(0), sbuf_free.get(1)) \
                    if dc == 0 else ()
                tmm = kb.op("pe", lambda e, pa=pa, lhsT=lhsT, dc=dc, slab=slab: e.matmul(
                    pa, lhsT, sb.PX[:, 2048 * dc + 512 * slab:2048 * dc + 512 * (slab + 1)],
                    start=(dc == 0), stop=(dc == 7)),
                    waits=w0, sig="pe" if dc == 7 else None)
            if L1:
                ev2[slab] = kb.op("dve", lambda e, slab=slab, fc=fc, pa=pa: e.tensor_scalar(
                    sb.st[:, 2048 + 512 * slab:2048 + 512 * (slab + 1)], pa,
                    sb.biasf[:, 24 + fc:24 + fc + 1], None, ALU.add),
                    waits=(tmm,), sig="dve")
            else:
                ev2[slab] = kb.op("dve", lambda e, slab=slab, pa=pa: e.tensor_copy(
                    sb.st[:, 2048 + 512 * slab:2048 + 512 * (slab + 1)], pa),
                    waits=(tmm,), sig="dve")
        tsd = kb.op("pool", lambda e, fc=fc: e.dma_start(
            io.dd[128 * fc:128 * (fc + 1), :], sb.st[:, 2048:4096]),
            waits=(ev2[0], ev2[1], ev2[2], ev2[3]), dma="ddl")
        for slab in range(4):
            ev2[slab] = tsd
        t_dd = tsd
    return t_dd, ("pe", kb.cnt["pe"])



def stage_gate(kb, io, sb, ps, seq, t_px):
    t_prev = {0: None, 1: None, 2: None, 3: None}
    selb_prev = {0: None, 1: None, 2: None, 3: None}
    te1_prev = [None]
    for tc in range(NCH):
        slot = seq * NCH + tc
        par = tc % 4
        pg = ps.B[:, 1024 + 16 * par:1032 + 16 * par]
        tg = None
        for dc in range(8):
            tg = kb.op("pe", lambda e, dc=dc, tc=tc, pg=pg: e.matmul(
                pg, sb.PX[:, 2048 * dc + 128 * tc:2048 * dc + 128 * (tc + 1)],
                sb.gwT[:, 16 * dc:16 * dc + 8], start=(dc == 0), stop=(dc == 7)),
                waits=(t_px, t_prev[par]) if dc == 0 else (), sig="pe" if dc == 7 else None)
        g = sb.g8
        tm0 = kb.op("dve", lambda e, pg=pg: e.reduce_max(g[:, 8:9], pg, axis=AX.X, negate=True),
                    waits=(tg,), hard=(te1_prev[0],), sig="dve")
        te1 = kb.op("act", lambda e, pg=pg: e.activation(g[:, 16:24], pg, AF.Exp,
                                                         bias=g[:, 8:9], scale=1.0),
                    waits=(tg,), hard=(tm0,), sig="act")
        te1_prev[0] = te1
        ti1 = kb.op("dve", lambda e, pg=pg: e.tensor_scalar(g[:, 40:48], pg, g[:, 8:9],
                                                            0.0, ALU.add, ALU.is_ge),
                    hard=(tm0,), sig="dve")
        tk0 = kb.op("dve", lambda e, pg=pg: e.scalar_tensor_tensor(g[:, 48:56], g[:, 40:48], -BIG,
                                                                   pg, ALU.mult, ALU.add),
                    hard=(ti1,), sig="dve")
        tm2 = kb.op("dve", lambda e: e.reduce_max(g[:, 12:13], g[:, 48:56], axis=AX.X),
                    hard=(tk0,), sig="dve")
        tsel = kb.op("dve", lambda e, pg=pg: e.tensor_scalar(g[:, 56:64], pg, g[:, 12:13],
                                                             None, ALU.is_ge), hard=(tm2,), sig="dve")
        tt0 = kb.op("dve", lambda e: e.tensor_tensor(g[:, 24:32], g[:, 56:64], g[:, 16:24],
                                                     ALU.mult), waits=(te1,), hard=(tsel,), sig="dve")
        ts2 = kb.op("dve", lambda e: e.reduce_sum(g[:, 13:14], g[:, 24:32], axis=AX.X),
                    hard=(tt0,), sig="dve")
        tr2 = kb.op("dve", lambda e: e.reciprocal(g[:, 15:16], g[:, 13:14]), hard=(ts2,), sig="dve")
        twf = kb.op("dve", lambda e, slot=slot: e.tensor_scalar(
            sb.wfull[:, 8 * (slot % 64):8 * (slot % 64) + 8], g[:, 24:32], g[:, 15:16],
            None, ALU.mult), hard=(tr2,), sig="dve")
        sel = sb.selb[:, 8 * par:8 * (par + 1)]
        tsb = kb.op("dve", lambda e, sel=sel: e.tensor_copy(sel, g[:, 56:64]),
                    waits=(selb_prev[par],), hard=(tsel,), sig="dve")
        pp = ps.B[:, 1536 + 16 * par:1544 + 16 * par]
        tpm = kb.op("pe", lambda e, pp=pp, sel=sel: e.matmul(pp, sb.u128[:], sel,
                                                             start=True, stop=True),
                    waits=(tsb, t_prev[par]), sig="pe")
        tpe = kb.op("dve", lambda e, slot=slot, pp=pp: e.tensor_copy(
            sb.pref[:, 8 * (slot % 64):8 * (slot % 64) + 8], pp), waits=(tpm,), sig="dve")
        pt2 = ps.B[0:1, 1600 + 16 * par:1608 + 16 * par]
        ttm = kb.op("pe", lambda e, pt2=pt2, sel=sel: e.matmul(pt2, sb.ones_col_b[:], sel,
                                                               start=True, stop=True),
                    waits=(tsb,), sig="pe")
        tte = kb.op("dve", lambda e, slot=slot, pt2=pt2: e.tensor_copy(
            sb.Trow[0:1, 8 * (slot % 64):8 * (slot % 64) + 8], pt2), waits=(ttm,), sig="dve")
        t_prev[par] = tte
        selb_prev[par] = ("pe", kb.cnt["pe"])
    return tte, ("pe", kb.cnt["pe"])


def dump_gate(kb, io, sb, t_gate):
    t1 = kb.op("pool", lambda e: e.dma_start(io.gd_wfull, sb.wfull[:]), waits=(t_gate,), dma="ddl")
    t2 = kb.op("pool", lambda e: e.dma_start(io.gd_pref, sb.pref[:]), waits=(t_gate,), dma="ddl")
    t3 = kb.op("pool", lambda e: e.dma_start(io.gd_trow, sb.Trow), waits=(t_gate,), dma="ddl")
    return t3


def stage_counts_out(kb, io, sb, ps, t_gate, fused):
    tG = kb.op("dve", lambda e: e.tensor_reduce(
        sb.packA[0:1, 0:8], sb.Trow.rearrange("o (s e) -> o e s", e=8), AX.X, ALU.add),
        waits=(t_gate,), sig="dve")
    if not fused:
        return kb.op("pool", lambda e: e.dma_start(io.ccrow, sb.packA[0:1, 0:8]),
                     waits=(tG,), dma="mscp")
    tmr = kb.op("pe", lambda e: e.matmul(ps.B[0:8, 1024:1032], sb.ones_row_f[0:1, 0:8],
                                         sb.packA[0:1, 0:8], start=True, stop=True),
                waits=(tG,), sig="pe")
    tcc = kb.op("dve", lambda e: e.tensor_scalar(sb.CC[:], ps.B[0:8, 1024:1032],
                                                 sb.rowsel[:], None, ALU.mult),
                waits=(tmr,), sig="dve")
    tst = kb.op("pool", lambda e: e.dma_start(io.ccin_d, sb.CC[:]), waits=(tcc,), dma="mscp")
    tcol = kb.op("pool", lambda e: e.collective_compute(
        "AllReduce", ALU.add, replica_groups=[list(range(NCORES))],
        ins=[io.ccin_d.opt()], outs=[io.ccout_d.opt()]),
        waits=(tst,), sig="cc")
    return tcol


def stage_offsets(kb, io, sb, ps, t_cc, fused):
    cc_src = io.ccout_d if fused else io.cc
    tld = kb.op("sp", lambda e, cc_src=cc_src: e.dma_start(sb.CC[:], cc_src),
                waits=(t_cc,), dma="msc")
    tt1 = kb.op("pool", lambda e: e.dma_start(
        io.t64d.rearrange("p n -> (p n)").rearrange("(o x) -> o x", o=1), sb.Trow),
        waits=(t_cc,), dma="mscp")
    tt2 = kb.op("sp", lambda e: e.dma_start(sb.sc1[0:64, 16:24], io.t64d), waits=(tt1,), dma="msc")
    tt3 = kb.op("dve", lambda e: e.tensor_copy(sb.st[0:64, 0:8], sb.sc1[0:64, 16:24]),
                waits=(tt2,), sig="dve")
    tof = kb.op("pe", lambda e: e.matmul(ps.B[0:64, 1024:1032], sb.u128[0:64, 0:64],
                                         sb.st[0:64, 0:8], start=True, stop=True),
                waits=(tt3,), sig="pe")
    tofe = kb.op("dve", lambda e: e.tensor_copy(sb.sc1[0:64, 24:32], ps.B[0:64, 1024:1032]),
                 waits=(tof,), sig="dve")
    tco = kb.op("pe", lambda e: e.matmul(ps.B[0:1, 1536:1544], sb.rankmask[:], sb.CC[:],
                                         start=True, stop=True), waits=(tld,), sig="pe")
    tcoe = kb.op("dve", lambda e: e.tensor_copy(sb.packA[0:1, 8:16], ps.B[0:1, 1536:1544]),
                 waits=(tco,), sig="dve")
    tcor = kb.op("pe", lambda e: e.matmul(ps.B[0:64, 1032:1040], sb.ones_row_f[0:1, 0:64],
                                          sb.packA[0:1, 8:16], start=True, stop=True),
                 waits=(tcoe,), sig="pe")
    toff = kb.op("dve", lambda e: e.tensor_tensor(sb.sc1[0:64, 24:32], sb.sc1[0:64, 24:32],
                                                  ps.B[0:64, 1032:1040], ALU.add),
                 waits=(tcor,), hard=(tofe,), sig="dve")
    tst = kb.op("pool", lambda e: e.dma_start(
        io.offd.rearrange("o (p n) -> (o p) n", n=8)[0:64, :], sb.sc1[0:64, 24:32]),
        waits=(toff,), dma="mscp")
    trd = kb.op("sp", lambda e: e.dma_start(sb.OFFROW, io.offd), waits=(tst,), dma="msc")
    return trd


def stage_comb(kb, io, sb, ps, seq, t_off, t_wp_ready):
    t_last = None
    t_prev = None
    for tc in range(NCH):
        slot = seq * NCH + tc
        po = ps.B[:, 1032 + 8 * (tc % 2):1040 + 8 * (tc % 2)]
        trm = kb.op("pe", lambda e, slot=slot, po=po: e.matmul(
            po, sb.ones_row_f[:], sb.OFFROW[0:1, 8 * slot:8 * slot + 8],
            start=True, stop=True), waits=(t_off, t_prev), sig="pe")
        t1 = kb.op("dve", lambda e, slot=slot, po=po: e.tensor_tensor(
            sb.g8[:, 0:8], sb.pref[:, 8 * (slot % 64):8 * (slot % 64) + 8], po, ALU.add),
            waits=(trm, t_wp_ready), sig="dve")
        t2 = kb.op("dve", lambda e: e.tensor_scalar(sb.g8[:, 8:16], sb.g8[:, 0:8], CAP,
                                                    None, ALU.is_lt), hard=(t1,), sig="dve")
        t_last = kb.op("dve", lambda e, slot=slot: e.tensor_tensor(
            sb.comb[:, 8 * (slot % 64):8 * (slot % 64) + 8],
            sb.g8[:, 8:16], sb.wfull[:, 8 * (slot % 64):8 * (slot % 64) + 8], ALU.mult),
            hard=(t2,), sig="dve")
        t_prev = t_last
    return t_last



def stage_moe(kb, io, sb, ps, seq, t_px, t_comb, x_src, x_dst, want_ln, px_free_next,
              x_tok=None):
    t_store = None
    t_aff_all = []
    mcount = [0]
    t_matdone = {}
    y_done = None
    war = [("dve", kb.cnt["dve"])]

    def wreg():
        return 8192 * (mcount[0] % 2)

    def wload_img(img_ap):
        m = mcount[0]
        reg = 8192 * (m % 2)
        lane = "lw%d" % (m % 2)
        wtok = t_matdone.get(m - 2, px_free_next)
        t = kb.op("sp", lambda e, img_ap=img_ap, reg=reg: e.dma_start(
            sb.PK[:, reg:reg + 8192], img_ap), waits=(wtok, kb.tw_done), dma=lane)
        mcount[0] += 1
        return reg, t

    for half in range(2):
        toks = 1024 * half

        def gu_phase(wg_reg, wg_toks, n_ic, ic0, psum_off, dst_fn, extra_wait=None):
            ev = {}
            for i in range(n_ic):
                ic = ic0 + i
                P = ps.A if ic % 2 == 0 else ps.B
                for half2 in range(2):
                    pg = P[:, psum_off + 512 * half2:psum_off + 512 * (half2 + 1)]
                    for bc in range(8):
                        lhsT = sb.PK[:, wg_reg + 1024 * bc + 128 * (ic % 8):
                                     wg_reg + 1024 * bc + 128 * (ic % 8 + 1)]
                        w0 = (wg_toks, ev.get((ic % 2, half2)), t_px, extra_wait) \
                            if bc == 0 else ()
                        kb.op("pe", lambda e, pg=pg, lhsT=lhsT, bc=bc, half2=half2, toks=toks: e.matmul(
                            pg, lhsT,
                            sb.PX[:, 2048 * bc + toks + 512 * half2:2048 * bc + toks + 512 * (half2 + 1)],
                            start=(bc == 0), stop=(bc == 7)),
                            waits=w0, sig="pe" if bc == 7 else None)
                    tgm = ("pe", kb.cnt["pe"])
                    ev[(ic % 2, half2)] = dst_fn(ic, half2, pg, tgm)
            return ev

        def silu_dst(ic, half2, pg, tgm):
            return kb.op("act", lambda e, pg=pg, ic=ic, half2=half2: e.activation(
                sb.PQ[:, 8192 + 1024 * (ic % 8) + 512 * half2:8192 + 1024 * (ic % 8) + 512 * (half2 + 1)],
                pg, AF.Silu, bias=0.0, scale=1.0), waits=(tgm, kb.tw_done), sig="act")

        def hs_dst(ic, half2, pg, tgm, tsil):
            return kb.op("dve", lambda e, pg=pg, ic=ic, half2=half2: e.tensor_tensor(
                sb.PV[:, 1024 * ic + 512 * half2:1024 * ic + 512 * (half2 + 1)],
                sb.PQ[:, 8192 + 1024 * (ic % 8) + 512 * half2:8192 + 1024 * (ic % 8) + 512 * (half2 + 1)],
                pg, ALU.mult), waits=(tgm, tsil, kb.tw_done), sig="dve")

        def down_phase(wd_reg, wd_toks, n_ic, comb_e, t_hs_last, nblk1024):
            ev_d = {}
            t_last = None
            for tc in range(8):
                slot = seq * NCH + half * 8 + tc
                pos = tc % 4
                P = ps.A if pos % 2 == 0 else ps.B
                coff = 1024 * (pos // 2)
                for hh in range(2):
                    pd = P[:, coff + 512 * hh:coff + 512 * (hh + 1)]
                    for i in range(n_ic):
                        lw = wd_toks if not isinstance(wd_toks, list) else wd_toks[i]
                        lhsT = sb.PV[:, 1024 * i + 128 * tc:1024 * i + 128 * (tc + 1)]
                        w0 = (lw, ev_d.get((pos, hh)), t_hs_last) if i == 0 else (lw,)
                        roff = wd_reg(i) if callable(wd_reg) else wd_reg
                        kb.op("pe", lambda e, pd=pd, lhsT=lhsT, i=i, hh=hh, roff=roff: e.matmul(
                            pd, lhsT,
                            sb.PK[:, roff + 1024 * (i % 8) + 512 * hh:roff + 1024 * (i % 8) + 512 * (hh + 1)],
                            start=(i == 0), stop=(i == n_ic - 1)),
                            waits=w0, sig="pe" if i == n_ic - 1 else None)
                    tdm = ("pe", kb.cnt["pe"])
                    ysl = sb.POf[:, 1024 * tc + 512 * hh:1024 * tc + 512 * (hh + 1)]
                    if comb_e is not None and comb_e >= 0:
                        ccol = sb.comb[:, 8 * (slot % 64) + comb_e:8 * (slot % 64) + comb_e + 1]
                        tde = kb.op("dve", lambda e, pd=pd, ysl=ysl, ccol=ccol: e.scalar_tensor_tensor(
                            ysl, pd, ccol, ysl, ALU.mult, ALU.add), waits=(tdm,), hard=(t_comb,), sig="dve")
                    elif comb_e == -1:
                        ccol = sb.comb[:, 8 * (slot % 64):8 * (slot % 64) + 1]
                        yw = y_done if isinstance(y_done, tuple) and len(y_done) == 2 \
                            and not isinstance(y_done[0], str) else (y_done,)
                        tde = kb.op("dve", lambda e, pd=pd, ysl=ysl, ccol=ccol: e.tensor_scalar(
                            ysl, pd, ccol, None, ALU.mult),
                            waits=(tdm,) + tuple(yw), hard=(t_comb,), sig="dve")
                    else:
                        tde = kb.op("dve", lambda e, pd=pd, ysl=ysl: e.tensor_tensor(
                            ysl, ysl, pd, ALU.add), waits=(tdm,), sig="dve")
                    ev_d[(pos, hh)] = tde
                    t_last = tde
            return t_last

        for ee in range(8):
            gm = mcount[0]
            reg_g, wg = wload_img(io.tw["weg"][ee])
            ev_g = gu_phase(reg_g, wg, 8, 0, 0, silu_dst, extra_wait=war[0])
            t_matdone[gm] = ("pe", kb.cnt["pe"])
            um = mcount[0]
            reg_u, wu = wload_img(io.tw["weu"][ee])
            evh = gu_phase(reg_u, wu, 8, 0, 1024,
                           lambda ic, half2, pg, tgm: hs_dst(ic, half2, pg, tgm,
                                                             ev_g[(ic % 2, half2)]),
                           extra_wait=war[0])
            t_hs_all = ("dve", kb.cnt["dve"])
            t_matdone[um] = ("pe", kb.cnt["pe"])
            dm = mcount[0]
            reg_d, wd = wload_img(io.tw["wed"][ee])
            down_phase(reg_d, wd, 8, (-1 if ee == 0 else ee), t_hs_all, False)
            t_matdone[dm] = ("pe", kb.cnt["pe"])
            war[0] = ("dve", kb.cnt["dve"])

        t_hs_sh = []
        greg = {}
        for grp in range(2):
            gm = mcount[0]
            reg_g, wg = wload_img(io.tw["wsg"][grp])
            ev_g = gu_phase(reg_g, wg, 8, 8 * grp, 0, silu_dst, extra_wait=war[0])
            t_matdone[gm] = ("pe", kb.cnt["pe"])
            um = mcount[0]
            reg_u, wu = wload_img(io.tw["wsu"][grp])
            evh = gu_phase(reg_u, wu, 8, 8 * grp, 1024,
                           lambda ic, half2, pg, tgm: hs_dst(ic, half2, pg, tgm,
                                                             ev_g[(ic % 2, half2)]),
                           extra_wait=war[0])
            t_hs_sh.append(("dve", kb.cnt["dve"]))
            t_matdone[um] = ("pe", kb.cnt["pe"])
        dregs = []
        wd_last = None
        for grp in range(2):
            dm = mcount[0]
            reg_d, wd = wload_img(io.tw["wsd"][grp])
            dregs.append(reg_d)
            wd_last = wd
            if grp == 0:
                dm0 = dm
        t_down = down_phase(lambda i: dregs[i // 8], wd_last, 16, None,
                            t_hs_sh[-1], False)
        war[0] = ("dve", kb.cnt["dve"])
        t_matdone[dm0] = ("pe", kb.cnt["pe"])
        t_matdone[mcount[0] - 1] = ("pe", kb.cnt["pe"])
        y_done2 = t_down

        t_store, t_xln_h = pass_residual_ln(
            kb, io, sb, ps, seq, "y", want_ln, x_src, x_dst,
            chunks=range(half * 8, half * 8 + 8), y_tok=y_done2,
            px_free=px_free_next if half == 0 else None, x_tok=x_tok,
            pq_tok=t_aff_all[-1] if t_aff_all else None)
        if t_xln_h is not None:
            t_aff_all.append(t_xln_h)
        y_done = t_store
    t_xln = t_aff_all[-1] if t_aff_all else None
    return t_store, t_xln, ("pe", kb.cnt["pe"])



def build_program(phase):
    nc = bass.Bass("TRN2", target_bir_lowering=False)
    io = declare_io(nc, phase)
    with ExitStack() as ctx:
        kb = KB(nc)
        kb.make_sems(ctx)
        sb = alloc_sbuf(nc, ctx)
        ps = alloc_psum(nc, ctx)
        t_pw, t_pr = stage_preprocess(kb, io, sb, ps)
        pre_done = (("pe", kb.cnt["pe"]), ("dve", kb.cnt["dve"]), t_pw, t_pr)
        _phase_body(kb, io, sb, ps, phase, pre_done)
        kb.emit()
    return nc


def _phase_body(kb, io, sb, ps, phase, pre_done):
    fused = (phase == "all")

    def is_final(sl):
        return sl == UPTO

    def final_wait():
        kb.wait("sp", [(lane, kb.cnt[lane]) for lane in
                       ("xst0", "xst1", "xlnd", "ddl", "msc", "mscp", "pw", "pr", "cc")])

    px_free = pre_done[0]
    xtk = None
    jobs = image_jobs(io)
    kb.tw_done = None
    if True:
        stage_images(kb, io, sb, jobs, None)
    q = (len(jobs) + 3) // 4
    jgroups = [jobs[i * q:(i + 1) * q] for i in range(4)]

    if DEVSTOP == 1:
        final_wait()
        return

    if phase in ("p1", "all"):
        t_gate = None
        first = True
        for seq in range(SEQS):
            xtk_in = xtk
            tx, t_xln = pass_residual_ln(kb, io, sb, ps, seq, "init", True,
                                         io.x_in, io.xs,
                                         px_free=pre_done[1] if first else px_free,
                                         x_tok=xtk)
            xtk = tx
            if DEVSTOP == 2:
                final_wait()
                return
            t_px = load_px(kb, io, sb, seq, t_xln, px_free=px_free)
            t_dd, px_free = stage_ffn0(kb, io, sb, ps, t_px,
                                       wfree=pre_done[2] if first else px_free)
            first = False
            if DEVSTOP == 3:
                final_wait()
                return
            if UPTO == 1:
                tx, _ = pass_residual_ln(kb, io, sb, ps, seq, "delta", False,
                                         io.xs, io.out, dd_tok=t_dd, x_tok=xtk)
                xtk = tx
                continue
            tx, t_xln = pass_residual_ln(kb, io, sb, ps, seq, "delta", True,
                                         io.xs, io.xs, px_free=px_free, dd_tok=t_dd,
                                         x_tok=xtk)
            xtk = tx
            t_px = load_px(kb, io, sb, seq, t_xln, px_free=px_free)
            t_dd, px_free = stage_attn(kb, io, sb, ps, seq, 0, t_px, wfree=px_free,
                                       pv_tok=kb.tw_done)
            if UPTO == 2:
                tx, _ = pass_residual_ln(kb, io, sb, ps, seq, "l0inner", False,
                                         io.xs, io.out, dd_tok=t_dd, x_tok=xtk)
                xtk = tx
                continue
            tx, t_xln = pass_residual_ln(kb, io, sb, ps, seq, "l0inner", True,
                                         io.xs, io.xs, px_free=px_free, dd_tok=t_dd,
                                         x_tok=xtk)
            xtk = tx
            t_px = load_px(kb, io, sb, seq, t_xln, px_free=px_free)
            t_dd, px_free = stage_ffn0(kb, io, sb, ps, t_px, wfree=px_free)
            if UPTO == 3:
                tx, _ = pass_residual_ln(kb, io, sb, ps, seq, "delta", False,
                                         io.xs, io.out, dd_tok=t_dd, x_tok=xtk)
                xtk = tx
                continue
            tx, t_xln = pass_residual_ln(kb, io, sb, ps, seq, "delta", True,
                                         io.xs, io.xs, px_free=px_free, dd_tok=t_dd,
                                         x_tok=xtk)
            xtk = tx
            t_px = load_px(kb, io, sb, seq, t_xln, px_free=px_free)
            t_gate, px_free = stage_gate(kb, io, sb, ps, seq, t_px)
        if UPTO <= 3:
            final_wait()
            return
        t_gd = dump_gate(kb, io, sb, t_gate)
        t_cc = stage_counts_out(kb, io, sb, ps, t_gate, fused)
        if phase == "p1":
            final_wait()
            return
        t_p1_st = ("xlnd", kb.cnt["xlnd"])

    if phase in ("p2", "all"):
        if phase == "p2":
            t1 = kb.op("sp", lambda e: e.dma_start(sb.wfull[:], io.gdi_wfull), dma="msc")
            t2 = kb.op("sp", lambda e: e.dma_start(sb.pref[:], io.gdi_pref), dma="msc")
            t3 = kb.op("sp", lambda e: e.dma_start(sb.Trow, io.gdi_trow), dma="msc")
            t_wp = t3
            t_cc = t3
            xlsrc = io.xlnd_in
            xln_tok = None
            px_free = pre_done[0]
            xtk = None
        else:
            t_wp = None
            xlsrc = io.xlnd
            xln_tok = t_p1_st
        t_off = stage_offsets(kb, io, sb, ps, t_cc, fused)
        t_gate = None
        for seq in range(SEQS):
            t_comb = stage_comb(kb, io, sb, ps, seq, t_off, t_wp)
            t_px = load_px(kb, io, sb, seq, xln_tok if fused else t_wp,
                           px_free=px_free, src=xlsrc)
            xsrc0 = io.xs if fused else io.x_in
            if is_final(4):
                tx, _, px_free = stage_moe(kb, io, sb, ps, seq, t_px, t_comb,
                                           xsrc0, io.out, False, px_free, x_tok=xtk)
                xtk = tx
                continue
            tx, t_xln, px_free = stage_moe(kb, io, sb, ps, seq, t_px, t_comb,
                                           xsrc0, io.xs, True, px_free, x_tok=xtk)
            xtk = tx
            t_px = load_px(kb, io, sb, seq, t_xln, px_free=px_free)
            t_dd, px_free = stage_attn(kb, io, sb, ps, seq, 1, t_px, wfree=px_free, pv_tok=kb.tw_done)
            if is_final(5):
                tx, _ = pass_residual_ln(kb, io, sb, ps, seq, "delta", False,
                                         io.xs, io.out, dd_tok=t_dd, x_tok=xtk)
                xtk = tx
                continue
            tx, t_xln = pass_residual_ln(kb, io, sb, ps, seq, "delta", True,
                                         io.xs, io.xs, px_free=px_free, dd_tok=t_dd,
                                         x_tok=xtk)
            xtk = tx
            t_px = load_px(kb, io, sb, seq, t_xln, px_free=px_free)
            t_gate, px_free = stage_gate(kb, io, sb, ps, seq, t_px)
        if UPTO <= 5:
            final_wait()
            return
        t_gd = dump_gate(kb, io, sb, t_gate)
        t_cc = stage_counts_out(kb, io, sb, ps, t_gate, fused)
        if phase == "p2":
            final_wait()
            return
        t_p2_st = ("xlnd", kb.cnt["xlnd"])

    if phase == "p3":
        t1 = kb.op("sp", lambda e: e.dma_start(sb.wfull[:], io.gdi_wfull), dma="msc")
        t2 = kb.op("sp", lambda e: e.dma_start(sb.pref[:], io.gdi_pref), dma="msc")
        t3 = kb.op("sp", lambda e: e.dma_start(sb.Trow, io.gdi_trow), dma="msc")
        t_wp = t3
        t_cc = t3
        xlsrc = io.xlnd_in
        xsrc = io.x_in
        xln_tok = None
        px_free = pre_done[0]
        xtk = None
    else:
        t_wp = None
        xlsrc = io.xlnd
        xsrc = io.xs
        xln_tok = t_p2_st
    t_off = stage_offsets(kb, io, sb, ps, t_cc, fused)
    for seq in range(SEQS):
        t_comb = stage_comb(kb, io, sb, ps, seq, t_off, t_wp)
        t_px = load_px(kb, io, sb, seq, xln_tok if fused else t_wp,
                       px_free=px_free, src=xlsrc)
        tx, _, px_free = stage_moe(kb, io, sb, ps, seq, t_px, t_comb,
                                   xsrc, io.out, False, px_free, x_tok=xtk)
        xtk = tx
    final_wait()



def host_consts(core):
    k = np.arange(128)[:, None]
    m = np.arange(128)[None, :]
    return {
        "u128": (k < m).astype(ml_dtypes.bfloat16),
        "cmask": np.where(k <= m, 0.0, -BIG).astype(ml_dtypes.bfloat16),
        "rankmask": (np.arange(8) < core).astype(np.float32)[:, None],
        "rowsel": (np.arange(8) == core).astype(np.float32)[:, None],
    }


def kernel(**inputs):
    x_full = np.ascontiguousarray(np.asarray(inputs["L0_attn_h"], dtype=np.float32))
    shards = [np.ascontiguousarray(x_full[4 * c:4 * (c + 1)].reshape(TOK, D))
              for c in range(NCORES)]
    weights = {nm: np.ascontiguousarray(np.asarray(inputs[nm], dtype=np.float32))
               for nm, _ in W_NAMES if nm != "L0_attn_h"}
    weights.pop("L0_attn_h", None)

    def maps(extra_per_core):
        out = []
        for c in range(NCORES):
            m = dict(weights)
            m.update(host_consts(c))
            m.update(extra_per_core[c])
            out.append(m)
        return out

    if FUSED:
        nc = build_program("all")
        in_maps = maps([{"x": shards[c]} for c in range(NCORES)])
        res = run_bass_kernel_spmd(nc, in_maps, core_ids=list(range(NCORES)))
        outs = [np.asarray(r["out"]) for r in res.results]
    else:
        nc1 = build_program("p1")
        in_maps = maps([{"x": shards[c]} for c in range(NCORES)])
        r1 = run_bass_kernel_spmd(nc1, in_maps, core_ids=list(range(NCORES)))
        if UPTO <= 3:
            outs = [np.asarray(r["out"]) for r in r1.results]
        else:
            cc = np.concatenate([np.asarray(r["ccrow"]) for r in r1.results], axis=0)
            nc2 = build_program("p2")
            in_maps = maps([{
                "xs_in": np.asarray(r1.results[c]["xs_out"]),
                "xlnd_in": np.asarray(r1.results[c]["xlnd_out"]),
                "gdi_wfull": np.asarray(r1.results[c]["gdo_wfull"]),
                "gdi_pref": np.asarray(r1.results[c]["gdo_pref"]),
                "gdi_trow": np.asarray(r1.results[c]["gdo_trow"]),
                "cc": cc,
            } for c in range(NCORES)])
            r2 = run_bass_kernel_spmd(nc2, in_maps, core_ids=list(range(NCORES)))
            if UPTO <= 5:
                outs = [np.asarray(r["out"]) for r in r2.results]
            else:
                cc2 = np.concatenate([np.asarray(r["ccrow"]) for r in r2.results], axis=0)
                nc3 = build_program("p3")
                in_maps = maps([{
                    "xs_in": np.asarray(r2.results[c]["xs_out"]),
                    "xlnd_in": np.asarray(r2.results[c]["xlnd_out"]),
                    "gdi_wfull": np.asarray(r2.results[c]["gdo_wfull"]),
                    "gdi_pref": np.asarray(r2.results[c]["gdo_pref"]),
                    "gdi_trow": np.asarray(r2.results[c]["gdo_trow"]),
                    "cc": cc2,
                } for c in range(NCORES)])
                r3 = run_bass_kernel_spmd(nc3, in_maps, core_ids=list(range(NCORES)))
                outs = [np.asarray(r["out"]) for r in r3.results]

    full = np.concatenate([o.reshape(4, S, D) for o in outs], axis=0)
    return full.astype(np.float32)


# version 23 (final)
# speedup vs baseline: 1.1061x; 1.0151x over previous
import numpy as np
import ml_dtypes
import concourse.bass as bass
import concourse.mybir as mybir
from concourse.bass_utils import run_bass_kernel_spmd
from contextlib import ExitStack

F32 = mybir.dt.float32
BF16 = mybir.dt.bfloat16
AF = mybir.ActivationFunctionType
ALU = mybir.AluOpType
AX = mybir.AxisListType

NCORES = 8
SEQS = 4
S = 2048
D = 1024
NCH = 16
TOK = SEQS * S
CAP = 20480.0
LN_EPS = 1e-5
BIG = 1.0e30
MAGIC = 1.5 * 2.0 ** 23

UPTO = 6
DEVSTOP = 0
FUSED = True

ENG2BLK = {"pe": "tensor", "act": "scalar", "dve": "vector", "pool": "gpsimd", "sp": "sync"}


class KB:
    def __init__(self, nc):
        self.nc = nc
        self.ops = {e: [] for e in ENG2BLK}
        self.cnt = {}
        self.sems = {}

    LANES = ("xc0", "xc1", "px", "lw0", "lw1", "qw0", "qw1", "kq",
             "xst0", "xst1", "xlnd", "ddl", "msc", "mscp", "pw", "pr", "cc", "tw")

    def make_sems(self, ctx):
        for name in ("pe", "act", "dve", "pool", "sp") + KB.LANES:
            self.sems[name] = ctx.enter_context(self.nc.semaphore("sem_" + name))
            self.cnt[name] = 0

    def op(self, eng, fn, waits=(), hard=(), sig=None, dma=None):
        incs = []
        tok = None
        if eng in ("dve", "act", "pool") and dma is None:
            hard = tuple(hard) + ((eng, self.cnt[eng]),)
            if sig is None:
                sig = eng
        if sig is not None:
            self.cnt[sig] += 1
            incs.append((sig, 1))
            tok = (sig, self.cnt[sig])
        if dma is not None:
            self.cnt[dma] += 16
            incs.append((dma, 16))
            tok = (dma, self.cnt[dma])
        ws = tuple(w for w in waits if w is not None)
        hs = tuple(h for h in hard if h is not None)
        for t in ws + hs:
            assert t[1] <= self.cnt[t[0]], f"wait on future token {t} (cnt={self.cnt[t[0]]})"
        self.ops[eng].append((ws, hs, fn, incs))
        return tok

    def wait(self, eng, waits):
        ws = tuple(w for w in waits if w is not None)
        for t in ws:
            assert t[1] <= self.cnt[t[0]], f"wait on future token {t}"
        self.ops[eng].append((ws, (), None, []))

    def emit(self):
        with self.nc.Block() as blk:
            for eng in ENG2BLK:
                ops = self.ops[eng]
                own = eng

                def body(e, ops=ops, own=own):
                    waited = {}
                    own_cnt = 0
                    for ws, hs, fn, incs in ops:
                        need = {}
                        for s, c in ws:
                            if s == own and c <= own_cnt:
                                continue
                            if need.get(s, 0) < c:
                                need[s] = c
                        for s, c in hs:
                            if need.get(s, 0) < c:
                                need[s] = c
                        for s, c in need.items():
                            if waited.get(s, 0) < c:
                                e.wait_ge(self.sems[s], c)
                                waited[s] = c
                        if fn is not None:
                            instr = fn(e)
                            for s, n in incs:
                                instr.then_inc(self.sems[s], n)
                            own_cnt += sum(1 for s, n in incs if s == own)

                getattr(blk, ENG2BLK[eng])(body)


class NS:
    pass


W_NAMES = [
    ("L0_attn_Wq", [D, D]), ("L0_attn_Wk", [D, D]), ("L0_attn_Wv", [D, D]),
    ("L0_attn_Wo", [D, D]), ("L0_attn_gamma", [D]), ("L0_attn_beta", [D]),
    ("L0_ffn_kernel", [D, D]), ("L0_ffn_bias", [D]), ("L0_ffn_a", [D, 16]),
    ("L0_ffn_b", [16, D]), ("L1_attn_in_proj_w", [3 * D, D]),
    ("L1_attn_in_proj_b", [3 * D]), ("L1_attn_out_proj_w", [D, D]),
    ("L1_attn_out_proj_b", [D]), ("L1_ffn_gate_w", [8, D]),
    ("L1_ffn_we_gate", [8, D, D]), ("L1_ffn_we_up", [8, D, D]),
    ("L1_ffn_we_down", [8, D, D]), ("L1_ffn_ws_gate", [2 * D, D]),
    ("L1_ffn_ws_up", [2 * D, D]), ("L1_ffn_ws_down", [D, 2 * D]),
]

SC_NAMES = [
    ("q", [D, D]), ("k", [D, D]), ("v", [D, D]), ("o", [D, D]),
    ("kq", [D, D]), ("ip", [3 * D, D]), ("op", [D, D]),
    ("weg", [8, D, D]), ("weu", [8, D, D]), ("wed", [8, D, D]),
    ("wsg", [2 * D, D]), ("wsu", [2 * D, D]), ("wsd", [D, 2 * D]),
    ("gw", [16, D]),
    ("bv", [16, 4 * D]),
]


def declare_io(nc, phase):
    io = NS()
    din = lambda name, shape, dt=F32: nc.dram_tensor(name, shape, dt, kind="ExternalInput").ap()
    dout = lambda name, shape, dt=F32: nc.dram_tensor(name, shape, dt, kind="ExternalOutput").ap()
    dint = lambda name, shape, dt=BF16: nc.dram_tensor(name, shape, dt).ap()

    io.w = {nm: din(nm, shp) for nm, shp in W_NAMES}
    io.u128 = din("u128", [128, 128], BF16)
    io.cmask = din("cmask", [128, 128], BF16)
    io.rankmask = din("rankmask", [8, 1])
    io.rowsel = din("rowsel", [8, 1])

    last = (phase == "all") or (phase == "p1" and UPTO <= 3) or \
           (phase == "p2" and UPTO <= 5) or (phase == "p3")
    io.x_in = din("x", [TOK, D]) if phase in ("all", "p1") else din("xs_in", [TOK, D])
    io.out = dout("out", [TOK, D])
    io.xs = dint("xs", [TOK, D], F32) if (phase == "all" or last) else dout("xs_out", [TOK, D], F32)
    if phase == "all":
        io.xlnd = dint("xlnd", [TOK, D], BF16)
        io.xlnd_in = io.xlnd
    else:
        io.xlnd = io_xlnd_out = (dint("xlnd", [TOK, D], BF16) if last
                                 else dout("xlnd_out", [TOK, D], BF16))
        io.xlnd_in = din("xlnd_in", [TOK, D], BF16) if phase in ("p2", "p3") else io.xlnd
    if phase == "all":
        io.gd_wfull = dint("gd_wfull", [128, 512], F32)
        io.gd_pref = dint("gd_pref", [128, 512], F32)
        io.gd_trow = dint("gd_trow", [1, 512], F32)
        io.ccrow = None
        io.gdi_wfull = io.gd_wfull
        io.gdi_pref = io.gd_pref
        io.gdi_trow = io.gd_trow
        io.cc = None
    else:
        mkout = phase in ("p1", "p2") and not last
        f = dout if mkout else dint
        io.gd_wfull = f("gdo_wfull" if mkout else "gd_wfull", [128, 512], F32)
        io.gd_pref = f("gdo_pref" if mkout else "gd_pref", [128, 512], F32)
        io.gd_trow = f("gdo_trow" if mkout else "gd_trow", [1, 512], F32)
        io.ccrow = dout("ccrow", [1, 8], F32) if mkout else None
        if phase in ("p2", "p3"):
            io.gdi_wfull = din("gdi_wfull", [128, 512], F32)
            io.gdi_pref = din("gdi_pref", [128, 512], F32)
            io.gdi_trow = din("gdi_trow", [1, 512], F32)
            io.cc = din("cc", [8, 8], F32)
        else:
            io.gdi_wfull = io.gdi_pref = io.gdi_trow = io.cc = None

    io.sc = {nm: dint("sc_" + nm, shp, BF16) for nm, shp in SC_NAMES}
    io.tw = {}
    for nm, shp in [("ipq", [128, 8192]), ("ipk", [128, 8192]), ("ipv", [128, 8192]),
                    ("op", [128, 8192]), ("weg", [8, 128, 8192]), ("weu", [8, 128, 8192]),
                    ("wed", [8, 128, 8192]), ("wsg", [2, 128, 8192]),
                    ("wsu", [2, 128, 8192]), ("wsd", [2, 128, 8192])]:
        io.tw[nm] = dint("tw_" + nm, shp, BF16)
    io.dd = dint("dd", [D, S], BF16)
    io.bounce = dint("bounce", [1, 1024], F32)
    io.bounce2 = dint("bounce2", [1024, 1], F32)
    io.t64d = dint("t64d", [64, 8], F32)
    io.offd = dint("offd", [1, 512], F32)
    io.ccin_d = dint("ccin_d", [8, 8], F32)
    io.ccout_d = dint("ccout_d", [8, 8], F32)
    return io


def alloc_sbuf(nc, ctx):
    sb = NS()
    t = lambda n, s, d=BF16: ctx.enter_context(nc.sbuf_tensor(n, s, d))
    sb.PX = t("PX", [128, 16384])
    sb.PQ = t("PQ", [128, 16384])
    sb.PK = t("PK", [128, 16384])
    sb.PV = t("PV", [128, 16640])
    sb.PO = t("PO", [128, 16384])
    sb.POf = sb.PO[:].bitcast(F32)
    sb.xc = t("xc", [128, 2048], F32)
    sb.dtb = t("dtb", [128, 2048], BF16)
    sb.st = t("stg", [128, 4096], BF16)
    sb.tt = sb.st[:, 0:2048].bitcast(F32)
    sb.bn6 = t("bn6", [128, 12], F32)
    sb.stats = t("stats", [128, 32], F32)
    sb.mu16 = t("mu16", [128, 16], F32)
    sb.rstd16 = t("rstd16", [128, 16], F32)
    sb.sc1 = t("sc1", [128, 64], F32)
    sb.packA = t("packA", [128, 2048], F32)
    sb.row = sb.packA[0:1, 0:1024]
    sb.Trow = sb.packA[0:1, 1024:1536]
    sb.OFFROW = sb.packA[0:1, 1536:2048]
    sb.vb_row = t("vb_row", [1, 1024], BF16)
    sb.biasf = t("biasf", [128, 32], F32)
    sb.rowb = t("rowbufb", [1, 512], BF16)
    sb.u128 = t("u128c", [128, 128], BF16)
    sb.cmask = t("cmaskc", [128, 128], BF16)
    sb.rankmask = t("rankmaskc", [8, 1], F32)
    sb.rowsel = t("rowselc", [8, 1], F32)
    sb.ones_row_b = t("ones_row_b", [1, 128], BF16)
    sb.ones512_b = t("ones512_b", [1, 512], BF16)
    sb.ones_row_f = t("ones_row_f", [1, 128], F32)
    sb.ones_col_b = t("ones_col_b", [128, 1], BF16)
    sb.gamma_rep = t("gamma_rep", [128, D], BF16)
    sb.beta_rep = t("beta_rep", [128, D], BF16)
    sb.gwT = t("gwT", [128, 128], BF16)
    sb.quant = t("quant", [128, 8], F32)
    sb.rrep = t("rrep", [128, 2], F32)
    sb.wfull = t("wfull", [128, 512], F32)
    sb.pref = t("pref", [128, 512], F32)
    sb.comb = t("comb", [128, 512], F32)
    sb.CC = t("CCt", [8, 8], F32)
    sb.g8 = t("g8", [128, 64], F32)
    sb.selb = t("selb", [128, 32], BF16)
    sb.cb = t("cb", [128, 4], F32)
    return sb


def alloc_psum(nc, ctx):
    ps = NS()
    ps.A = ctx.enter_context(nc.psum_tensor("PSA", [128, 2048], F32))
    ps.B = ctx.enter_context(nc.psum_tensor("PSB", [128, 2048], F32))
    return ps



def stage_preprocess(kb, io, sb, ps):
    row1 = lambda ap: ap.rearrange("(o n) -> o n", o=1)
    for dst, src in [(sb.u128, io.u128), (sb.cmask, io.cmask),
                     (sb.rankmask, io.rankmask), (sb.rowsel, io.rowsel)]:
        kb.op("sp", lambda e, d=dst, s=src: e.dma_start(d[:], s), dma="msc")
    t_mp = kb.op("pool", lambda e: e.dma_start(
        sb.vb_row[:], io.w["L1_attn_in_proj_b"].rearrange("(o n) -> o n", o=1)[0:1, 2 * D:3 * D]),
        dma="mscp")
    t_g1 = kb.op("sp", lambda e: e.dma_start(sb.packA[0:1, 0:D], row1(io.w["L0_attn_gamma"])), dma="msc")
    kb.op("dve", lambda e: e.memset(sb.ones_row_b[:], 1.0), sig="dve")
    kb.op("dve", lambda e: e.memset(sb.ones512_b[:], 1.0), sig="dve")
    kb.op("dve", lambda e: e.memset(sb.ones_row_f[:], 1.0), sig="dve")
    t_ones = kb.op("dve", lambda e: e.memset(sb.ones_col_b[:], 1.0), sig="dve")
    tg_ev = []
    for j in range(2):
        tm = kb.op("pe", lambda e, j=j: e.matmul(
            ps.A[:, 512 * j:512 * (j + 1)], sb.ones_row_f[:],
            sb.packA[0:1, 512 * j:512 * (j + 1)], start=True, stop=True),
            waits=(t_g1, t_ones), sig="pe")
        tg_ev.append(kb.op("dve", lambda e, j=j: e.tensor_copy(
            sb.gamma_rep[:, 512 * j:512 * (j + 1)], ps.A[:, 512 * j:512 * (j + 1)]),
            waits=(tm,), sig="dve"))
    t_b1 = kb.op("sp", lambda e: e.dma_start(sb.packA[0:1, 0:D], row1(io.w["L0_attn_beta"])),
                 waits=(tg_ev[0], tg_ev[1]), dma="msc")
    for j in range(2):
        tm = kb.op("pe", lambda e, j=j: e.matmul(
            ps.A[:, 1024 + 512 * j:1024 + 512 * (j + 1)], sb.ones_row_f[:],
            sb.packA[0:1, 512 * j:512 * (j + 1)], start=True, stop=True),
            waits=(t_b1,), sig="pe")
        kb.op("dve", lambda e, j=j: e.tensor_copy(
            sb.beta_rep[:, 512 * j:512 * (j + 1)], ps.A[:, 1024 + 512 * j:1024 + 512 * (j + 1)]),
            waits=(tm,), sig="dve")

    cast_jobs = []

    def cast_copy(w_ap, sc_ap):
        rows = w_ap.shape[-2]
        for r0 in range(0, rows, 128):
            cast_jobs.append((w_ap, sc_ap, r0))

    for nm, src in [("q", io.w["L0_attn_Wq"]), ("k", io.w["L0_attn_Wk"]),
                    ("v", io.w["L0_attn_Wv"]), ("o", io.w["L0_attn_Wo"]),
                    ("op", io.w["L1_attn_out_proj_w"]), ("ip", io.w["L1_attn_in_proj_w"]),
                    ("wsg", io.w["L1_ffn_ws_gate"]), ("wsu", io.w["L1_ffn_ws_up"]),
                    ("wsd", io.w["L1_ffn_ws_down"])]:
        cast_copy(src, io.sc[nm])
    for nm, src in [("weg", io.w["L1_ffn_we_gate"]), ("weu", io.w["L1_ffn_we_up"]),
                    ("wed", io.w["L1_ffn_we_down"])]:
        for ee in range(8):
            cast_copy(src[ee], io.sc[nm][ee])
    t_r = ("dve", kb.cnt["dve"])
    for b0 in range(0, len(cast_jobs), 8):
        batch = cast_jobs[b0:b0 + 8]
        t_w = None
        for i, (w_ap, sc_ap, r0) in enumerate(batch):
            cols = w_ap.shape[-1]
            t_w = kb.op("pool", lambda e, w_ap=w_ap, r0=r0, c0=2048 * i, cols=cols: e.dma_start(
                sb.PO[:, c0:c0 + cols], w_ap[r0:r0 + 128, :]), waits=(t_r,), dma="pw")
        for i, (w_ap, sc_ap, r0) in enumerate(batch):
            cols = w_ap.shape[-1]
            t_r = kb.op("sp", lambda e, sc_ap=sc_ap, r0=r0, c0=2048 * i, cols=cols: e.dma_start(
                sc_ap[r0:r0 + 128, :], sb.PO[:, c0:c0 + cols]), waits=(t_w,), dma="pr")
    tgwz = kb.op("dve", lambda e: e.memset(sb.PO[0:16, 0:D], 0.0), waits=(t_r,), sig="dve")
    tgwc = kb.op("pool", lambda e: e.dma_start(sb.PO[0:8, 0:D], io.w["L1_ffn_gate_w"]),
                 waits=(tgwz, t_r, t_mp), dma="mscp")
    tgws = kb.op("sp", lambda e: e.dma_start(io.sc["gw"], sb.PO[0:16, 0:D]),
                 waits=(tgwc,), dma="msc")
    tgwT = None
    for dc in range(8):
        tgwT = kb.op("sp", lambda e, dc=dc: e.dma_start(
            sb.gwT[:, 16 * dc:16 * (dc + 1)], io.sc["gw"][:, 128 * dc:128 * (dc + 1)],
            transpose=True), waits=(tgws,), dma="msc")

    POf = sb.POf
    tload = None
    for dc in range(8):
        tload = kb.op("sp", lambda e, dc=dc: e.dma_start(
            POf[:, 1024 * dc:1024 * (dc + 1)], io.w["L0_ffn_kernel"][128 * dc:128 * (dc + 1), :]),
            dma="pr")
    tred = []
    for dc in range(8):
        tred.append(kb.op("dve", lambda e, dc=dc: e.reduce_max(
            sb.quant[:, dc:dc + 1], POf[:, 1024 * dc:1024 * (dc + 1)],
            axis=AX.X, apply_absolute_value=True), waits=(tload,), sig="dve"))
    tm = kb.op("dve", lambda e: e.reduce_max(sb.sc1[:, 0:1], sb.quant[:, 0:8], axis=AX.X),
               hard=(tred[-1],), sig="dve")
    tb = kb.op("pool", lambda e: e.dma_start(io.bounce2[0:128, :], sb.sc1[:, 0:1]),
               waits=(tm, t_mp), dma="mscp")
    tb2 = kb.op("sp", lambda e: e.dma_start(
        sb.packA[0:1, 0:128], io.bounce2.rearrange("(o p) n -> o (p n)", o=1)[0:1, 0:128]),
        waits=(tb,), dma="msc")
    tmax = kb.op("dve", lambda e: e.reduce_max(sb.packA[0:1, 128:129], sb.packA[0:1, 0:128], axis=AX.X),
                 waits=(tb2,), sig="dve")
    tsc = kb.op("dve", lambda e: e.tensor_scalar(sb.packA[0:1, 129:130], sb.packA[0:1, 128:129],
                                                 float(1.0 / 7.0), None, ALU.mult),
                hard=(tmax,), sig="dve")
    trc = kb.op("dve", lambda e: e.reciprocal(sb.packA[0:1, 130:131], sb.packA[0:1, 129:130]),
                hard=(tsc,), sig="dve")
    tb3 = kb.op("pool", lambda e: e.dma_start(io.bounce[0:1, 0:2], sb.packA[0:1, 129:131]),
                waits=(trc,), dma="mscp")
    tb4 = kb.op("sp", lambda e: e.dma_start(sb.packA[0:1, 132:134], io.bounce[0:1, 0:2]),
                waits=(tb3,), dma="msc")
    tmr = kb.op("pe", lambda e: e.matmul(ps.B[:, 0:2], sb.ones_row_f[:], sb.packA[0:1, 132:134],
                                         start=True, stop=True), waits=(tb4,), sig="pe")
    trr = kb.op("dve", lambda e: e.tensor_copy(sb.rrep[:, 0:2], ps.B[:, 0:2]),
                waits=(tmr,), sig="dve")
    t_qst = []
    for dc in range(8):
        stg = sb.st[:, 0:1024] if dc % 2 == 0 else sb.st[:, 1024:2048]
        kb.op("dve", lambda e, dc=dc: e.tensor_scalar(
            POf[:, 1024 * dc:1024 * (dc + 1)], POf[:, 1024 * dc:1024 * (dc + 1)],
            sb.rrep[:, 1:2], MAGIC, ALU.mult, ALU.add), hard=(trr,), sig="dve")
        t2 = kb.op("dve", lambda e, dc=dc: e.tensor_scalar(
            POf[:, 1024 * dc:1024 * (dc + 1)], POf[:, 1024 * dc:1024 * (dc + 1)],
            -MAGIC, None, ALU.add), sig="dve")
        t3 = kb.op("dve", lambda e, dc=dc, stg=stg: e.tensor_scalar(
            stg, POf[:, 1024 * dc:1024 * (dc + 1)], sb.rrep[:, 0:1], None, ALU.mult),
            waits=(t_qst[dc - 2] if dc >= 2 else None,), hard=(t2,), sig="dve")
        t_qst.append(kb.op("pool", lambda e, dc=dc, stg=stg: e.dma_start(
            io.sc["kq"][128 * dc:128 * (dc + 1), :], stg),
            waits=(t3, t_qst[dc - 1] if dc >= 1 else None), dma="pw"))
    tbz = kb.op("dve", lambda e: e.memset(sb.PO[0:16, 4096:8192], 0.0), waits=(t_r,), sig="dve")
    tb0 = kb.op("pool", lambda e: e.dma_start(
        sb.PO[0:1, 4096:5120], row1(io.w["L0_ffn_bias"])), waits=(tbz, t_mp), dma="mscp")
    tb1x = kb.op("pool", lambda e: e.dma_start(
        sb.PO[0:1, 5120:7168],
        io.w["L1_attn_in_proj_b"].rearrange("(o n) -> o n", o=1)[0:1, 0:2 * D]),
        waits=(tb0,), dma="mscp")
    tb2x = kb.op("pool", lambda e: e.dma_start(
        sb.PO[0:1, 7168:8192], row1(io.w["L1_attn_out_proj_b"])), waits=(tb1x,), dma="mscp")
    tbs = kb.op("sp", lambda e: e.dma_start(io.sc["bv"], sb.PO[0:16, 4096:8192]),
                waits=(tb0, tb1x, tb2x), dma="msc")
    tbt = None
    for ch in range(32):
        tbt = kb.op("sp", lambda e, ch=ch: e.dma_start(
            sb.st[:, 16 * ch:16 * (ch + 1)], io.sc["bv"][:, 128 * ch:128 * (ch + 1)],
            transpose=True), waits=(tbs,), dma="msc")
    kb.op("dve", lambda e: e.tensor_copy(
        sb.biasf[:], sb.st[:].rearrange("p (c r) -> p c r", r=16)[:, 0:32, 0:1]
        .rearrange("p c r -> p (c r)")), waits=(tbt,), sig="dve")

    kb.t_pr_all = ("pr", kb.cnt["pr"])
    return t_qst[-1], ("pr", kb.cnt["pr"])


def image_jobs(io):
    def b_io(ooff):
        return [(1024 * dc + 512 * oh, ooff + 512 * oh, 512, 128 * dc)
                for dc in range(8) for oh in range(2)]
    jobs = []
    for nm in ("weg", "weu"):
        for ee in range(8):
            jobs.append((io.tw[nm][ee],
                         [(1024 * bc, 0, 1024, 128 * bc) for bc in range(8)],
                         io.sc[nm][ee]))
    for ee in range(8):
        jobs.append((io.tw["wed"][ee],
                     [(1024 * ic + 512 * hh, 512 * hh, 512, 128 * ic)
                      for ic in range(8) for hh in range(2)], io.sc["wed"][ee]))
    for grp in range(2):
        jobs.append((io.tw["wsg"][grp],
                     [(1024 * bc, 1024 * grp, 1024, 128 * bc) for bc in range(8)], io.sc["wsg"]))
        jobs.append((io.tw["wsu"][grp],
                     [(1024 * bc, 1024 * grp, 1024, 128 * bc) for bc in range(8)], io.sc["wsu"]))
        jobs.append((io.tw["wsd"][grp],
                     [(1024 * ic + 512 * hh, 512 * hh, 512, 128 * (8 * grp + ic))
                      for ic in range(8) for hh in range(2)], io.sc["wsd"]))
    for nm, ooff in (("ipq", 0), ("ipk", D), ("ipv", 2 * D)):
        jobs.append((io.tw[nm], b_io(ooff), io.sc["ip"]))
    jobs.append((io.tw["op"], b_io(0), io.sc["op"]))
    return jobs


def stage_images(kb, io, sb, jobs, after_tok):
    slots = (sb.PV[:, 0:8192], sb.PV[:, 8448:16640])
    t_store = [kb.tw_done, kb.tw_done]
    for i, (dst, blocks, src) in enumerate(jobs):
        stg = slots[i % 2]
        t_tr = None
        for (doff, r0, rn, c0) in blocks:
            t_tr = kb.op("act", lambda e, stg=stg, doff=doff, r0=r0, rn=rn, c0=c0, src=src: e.dma_start(
                stg[:, doff:doff + rn], src[r0:r0 + rn, c0:c0 + 128], transpose=True),
                waits=(t_store[i % 2], kb.t_pr_all, after_tok), dma="tw")
        t_store[i % 2] = kb.op("act", lambda e, dst=dst, stg=stg: e.dma_start(dst, stg),
                               waits=(t_tr,), dma="tw")
    kb.tw_done = t_store[(len(jobs) - 1) % 2]
    kb.tw_done = (kb.tw_done[0], max(kb.cnt["tw"], kb.tw_done[1]))
    return kb.tw_done



def pass_residual_ln(kb, io, sb, ps, seq, mode, want_ln, x_src, x_dst,
                     scale_delta=1.0, px_free=None, chunks=None, y_tok=None,
                     dd_tok=None, x_tok=None, pq_tok=None):
    if chunks is None:
        chunks = list(range(NCH))
    else:
        chunks = list(chunks)
    t_aff = None
    last_store = None
    st_tok = {}
    prev_v = {}
    batch_tok = [pq_tok]
    x_tok = tuple(x_tok) if isinstance(x_tok, (tuple, list)) else (x_tok,)
    for c in chunks:
        xcs = sb.xc[:, 1024 * (c % 2):1024 * (c % 2 + 1)]
        row0 = seq * S + 128 * c
        lane = "xc%d" % (c % 2)
        pv = prev_v.get(c % 2, ())
        tld = kb.op("sp", lambda e, xcs=xcs, row0=row0, x_src=x_src: e.dma_start(
            xcs, x_src[row0:row0 + 128, :]), waits=pv + x_tok, dma=lane)
        tv = tld
        if mode == "delta":
            dtb = sb.dtb[:, 1024 * (c % 2):1024 * (c % 2 + 1)]
            tdt = kb.op("sp", lambda e, dtb=dtb, c=c: e.dma_start(
                dtb, io.dd[:, 128 * c:128 * (c + 1)], transpose=True),
                waits=pv + (dd_tok,), dma=lane)
            tv = kb.op("dve", lambda e, xcs=xcs, dtb=dtb: e.scalar_tensor_tensor(
                xcs, dtb, scale_delta, xcs, ALU.mult, ALU.add), waits=(tld, tdt), sig="dve")
        elif mode == "y":
            ysl = sb.POf[:, 1024 * (c % 8):1024 * (c % 8 + 1)]
            tv = kb.op("dve", lambda e, xcs=xcs, ysl=ysl: e.scalar_tensor_tensor(
                xcs, ysl, 0.5, xcs, ALU.mult, ALU.add), waits=(tld, y_tok), sig="dve")
        elif mode == "l0inner":
            dtb = sb.dtb[:, 1024 * (c % 2):1024 * (c % 2 + 1)]
            tdt = kb.op("sp", lambda e, dtb=dtb, c=c: e.dma_start(
                dtb, io.dd[:, 128 * c:128 * (c + 1)], transpose=True),
                waits=pv + (dd_tok,), dma=lane)
            tt = sb.tt[:, 0:1024]
            kb.op("dve", lambda e, xcs=xcs, tt=tt, c=c: e.tensor_scalar(
                tt, xcs, sb.stats[:, 2 * c:2 * c + 1], sb.rstd16[:, c:c + 1],
                ALU.subtract, ALU.mult), waits=(tld,), sig="dve")
            kb.op("dve", lambda e, tt=tt, dtb=dtb: e.tensor_tensor(
                tt, tt, dtb, ALU.add), waits=(tdt,), sig="dve")
            kb.op("dve", lambda e, tt=tt: e.bn_stats(sb.bn6[:, 0:6], tt[:, 0:512]), sig="dve")
            tb1 = kb.op("dve", lambda e, tt=tt: e.bn_stats(sb.bn6[:, 6:12], tt[:, 512:1024]), sig="dve")
            tb2 = kb.op("dve", lambda e: e.bn_aggr(sb.sc1[:, 2:4], sb.bn6[:]),
                        hard=(tb1,), sig="dve")
            tv1 = kb.op("dve", lambda e: e.tensor_scalar(
                sb.sc1[:, 4:5], sb.sc1[:, 3:4], LN_EPS, None, ALU.add), hard=(tb2,), sig="dve")
            tl1 = kb.op("act", lambda e: e.activation(sb.sc1[:, 5:6], sb.sc1[:, 4:5], AF.Ln,
                                                      bias=0.0, scale=1.0), waits=(tv1,), sig="act")
            tl2 = kb.op("act", lambda e: e.activation(sb.sc1[:, 6:7], sb.sc1[:, 5:6], AF.Exp,
                                                      bias=0.0, scale=-0.5), hard=(tl1,), sig="act")
            kb.op("dve", lambda e, tt=tt: e.tensor_scalar(
                tt, tt, sb.sc1[:, 2:3], sb.sc1[:, 6:7], ALU.subtract, ALU.mult),
                hard=(tl2, tb2), sig="dve")
            kb.op("dve", lambda e, tt=tt: e.tensor_tensor(tt, tt, sb.gamma_rep[:], ALU.mult),
                  sig="dve")
            kb.op("dve", lambda e, tt=tt: e.tensor_tensor(tt, tt, sb.beta_rep[:], ALU.add),
                  sig="dve")
            tv = kb.op("dve", lambda e, xcs=xcs, tt=tt: e.tensor_tensor(xcs, xcs, tt, ALU.add),
                       sig="dve")
        last_store = kb.op("pool", lambda e, xcs=xcs, row0=row0, x_dst=x_dst: e.dma_start(
            x_dst[row0:row0 + 128, :], xcs), waits=(tv,), dma="xst%d" % (c % 2))
        st_tok[c % 2] = last_store
        if want_ln:
            kb.op("dve", lambda e, xcs=xcs: e.bn_stats(sb.bn6[:, 0:6], xcs[:, 0:512]),
                  waits=(tv,), sig="dve")
            tb1 = kb.op("dve", lambda e, xcs=xcs: e.bn_stats(sb.bn6[:, 6:12], xcs[:, 512:1024]),
                  sig="dve")
            tb2 = kb.op("dve", lambda e, c=c: e.bn_aggr(sb.stats[:, 2 * c:2 * c + 2], sb.bn6[:]),
                        hard=(tb1,), sig="dve")
            tv1 = kb.op("dve", lambda e, c=c: e.tensor_scalar(
                sb.sc1[:, 8:9], sb.stats[:, 2 * c + 1:2 * c + 2], LN_EPS, None, ALU.add),
                hard=(tb2,), sig="dve")
            tl1 = kb.op("act", lambda e: e.activation(
                sb.sc1[:, 9:10], sb.sc1[:, 8:9], AF.Ln, bias=0.0, scale=1.0),
                waits=(tv1,), sig="act")
            tl2 = kb.op("act", lambda e, c=c: e.activation(
                sb.rstd16[:, c:c + 1], sb.sc1[:, 9:10], AF.Exp, bias=0.0, scale=-0.5),
                hard=(tl1,), sig="act")
            t_aff = kb.op("dve", lambda e, xcs=xcs, c=c: e.tensor_scalar(
                sb.PQ[:, 1024 * (c % 8):1024 * (c % 8 + 1)], xcs,
                sb.stats[:, 2 * c:2 * c + 1], sb.rstd16[:, c:c + 1],
                ALU.subtract, ALU.mult),
                waits=(px_free, batch_tok[0]),
                hard=(tl2, tb2), sig="dve")
            prev_v[c % 2] = (last_store, t_aff)
            if want_ln and c % 8 == 7:
                c0b = c - 7
                batch_tok[0] = kb.op("pool", lambda e, c0b=c0b, seq=seq: e.dma_start(
                    io.xlnd[seq * S + 128 * c0b:seq * S + 128 * (c0b + 8), :]
                    .rearrange("(c p) n -> p c n", p=128),
                    sb.PQ[:, 0:8192].rearrange("p (c n) -> p c n", n=1024)),
                    waits=(t_aff,), dma="xlnd")
        else:
            prev_v[c % 2] = (last_store, tv)
    ret_store = (st_tok.get(0), st_tok.get(1))
    if not want_ln:
        return ret_store, None
    return ret_store, batch_tok[0]


def load_px(kb, io, sb, seq, t_ready, px_free=None, src=None):
    t = None
    xl = src if src is not None else io.xlnd
    for dc in range(8):
        t = kb.op("sp", lambda e, dc=dc, xl=xl, seq=seq: e.dma_start(
            sb.PX[:, 2048 * dc:2048 * (dc + 1)],
            xl[seq * S:(seq + 1) * S, 128 * dc:128 * (dc + 1)], transpose=True),
            waits=(t_ready, px_free), dma="px")
    return t



def fm_proj(kb, sb, ps, dst, reg, wtok, t_px, bias_col=None,
            transposed=False, war0=None, evac_eng="dve"):
    ev = {}
    t_last = None
    for fc in range(8):
        for slab in range(4):
            pa = ps.A[:, 512 * slab:512 * (slab + 1)]
            for dc in range(8):
                if transposed:
                    off = reg + 1024 * dc + 512 * (fc // 4) + 128 * (fc % 4)
                else:
                    off = reg + 1024 * dc + 128 * fc
                lhsT = sb.PO[:, off:off + 128]
                w0 = (wtok, ev.get(slab), t_px, war0) if dc == 0 else ()
                tmm = kb.op("pe", lambda e, pa=pa, lhsT=lhsT, dc=dc, slab=slab: e.matmul(
                    pa, lhsT, sb.PX[:, 2048 * dc + 512 * slab:2048 * dc + 512 * (slab + 1)],
                    start=(dc == 0), stop=(dc == 7)),
                    waits=w0, sig="pe" if dc == 7 else None)
            if bias_col is not None:
                ev[slab] = kb.op("dve", lambda e, fc=fc, slab=slab, pa=pa, dst=dst: e.tensor_scalar(
                    dst[:, 2048 * fc + 512 * slab:2048 * fc + 512 * (slab + 1)], pa,
                    sb.biasf[:, bias_col + fc:bias_col + fc + 1], None, ALU.add),
                    waits=(tmm,), sig="dve")
            elif evac_eng == "dve":
                ev[slab] = kb.op("dve", lambda e, fc=fc, slab=slab, pa=pa, dst=dst: e.tensor_copy(
                    dst[:, 2048 * fc + 512 * slab:2048 * fc + 512 * (slab + 1)], pa),
                    waits=(tmm,), sig="dve")
            else:
                ev[slab] = kb.op("act", lambda e, fc=fc, slab=slab, pa=pa, dst=dst: e.activation(
                    dst[:, 2048 * fc + 512 * slab:2048 * fc + 512 * (slab + 1)], pa,
                    AF.Copy, bias=0.0, scale=1.0), waits=(tmm,), sig="act")
            t_last = ev[slab]
    return t_last



def stage_ffn0(kb, io, sb, ps, t_px, wfree=None):
    t_w = None
    for dc in range(8):
        t_w = kb.op("sp", lambda e, dc=dc: e.dma_start(
            sb.PK[:, 1024 * dc:1024 * (dc + 1)], io.sc["kq"][128 * dc:128 * (dc + 1), :]),
            waits=(wfree,), dma="kq")
    aT = sb.PK[:, 9216:9344]
    bT = sb.PK[0:16, 8192:9216]
    kb.op("pool", lambda e, aT=aT: e.dma_start(
        aT.rearrange("p (c r) -> p c r", r=16),
        io.w["L0_ffn_a"].rearrange("(c p) r -> p c r", p=128)), waits=(wfree,), dma="pw")
    t_ab2 = kb.op("pool", lambda e, bT=bT: e.dma_start(bT, io.w["L0_ffn_b"]),
                  waits=(wfree,), dma="pw")
    tl1 = []
    for slab in range(4):
        t = None
        for dc in range(8):
            t = kb.op("pe", lambda e, slab=slab, dc=dc, aT=aT: e.matmul(
                ps.B[0:16, 512 * slab:512 * (slab + 1)],
                aT[:, 16 * dc:16 * (dc + 1)],
                sb.PX[:, 2048 * dc + 512 * slab:2048 * dc + 512 * (slab + 1)],
                start=(dc == 0), stop=(dc == 7)),
                waits=(t_px, t_ab2) if dc == 0 else (), sig="pe" if dc == 7 else None)
        tl1.append(t)
    tl1b = []
    for slab in range(4):
        tl1b.append(kb.op("dve", lambda e, slab=slab: e.tensor_copy(
            sb.st[0:16, 512 * slab:512 * (slab + 1)], ps.B[0:16, 512 * slab:512 * (slab + 1)]),
            waits=(tl1[slab],), sig="dve"))
    ev = {}
    t_dd = None
    for fc in range(8):
        for slab in range(4):
            pa = ps.A[:, 512 * slab:512 * (slab + 1)]
            for dc in range(8):
                kb.op("pe", lambda e, fc=fc, slab=slab, dc=dc, pa=pa: e.matmul(
                    pa, sb.PK[:, 1024 * dc + 128 * fc:1024 * dc + 128 * (fc + 1)],
                    sb.PX[:, 2048 * dc + 512 * slab:2048 * dc + 512 * (slab + 1)],
                    start=(dc == 0), stop=False),
                    waits=(t_w, ev.get(slab)) if dc == 0 else ())
            tmm = kb.op("pe", lambda e, fc=fc, slab=slab, pa=pa, bT=bT: e.matmul(
                pa, bT[:, 128 * fc:128 * (fc + 1)],
                sb.st[0:16, 512 * slab:512 * (slab + 1)], start=False, stop=True),
                waits=(tl1b[slab], t_ab2), sig="pe")
            ev[slab] = kb.op("dve", lambda e, slab=slab, fc=fc, pa=pa: e.tensor_scalar(
                sb.st[:, 2048 + 512 * slab:2048 + 512 * (slab + 1)], pa,
                sb.biasf[:, fc:fc + 1], 0.5, ALU.add, ALU.mult),
                waits=(tmm,), sig="dve")
        tsd = kb.op("pool", lambda e, fc=fc: e.dma_start(
            io.dd[128 * fc:128 * (fc + 1), :], sb.st[:, 2048:4096]),
            waits=(ev[0], ev[1], ev[2], ev[3]), dma="ddl")
        for slab in range(4):
            ev[slab] = tsd
        t_dd = tsd
    return t_dd, ("pe", kb.cnt["pe"])



def stage_attn(kb, io, sb, ps, seq, layer, t_px, wfree=None, pv_tok=None):
    L1 = (layer == 1)
    regA, regB = 0, 8192
    SC = 0.125

    def load_direct(scname, reg, wtok=None):
        lane = "qw0" if reg == 0 else "qw1"
        t = None
        for dc in range(8):
            t = kb.op("sp", lambda e, dc=dc, scname=scname, reg=reg: e.dma_start(
                sb.PO[:, reg + 1024 * dc:reg + 1024 * (dc + 1)],
                io.sc[scname][128 * dc:128 * (dc + 1), :]), waits=(wtok,), dma=lane)
        return t

    def load_T(twname, reg, wtok=None):
        lane = "qw0" if reg == 0 else "qw1"
        return kb.op("sp", lambda e, twname=twname, reg=reg: e.dma_start(
            sb.PO[:, reg:reg + 8192], io.tw[twname]),
            waits=(wtok, kb.tw_done), dma=lane)

    if L1:
        wq = load_T("ipq", regA, wtok=wfree)
        tq = fm_proj(kb, sb, ps, sb.PQ, regA, wq, t_px, bias_col=8, transposed=True)
        wk = load_T("ipk", regB)
        tk = fm_proj(kb, sb, ps, sb.PK, regB, wk, t_px, bias_col=16, transposed=True)
    else:
        wq = load_direct("q", regA, wtok=wfree)
        tq = fm_proj(kb, sb, ps, sb.PQ, regA, wq, t_px)
        wk = load_direct("k", regB)
        tk = fm_proj(kb, sb, ps, sb.PK, regB, wk, t_px, evac_eng="act")

    tvm = kb.op("dve", lambda e: e.memset(sb.PV[:], 1.0), waits=(pv_tok,), sig="dve")
    if L1:
        wv = load_T("ipv", regA, wtok=tq)
    else:
        wv = load_direct("v", regA, wtok=tq)
    ev = {}
    t_v = None
    for tc in range(NCH):
        for half in range(2):
            bank = (tc * 2 + half) % 4
            pb = ps.B[:, 512 * bank:512 * (bank + 1)]
            for dc in range(8):
                lhsT = sb.PX[:, 2048 * dc + 128 * tc:2048 * dc + 128 * (tc + 1)]
                w0 = (wv, ev.get(bank)) if dc == 0 else ()
                kb.op("pe", lambda e, pb=pb, lhsT=lhsT, dc=dc, half=half: e.matmul(
                    pb, lhsT,
                    sb.PO[:, regA + 1024 * dc + 512 * half:regA + 1024 * dc + 512 * (half + 1)],
                    start=(dc == 0), stop=(dc == 7 and not L1)),
                    waits=w0, sig="pe" if (dc == 7 and not L1) else None)
            if L1:
                tmm = kb.op("pe", lambda e, pb=pb, half=half: e.matmul(
                    pb, sb.ones_row_b[:],
                    sb.vb_row[0:1, 512 * half:512 * (half + 1)],
                    start=False, stop=True), sig="pe")
            else:
                tmm = ("pe", kb.cnt["pe"])
            ev_t = None
            for hh in range(8):
                h = half * 8 + hh
                ev_t = kb.op("dve", lambda e, pb=pb, tc=tc, h=h, hh=hh: e.tensor_copy(
                    sb.PV[:, 1040 * tc + 65 * h:1040 * tc + 65 * h + 64],
                    pb[:, 64 * hh:64 * (hh + 1)]), waits=(tmm, tvm), sig="dve")
            ev[bank] = ev_t
            t_v = ev_t

    t_cb_prev = None
    sbuf_free = {}
    t_px_writes = []
    t_cb = None
    t_opx = {}
    pend_fin = [None]

    def flush_fin():
        f = pend_fin[0]
        pend_fin[0] = None
        if f:
            f()
    hq = 0
    for h in range(16):
        fc, pb0 = h // 2, 64 * (h % 2)
        for qh in range(2):
            par = hq % 2
            ob = ps.B[0:65, 1024 * par:1024 * par + 1024]
            kcs = list(range(NCH)) if not L1 else list(range(8 * qh + 8))
            pend_pv = []
            t_pv = None
            pv_started = [False]

            def emit_pv(first_overall, last_overall):
                nonlocal t_pv
                kc, te, tz, pt = pend_pv.pop(0)
                for j in range(2):
                    last = (j == 1 and last_overall)
                    t_pv = kb.op("pe", lambda e, kc=kc, h=h, j=j, pt=pt, fo=(first_overall and j == 0 or first_overall and j == 1), first=first_overall, last2=last_overall, ob=ob: e.matmul(
                        ob[:, 512 * j:512 * (j + 1)],
                        sb.PV[:, 1040 * kc + 65 * h:1040 * kc + 65 * (h + 1)],
                        pt[:, 512 * j:512 * (j + 1)], start=first, stop=last2),
                        waits=(te, tz, t_v, t_opx.get(par) if first_overall else None),
                        sig="pe" if last else None)
                sbuf_free[("pt", kc % (2 if L1 else 3))] = ("pe", kb.cnt["pe"])

            for i, kc in enumerate(kcs):
                if i == 8 - 2 * L1:
                    flush_fin()
                sbi = kc % 2
                pA = ps.A[:, 1024 * sbi:1024 * (sbi + 1)]
                dprime = kc - 8 * qh if L1 else -1
                qlo = 128 * dprime if (L1 and dprime >= 0) else 0
                tsm = None
                for j in range(2):
                    c0, c1 = max(512 * j, qlo), 512 * (j + 1)
                    if c0 >= c1:
                        continue
                    tsm = kb.op("pe", lambda e, pA=pA, fc=fc, pb0=pb0, kc=kc, qh=qh, c0=c0, c1=c1: e.matmul(
                        pA[:, c0:c1],
                        sb.PK[pb0:pb0 + 64, 2048 * fc + 128 * kc:2048 * fc + 128 * (kc + 1)],
                        sb.PQ[pb0:pb0 + 64, 2048 * fc + 1024 * qh + c0:2048 * fc + 1024 * qh + c1],
                        start=True, stop=True, tile_position=(pb0, 0)),
                        waits=(tq, tk, sbuf_free.get(sbi),
                               t_cb if ((not L1) and qh == 0 and kc == 1) else None),
                        sig="pe")
                if (not L1) and qh == 0 and kc == 0:
                    tr = kb.op("dve", lambda e, pA=pA: e.reduce_max(
                        sb.sc1[0:1, 16:17], pA[0:1, 0:1024], axis=AX.X, negate=True),
                        waits=(tsm,), sig="dve")
                    tr2 = kb.op("dve", lambda e: e.tensor_scalar(
                        sb.sc1[0:1, 17:18], sb.sc1[0:1, 16:17], SC, None, ALU.mult),
                        hard=(tr,), sig="dve")
                    tr3 = kb.op("dve", lambda e: e.tensor_copy(sb.rowb[0:1, 0:1], sb.sc1[0:1, 17:18]),
                                hard=(tr2,), sig="dve")
                    trm = kb.op("pe", lambda e: e.matmul(
                        ps.A[:, 2047:2048], sb.ones_row_b[:], sb.rowb[0:1, 0:1],
                        start=True, stop=True), waits=(tr3, sbuf_free.get(1)), sig="pe")
                    t_cb = kb.op("dve", lambda e: e.tensor_copy(sb.cb[:, 0:1], ps.A[:, 2047:2048]),
                                 waits=(trm,), hard=(t_cb_prev,), sig="dve")
                if L1 and dprime >= 0:
                    tsm = kb.op("dve", lambda e, pA=pA, qlo=qlo: e.tensor_tensor(
                        pA[:, qlo:qlo + 128], pA[:, qlo:qlo + 128], sb.cmask[:], ALU.add),
                        waits=(tsm,), sig="dve")
                sb3 = kc % (2 if L1 else 3)
                pt = sb.PO[:, 1024 * sb3:1024 * (sb3 + 1)]
                tz = None
                if qlo > 0:
                    tz = kb.op("pool", lambda e, pt=pt, qlo=qlo: e.memset(pt[:, 0:qlo], 0.0),
                               waits=(sbuf_free.get(("pt", sb3)) or t_v,), sig="pool")
                if L1:
                    te = kb.op("act", lambda e, pA=pA, pt=pt, qlo=qlo: e.activation(
                        pt[:, qlo:1024], pA[:, qlo:1024], AF.Exp, bias=0.0, scale=SC),
                        waits=(tsm, sbuf_free.get(("pt", sb3))), sig="act")
                else:
                    te = kb.op("act", lambda e, pA=pA, pt=pt: e.activation(
                        pt[:], pA[:], AF.Exp, bias=sb.cb[:, 0:1], scale=SC),
                        waits=(tsm, sbuf_free.get(("pt", sb3))), hard=(t_cb,), sig="act")
                t_cb_prev = te
                sbuf_free[sbi] = te
                pend_pv.append((kc, te, tz, pt))

                def pump(last):
                    emit_pv(not pv_started[0], last)
                    pv_started[0] = True
                if L1:
                    if qh == 1 and kc < 7:
                        if len(pend_pv) > 1:
                            pump(False)
                    else:
                        while pend_pv:
                            pump(i == len(kcs) - 1 and len(pend_pv) == 1)
                elif len(pend_pv) > 2:
                    pump(False)
            while pend_pv:
                pump(len(pend_pv) == 1)
            def fin(ob=ob, par=par, fc=fc, pb0=pb0, qh=qh, t_pv=t_pv):
                stv = sb.st[64:128, 1024 * par:1024 * (par + 1)]
                t0 = kb.op("dve", lambda e, ob=ob: e.tensor_copy(
                    sb.row, ob[64:65, 0:1024]), waits=(t_pv,), sig="dve")
                tex = kb.op("dve", lambda e: e.reciprocal(sb.row, sb.row),
                            hard=(t0,), sig="dve")
                tev = kb.op("act", lambda e, ob=ob, stv=stv: e.activation(
                    stv, ob[0:64, 0:1024], AF.Copy, bias=0.0, scale=1.0),
                    waits=(t_pv, t_opx.get(par)), sig="act")
                trep = []
                for j in range(2):
                    trep.append(kb.op("pe", lambda e, j=j, ob=ob: e.matmul(
                        ob[0:64, 512 * j:512 * (j + 1)],
                        sb.ones_row_f[0:1, 0:64], sb.packA[0:1, 512 * j:512 * (j + 1)],
                        start=True, stop=True), waits=(tex, tev), sig="pe"))
                t_ox = kb.op("dve", lambda e, fc=fc, pb0=pb0, qh=qh, ob=ob, stv=stv: e.tensor_tensor(
                    sb.PX[pb0:pb0 + 64, 2048 * fc + 1024 * qh:2048 * fc + 1024 * (qh + 1)],
                    stv, ob[0:64, 0:1024], ALU.mult),
                    waits=(tev, trep[0], trep[1]), sig="dve")
                t_px_writes.append(t_ox)
                t_opx[par] = t_ox
            pend_fin[0] = fin
            hq += 1

    flush_fin()
    if L1:
        wo = load_T("op", regB, wtok=tk)
    else:
        wo = load_direct("o", regB, wtok=tk)
    t_all_px = t_px_writes[-1]
    ev2 = {}
    t_dd = None
    for fc in range(8):
        for slab in range(4):
            pa = ps.A[:, 512 * slab:512 * (slab + 1)]
            for dc in range(8):
                if L1:
                    off = regB + 1024 * dc + 512 * (fc // 4) + 128 * (fc % 4)
                else:
                    off = regB + 1024 * dc + 128 * fc
                lhsT = sb.PO[:, off:off + 128]
                w0 = (wo, ev2.get(slab), t_all_px, sbuf_free.get(0), sbuf_free.get(1)) \
                    if dc == 0 else ()
                tmm = kb.op("pe", lambda e, pa=pa, lhsT=lhsT, dc=dc, slab=slab: e.matmul(
                    pa, lhsT, sb.PX[:, 2048 * dc + 512 * slab:2048 * dc + 512 * (slab + 1)],
                    start=(dc == 0), stop=(dc == 7)),
                    waits=w0, sig="pe" if dc == 7 else None)
            if L1:
                ev2[slab] = kb.op("dve", lambda e, slab=slab, fc=fc, pa=pa: e.tensor_scalar(
                    sb.st[:, 2048 + 512 * slab:2048 + 512 * (slab + 1)], pa,
                    sb.biasf[:, 24 + fc:24 + fc + 1], None, ALU.add),
                    waits=(tmm,), sig="dve")
            else:
                ev2[slab] = kb.op("dve", lambda e, slab=slab, pa=pa: e.tensor_copy(
                    sb.st[:, 2048 + 512 * slab:2048 + 512 * (slab + 1)], pa),
                    waits=(tmm,), sig="dve")
        tsd = kb.op("pool", lambda e, fc=fc: e.dma_start(
            io.dd[128 * fc:128 * (fc + 1), :], sb.st[:, 2048:4096]),
            waits=(ev2[0], ev2[1], ev2[2], ev2[3]), dma="ddl")
        for slab in range(4):
            ev2[slab] = tsd
        t_dd = tsd
    return t_dd, ("pe", kb.cnt["pe"])



def stage_gate(kb, io, sb, ps, seq, t_px):
    t_prev = {0: None, 1: None, 2: None, 3: None}
    selb_prev = {0: None, 1: None, 2: None, 3: None}
    te1_prev = [None]
    for tc in range(NCH):
        slot = seq * NCH + tc
        par = tc % 4
        pg = ps.B[:, 1024 + 16 * par:1032 + 16 * par]
        tg = None
        for dc in range(8):
            tg = kb.op("pe", lambda e, dc=dc, tc=tc, pg=pg: e.matmul(
                pg, sb.PX[:, 2048 * dc + 128 * tc:2048 * dc + 128 * (tc + 1)],
                sb.gwT[:, 16 * dc:16 * dc + 8], start=(dc == 0), stop=(dc == 7)),
                waits=(t_px, t_prev[par]) if dc == 0 else (), sig="pe" if dc == 7 else None)
        g = sb.g8
        tm0 = kb.op("dve", lambda e, pg=pg: e.reduce_max(g[:, 8:9], pg, axis=AX.X, negate=True),
                    waits=(tg,), hard=(te1_prev[0],), sig="dve")
        te1 = kb.op("act", lambda e, pg=pg: e.activation(g[:, 16:24], pg, AF.Exp,
                                                         bias=g[:, 8:9], scale=1.0),
                    waits=(tg,), hard=(tm0,), sig="act")
        te1_prev[0] = te1
        ti1 = kb.op("dve", lambda e, pg=pg: e.tensor_scalar(g[:, 40:48], pg, g[:, 8:9],
                                                            0.0, ALU.add, ALU.is_ge),
                    hard=(tm0,), sig="dve")
        tk0 = kb.op("dve", lambda e, pg=pg: e.scalar_tensor_tensor(g[:, 48:56], g[:, 40:48], -BIG,
                                                                   pg, ALU.mult, ALU.add),
                    hard=(ti1,), sig="dve")
        tm2 = kb.op("dve", lambda e: e.reduce_max(g[:, 12:13], g[:, 48:56], axis=AX.X),
                    hard=(tk0,), sig="dve")
        tsel = kb.op("dve", lambda e, pg=pg: e.tensor_scalar(g[:, 56:64], pg, g[:, 12:13],
                                                             None, ALU.is_ge), hard=(tm2,), sig="dve")
        tt0 = kb.op("dve", lambda e: e.tensor_tensor(g[:, 24:32], g[:, 56:64], g[:, 16:24],
                                                     ALU.mult), waits=(te1,), hard=(tsel,), sig="dve")
        ts2 = kb.op("dve", lambda e: e.reduce_sum(g[:, 13:14], g[:, 24:32], axis=AX.X),
                    hard=(tt0,), sig="dve")
        tr2 = kb.op("dve", lambda e: e.reciprocal(g[:, 15:16], g[:, 13:14]), hard=(ts2,), sig="dve")
        twf = kb.op("dve", lambda e, slot=slot: e.tensor_scalar(
            sb.wfull[:, 8 * (slot % 64):8 * (slot % 64) + 8], g[:, 24:32], g[:, 15:16],
            None, ALU.mult), hard=(tr2,), sig="dve")
        sel = sb.selb[:, 8 * par:8 * (par + 1)]
        tsb = kb.op("dve", lambda e, sel=sel: e.tensor_copy(sel, g[:, 56:64]),
                    waits=(selb_prev[par],), hard=(tsel,), sig="dve")
        pp = ps.B[:, 1536 + 16 * par:1544 + 16 * par]
        tpm = kb.op("pe", lambda e, pp=pp, sel=sel: e.matmul(pp, sb.u128[:], sel,
                                                             start=True, stop=True),
                    waits=(tsb, t_prev[par]), sig="pe")
        tpe = kb.op("dve", lambda e, slot=slot, pp=pp: e.tensor_copy(
            sb.pref[:, 8 * (slot % 64):8 * (slot % 64) + 8], pp), waits=(tpm,), sig="dve")
        pt2 = ps.B[0:1, 1600 + 16 * par:1608 + 16 * par]
        ttm = kb.op("pe", lambda e, pt2=pt2, sel=sel: e.matmul(pt2, sb.ones_col_b[:], sel,
                                                               start=True, stop=True),
                    waits=(tsb,), sig="pe")
        tte = kb.op("dve", lambda e, slot=slot, pt2=pt2: e.tensor_copy(
            sb.Trow[0:1, 8 * (slot % 64):8 * (slot % 64) + 8], pt2), waits=(ttm,), sig="dve")
        t_prev[par] = tte
        selb_prev[par] = ("pe", kb.cnt["pe"])
    return tte, ("pe", kb.cnt["pe"])


def dump_gate(kb, io, sb, t_gate):
    t1 = kb.op("pool", lambda e: e.dma_start(io.gd_wfull, sb.wfull[:]), waits=(t_gate,), dma="ddl")
    t2 = kb.op("pool", lambda e: e.dma_start(io.gd_pref, sb.pref[:]), waits=(t_gate,), dma="ddl")
    t3 = kb.op("pool", lambda e: e.dma_start(io.gd_trow, sb.Trow), waits=(t_gate,), dma="ddl")
    return t3


def stage_counts_out(kb, io, sb, ps, t_gate, fused):
    tG = kb.op("dve", lambda e: e.tensor_reduce(
        sb.packA[0:1, 0:8], sb.Trow.rearrange("o (s e) -> o e s", e=8), AX.X, ALU.add),
        waits=(t_gate,), sig="dve")
    if not fused:
        return kb.op("pool", lambda e: e.dma_start(io.ccrow, sb.packA[0:1, 0:8]),
                     waits=(tG,), dma="mscp")
    tmr = kb.op("pe", lambda e: e.matmul(ps.B[0:8, 1024:1032], sb.ones_row_f[0:1, 0:8],
                                         sb.packA[0:1, 0:8], start=True, stop=True),
                waits=(tG,), sig="pe")
    tcc = kb.op("dve", lambda e: e.tensor_scalar(sb.CC[:], ps.B[0:8, 1024:1032],
                                                 sb.rowsel[:], None, ALU.mult),
                waits=(tmr,), sig="dve")
    tst = kb.op("pool", lambda e: e.dma_start(io.ccin_d, sb.CC[:]), waits=(tcc,), dma="mscp")
    tcol = kb.op("pool", lambda e: e.collective_compute(
        "AllReduce", ALU.add, replica_groups=[list(range(NCORES))],
        ins=[io.ccin_d.opt()], outs=[io.ccout_d.opt()]),
        waits=(tst,), sig="cc")
    return tcol


def stage_offsets(kb, io, sb, ps, t_cc, fused):
    cc_src = io.ccout_d if fused else io.cc
    tld = kb.op("sp", lambda e, cc_src=cc_src: e.dma_start(sb.CC[:], cc_src),
                waits=(t_cc,), dma="msc")
    tt1 = kb.op("pool", lambda e: e.dma_start(
        io.t64d.rearrange("p n -> (p n)").rearrange("(o x) -> o x", o=1), sb.Trow),
        waits=(t_cc,), dma="mscp")
    tt2 = kb.op("sp", lambda e: e.dma_start(sb.sc1[0:64, 16:24], io.t64d), waits=(tt1,), dma="msc")
    tt3 = kb.op("dve", lambda e: e.tensor_copy(sb.st[0:64, 0:8], sb.sc1[0:64, 16:24]),
                waits=(tt2,), sig="dve")
    tof = kb.op("pe", lambda e: e.matmul(ps.B[0:64, 1024:1032], sb.u128[0:64, 0:64],
                                         sb.st[0:64, 0:8], start=True, stop=True),
                waits=(tt3,), sig="pe")
    tofe = kb.op("dve", lambda e: e.tensor_copy(sb.sc1[0:64, 24:32], ps.B[0:64, 1024:1032]),
                 waits=(tof,), sig="dve")
    tco = kb.op("pe", lambda e: e.matmul(ps.B[0:1, 1536:1544], sb.rankmask[:], sb.CC[:],
                                         start=True, stop=True), waits=(tld,), sig="pe")
    tcoe = kb.op("dve", lambda e: e.tensor_copy(sb.packA[0:1, 8:16], ps.B[0:1, 1536:1544]),
                 waits=(tco,), sig="dve")
    tcor = kb.op("pe", lambda e: e.matmul(ps.B[0:64, 1032:1040], sb.ones_row_f[0:1, 0:64],
                                          sb.packA[0:1, 8:16], start=True, stop=True),
                 waits=(tcoe,), sig="pe")
    toff = kb.op("dve", lambda e: e.tensor_tensor(sb.sc1[0:64, 24:32], sb.sc1[0:64, 24:32],
                                                  ps.B[0:64, 1032:1040], ALU.add),
                 waits=(tcor,), hard=(tofe,), sig="dve")
    tst = kb.op("pool", lambda e: e.dma_start(
        io.offd.rearrange("o (p n) -> (o p) n", n=8)[0:64, :], sb.sc1[0:64, 24:32]),
        waits=(toff,), dma="mscp")
    trd = kb.op("sp", lambda e: e.dma_start(sb.OFFROW, io.offd), waits=(tst,), dma="msc")
    return trd


def stage_comb(kb, io, sb, ps, seq, t_off, t_wp_ready):
    t_last = None
    t_prev = None
    for tc in range(NCH):
        slot = seq * NCH + tc
        po = ps.B[:, 1032 + 8 * (tc % 2):1040 + 8 * (tc % 2)]
        trm = kb.op("pe", lambda e, slot=slot, po=po: e.matmul(
            po, sb.ones_row_f[:], sb.OFFROW[0:1, 8 * slot:8 * slot + 8],
            start=True, stop=True), waits=(t_off, t_prev), sig="pe")
        t1 = kb.op("dve", lambda e, slot=slot, po=po: e.tensor_tensor(
            sb.g8[:, 0:8], sb.pref[:, 8 * (slot % 64):8 * (slot % 64) + 8], po, ALU.add),
            waits=(trm, t_wp_ready), sig="dve")
        t2 = kb.op("dve", lambda e: e.tensor_scalar(sb.g8[:, 8:16], sb.g8[:, 0:8], CAP,
                                                    None, ALU.is_lt), hard=(t1,), sig="dve")
        t_last = kb.op("dve", lambda e, slot=slot: e.tensor_tensor(
            sb.comb[:, 8 * (slot % 64):8 * (slot % 64) + 8],
            sb.g8[:, 8:16], sb.wfull[:, 8 * (slot % 64):8 * (slot % 64) + 8], ALU.mult),
            hard=(t2,), sig="dve")
        t_prev = t_last
    return t_last



def stage_moe(kb, io, sb, ps, seq, t_px, t_comb, x_src, x_dst, want_ln, px_free_next,
              x_tok=None):
    t_store = None
    t_aff_all = []
    mcount = [0]
    t_matdone = {}
    y_done = None
    war = [("dve", kb.cnt["dve"])]

    def wreg():
        return 8192 * (mcount[0] % 2)

    def wload_img(img_ap):
        m = mcount[0]
        reg = 8192 * (m % 2)
        lane = "lw%d" % (m % 2)
        wtok = t_matdone.get(m - 2, px_free_next)
        t = kb.op("sp", lambda e, img_ap=img_ap, reg=reg: e.dma_start(
            sb.PK[:, reg:reg + 8192], img_ap), waits=(wtok, kb.tw_done), dma=lane)
        mcount[0] += 1
        return reg, t

    for half in range(2):
        toks = 1024 * half

        def gu_phase(wg_reg, wg_toks, n_ic, ic0, psum_off, dst_fn, extra_wait=None):
            ev = {}
            for i in range(n_ic):
                ic = ic0 + i
                P = ps.A if ic % 2 == 0 else ps.B
                for half2 in range(2):
                    pg = P[:, psum_off + 512 * half2:psum_off + 512 * (half2 + 1)]
                    for bc in range(8):
                        lhsT = sb.PK[:, wg_reg + 1024 * bc + 128 * (ic % 8):
                                     wg_reg + 1024 * bc + 128 * (ic % 8 + 1)]
                        w0 = (wg_toks, ev.get((ic % 2, half2)), t_px, extra_wait) \
                            if bc == 0 else ()
                        kb.op("pe", lambda e, pg=pg, lhsT=lhsT, bc=bc, half2=half2, toks=toks: e.matmul(
                            pg, lhsT,
                            sb.PX[:, 2048 * bc + toks + 512 * half2:2048 * bc + toks + 512 * (half2 + 1)],
                            start=(bc == 0), stop=(bc == 7)),
                            waits=w0, sig="pe" if bc == 7 else None)
                    tgm = ("pe", kb.cnt["pe"])
                    ev[(ic % 2, half2)] = dst_fn(ic, half2, pg, tgm)
            return ev

        def silu_dst(ic, half2, pg, tgm):
            return kb.op("act", lambda e, pg=pg, ic=ic, half2=half2: e.activation(
                sb.PQ[:, 8192 + 1024 * (ic % 8) + 512 * half2:8192 + 1024 * (ic % 8) + 512 * (half2 + 1)],
                pg, AF.Silu, bias=0.0, scale=1.0), waits=(tgm, kb.tw_done), sig="act")

        def hs_dst(ic, half2, pg, tgm, tsil):
            return kb.op("dve", lambda e, pg=pg, ic=ic, half2=half2: e.tensor_tensor(
                sb.PV[:, 1024 * ic + 512 * half2:1024 * ic + 512 * (half2 + 1)],
                sb.PQ[:, 8192 + 1024 * (ic % 8) + 512 * half2:8192 + 1024 * (ic % 8) + 512 * (half2 + 1)],
                pg, ALU.mult), waits=(tgm, tsil, kb.tw_done), sig="dve")

        def down_phase(wd_reg, wd_toks, n_ic, comb_e, t_hs_last, nblk1024):
            ev_d = {}
            t_last = None
            for tc in range(8):
                slot = seq * NCH + half * 8 + tc
                pos = tc % 4
                P = ps.A if pos % 2 == 0 else ps.B
                coff = 1024 * (pos // 2)
                for hh in range(2):
                    pd = P[:, coff + 512 * hh:coff + 512 * (hh + 1)]
                    for i in range(n_ic):
                        lw = wd_toks if not isinstance(wd_toks, list) else wd_toks[i]
                        lhsT = sb.PV[:, 1024 * i + 128 * tc:1024 * i + 128 * (tc + 1)]
                        w0 = (lw, ev_d.get((pos, hh)), t_hs_last) if i == 0 else (lw,)
                        roff = wd_reg(i) if callable(wd_reg) else wd_reg
                        kb.op("pe", lambda e, pd=pd, lhsT=lhsT, i=i, hh=hh, roff=roff: e.matmul(
                            pd, lhsT,
                            sb.PK[:, roff + 1024 * (i % 8) + 512 * hh:roff + 1024 * (i % 8) + 512 * (hh + 1)],
                            start=(i == 0), stop=(i == n_ic - 1)),
                            waits=w0, sig="pe" if i == n_ic - 1 else None)
                    tdm = ("pe", kb.cnt["pe"])
                    ysl = sb.POf[:, 1024 * tc + 512 * hh:1024 * tc + 512 * (hh + 1)]
                    if comb_e is not None and comb_e >= 0:
                        ccol = sb.comb[:, 8 * (slot % 64) + comb_e:8 * (slot % 64) + comb_e + 1]
                        tde = kb.op("dve", lambda e, pd=pd, ysl=ysl, ccol=ccol: e.scalar_tensor_tensor(
                            ysl, pd, ccol, ysl, ALU.mult, ALU.add), waits=(tdm,), hard=(t_comb,), sig="dve")
                    elif comb_e == -1:
                        ccol = sb.comb[:, 8 * (slot % 64):8 * (slot % 64) + 1]
                        yw = y_done if isinstance(y_done, tuple) and len(y_done) == 2 \
                            and not isinstance(y_done[0], str) else (y_done,)
                        tde = kb.op("dve", lambda e, pd=pd, ysl=ysl, ccol=ccol: e.tensor_scalar(
                            ysl, pd, ccol, None, ALU.mult),
                            waits=(tdm,) + tuple(yw), hard=(t_comb,), sig="dve")
                    else:
                        tde = kb.op("dve", lambda e, pd=pd, ysl=ysl: e.tensor_tensor(
                            ysl, ysl, pd, ALU.add), waits=(tdm,), sig="dve")
                    ev_d[(pos, hh)] = tde
                    t_last = tde
            return t_last

        for ee in range(8):
            gm = mcount[0]
            reg_g, wg = wload_img(io.tw["weg"][ee])
            ev_g = gu_phase(reg_g, wg, 8, 0, 0, silu_dst, extra_wait=war[0])
            t_matdone[gm] = ("pe", kb.cnt["pe"])
            um = mcount[0]
            reg_u, wu = wload_img(io.tw["weu"][ee])
            evh = gu_phase(reg_u, wu, 8, 0, 1024,
                           lambda ic, half2, pg, tgm: hs_dst(ic, half2, pg, tgm,
                                                             ev_g[(ic % 2, half2)]),
                           extra_wait=war[0])
            t_hs_all = ("dve", kb.cnt["dve"])
            t_matdone[um] = ("pe", kb.cnt["pe"])
            dm = mcount[0]
            reg_d, wd = wload_img(io.tw["wed"][ee])
            down_phase(reg_d, wd, 8, (-1 if ee == 0 else ee), t_hs_all, False)
            t_matdone[dm] = ("pe", kb.cnt["pe"])
            war[0] = ("dve", kb.cnt["dve"])

        t_hs_sh = []
        greg = {}
        for grp in range(2):
            gm = mcount[0]
            reg_g, wg = wload_img(io.tw["wsg"][grp])
            ev_g = gu_phase(reg_g, wg, 8, 8 * grp, 0, silu_dst, extra_wait=war[0])
            t_matdone[gm] = ("pe", kb.cnt["pe"])
            um = mcount[0]
            reg_u, wu = wload_img(io.tw["wsu"][grp])
            evh = gu_phase(reg_u, wu, 8, 8 * grp, 1024,
                           lambda ic, half2, pg, tgm: hs_dst(ic, half2, pg, tgm,
                                                             ev_g[(ic % 2, half2)]),
                           extra_wait=war[0])
            t_hs_sh.append(("dve", kb.cnt["dve"]))
            t_matdone[um] = ("pe", kb.cnt["pe"])
        dregs = []
        wd_last = None
        for grp in range(2):
            dm = mcount[0]
            reg_d, wd = wload_img(io.tw["wsd"][grp])
            dregs.append(reg_d)
            wd_last = wd
            if grp == 0:
                dm0 = dm
        t_down = down_phase(lambda i: dregs[i // 8], wd_last, 16, None,
                            t_hs_sh[-1], False)
        war[0] = ("dve", kb.cnt["dve"])
        t_matdone[dm0] = ("pe", kb.cnt["pe"])
        t_matdone[mcount[0] - 1] = ("pe", kb.cnt["pe"])
        y_done2 = t_down

        t_store, t_xln_h = pass_residual_ln(
            kb, io, sb, ps, seq, "y", want_ln, x_src, x_dst,
            chunks=range(half * 8, half * 8 + 8), y_tok=y_done2,
            px_free=px_free_next if half == 0 else None, x_tok=x_tok,
            pq_tok=t_aff_all[-1] if t_aff_all else None)
        if t_xln_h is not None:
            t_aff_all.append(t_xln_h)
        y_done = t_store
    t_xln = t_aff_all[-1] if t_aff_all else None
    return t_store, t_xln, ("pe", kb.cnt["pe"])



def build_program(phase):
    nc = bass.Bass("TRN2", target_bir_lowering=False)
    io = declare_io(nc, phase)
    with ExitStack() as ctx:
        kb = KB(nc)
        kb.make_sems(ctx)
        sb = alloc_sbuf(nc, ctx)
        ps = alloc_psum(nc, ctx)
        t_pw, t_pr = stage_preprocess(kb, io, sb, ps)
        pre_done = (("pe", kb.cnt["pe"]), ("dve", kb.cnt["dve"]), t_pw, t_pr)
        _phase_body(kb, io, sb, ps, phase, pre_done)
        kb.emit()
    return nc


def _phase_body(kb, io, sb, ps, phase, pre_done):
    fused = (phase == "all")

    def is_final(sl):
        return sl == UPTO

    def final_wait():
        kb.wait("sp", [(lane, kb.cnt[lane]) for lane in
                       ("xst0", "xst1", "xlnd", "ddl", "msc", "mscp", "pw", "pr", "cc")])

    px_free = pre_done[0]
    xtk = None
    jobs = image_jobs(io)
    kb.tw_done = None
    if True:
        stage_images(kb, io, sb, jobs, None)
    q = (len(jobs) + 3) // 4
    jgroups = [jobs[i * q:(i + 1) * q] for i in range(4)]

    if DEVSTOP == 1:
        final_wait()
        return

    if phase in ("p1", "all"):
        t_gate = None
        first = True
        for seq in range(SEQS):
            xtk_in = xtk
            tx, t_xln = pass_residual_ln(kb, io, sb, ps, seq, "init", True,
                                         io.x_in, io.xs,
                                         px_free=pre_done[1] if first else px_free,
                                         x_tok=xtk)
            xtk = tx
            if DEVSTOP == 2:
                final_wait()
                return
            t_px = load_px(kb, io, sb, seq, t_xln, px_free=px_free)
            t_dd, px_free = stage_ffn0(kb, io, sb, ps, t_px,
                                       wfree=pre_done[2] if first else px_free)
            first = False
            if DEVSTOP == 3:
                final_wait()
                return
            if UPTO == 1:
                tx, _ = pass_residual_ln(kb, io, sb, ps, seq, "delta", False,
                                         io.xs, io.out, dd_tok=t_dd, x_tok=xtk)
                xtk = tx
                continue
            tx, t_xln = pass_residual_ln(kb, io, sb, ps, seq, "delta", True,
                                         io.xs, io.xs, px_free=px_free, dd_tok=t_dd,
                                         x_tok=xtk)
            xtk = tx
            t_px = load_px(kb, io, sb, seq, t_xln, px_free=px_free)
            t_dd, px_free = stage_attn(kb, io, sb, ps, seq, 0, t_px, wfree=px_free,
                                       pv_tok=kb.tw_done)
            if UPTO == 2:
                tx, _ = pass_residual_ln(kb, io, sb, ps, seq, "l0inner", False,
                                         io.xs, io.out, dd_tok=t_dd, x_tok=xtk)
                xtk = tx
                continue
            tx, t_xln = pass_residual_ln(kb, io, sb, ps, seq, "l0inner", True,
                                         io.xs, io.xs, px_free=px_free, dd_tok=t_dd,
                                         x_tok=xtk)
            xtk = tx
            t_px = load_px(kb, io, sb, seq, t_xln, px_free=px_free)
            t_dd, px_free = stage_ffn0(kb, io, sb, ps, t_px, wfree=px_free)
            if UPTO == 3:
                tx, _ = pass_residual_ln(kb, io, sb, ps, seq, "delta", False,
                                         io.xs, io.out, dd_tok=t_dd, x_tok=xtk)
                xtk = tx
                continue
            tx, t_xln = pass_residual_ln(kb, io, sb, ps, seq, "delta", True,
                                         io.xs, io.xs, px_free=px_free, dd_tok=t_dd,
                                         x_tok=xtk)
            xtk = tx
            t_px = load_px(kb, io, sb, seq, t_xln, px_free=px_free)
            t_gate, px_free = stage_gate(kb, io, sb, ps, seq, t_px)
        if UPTO <= 3:
            final_wait()
            return
        t_gd = dump_gate(kb, io, sb, t_gate)
        t_cc = stage_counts_out(kb, io, sb, ps, t_gate, fused)
        if phase == "p1":
            final_wait()
            return
        t_p1_st = ("xlnd", kb.cnt["xlnd"])

    if phase in ("p2", "all"):
        if phase == "p2":
            t1 = kb.op("sp", lambda e: e.dma_start(sb.wfull[:], io.gdi_wfull), dma="msc")
            t2 = kb.op("sp", lambda e: e.dma_start(sb.pref[:], io.gdi_pref), dma="msc")
            t3 = kb.op("sp", lambda e: e.dma_start(sb.Trow, io.gdi_trow), dma="msc")
            t_wp = t3
            t_cc = t3
            xlsrc = io.xlnd_in
            xln_tok = None
            px_free = pre_done[0]
            xtk = None
        else:
            t_wp = None
            xlsrc = io.xlnd
            xln_tok = t_p1_st
        t_off = stage_offsets(kb, io, sb, ps, t_cc, fused)
        t_gate = None
        for seq in range(SEQS):
            t_comb = stage_comb(kb, io, sb, ps, seq, t_off, t_wp)
            t_px = load_px(kb, io, sb, seq, xln_tok if fused else t_wp,
                           px_free=px_free, src=xlsrc)
            xsrc0 = io.xs if fused else io.x_in
            if is_final(4):
                tx, _, px_free = stage_moe(kb, io, sb, ps, seq, t_px, t_comb,
                                           xsrc0, io.out, False, px_free, x_tok=xtk)
                xtk = tx
                continue
            tx, t_xln, px_free = stage_moe(kb, io, sb, ps, seq, t_px, t_comb,
                                           xsrc0, io.xs, True, px_free, x_tok=xtk)
            xtk = tx
            t_px = load_px(kb, io, sb, seq, t_xln, px_free=px_free)
            t_dd, px_free = stage_attn(kb, io, sb, ps, seq, 1, t_px, wfree=px_free, pv_tok=kb.tw_done)
            if is_final(5):
                tx, _ = pass_residual_ln(kb, io, sb, ps, seq, "delta", False,
                                         io.xs, io.out, dd_tok=t_dd, x_tok=xtk)
                xtk = tx
                continue
            tx, t_xln = pass_residual_ln(kb, io, sb, ps, seq, "delta", True,
                                         io.xs, io.xs, px_free=px_free, dd_tok=t_dd,
                                         x_tok=xtk)
            xtk = tx
            t_px = load_px(kb, io, sb, seq, t_xln, px_free=px_free)
            t_gate, px_free = stage_gate(kb, io, sb, ps, seq, t_px)
        if UPTO <= 5:
            final_wait()
            return
        t_gd = dump_gate(kb, io, sb, t_gate)
        t_cc = stage_counts_out(kb, io, sb, ps, t_gate, fused)
        if phase == "p2":
            final_wait()
            return
        t_p2_st = ("xlnd", kb.cnt["xlnd"])

    if phase == "p3":
        t1 = kb.op("sp", lambda e: e.dma_start(sb.wfull[:], io.gdi_wfull), dma="msc")
        t2 = kb.op("sp", lambda e: e.dma_start(sb.pref[:], io.gdi_pref), dma="msc")
        t3 = kb.op("sp", lambda e: e.dma_start(sb.Trow, io.gdi_trow), dma="msc")
        t_wp = t3
        t_cc = t3
        xlsrc = io.xlnd_in
        xsrc = io.x_in
        xln_tok = None
        px_free = pre_done[0]
        xtk = None
    else:
        t_wp = None
        xlsrc = io.xlnd
        xsrc = io.xs
        xln_tok = t_p2_st
    t_off = stage_offsets(kb, io, sb, ps, t_cc, fused)
    for seq in range(SEQS):
        t_comb = stage_comb(kb, io, sb, ps, seq, t_off, t_wp)
        t_px = load_px(kb, io, sb, seq, xln_tok if fused else t_wp,
                       px_free=px_free, src=xlsrc)
        tx, _, px_free = stage_moe(kb, io, sb, ps, seq, t_px, t_comb,
                                   xsrc, io.out, False, px_free, x_tok=xtk)
        xtk = tx
    final_wait()



def host_consts(core):
    k = np.arange(128)[:, None]
    m = np.arange(128)[None, :]
    return {
        "u128": (k < m).astype(ml_dtypes.bfloat16),
        "cmask": np.where(k <= m, 0.0, -BIG).astype(ml_dtypes.bfloat16),
        "rankmask": (np.arange(8) < core).astype(np.float32)[:, None],
        "rowsel": (np.arange(8) == core).astype(np.float32)[:, None],
    }


def kernel(**inputs):
    x_full = np.ascontiguousarray(np.asarray(inputs["L0_attn_h"], dtype=np.float32))
    shards = [np.ascontiguousarray(x_full[4 * c:4 * (c + 1)].reshape(TOK, D))
              for c in range(NCORES)]
    weights = {nm: np.ascontiguousarray(np.asarray(inputs[nm], dtype=np.float32))
               for nm, _ in W_NAMES if nm != "L0_attn_h"}
    weights.pop("L0_attn_h", None)

    def maps(extra_per_core):
        out = []
        for c in range(NCORES):
            m = dict(weights)
            m.update(host_consts(c))
            m.update(extra_per_core[c])
            out.append(m)
        return out

    if FUSED:
        nc = build_program("all")
        in_maps = maps([{"x": shards[c]} for c in range(NCORES)])
        res = run_bass_kernel_spmd(nc, in_maps, core_ids=list(range(NCORES)))
        outs = [np.asarray(r["out"]) for r in res.results]
    else:
        nc1 = build_program("p1")
        in_maps = maps([{"x": shards[c]} for c in range(NCORES)])
        r1 = run_bass_kernel_spmd(nc1, in_maps, core_ids=list(range(NCORES)))
        if UPTO <= 3:
            outs = [np.asarray(r["out"]) for r in r1.results]
        else:
            cc = np.concatenate([np.asarray(r["ccrow"]) for r in r1.results], axis=0)
            nc2 = build_program("p2")
            in_maps = maps([{
                "xs_in": np.asarray(r1.results[c]["xs_out"]),
                "xlnd_in": np.asarray(r1.results[c]["xlnd_out"]),
                "gdi_wfull": np.asarray(r1.results[c]["gdo_wfull"]),
                "gdi_pref": np.asarray(r1.results[c]["gdo_pref"]),
                "gdi_trow": np.asarray(r1.results[c]["gdo_trow"]),
                "cc": cc,
            } for c in range(NCORES)])
            r2 = run_bass_kernel_spmd(nc2, in_maps, core_ids=list(range(NCORES)))
            if UPTO <= 5:
                outs = [np.asarray(r["out"]) for r in r2.results]
            else:
                cc2 = np.concatenate([np.asarray(r["ccrow"]) for r in r2.results], axis=0)
                nc3 = build_program("p3")
                in_maps = maps([{
                    "xs_in": np.asarray(r2.results[c]["xs_out"]),
                    "xlnd_in": np.asarray(r2.results[c]["xlnd_out"]),
                    "gdi_wfull": np.asarray(r2.results[c]["gdo_wfull"]),
                    "gdi_pref": np.asarray(r2.results[c]["gdo_pref"]),
                    "gdi_trow": np.asarray(r2.results[c]["gdo_trow"]),
                    "cc": cc2,
                } for c in range(NCORES)])
                r3 = run_bass_kernel_spmd(nc3, in_maps, core_ids=list(range(NCORES)))
                outs = [np.asarray(r["out"]) for r in r3.results]

    full = np.concatenate([o.reshape(4, S, D) for o in outs], axis=0)
    return full.astype(np.float32)


# version 24
# speedup vs baseline: 1.1063x; 1.0002x over previous
import numpy as np
import ml_dtypes
import concourse.bass as bass
import concourse.mybir as mybir
from concourse.bass_utils import run_bass_kernel_spmd
from contextlib import ExitStack

F32 = mybir.dt.float32
BF16 = mybir.dt.bfloat16
AF = mybir.ActivationFunctionType
ALU = mybir.AluOpType
AX = mybir.AxisListType

NCORES = 8
SEQS = 4
S = 2048
D = 1024
NCH = 16
TOK = SEQS * S
CAP = 20480.0
LN_EPS = 1e-5
BIG = 1.0e30
MAGIC = 1.5 * 2.0 ** 23

UPTO = 6
DEVSTOP = 0
FUSED = True

ENG2BLK = {"pe": "tensor", "act": "scalar", "dve": "vector", "pool": "gpsimd", "sp": "sync"}


class KB:
    def __init__(self, nc):
        self.nc = nc
        self.ops = {e: [] for e in ENG2BLK}
        self.cnt = {}
        self.sems = {}

    LANES = ("xc0", "xc1", "px", "lw0", "lw1", "qw0", "qw1", "kq",
             "xst0", "xst1", "xlnd", "ddl", "msc", "mscp", "pw", "pr", "cc", "tw")

    def make_sems(self, ctx):
        for name in ("pe", "act", "dve", "pool", "sp") + KB.LANES:
            self.sems[name] = ctx.enter_context(self.nc.semaphore("sem_" + name))
            self.cnt[name] = 0

    def op(self, eng, fn, waits=(), hard=(), sig=None, dma=None):
        incs = []
        tok = None
        if eng in ("dve", "act", "pool") and dma is None:
            hard = tuple(hard) + ((eng, self.cnt[eng]),)
            if sig is None:
                sig = eng
        if sig is not None:
            self.cnt[sig] += 1
            incs.append((sig, 1))
            tok = (sig, self.cnt[sig])
        if dma is not None:
            self.cnt[dma] += 16
            incs.append((dma, 16))
            tok = (dma, self.cnt[dma])
        ws = tuple(w for w in waits if w is not None)
        hs = tuple(h for h in hard if h is not None)
        for t in ws + hs:
            assert t[1] <= self.cnt[t[0]], f"wait on future token {t} (cnt={self.cnt[t[0]]})"
        self.ops[eng].append((ws, hs, fn, incs))
        return tok

    def wait(self, eng, waits):
        ws = tuple(w for w in waits if w is not None)
        for t in ws:
            assert t[1] <= self.cnt[t[0]], f"wait on future token {t}"
        self.ops[eng].append((ws, (), None, []))

    def emit(self):
        with self.nc.Block() as blk:
            for eng in ENG2BLK:
                ops = self.ops[eng]
                own = eng

                def body(e, ops=ops, own=own):
                    waited = {}
                    own_cnt = 0
                    for ws, hs, fn, incs in ops:
                        need = {}
                        for s, c in ws:
                            if s == own and c <= own_cnt:
                                continue
                            if need.get(s, 0) < c:
                                need[s] = c
                        for s, c in hs:
                            if need.get(s, 0) < c:
                                need[s] = c
                        for s, c in need.items():
                            if waited.get(s, 0) < c:
                                e.wait_ge(self.sems[s], c)
                                waited[s] = c
                        if fn is not None:
                            instr = fn(e)
                            for s, n in incs:
                                instr.then_inc(self.sems[s], n)
                            own_cnt += sum(1 for s, n in incs if s == own)

                getattr(blk, ENG2BLK[eng])(body)


class NS:
    pass


W_NAMES = [
    ("L0_attn_Wq", [D, D]), ("L0_attn_Wk", [D, D]), ("L0_attn_Wv", [D, D]),
    ("L0_attn_Wo", [D, D]), ("L0_attn_gamma", [D]), ("L0_attn_beta", [D]),
    ("L0_ffn_kernel", [D, D]), ("L0_ffn_bias", [D]), ("L0_ffn_a", [D, 16]),
    ("L0_ffn_b", [16, D]), ("L1_attn_in_proj_w", [3 * D, D]),
    ("L1_attn_in_proj_b", [3 * D]), ("L1_attn_out_proj_w", [D, D]),
    ("L1_attn_out_proj_b", [D]), ("L1_ffn_gate_w", [8, D]),
    ("L1_ffn_we_gate", [8, D, D]), ("L1_ffn_we_up", [8, D, D]),
    ("L1_ffn_we_down", [8, D, D]), ("L1_ffn_ws_gate", [2 * D, D]),
    ("L1_ffn_ws_up", [2 * D, D]), ("L1_ffn_ws_down", [D, 2 * D]),
]

SC_NAMES = [
    ("q", [D, D]), ("k", [D, D]), ("v", [D, D]), ("o", [D, D]),
    ("kq", [D, D]), ("ip", [3 * D, D]), ("op", [D, D]),
    ("weg", [8, D, D]), ("weu", [8, D, D]), ("wed", [8, D, D]),
    ("wsg", [2 * D, D]), ("wsu", [2 * D, D]), ("wsd", [D, 2 * D]),
    ("gw", [16, D]),
    ("bv", [16, 4 * D]),
]


def declare_io(nc, phase):
    io = NS()
    din = lambda name, shape, dt=F32: nc.dram_tensor(name, shape, dt, kind="ExternalInput").ap()
    dout = lambda name, shape, dt=F32: nc.dram_tensor(name, shape, dt, kind="ExternalOutput").ap()
    dint = lambda name, shape, dt=BF16: nc.dram_tensor(name, shape, dt).ap()

    io.w = {nm: din(nm, shp) for nm, shp in W_NAMES}
    io.u128 = din("u128", [128, 128], BF16)
    io.cmask = din("cmask", [128, 128], BF16)
    io.rankmask = din("rankmask", [8, 1])
    io.rowsel = din("rowsel", [8, 1])

    last = (phase == "all") or (phase == "p1" and UPTO <= 3) or \
           (phase == "p2" and UPTO <= 5) or (phase == "p3")
    io.x_in = din("x", [TOK, D]) if phase in ("all", "p1") else din("xs_in", [TOK, D])
    io.out = dout("out", [TOK, D])
    io.xs = dint("xs", [TOK, D], F32) if (phase == "all" or last) else dout("xs_out", [TOK, D], F32)
    if phase == "all":
        io.xlnd = dint("xlnd", [TOK, D], BF16)
        io.xlnd_in = io.xlnd
    else:
        io.xlnd = io_xlnd_out = (dint("xlnd", [TOK, D], BF16) if last
                                 else dout("xlnd_out", [TOK, D], BF16))
        io.xlnd_in = din("xlnd_in", [TOK, D], BF16) if phase in ("p2", "p3") else io.xlnd
    if phase == "all":
        io.gd_wfull = dint("gd_wfull", [128, 512], F32)
        io.gd_pref = dint("gd_pref", [128, 512], F32)
        io.gd_trow = dint("gd_trow", [1, 512], F32)
        io.ccrow = None
        io.gdi_wfull = io.gd_wfull
        io.gdi_pref = io.gd_pref
        io.gdi_trow = io.gd_trow
        io.cc = None
    else:
        mkout = phase in ("p1", "p2") and not last
        f = dout if mkout else dint
        io.gd_wfull = f("gdo_wfull" if mkout else "gd_wfull", [128, 512], F32)
        io.gd_pref = f("gdo_pref" if mkout else "gd_pref", [128, 512], F32)
        io.gd_trow = f("gdo_trow" if mkout else "gd_trow", [1, 512], F32)
        io.ccrow = dout("ccrow", [1, 8], F32) if mkout else None
        if phase in ("p2", "p3"):
            io.gdi_wfull = din("gdi_wfull", [128, 512], F32)
            io.gdi_pref = din("gdi_pref", [128, 512], F32)
            io.gdi_trow = din("gdi_trow", [1, 512], F32)
            io.cc = din("cc", [8, 8], F32)
        else:
            io.gdi_wfull = io.gdi_pref = io.gdi_trow = io.cc = None

    io.sc = {nm: dint("sc_" + nm, shp, BF16) for nm, shp in SC_NAMES}
    io.tw = {}
    for nm, shp in [("ipq", [128, 8192]), ("ipk", [128, 8192]), ("ipv", [128, 8192]),
                    ("op", [128, 8192]), ("weg", [8, 128, 8192]), ("weu", [8, 128, 8192]),
                    ("wed", [8, 128, 8192]), ("wsg", [2, 128, 8192]),
                    ("wsu", [2, 128, 8192]), ("wsd", [2, 128, 8192])]:
        io.tw[nm] = dint("tw_" + nm, shp, BF16)
    io.dd = dint("dd", [D, S], BF16)
    io.bounce = dint("bounce", [1, 1024], F32)
    io.bounce2 = dint("bounce2", [1024, 1], F32)
    io.t64d = dint("t64d", [64, 8], F32)
    io.offd = dint("offd", [1, 512], F32)
    io.ccin_d = dint("ccin_d", [8, 8], F32)
    io.ccout_d = dint("ccout_d", [8, 8], F32)
    return io


def alloc_sbuf(nc, ctx):
    sb = NS()
    t = lambda n, s, d=BF16: ctx.enter_context(nc.sbuf_tensor(n, s, d))
    sb.PX = t("PX", [128, 16384])
    sb.PQ = t("PQ", [128, 16384])
    sb.PK = t("PK", [128, 16384])
    sb.PV = t("PV", [128, 16640])
    sb.PO = t("PO", [128, 16384])
    sb.POf = sb.PO[:].bitcast(F32)
    sb.xc = t("xc", [128, 2048], F32)
    sb.dtb = t("dtb", [128, 2048], BF16)
    sb.st = t("stg", [128, 4096], BF16)
    sb.tt = sb.st[:, 0:2048].bitcast(F32)
    sb.bn6 = t("bn6", [128, 12], F32)
    sb.stats = t("stats", [128, 32], F32)
    sb.mu16 = t("mu16", [128, 16], F32)
    sb.rstd16 = t("rstd16", [128, 16], F32)
    sb.sc1 = t("sc1", [128, 64], F32)
    sb.packA = t("packA", [128, 2048], F32)
    sb.row = sb.packA[0:1, 0:1024]
    sb.Trow = sb.packA[0:1, 1024:1536]
    sb.OFFROW = sb.packA[0:1, 1536:2048]
    sb.vb_row = t("vb_row", [1, 1024], BF16)
    sb.biasf = t("biasf", [128, 32], F32)
    sb.rowb = t("rowbufb", [1, 512], BF16)
    sb.u128 = t("u128c", [128, 128], BF16)
    sb.cmask = t("cmaskc", [128, 128], BF16)
    sb.rankmask = t("rankmaskc", [8, 1], F32)
    sb.rowsel = t("rowselc", [8, 1], F32)
    sb.ones_row_b = t("ones_row_b", [1, 128], BF16)
    sb.ones512_b = t("ones512_b", [1, 512], BF16)
    sb.ones_row_f = t("ones_row_f", [1, 128], F32)
    sb.ones_col_b = t("ones_col_b", [128, 1], BF16)
    sb.gamma_rep = t("gamma_rep", [128, D], BF16)
    sb.beta_rep = t("beta_rep", [128, D], BF16)
    sb.gwT = t("gwT", [128, 128], BF16)
    sb.quant = t("quant", [128, 8], F32)
    sb.rrep = t("rrep", [128, 2], F32)
    sb.wfull = t("wfull", [128, 512], F32)
    sb.pref = t("pref", [128, 512], F32)
    sb.comb = t("comb", [128, 512], F32)
    sb.CC = t("CCt", [8, 8], F32)
    sb.g8 = t("g8", [128, 64], F32)
    sb.selb = t("selb", [128, 32], BF16)
    sb.cb = t("cb", [128, 4], F32)
    return sb


def alloc_psum(nc, ctx):
    ps = NS()
    ps.A = ctx.enter_context(nc.psum_tensor("PSA", [128, 2048], F32))
    ps.B = ctx.enter_context(nc.psum_tensor("PSB", [128, 2048], F32))
    return ps



def stage_preprocess(kb, io, sb, ps):
    row1 = lambda ap: ap.rearrange("(o n) -> o n", o=1)
    for dst, src in [(sb.u128, io.u128), (sb.cmask, io.cmask),
                     (sb.rankmask, io.rankmask), (sb.rowsel, io.rowsel)]:
        kb.op("sp", lambda e, d=dst, s=src: e.dma_start(d[:], s), dma="msc")
    t_mp = kb.op("pool", lambda e: e.dma_start(
        sb.vb_row[:], io.w["L1_attn_in_proj_b"].rearrange("(o n) -> o n", o=1)[0:1, 2 * D:3 * D]),
        dma="mscp")
    t_g1 = kb.op("sp", lambda e: e.dma_start(sb.packA[0:1, 0:D], row1(io.w["L0_attn_gamma"])), dma="msc")
    kb.op("dve", lambda e: e.memset(sb.ones_row_b[:], 1.0), sig="dve")
    kb.op("dve", lambda e: e.memset(sb.ones512_b[:], 1.0), sig="dve")
    kb.op("dve", lambda e: e.memset(sb.ones_row_f[:], 1.0), sig="dve")
    t_ones = kb.op("dve", lambda e: e.memset(sb.ones_col_b[:], 1.0), sig="dve")
    tg_ev = []
    for j in range(2):
        tm = kb.op("pe", lambda e, j=j: e.matmul(
            ps.A[:, 512 * j:512 * (j + 1)], sb.ones_row_f[:],
            sb.packA[0:1, 512 * j:512 * (j + 1)], start=True, stop=True),
            waits=(t_g1, t_ones), sig="pe")
        tg_ev.append(kb.op("dve", lambda e, j=j: e.tensor_copy(
            sb.gamma_rep[:, 512 * j:512 * (j + 1)], ps.A[:, 512 * j:512 * (j + 1)]),
            waits=(tm,), sig="dve"))
    t_b1 = kb.op("sp", lambda e: e.dma_start(sb.packA[0:1, 0:D], row1(io.w["L0_attn_beta"])),
                 waits=(tg_ev[0], tg_ev[1]), dma="msc")
    for j in range(2):
        tm = kb.op("pe", lambda e, j=j: e.matmul(
            ps.A[:, 1024 + 512 * j:1024 + 512 * (j + 1)], sb.ones_row_f[:],
            sb.packA[0:1, 512 * j:512 * (j + 1)], start=True, stop=True),
            waits=(t_b1,), sig="pe")
        kb.op("dve", lambda e, j=j: e.tensor_copy(
            sb.beta_rep[:, 512 * j:512 * (j + 1)], ps.A[:, 1024 + 512 * j:1024 + 512 * (j + 1)]),
            waits=(tm,), sig="dve")

    cast_jobs = []

    def cast_copy(w_ap, sc_ap):
        rows = w_ap.shape[-2]
        for r0 in range(0, rows, 128):
            cast_jobs.append((w_ap, sc_ap, r0))

    for nm, src in [("q", io.w["L0_attn_Wq"]), ("k", io.w["L0_attn_Wk"]),
                    ("v", io.w["L0_attn_Wv"]), ("o", io.w["L0_attn_Wo"]),
                    ("op", io.w["L1_attn_out_proj_w"]), ("ip", io.w["L1_attn_in_proj_w"]),
                    ("wsg", io.w["L1_ffn_ws_gate"]), ("wsu", io.w["L1_ffn_ws_up"]),
                    ("wsd", io.w["L1_ffn_ws_down"])]:
        cast_copy(src, io.sc[nm])
    for nm, src in [("weg", io.w["L1_ffn_we_gate"]), ("weu", io.w["L1_ffn_we_up"]),
                    ("wed", io.w["L1_ffn_we_down"])]:
        for ee in range(8):
            cast_copy(src[ee], io.sc[nm][ee])
    t_r = ("dve", kb.cnt["dve"])
    for b0 in range(0, len(cast_jobs), 8):
        batch = cast_jobs[b0:b0 + 8]
        t_w = None
        for i, (w_ap, sc_ap, r0) in enumerate(batch):
            cols = w_ap.shape[-1]
            t_w = kb.op("pool", lambda e, w_ap=w_ap, r0=r0, c0=2048 * i, cols=cols: e.dma_start(
                sb.PO[:, c0:c0 + cols], w_ap[r0:r0 + 128, :]), waits=(t_r,), dma="pw")
        for i, (w_ap, sc_ap, r0) in enumerate(batch):
            cols = w_ap.shape[-1]
            t_r = kb.op("sp", lambda e, sc_ap=sc_ap, r0=r0, c0=2048 * i, cols=cols: e.dma_start(
                sc_ap[r0:r0 + 128, :], sb.PO[:, c0:c0 + cols]), waits=(t_w,), dma="pr")
    tgwz = kb.op("dve", lambda e: e.memset(sb.PO[0:16, 0:D], 0.0), waits=(t_r,), sig="dve")
    tgwc = kb.op("pool", lambda e: e.dma_start(sb.PO[0:8, 0:D], io.w["L1_ffn_gate_w"]),
                 waits=(tgwz, t_r, t_mp), dma="mscp")
    tgws = kb.op("sp", lambda e: e.dma_start(io.sc["gw"], sb.PO[0:16, 0:D]),
                 waits=(tgwc,), dma="msc")
    tgwT = None
    for dc in range(8):
        tgwT = kb.op("sp", lambda e, dc=dc: e.dma_start(
            sb.gwT[:, 16 * dc:16 * (dc + 1)], io.sc["gw"][:, 128 * dc:128 * (dc + 1)],
            transpose=True), waits=(tgws,), dma="msc")

    POf = sb.POf
    tload = None
    for dc in range(8):
        tload = kb.op("sp", lambda e, dc=dc: e.dma_start(
            POf[:, 1024 * dc:1024 * (dc + 1)], io.w["L0_ffn_kernel"][128 * dc:128 * (dc + 1), :]),
            dma="pr")
    tred = []
    for dc in range(8):
        tred.append(kb.op("dve", lambda e, dc=dc: e.reduce_max(
            sb.quant[:, dc:dc + 1], POf[:, 1024 * dc:1024 * (dc + 1)],
            axis=AX.X, apply_absolute_value=True), waits=(tload,), sig="dve"))
    tm = kb.op("dve", lambda e: e.reduce_max(sb.sc1[:, 0:1], sb.quant[:, 0:8], axis=AX.X),
               hard=(tred[-1],), sig="dve")
    tb = kb.op("pool", lambda e: e.dma_start(io.bounce2[0:128, :], sb.sc1[:, 0:1]),
               waits=(tm, t_mp), dma="mscp")
    tb2 = kb.op("sp", lambda e: e.dma_start(
        sb.packA[0:1, 0:128], io.bounce2.rearrange("(o p) n -> o (p n)", o=1)[0:1, 0:128]),
        waits=(tb,), dma="msc")
    tmax = kb.op("dve", lambda e: e.reduce_max(sb.packA[0:1, 128:129], sb.packA[0:1, 0:128], axis=AX.X),
                 waits=(tb2,), sig="dve")
    tsc = kb.op("dve", lambda e: e.tensor_scalar(sb.packA[0:1, 129:130], sb.packA[0:1, 128:129],
                                                 float(1.0 / 7.0), None, ALU.mult),
                hard=(tmax,), sig="dve")
    trc = kb.op("dve", lambda e: e.reciprocal(sb.packA[0:1, 130:131], sb.packA[0:1, 129:130]),
                hard=(tsc,), sig="dve")
    tb3 = kb.op("pool", lambda e: e.dma_start(io.bounce[0:1, 0:2], sb.packA[0:1, 129:131]),
                waits=(trc,), dma="mscp")
    tb4 = kb.op("sp", lambda e: e.dma_start(sb.packA[0:1, 132:134], io.bounce[0:1, 0:2]),
                waits=(tb3,), dma="msc")
    tmr = kb.op("pe", lambda e: e.matmul(ps.B[:, 0:2], sb.ones_row_f[:], sb.packA[0:1, 132:134],
                                         start=True, stop=True), waits=(tb4,), sig="pe")
    trr = kb.op("dve", lambda e: e.tensor_copy(sb.rrep[:, 0:2], ps.B[:, 0:2]),
                waits=(tmr,), sig="dve")
    t_qst = []
    for dc in range(8):
        stg = sb.st[:, 0:1024] if dc % 2 == 0 else sb.st[:, 1024:2048]
        kb.op("dve", lambda e, dc=dc: e.tensor_scalar(
            POf[:, 1024 * dc:1024 * (dc + 1)], POf[:, 1024 * dc:1024 * (dc + 1)],
            sb.rrep[:, 1:2], MAGIC, ALU.mult, ALU.add), hard=(trr,), sig="dve")
        t2 = kb.op("dve", lambda e, dc=dc: e.tensor_scalar(
            POf[:, 1024 * dc:1024 * (dc + 1)], POf[:, 1024 * dc:1024 * (dc + 1)],
            -MAGIC, None, ALU.add), sig="dve")
        t3 = kb.op("dve", lambda e, dc=dc, stg=stg: e.tensor_scalar(
            stg, POf[:, 1024 * dc:1024 * (dc + 1)], sb.rrep[:, 0:1], None, ALU.mult),
            waits=(t_qst[dc - 2] if dc >= 2 else None,), hard=(t2,), sig="dve")
        t_qst.append(kb.op("pool", lambda e, dc=dc, stg=stg: e.dma_start(
            io.sc["kq"][128 * dc:128 * (dc + 1), :], stg),
            waits=(t3, t_qst[dc - 1] if dc >= 1 else None), dma="pw"))
    tbz = kb.op("dve", lambda e: e.memset(sb.PO[0:16, 4096:8192], 0.0), waits=(t_r,), sig="dve")
    tb0 = kb.op("pool", lambda e: e.dma_start(
        sb.PO[0:1, 4096:5120], row1(io.w["L0_ffn_bias"])), waits=(tbz, t_mp), dma="mscp")
    tb1x = kb.op("pool", lambda e: e.dma_start(
        sb.PO[0:1, 5120:7168],
        io.w["L1_attn_in_proj_b"].rearrange("(o n) -> o n", o=1)[0:1, 0:2 * D]),
        waits=(tb0,), dma="mscp")
    tb2x = kb.op("pool", lambda e: e.dma_start(
        sb.PO[0:1, 7168:8192], row1(io.w["L1_attn_out_proj_b"])), waits=(tb1x,), dma="mscp")
    tbs = kb.op("sp", lambda e: e.dma_start(io.sc["bv"], sb.PO[0:16, 4096:8192]),
                waits=(tb0, tb1x, tb2x), dma="msc")
    tbt = None
    for ch in range(32):
        tbt = kb.op("sp", lambda e, ch=ch: e.dma_start(
            sb.st[:, 16 * ch:16 * (ch + 1)], io.sc["bv"][:, 128 * ch:128 * (ch + 1)],
            transpose=True), waits=(tbs,), dma="msc")
    kb.op("dve", lambda e: e.tensor_copy(
        sb.biasf[:], sb.st[:].rearrange("p (c r) -> p c r", r=16)[:, 0:32, 0:1]
        .rearrange("p c r -> p (c r)")), waits=(tbt,), sig="dve")

    kb.t_pr_all = ("pr", kb.cnt["pr"])
    return t_qst[-1], ("pr", kb.cnt["pr"])


def image_jobs(io):
    def b_io(ooff):
        return [(1024 * dc + 512 * oh, ooff + 512 * oh, 512, 128 * dc)
                for dc in range(8) for oh in range(2)]
    jobs = []
    for nm in ("weg", "weu"):
        for ee in range(8):
            jobs.append((io.tw[nm][ee],
                         [(1024 * bc, 0, 1024, 128 * bc) for bc in range(8)],
                         io.sc[nm][ee]))
    for ee in range(8):
        jobs.append((io.tw["wed"][ee],
                     [(1024 * ic + 512 * hh, 512 * hh, 512, 128 * ic)
                      for ic in range(8) for hh in range(2)], io.sc["wed"][ee]))
    for grp in range(2):
        jobs.append((io.tw["wsg"][grp],
                     [(1024 * bc, 1024 * grp, 1024, 128 * bc) for bc in range(8)], io.sc["wsg"]))
        jobs.append((io.tw["wsu"][grp],
                     [(1024 * bc, 1024 * grp, 1024, 128 * bc) for bc in range(8)], io.sc["wsu"]))
        jobs.append((io.tw["wsd"][grp],
                     [(1024 * ic + 512 * hh, 512 * hh, 512, 128 * (8 * grp + ic))
                      for ic in range(8) for hh in range(2)], io.sc["wsd"]))
    for nm, ooff in (("ipq", 0), ("ipk", D), ("ipv", 2 * D)):
        jobs.append((io.tw[nm], b_io(ooff), io.sc["ip"]))
    jobs.append((io.tw["op"], b_io(0), io.sc["op"]))
    return jobs


def stage_images(kb, io, sb, jobs, after_tok):
    slots = (sb.PV[:, 0:8192], sb.PV[:, 8448:16640])
    t_store = [kb.tw_done, kb.tw_done]
    for i, (dst, blocks, src) in enumerate(jobs):
        stg = slots[i % 2]
        t_tr = None
        for (doff, r0, rn, c0) in blocks:
            t_tr = kb.op("act", lambda e, stg=stg, doff=doff, r0=r0, rn=rn, c0=c0, src=src: e.dma_start(
                stg[:, doff:doff + rn], src[r0:r0 + rn, c0:c0 + 128], transpose=True),
                waits=(t_store[i % 2], kb.t_pr_all, after_tok), dma="tw")
        t_store[i % 2] = kb.op("act", lambda e, dst=dst, stg=stg: e.dma_start(dst, stg),
                               waits=(t_tr,), dma="tw")
    kb.tw_done = t_store[(len(jobs) - 1) % 2]
    kb.tw_done = (kb.tw_done[0], max(kb.cnt["tw"], kb.tw_done[1]))
    return kb.tw_done



def pass_residual_ln(kb, io, sb, ps, seq, mode, want_ln, x_src, x_dst,
                     scale_delta=1.0, px_free=None, chunks=None, y_tok=None,
                     dd_tok=None, x_tok=None, pq_tok=None):
    if chunks is None:
        chunks = list(range(NCH))
    else:
        chunks = list(chunks)
    t_aff = None
    last_store = None
    st_tok = {}
    prev_v = {}
    batch_tok = [pq_tok]
    x_tok = tuple(x_tok) if isinstance(x_tok, (tuple, list)) else (x_tok,)
    for c in chunks:
        xcs = sb.xc[:, 1024 * (c % 2):1024 * (c % 2 + 1)]
        row0 = seq * S + 128 * c
        lane = "xc%d" % (c % 2)
        pv = prev_v.get(c % 2, ())
        tld = kb.op("sp", lambda e, xcs=xcs, row0=row0, x_src=x_src: e.dma_start(
            xcs, x_src[row0:row0 + 128, :]), waits=pv + x_tok, dma=lane)
        tv = tld
        if mode == "delta":
            dtb = sb.dtb[:, 1024 * (c % 2):1024 * (c % 2 + 1)]
            tdt = kb.op("sp", lambda e, dtb=dtb, c=c: e.dma_start(
                dtb, io.dd[:, 128 * c:128 * (c + 1)], transpose=True),
                waits=pv + (dd_tok,), dma=lane)
            tv = kb.op("dve", lambda e, xcs=xcs, dtb=dtb: e.scalar_tensor_tensor(
                xcs, dtb, scale_delta, xcs, ALU.mult, ALU.add), waits=(tld, tdt), sig="dve")
        elif mode == "y":
            ysl = sb.POf[:, 1024 * (c % 8):1024 * (c % 8 + 1)]
            tv = kb.op("dve", lambda e, xcs=xcs, ysl=ysl: e.scalar_tensor_tensor(
                xcs, ysl, 0.5, xcs, ALU.mult, ALU.add), waits=(tld, y_tok), sig="dve")
        elif mode == "l0inner":
            dtb = sb.dtb[:, 1024 * (c % 2):1024 * (c % 2 + 1)]
            tdt = kb.op("sp", lambda e, dtb=dtb, c=c: e.dma_start(
                dtb, io.dd[:, 128 * c:128 * (c + 1)], transpose=True),
                waits=pv + (dd_tok,), dma=lane)
            tt = sb.tt[:, 0:1024]
            kb.op("dve", lambda e, xcs=xcs, tt=tt, c=c: e.tensor_scalar(
                tt, xcs, sb.stats[:, 2 * c:2 * c + 1], sb.rstd16[:, c:c + 1],
                ALU.subtract, ALU.mult), waits=(tld,), sig="dve")
            kb.op("dve", lambda e, tt=tt, dtb=dtb: e.tensor_tensor(
                tt, tt, dtb, ALU.add), waits=(tdt,), sig="dve")
            kb.op("dve", lambda e, tt=tt: e.bn_stats(sb.bn6[:, 0:6], tt[:, 0:512]), sig="dve")
            tb1 = kb.op("dve", lambda e, tt=tt: e.bn_stats(sb.bn6[:, 6:12], tt[:, 512:1024]), sig="dve")
            tb2 = kb.op("dve", lambda e: e.bn_aggr(sb.sc1[:, 2:4], sb.bn6[:]),
                        hard=(tb1,), sig="dve")
            tv1 = kb.op("dve", lambda e: e.tensor_scalar(
                sb.sc1[:, 4:5], sb.sc1[:, 3:4], LN_EPS, None, ALU.add), hard=(tb2,), sig="dve")
            tl1 = kb.op("act", lambda e: e.activation(sb.sc1[:, 5:6], sb.sc1[:, 4:5], AF.Ln,
                                                      bias=0.0, scale=1.0), waits=(tv1,), sig="act")
            tl2 = kb.op("act", lambda e: e.activation(sb.sc1[:, 6:7], sb.sc1[:, 5:6], AF.Exp,
                                                      bias=0.0, scale=-0.5), hard=(tl1,), sig="act")
            kb.op("dve", lambda e, tt=tt: e.tensor_scalar(
                tt, tt, sb.sc1[:, 2:3], sb.sc1[:, 6:7], ALU.subtract, ALU.mult),
                hard=(tl2, tb2), sig="dve")
            kb.op("dve", lambda e, tt=tt: e.tensor_tensor(tt, tt, sb.gamma_rep[:], ALU.mult),
                  sig="dve")
            kb.op("dve", lambda e, tt=tt: e.tensor_tensor(tt, tt, sb.beta_rep[:], ALU.add),
                  sig="dve")
            tv = kb.op("dve", lambda e, xcs=xcs, tt=tt: e.tensor_tensor(xcs, xcs, tt, ALU.add),
                       sig="dve")
        last_store = kb.op("pool", lambda e, xcs=xcs, row0=row0, x_dst=x_dst: e.dma_start(
            x_dst[row0:row0 + 128, :], xcs), waits=(tv,), dma="xst%d" % (c % 2))
        st_tok[c % 2] = last_store
        if want_ln:
            kb.op("dve", lambda e, xcs=xcs: e.bn_stats(sb.bn6[:, 0:6], xcs[:, 0:512]),
                  waits=(tv,), sig="dve")
            tb1 = kb.op("dve", lambda e, xcs=xcs: e.bn_stats(sb.bn6[:, 6:12], xcs[:, 512:1024]),
                  sig="dve")
            tb2 = kb.op("dve", lambda e, c=c: e.bn_aggr(sb.stats[:, 2 * c:2 * c + 2], sb.bn6[:]),
                        hard=(tb1,), sig="dve")
            tv1 = kb.op("dve", lambda e, c=c: e.tensor_scalar(
                sb.sc1[:, 8:9], sb.stats[:, 2 * c + 1:2 * c + 2], LN_EPS, None, ALU.add),
                hard=(tb2,), sig="dve")
            tl1 = kb.op("act", lambda e: e.activation(
                sb.sc1[:, 9:10], sb.sc1[:, 8:9], AF.Ln, bias=0.0, scale=1.0),
                waits=(tv1,), sig="act")
            tl2 = kb.op("act", lambda e, c=c: e.activation(
                sb.rstd16[:, c:c + 1], sb.sc1[:, 9:10], AF.Exp, bias=0.0, scale=-0.5),
                hard=(tl1,), sig="act")
            t_aff = kb.op("dve", lambda e, xcs=xcs, c=c: e.tensor_scalar(
                sb.PQ[:, 1024 * (c % 8):1024 * (c % 8 + 1)], xcs,
                sb.stats[:, 2 * c:2 * c + 1], sb.rstd16[:, c:c + 1],
                ALU.subtract, ALU.mult),
                waits=(px_free, batch_tok[0]),
                hard=(tl2, tb2), sig="dve")
            prev_v[c % 2] = (last_store, t_aff)
            if want_ln and c % 8 == 7:
                c0b = c - 7
                batch_tok[0] = kb.op("pool", lambda e, c0b=c0b, seq=seq: e.dma_start(
                    io.xlnd[seq * S + 128 * c0b:seq * S + 128 * (c0b + 8), :]
                    .rearrange("(c p) n -> p c n", p=128),
                    sb.PQ[:, 0:8192].rearrange("p (c n) -> p c n", n=1024)),
                    waits=(t_aff,), dma="xlnd")
        else:
            prev_v[c % 2] = (last_store, tv)
    ret_store = (st_tok.get(0), st_tok.get(1))
    if not want_ln:
        return ret_store, None
    return ret_store, batch_tok[0]


def load_px(kb, io, sb, seq, t_ready, px_free=None, src=None):
    t = None
    xl = src if src is not None else io.xlnd
    for dc in range(8):
        t = kb.op("sp", lambda e, dc=dc, xl=xl, seq=seq: e.dma_start(
            sb.PX[:, 2048 * dc:2048 * (dc + 1)],
            xl[seq * S:(seq + 1) * S, 128 * dc:128 * (dc + 1)], transpose=True),
            waits=(t_ready, px_free), dma="px")
    return t



def fm_proj(kb, sb, ps, dst, reg, wtok, t_px, bias_col=None,
            transposed=False, war0=None, evac_eng="dve"):
    ev = {}
    t_last = None
    for fc in range(8):
        for slab in range(4):
            pa = ps.A[:, 512 * slab:512 * (slab + 1)]
            for dc in range(8):
                if transposed:
                    off = reg + 1024 * dc + 512 * (fc // 4) + 128 * (fc % 4)
                else:
                    off = reg + 1024 * dc + 128 * fc
                lhsT = sb.PO[:, off:off + 128]
                w0 = (wtok, ev.get(slab), t_px, war0) if dc == 0 else ()
                tmm = kb.op("pe", lambda e, pa=pa, lhsT=lhsT, dc=dc, slab=slab: e.matmul(
                    pa, lhsT, sb.PX[:, 2048 * dc + 512 * slab:2048 * dc + 512 * (slab + 1)],
                    start=(dc == 0), stop=(dc == 7)),
                    waits=w0, sig="pe" if dc == 7 else None)
            if bias_col is not None:
                ev[slab] = kb.op("dve", lambda e, fc=fc, slab=slab, pa=pa, dst=dst: e.tensor_scalar(
                    dst[:, 2048 * fc + 512 * slab:2048 * fc + 512 * (slab + 1)], pa,
                    sb.biasf[:, bias_col + fc:bias_col + fc + 1], None, ALU.add),
                    waits=(tmm,), sig="dve")
            elif evac_eng == "dve":
                ev[slab] = kb.op("dve", lambda e, fc=fc, slab=slab, pa=pa, dst=dst: e.tensor_copy(
                    dst[:, 2048 * fc + 512 * slab:2048 * fc + 512 * (slab + 1)], pa),
                    waits=(tmm,), sig="dve")
            else:
                ev[slab] = kb.op("act", lambda e, fc=fc, slab=slab, pa=pa, dst=dst: e.activation(
                    dst[:, 2048 * fc + 512 * slab:2048 * fc + 512 * (slab + 1)], pa,
                    AF.Copy, bias=0.0, scale=1.0), waits=(tmm,), sig="act")
            t_last = ev[slab]
    return t_last



def stage_ffn0(kb, io, sb, ps, t_px, wfree=None):
    t_w = None
    for dc in range(8):
        t_w = kb.op("sp", lambda e, dc=dc: e.dma_start(
            sb.PK[:, 1024 * dc:1024 * (dc + 1)], io.sc["kq"][128 * dc:128 * (dc + 1), :]),
            waits=(wfree,), dma="kq")
    aT = sb.PK[:, 9216:9344]
    bT = sb.PK[0:16, 8192:9216]
    kb.op("pool", lambda e, aT=aT: e.dma_start(
        aT.rearrange("p (c r) -> p c r", r=16),
        io.w["L0_ffn_a"].rearrange("(c p) r -> p c r", p=128)), waits=(wfree,), dma="pw")
    t_ab2 = kb.op("pool", lambda e, bT=bT: e.dma_start(bT, io.w["L0_ffn_b"]),
                  waits=(wfree,), dma="pw")
    tl1 = []
    for slab in range(4):
        t = None
        for dc in range(8):
            t = kb.op("pe", lambda e, slab=slab, dc=dc, aT=aT: e.matmul(
                ps.B[0:16, 512 * slab:512 * (slab + 1)],
                aT[:, 16 * dc:16 * (dc + 1)],
                sb.PX[:, 2048 * dc + 512 * slab:2048 * dc + 512 * (slab + 1)],
                start=(dc == 0), stop=(dc == 7)),
                waits=(t_px, t_ab2) if dc == 0 else (), sig="pe" if dc == 7 else None)
        tl1.append(t)
    tl1b = []
    for slab in range(4):
        tl1b.append(kb.op("dve", lambda e, slab=slab: e.tensor_copy(
            sb.st[0:16, 512 * slab:512 * (slab + 1)], ps.B[0:16, 512 * slab:512 * (slab + 1)]),
            waits=(tl1[slab],), sig="dve"))
    ev = {}
    t_dd = None
    for fc in range(8):
        for slab in range(4):
            pa = ps.A[:, 512 * slab:512 * (slab + 1)]
            for dc in range(8):
                kb.op("pe", lambda e, fc=fc, slab=slab, dc=dc, pa=pa: e.matmul(
                    pa, sb.PK[:, 1024 * dc + 128 * fc:1024 * dc + 128 * (fc + 1)],
                    sb.PX[:, 2048 * dc + 512 * slab:2048 * dc + 512 * (slab + 1)],
                    start=(dc == 0), stop=False),
                    waits=(t_w, ev.get(slab)) if dc == 0 else ())
            tmm = kb.op("pe", lambda e, fc=fc, slab=slab, pa=pa, bT=bT: e.matmul(
                pa, bT[:, 128 * fc:128 * (fc + 1)],
                sb.st[0:16, 512 * slab:512 * (slab + 1)], start=False, stop=True),
                waits=(tl1b[slab], t_ab2), sig="pe")
            ev[slab] = kb.op("dve", lambda e, slab=slab, fc=fc, pa=pa: e.tensor_scalar(
                sb.st[:, 2048 + 512 * slab:2048 + 512 * (slab + 1)], pa,
                sb.biasf[:, fc:fc + 1], 0.5, ALU.add, ALU.mult),
                waits=(tmm,), sig="dve")
        tsd = kb.op("pool", lambda e, fc=fc: e.dma_start(
            io.dd[128 * fc:128 * (fc + 1), :], sb.st[:, 2048:4096]),
            waits=(ev[0], ev[1], ev[2], ev[3]), dma="ddl")
        for slab in range(4):
            ev[slab] = tsd
        t_dd = tsd
    return t_dd, ("pe", kb.cnt["pe"])



def stage_attn(kb, io, sb, ps, seq, layer, t_px, wfree=None, pv_tok=None):
    L1 = (layer == 1)
    regA, regB = 0, 8192
    SC = 0.125

    def load_direct(scname, reg, wtok=None):
        lane = "qw0" if reg == 0 else "qw1"
        t = None
        for dc in range(8):
            t = kb.op("sp", lambda e, dc=dc, scname=scname, reg=reg: e.dma_start(
                sb.PO[:, reg + 1024 * dc:reg + 1024 * (dc + 1)],
                io.sc[scname][128 * dc:128 * (dc + 1), :]), waits=(wtok,), dma=lane)
        return t

    def load_T(twname, reg, wtok=None):
        lane = "qw0" if reg == 0 else "qw1"
        return kb.op("sp", lambda e, twname=twname, reg=reg: e.dma_start(
            sb.PO[:, reg:reg + 8192], io.tw[twname]),
            waits=(wtok, kb.tw_done), dma=lane)

    if L1:
        wq = load_T("ipq", regA, wtok=wfree)
        tq = fm_proj(kb, sb, ps, sb.PQ, regA, wq, t_px, bias_col=8, transposed=True)
        wk = load_T("ipk", regB)
        tk = fm_proj(kb, sb, ps, sb.PK, regB, wk, t_px, bias_col=16, transposed=True)
    else:
        wq = load_direct("q", regA, wtok=wfree)
        tq = fm_proj(kb, sb, ps, sb.PQ, regA, wq, t_px)
        wk = load_direct("k", regB)
        tk = fm_proj(kb, sb, ps, sb.PK, regB, wk, t_px, evac_eng="act")

    tvm = kb.op("dve", lambda e: e.memset(sb.PV[:], 1.0), waits=(pv_tok,), sig="dve")
    if L1:
        wv = load_T("ipv", regA, wtok=tq)
    else:
        wv = load_direct("v", regA, wtok=tq)
    ev = {}
    t_v = None
    for tc in range(NCH):
        for half in range(2):
            bank = (tc * 2 + half) % 4
            pb = ps.B[:, 512 * bank:512 * (bank + 1)]
            for dc in range(8):
                lhsT = sb.PX[:, 2048 * dc + 128 * tc:2048 * dc + 128 * (tc + 1)]
                w0 = (wv, ev.get(bank)) if dc == 0 else ()
                kb.op("pe", lambda e, pb=pb, lhsT=lhsT, dc=dc, half=half: e.matmul(
                    pb, lhsT,
                    sb.PO[:, regA + 1024 * dc + 512 * half:regA + 1024 * dc + 512 * (half + 1)],
                    start=(dc == 0), stop=(dc == 7 and not L1)),
                    waits=w0, sig="pe" if (dc == 7 and not L1) else None)
            if L1:
                tmm = kb.op("pe", lambda e, pb=pb, half=half: e.matmul(
                    pb, sb.ones_row_b[:],
                    sb.vb_row[0:1, 512 * half:512 * (half + 1)],
                    start=False, stop=True), sig="pe")
            else:
                tmm = ("pe", kb.cnt["pe"])
            ev_t = None
            for hh in range(8):
                h = half * 8 + hh
                ev_t = kb.op("dve", lambda e, pb=pb, tc=tc, h=h, hh=hh: e.tensor_copy(
                    sb.PV[:, 1040 * tc + 65 * h:1040 * tc + 65 * h + 64],
                    pb[:, 64 * hh:64 * (hh + 1)]), waits=(tmm, tvm), sig="dve")
            ev[bank] = ev_t
            t_v = ev_t

    t_cb_prev = None
    sbuf_free = {}
    t_px_writes = []
    t_cb = None
    t_opx = {}
    pend_fin = [None]

    def flush_fin():
        f = pend_fin[0]
        pend_fin[0] = None
        if f:
            f()
    hq = 0
    for h in range(16):
        fc, pb0 = h // 2, 64 * (h % 2)
        for qh in range(2):
            par = hq % 2
            ob = ps.B[0:65, 1024 * par:1024 * par + 1024]
            kcs = list(range(NCH)) if not L1 else list(range(8 * qh + 8))
            pend_pv = []
            t_pv = None
            pv_started = [False]

            def emit_pv(first_overall, last_overall):
                nonlocal t_pv
                kc, te, tz, pt = pend_pv.pop(0)
                for j in range(2):
                    last = (j == 1 and last_overall)
                    t_pv = kb.op("pe", lambda e, kc=kc, h=h, j=j, pt=pt, fo=(first_overall and j == 0 or first_overall and j == 1), first=first_overall, last2=last_overall, ob=ob: e.matmul(
                        ob[:, 512 * j:512 * (j + 1)],
                        sb.PV[:, 1040 * kc + 65 * h:1040 * kc + 65 * (h + 1)],
                        pt[:, 512 * j:512 * (j + 1)], start=first, stop=last2),
                        waits=(te, tz, t_v, t_opx.get(par) if first_overall else None),
                        sig="pe" if last else None)
                sbuf_free[("pt", kc % (2 if L1 else 3))] = ("pe", kb.cnt["pe"])

            for i, kc in enumerate(kcs):
                if i == 10 - 4 * L1:
                    flush_fin()
                sbi = kc % 2
                pA = ps.A[:, 1024 * sbi:1024 * (sbi + 1)]
                dprime = kc - 8 * qh if L1 else -1
                qlo = 128 * dprime if (L1 and dprime >= 0) else 0
                tsm = None
                for j in range(2):
                    c0, c1 = max(512 * j, qlo), 512 * (j + 1)
                    if c0 >= c1:
                        continue
                    tsm = kb.op("pe", lambda e, pA=pA, fc=fc, pb0=pb0, kc=kc, qh=qh, c0=c0, c1=c1: e.matmul(
                        pA[:, c0:c1],
                        sb.PK[pb0:pb0 + 64, 2048 * fc + 128 * kc:2048 * fc + 128 * (kc + 1)],
                        sb.PQ[pb0:pb0 + 64, 2048 * fc + 1024 * qh + c0:2048 * fc + 1024 * qh + c1],
                        start=True, stop=True, tile_position=(pb0, 0)),
                        waits=(tq, tk, sbuf_free.get(sbi),
                               t_cb if ((not L1) and qh == 0 and kc == 1) else None),
                        sig="pe")
                if (not L1) and qh == 0 and kc == 0:
                    tr = kb.op("dve", lambda e, pA=pA: e.reduce_max(
                        sb.sc1[0:1, 16:17], pA[0:1, 0:1024], axis=AX.X, negate=True),
                        waits=(tsm,), sig="dve")
                    tr2 = kb.op("dve", lambda e: e.tensor_scalar(
                        sb.sc1[0:1, 17:18], sb.sc1[0:1, 16:17], SC, None, ALU.mult),
                        hard=(tr,), sig="dve")
                    tr3 = kb.op("dve", lambda e: e.tensor_copy(sb.rowb[0:1, 0:1], sb.sc1[0:1, 17:18]),
                                hard=(tr2,), sig="dve")
                    trm = kb.op("pe", lambda e: e.matmul(
                        ps.A[:, 2047:2048], sb.ones_row_b[:], sb.rowb[0:1, 0:1],
                        start=True, stop=True), waits=(tr3, sbuf_free.get(1)), sig="pe")
                    t_cb = kb.op("dve", lambda e: e.tensor_copy(sb.cb[:, 0:1], ps.A[:, 2047:2048]),
                                 waits=(trm,), hard=(t_cb_prev,), sig="dve")
                if L1 and dprime >= 0:
                    tsm = kb.op("dve", lambda e, pA=pA, qlo=qlo: e.tensor_tensor(
                        pA[:, qlo:qlo + 128], pA[:, qlo:qlo + 128], sb.cmask[:], ALU.add),
                        waits=(tsm,), sig="dve")
                sb3 = kc % (2 if L1 else 3)
                pt = sb.PO[:, 1024 * sb3:1024 * (sb3 + 1)]
                tz = None
                if qlo > 0:
                    tz = kb.op("pool", lambda e, pt=pt, qlo=qlo: e.memset(pt[:, 0:qlo], 0.0),
                               waits=(sbuf_free.get(("pt", sb3)) or t_v,), sig="pool")
                if L1:
                    te = kb.op("act", lambda e, pA=pA, pt=pt, qlo=qlo: e.activation(
                        pt[:, qlo:1024], pA[:, qlo:1024], AF.Exp, bias=0.0, scale=SC),
                        waits=(tsm, sbuf_free.get(("pt", sb3))), sig="act")
                else:
                    te = kb.op("act", lambda e, pA=pA, pt=pt: e.activation(
                        pt[:], pA[:], AF.Exp, bias=sb.cb[:, 0:1], scale=SC),
                        waits=(tsm, sbuf_free.get(("pt", sb3))), hard=(t_cb,), sig="act")
                t_cb_prev = te
                sbuf_free[sbi] = te
                pend_pv.append((kc, te, tz, pt))

                def pump(last):
                    emit_pv(not pv_started[0], last)
                    pv_started[0] = True
                if L1:
                    if qh == 1 and kc < 7:
                        if len(pend_pv) > 1:
                            pump(False)
                    else:
                        while pend_pv:
                            pump(i == len(kcs) - 1 and len(pend_pv) == 1)
                elif len(pend_pv) > 2:
                    pump(False)
            while pend_pv:
                pump(len(pend_pv) == 1)
            def fin(ob=ob, par=par, fc=fc, pb0=pb0, qh=qh, t_pv=t_pv):
                stv = sb.st[64:128, 1024 * par:1024 * (par + 1)]
                t0 = kb.op("dve", lambda e, ob=ob: e.tensor_copy(
                    sb.row, ob[64:65, 0:1024]), waits=(t_pv,), sig="dve")
                tex = kb.op("dve", lambda e: e.reciprocal(sb.row, sb.row),
                            hard=(t0,), sig="dve")
                tev = kb.op("act", lambda e, ob=ob, stv=stv: e.activation(
                    stv, ob[0:64, 0:1024], AF.Copy, bias=0.0, scale=1.0),
                    waits=(t_pv, t_opx.get(par)), sig="act")
                trep = []
                for j in range(2):
                    trep.append(kb.op("pe", lambda e, j=j, ob=ob: e.matmul(
                        ob[0:64, 512 * j:512 * (j + 1)],
                        sb.ones_row_f[0:1, 0:64], sb.packA[0:1, 512 * j:512 * (j + 1)],
                        start=True, stop=True), waits=(tex, tev), sig="pe"))
                t_ox = kb.op("dve", lambda e, fc=fc, pb0=pb0, qh=qh, ob=ob, stv=stv: e.tensor_tensor(
                    sb.PX[pb0:pb0 + 64, 2048 * fc + 1024 * qh:2048 * fc + 1024 * (qh + 1)],
                    stv, ob[0:64, 0:1024], ALU.mult),
                    waits=(tev, trep[0], trep[1]), sig="dve")
                t_px_writes.append(t_ox)
                t_opx[par] = t_ox
            pend_fin[0] = fin
            hq += 1

    flush_fin()
    if L1:
        wo = load_T("op", regB, wtok=tk)
    else:
        wo = load_direct("o", regB, wtok=tk)
    t_all_px = t_px_writes[-1]
    ev2 = {}
    t_dd = None
    for fc in range(8):
        for slab in range(4):
            pa = ps.A[:, 512 * slab:512 * (slab + 1)]
            for dc in range(8):
                if L1:
                    off = regB + 1024 * dc + 512 * (fc // 4) + 128 * (fc % 4)
                else:
                    off = regB + 1024 * dc + 128 * fc
                lhsT = sb.PO[:, off:off + 128]
                w0 = (wo, ev2.get(slab), t_all_px, sbuf_free.get(0), sbuf_free.get(1)) \
                    if dc == 0 else ()
                tmm = kb.op("pe", lambda e, pa=pa, lhsT=lhsT, dc=dc, slab=slab: e.matmul(
                    pa, lhsT, sb.PX[:, 2048 * dc + 512 * slab:2048 * dc + 512 * (slab + 1)],
                    start=(dc == 0), stop=(dc == 7)),
                    waits=w0, sig="pe" if dc == 7 else None)
            if L1:
                ev2[slab] = kb.op("dve", lambda e, slab=slab, fc=fc, pa=pa: e.tensor_scalar(
                    sb.st[:, 2048 + 512 * slab:2048 + 512 * (slab + 1)], pa,
                    sb.biasf[:, 24 + fc:24 + fc + 1], None, ALU.add),
                    waits=(tmm,), sig="dve")
            else:
                ev2[slab] = kb.op("dve", lambda e, slab=slab, pa=pa: e.tensor_copy(
                    sb.st[:, 2048 + 512 * slab:2048 + 512 * (slab + 1)], pa),
                    waits=(tmm,), sig="dve")
        tsd = kb.op("pool", lambda e, fc=fc: e.dma_start(
            io.dd[128 * fc:128 * (fc + 1), :], sb.st[:, 2048:4096]),
            waits=(ev2[0], ev2[1], ev2[2], ev2[3]), dma="ddl")
        for slab in range(4):
            ev2[slab] = tsd
        t_dd = tsd
    return t_dd, ("pe", kb.cnt["pe"])



def stage_gate(kb, io, sb, ps, seq, t_px):
    t_prev = {0: None, 1: None, 2: None, 3: None}
    selb_prev = {0: None, 1: None, 2: None, 3: None}
    te1_prev = [None]
    for tc in range(NCH):
        slot = seq * NCH + tc
        par = tc % 4
        pg = ps.B[:, 1024 + 16 * par:1032 + 16 * par]
        tg = None
        for dc in range(8):
            tg = kb.op("pe", lambda e, dc=dc, tc=tc, pg=pg: e.matmul(
                pg, sb.PX[:, 2048 * dc + 128 * tc:2048 * dc + 128 * (tc + 1)],
                sb.gwT[:, 16 * dc:16 * dc + 8], start=(dc == 0), stop=(dc == 7)),
                waits=(t_px, t_prev[par]) if dc == 0 else (), sig="pe" if dc == 7 else None)
        g = sb.g8
        tm0 = kb.op("dve", lambda e, pg=pg: e.reduce_max(g[:, 8:9], pg, axis=AX.X, negate=True),
                    waits=(tg,), hard=(te1_prev[0],), sig="dve")
        te1 = kb.op("act", lambda e, pg=pg: e.activation(g[:, 16:24], pg, AF.Exp,
                                                         bias=g[:, 8:9], scale=1.0),
                    waits=(tg,), hard=(tm0,), sig="act")
        te1_prev[0] = te1
        ti1 = kb.op("dve", lambda e, pg=pg: e.tensor_scalar(g[:, 40:48], pg, g[:, 8:9],
                                                            0.0, ALU.add, ALU.is_ge),
                    hard=(tm0,), sig="dve")
        tk0 = kb.op("dve", lambda e, pg=pg: e.scalar_tensor_tensor(g[:, 48:56], g[:, 40:48], -BIG,
                                                                   pg, ALU.mult, ALU.add),
                    hard=(ti1,), sig="dve")
        tm2 = kb.op("dve", lambda e: e.reduce_max(g[:, 12:13], g[:, 48:56], axis=AX.X),
                    hard=(tk0,), sig="dve")
        tsel = kb.op("dve", lambda e, pg=pg: e.tensor_scalar(g[:, 56:64], pg, g[:, 12:13],
                                                             None, ALU.is_ge), hard=(tm2,), sig="dve")
        tt0 = kb.op("dve", lambda e: e.tensor_tensor(g[:, 24:32], g[:, 56:64], g[:, 16:24],
                                                     ALU.mult), waits=(te1,), hard=(tsel,), sig="dve")
        ts2 = kb.op("dve", lambda e: e.reduce_sum(g[:, 13:14], g[:, 24:32], axis=AX.X),
                    hard=(tt0,), sig="dve")
        tr2 = kb.op("dve", lambda e: e.reciprocal(g[:, 15:16], g[:, 13:14]), hard=(ts2,), sig="dve")
        twf = kb.op("dve", lambda e, slot=slot: e.tensor_scalar(
            sb.wfull[:, 8 * (slot % 64):8 * (slot % 64) + 8], g[:, 24:32], g[:, 15:16],
            None, ALU.mult), hard=(tr2,), sig="dve")
        sel = sb.selb[:, 8 * par:8 * (par + 1)]
        tsb = kb.op("dve", lambda e, sel=sel: e.tensor_copy(sel, g[:, 56:64]),
                    waits=(selb_prev[par],), hard=(tsel,), sig="dve")
        pp = ps.B[:, 1536 + 16 * par:1544 + 16 * par]
        tpm = kb.op("pe", lambda e, pp=pp, sel=sel: e.matmul(pp, sb.u128[:], sel,
                                                             start=True, stop=True),
                    waits=(tsb, t_prev[par]), sig="pe")
        tpe = kb.op("dve", lambda e, slot=slot, pp=pp: e.tensor_copy(
            sb.pref[:, 8 * (slot % 64):8 * (slot % 64) + 8], pp), waits=(tpm,), sig="dve")
        pt2 = ps.B[0:1, 1600 + 16 * par:1608 + 16 * par]
        ttm = kb.op("pe", lambda e, pt2=pt2, sel=sel: e.matmul(pt2, sb.ones_col_b[:], sel,
                                                               start=True, stop=True),
                    waits=(tsb,), sig="pe")
        tte = kb.op("dve", lambda e, slot=slot, pt2=pt2: e.tensor_copy(
            sb.Trow[0:1, 8 * (slot % 64):8 * (slot % 64) + 8], pt2), waits=(ttm,), sig="dve")
        t_prev[par] = tte
        selb_prev[par] = ("pe", kb.cnt["pe"])
    return tte, ("pe", kb.cnt["pe"])


def dump_gate(kb, io, sb, t_gate):
    t1 = kb.op("pool", lambda e: e.dma_start(io.gd_wfull, sb.wfull[:]), waits=(t_gate,), dma="ddl")
    t2 = kb.op("pool", lambda e: e.dma_start(io.gd_pref, sb.pref[:]), waits=(t_gate,), dma="ddl")
    t3 = kb.op("pool", lambda e: e.dma_start(io.gd_trow, sb.Trow), waits=(t_gate,), dma="ddl")
    return t3


def stage_counts_out(kb, io, sb, ps, t_gate, fused):
    tG = kb.op("dve", lambda e: e.tensor_reduce(
        sb.packA[0:1, 0:8], sb.Trow.rearrange("o (s e) -> o e s", e=8), AX.X, ALU.add),
        waits=(t_gate,), sig="dve")
    if not fused:
        return kb.op("pool", lambda e: e.dma_start(io.ccrow, sb.packA[0:1, 0:8]),
                     waits=(tG,), dma="mscp")
    tmr = kb.op("pe", lambda e: e.matmul(ps.B[0:8, 1024:1032], sb.ones_row_f[0:1, 0:8],
                                         sb.packA[0:1, 0:8], start=True, stop=True),
                waits=(tG,), sig="pe")
    tcc = kb.op("dve", lambda e: e.tensor_scalar(sb.CC[:], ps.B[0:8, 1024:1032],
                                                 sb.rowsel[:], None, ALU.mult),
                waits=(tmr,), sig="dve")
    tst = kb.op("pool", lambda e: e.dma_start(io.ccin_d, sb.CC[:]), waits=(tcc,), dma="mscp")
    tcol = kb.op("pool", lambda e: e.collective_compute(
        "AllReduce", ALU.add, replica_groups=[list(range(NCORES))],
        ins=[io.ccin_d.opt()], outs=[io.ccout_d.opt()]),
        waits=(tst,), sig="cc")
    return tcol


def stage_offsets(kb, io, sb, ps, t_cc, fused):
    cc_src = io.ccout_d if fused else io.cc
    tld = kb.op("sp", lambda e, cc_src=cc_src: e.dma_start(sb.CC[:], cc_src),
                waits=(t_cc,), dma="msc")
    tt1 = kb.op("pool", lambda e: e.dma_start(
        io.t64d.rearrange("p n -> (p n)").rearrange("(o x) -> o x", o=1), sb.Trow),
        waits=(t_cc,), dma="mscp")
    tt2 = kb.op("sp", lambda e: e.dma_start(sb.sc1[0:64, 16:24], io.t64d), waits=(tt1,), dma="msc")
    tt3 = kb.op("dve", lambda e: e.tensor_copy(sb.st[0:64, 0:8], sb.sc1[0:64, 16:24]),
                waits=(tt2,), sig="dve")
    tof = kb.op("pe", lambda e: e.matmul(ps.B[0:64, 1024:1032], sb.u128[0:64, 0:64],
                                         sb.st[0:64, 0:8], start=True, stop=True),
                waits=(tt3,), sig="pe")
    tofe = kb.op("dve", lambda e: e.tensor_copy(sb.sc1[0:64, 24:32], ps.B[0:64, 1024:1032]),
                 waits=(tof,), sig="dve")
    tco = kb.op("pe", lambda e: e.matmul(ps.B[0:1, 1536:1544], sb.rankmask[:], sb.CC[:],
                                         start=True, stop=True), waits=(tld,), sig="pe")
    tcoe = kb.op("dve", lambda e: e.tensor_copy(sb.packA[0:1, 8:16], ps.B[0:1, 1536:1544]),
                 waits=(tco,), sig="dve")
    tcor = kb.op("pe", lambda e: e.matmul(ps.B[0:64, 1032:1040], sb.ones_row_f[0:1, 0:64],
                                          sb.packA[0:1, 8:16], start=True, stop=True),
                 waits=(tcoe,), sig="pe")
    toff = kb.op("dve", lambda e: e.tensor_tensor(sb.sc1[0:64, 24:32], sb.sc1[0:64, 24:32],
                                                  ps.B[0:64, 1032:1040], ALU.add),
                 waits=(tcor,), hard=(tofe,), sig="dve")
    tst = kb.op("pool", lambda e: e.dma_start(
        io.offd.rearrange("o (p n) -> (o p) n", n=8)[0:64, :], sb.sc1[0:64, 24:32]),
        waits=(toff,), dma="mscp")
    trd = kb.op("sp", lambda e: e.dma_start(sb.OFFROW, io.offd), waits=(tst,), dma="msc")
    return trd


def stage_comb(kb, io, sb, ps, seq, t_off, t_wp_ready):
    t_last = None
    t_prev = None
    for tc in range(NCH):
        slot = seq * NCH + tc
        po = ps.B[:, 1032 + 8 * (tc % 2):1040 + 8 * (tc % 2)]
        trm = kb.op("pe", lambda e, slot=slot, po=po: e.matmul(
            po, sb.ones_row_f[:], sb.OFFROW[0:1, 8 * slot:8 * slot + 8],
            start=True, stop=True), waits=(t_off, t_prev), sig="pe")
        t1 = kb.op("dve", lambda e, slot=slot, po=po: e.tensor_tensor(
            sb.g8[:, 0:8], sb.pref[:, 8 * (slot % 64):8 * (slot % 64) + 8], po, ALU.add),
            waits=(trm, t_wp_ready), sig="dve")
        t2 = kb.op("dve", lambda e: e.tensor_scalar(sb.g8[:, 8:16], sb.g8[:, 0:8], CAP,
                                                    None, ALU.is_lt), hard=(t1,), sig="dve")
        t_last = kb.op("dve", lambda e, slot=slot: e.tensor_tensor(
            sb.comb[:, 8 * (slot % 64):8 * (slot % 64) + 8],
            sb.g8[:, 8:16], sb.wfull[:, 8 * (slot % 64):8 * (slot % 64) + 8], ALU.mult),
            hard=(t2,), sig="dve")
        t_prev = t_last
    return t_last



def stage_moe(kb, io, sb, ps, seq, t_px, t_comb, x_src, x_dst, want_ln, px_free_next,
              x_tok=None):
    t_store = None
    t_aff_all = []
    mcount = [0]
    t_matdone = {}
    y_done = None
    war = [("dve", kb.cnt["dve"])]

    def wreg():
        return 8192 * (mcount[0] % 2)

    def wload_img(img_ap):
        m = mcount[0]
        reg = 8192 * (m % 2)
        lane = "lw%d" % (m % 2)
        wtok = t_matdone.get(m - 2, px_free_next)
        t = kb.op("sp", lambda e, img_ap=img_ap, reg=reg: e.dma_start(
            sb.PK[:, reg:reg + 8192], img_ap), waits=(wtok, kb.tw_done), dma=lane)
        mcount[0] += 1
        return reg, t

    for half in range(2):
        toks = 1024 * half

        def gu_phase(wg_reg, wg_toks, n_ic, ic0, psum_off, dst_fn, extra_wait=None):
            ev = {}
            for i in range(n_ic):
                ic = ic0 + i
                P = ps.A if ic % 2 == 0 else ps.B
                for half2 in range(2):
                    pg = P[:, psum_off + 512 * half2:psum_off + 512 * (half2 + 1)]
                    for bc in range(8):
                        lhsT = sb.PK[:, wg_reg + 1024 * bc + 128 * (ic % 8):
                                     wg_reg + 1024 * bc + 128 * (ic % 8 + 1)]
                        w0 = (wg_toks, ev.get((ic % 2, half2)), t_px, extra_wait) \
                            if bc == 0 else ()
                        kb.op("pe", lambda e, pg=pg, lhsT=lhsT, bc=bc, half2=half2, toks=toks: e.matmul(
                            pg, lhsT,
                            sb.PX[:, 2048 * bc + toks + 512 * half2:2048 * bc + toks + 512 * (half2 + 1)],
                            start=(bc == 0), stop=(bc == 7)),
                            waits=w0, sig="pe" if bc == 7 else None)
                    tgm = ("pe", kb.cnt["pe"])
                    ev[(ic % 2, half2)] = dst_fn(ic, half2, pg, tgm)
            return ev

        def silu_dst(ic, half2, pg, tgm):
            return kb.op("act", lambda e, pg=pg, ic=ic, half2=half2: e.activation(
                sb.PQ[:, 8192 + 1024 * (ic % 8) + 512 * half2:8192 + 1024 * (ic % 8) + 512 * (half2 + 1)],
                pg, AF.Silu, bias=0.0, scale=1.0), waits=(tgm, kb.tw_done), sig="act")

        def hs_dst(ic, half2, pg, tgm, tsil):
            return kb.op("dve", lambda e, pg=pg, ic=ic, half2=half2: e.tensor_tensor(
                sb.PV[:, 1024 * ic + 512 * half2:1024 * ic + 512 * (half2 + 1)],
                sb.PQ[:, 8192 + 1024 * (ic % 8) + 512 * half2:8192 + 1024 * (ic % 8) + 512 * (half2 + 1)],
                pg, ALU.mult), waits=(tgm, tsil, kb.tw_done), sig="dve")

        def down_phase(wd_reg, wd_toks, n_ic, comb_e, t_hs_last, nblk1024):
            ev_d = {}
            t_last = None
            for tc in range(8):
                slot = seq * NCH + half * 8 + tc
                pos = tc % 4
                P = ps.A if pos % 2 == 0 else ps.B
                coff = 1024 * (pos // 2)
                for hh in range(2):
                    pd = P[:, coff + 512 * hh:coff + 512 * (hh + 1)]
                    for i in range(n_ic):
                        lw = wd_toks if not isinstance(wd_toks, list) else wd_toks[i]
                        lhsT = sb.PV[:, 1024 * i + 128 * tc:1024 * i + 128 * (tc + 1)]
                        w0 = (lw, ev_d.get((pos, hh)), t_hs_last) if i == 0 else (lw,)
                        roff = wd_reg(i) if callable(wd_reg) else wd_reg
                        kb.op("pe", lambda e, pd=pd, lhsT=lhsT, i=i, hh=hh, roff=roff: e.matmul(
                            pd, lhsT,
                            sb.PK[:, roff + 1024 * (i % 8) + 512 * hh:roff + 1024 * (i % 8) + 512 * (hh + 1)],
                            start=(i == 0), stop=(i == n_ic - 1)),
                            waits=w0, sig="pe" if i == n_ic - 1 else None)
                    tdm = ("pe", kb.cnt["pe"])
                    ysl = sb.POf[:, 1024 * tc + 512 * hh:1024 * tc + 512 * (hh + 1)]
                    if comb_e is not None and comb_e >= 0:
                        ccol = sb.comb[:, 8 * (slot % 64) + comb_e:8 * (slot % 64) + comb_e + 1]
                        tde = kb.op("dve", lambda e, pd=pd, ysl=ysl, ccol=ccol: e.scalar_tensor_tensor(
                            ysl, pd, ccol, ysl, ALU.mult, ALU.add), waits=(tdm,), hard=(t_comb,), sig="dve")
                    elif comb_e == -1:
                        ccol = sb.comb[:, 8 * (slot % 64):8 * (slot % 64) + 1]
                        yw = y_done if isinstance(y_done, tuple) and len(y_done) == 2 \
                            and not isinstance(y_done[0], str) else (y_done,)
                        tde = kb.op("dve", lambda e, pd=pd, ysl=ysl, ccol=ccol: e.tensor_scalar(
                            ysl, pd, ccol, None, ALU.mult),
                            waits=(tdm,) + tuple(yw), hard=(t_comb,), sig="dve")
                    else:
                        tde = kb.op("dve", lambda e, pd=pd, ysl=ysl: e.tensor_tensor(
                            ysl, ysl, pd, ALU.add), waits=(tdm,), sig="dve")
                    ev_d[(pos, hh)] = tde
                    t_last = tde
            return t_last

        for ee in range(8):
            gm = mcount[0]
            reg_g, wg = wload_img(io.tw["weg"][ee])
            ev_g = gu_phase(reg_g, wg, 8, 0, 0, silu_dst, extra_wait=war[0])
            t_matdone[gm] = ("pe", kb.cnt["pe"])
            um = mcount[0]
            reg_u, wu = wload_img(io.tw["weu"][ee])
            evh = gu_phase(reg_u, wu, 8, 0, 1024,
                           lambda ic, half2, pg, tgm: hs_dst(ic, half2, pg, tgm,
                                                             ev_g[(ic % 2, half2)]),
                           extra_wait=war[0])
            t_hs_all = ("dve", kb.cnt["dve"])
            t_matdone[um] = ("pe", kb.cnt["pe"])
            dm = mcount[0]
            reg_d, wd = wload_img(io.tw["wed"][ee])
            down_phase(reg_d, wd, 8, (-1 if ee == 0 else ee), t_hs_all, False)
            t_matdone[dm] = ("pe", kb.cnt["pe"])
            war[0] = ("dve", kb.cnt["dve"])

        t_hs_sh = []
        greg = {}
        for grp in range(2):
            gm = mcount[0]
            reg_g, wg = wload_img(io.tw["wsg"][grp])
            ev_g = gu_phase(reg_g, wg, 8, 8 * grp, 0, silu_dst, extra_wait=war[0])
            t_matdone[gm] = ("pe", kb.cnt["pe"])
            um = mcount[0]
            reg_u, wu = wload_img(io.tw["wsu"][grp])
            evh = gu_phase(reg_u, wu, 8, 8 * grp, 1024,
                           lambda ic, half2, pg, tgm: hs_dst(ic, half2, pg, tgm,
                                                             ev_g[(ic % 2, half2)]),
                           extra_wait=war[0])
            t_hs_sh.append(("dve", kb.cnt["dve"]))
            t_matdone[um] = ("pe", kb.cnt["pe"])
        dregs = []
        wd_last = None
        for grp in range(2):
            dm = mcount[0]
            reg_d, wd = wload_img(io.tw["wsd"][grp])
            dregs.append(reg_d)
            wd_last = wd
            if grp == 0:
                dm0 = dm
        t_down = down_phase(lambda i: dregs[i // 8], wd_last, 16, None,
                            t_hs_sh[-1], False)
        war[0] = ("dve", kb.cnt["dve"])
        t_matdone[dm0] = ("pe", kb.cnt["pe"])
        t_matdone[mcount[0] - 1] = ("pe", kb.cnt["pe"])
        y_done2 = t_down

        t_store, t_xln_h = pass_residual_ln(
            kb, io, sb, ps, seq, "y", want_ln, x_src, x_dst,
            chunks=range(half * 8, half * 8 + 8), y_tok=y_done2,
            px_free=px_free_next if half == 0 else None, x_tok=x_tok,
            pq_tok=t_aff_all[-1] if t_aff_all else None)
        if t_xln_h is not None:
            t_aff_all.append(t_xln_h)
        y_done = t_store
    t_xln = t_aff_all[-1] if t_aff_all else None
    return t_store, t_xln, ("pe", kb.cnt["pe"])



def build_program(phase):
    nc = bass.Bass("TRN2", target_bir_lowering=False)
    io = declare_io(nc, phase)
    with ExitStack() as ctx:
        kb = KB(nc)
        kb.make_sems(ctx)
        sb = alloc_sbuf(nc, ctx)
        ps = alloc_psum(nc, ctx)
        t_pw, t_pr = stage_preprocess(kb, io, sb, ps)
        pre_done = (("pe", kb.cnt["pe"]), ("dve", kb.cnt["dve"]), t_pw, t_pr)
        _phase_body(kb, io, sb, ps, phase, pre_done)
        kb.emit()
    return nc


def _phase_body(kb, io, sb, ps, phase, pre_done):
    fused = (phase == "all")

    def is_final(sl):
        return sl == UPTO

    def final_wait():
        kb.wait("sp", [(lane, kb.cnt[lane]) for lane in
                       ("xst0", "xst1", "xlnd", "ddl", "msc", "mscp", "pw", "pr", "cc")])

    px_free = pre_done[0]
    xtk = None
    jobs = image_jobs(io)
    kb.tw_done = None
    if True:
        stage_images(kb, io, sb, jobs, None)
    q = (len(jobs) + 3) // 4
    jgroups = [jobs[i * q:(i + 1) * q] for i in range(4)]

    if DEVSTOP == 1:
        final_wait()
        return

    if phase in ("p1", "all"):
        t_gate = None
        first = True
        for seq in range(SEQS):
            xtk_in = xtk
            tx, t_xln = pass_residual_ln(kb, io, sb, ps, seq, "init", True,
                                         io.x_in, io.xs,
                                         px_free=pre_done[1] if first else px_free,
                                         x_tok=xtk)
            xtk = tx
            if DEVSTOP == 2:
                final_wait()
                return
            t_px = load_px(kb, io, sb, seq, t_xln, px_free=px_free)
            t_dd, px_free = stage_ffn0(kb, io, sb, ps, t_px,
                                       wfree=pre_done[2] if first else px_free)
            first = False
            if DEVSTOP == 3:
                final_wait()
                return
            if UPTO == 1:
                tx, _ = pass_residual_ln(kb, io, sb, ps, seq, "delta", False,
                                         io.xs, io.out, dd_tok=t_dd, x_tok=xtk)
                xtk = tx
                continue
            tx, t_xln = pass_residual_ln(kb, io, sb, ps, seq, "delta", True,
                                         io.xs, io.xs, px_free=px_free, dd_tok=t_dd,
                                         x_tok=xtk)
            xtk = tx
            t_px = load_px(kb, io, sb, seq, t_xln, px_free=px_free)
            t_dd, px_free = stage_attn(kb, io, sb, ps, seq, 0, t_px, wfree=px_free,
                                       pv_tok=kb.tw_done)
            if UPTO == 2:
                tx, _ = pass_residual_ln(kb, io, sb, ps, seq, "l0inner", False,
                                         io.xs, io.out, dd_tok=t_dd, x_tok=xtk)
                xtk = tx
                continue
            tx, t_xln = pass_residual_ln(kb, io, sb, ps, seq, "l0inner", True,
                                         io.xs, io.xs, px_free=px_free, dd_tok=t_dd,
                                         x_tok=xtk)
            xtk = tx
            t_px = load_px(kb, io, sb, seq, t_xln, px_free=px_free)
            t_dd, px_free = stage_ffn0(kb, io, sb, ps, t_px, wfree=px_free)
            if UPTO == 3:
                tx, _ = pass_residual_ln(kb, io, sb, ps, seq, "delta", False,
                                         io.xs, io.out, dd_tok=t_dd, x_tok=xtk)
                xtk = tx
                continue
            tx, t_xln = pass_residual_ln(kb, io, sb, ps, seq, "delta", True,
                                         io.xs, io.xs, px_free=px_free, dd_tok=t_dd,
                                         x_tok=xtk)
            xtk = tx
            t_px = load_px(kb, io, sb, seq, t_xln, px_free=px_free)
            t_gate, px_free = stage_gate(kb, io, sb, ps, seq, t_px)
        if UPTO <= 3:
            final_wait()
            return
        t_gd = dump_gate(kb, io, sb, t_gate)
        t_cc = stage_counts_out(kb, io, sb, ps, t_gate, fused)
        if phase == "p1":
            final_wait()
            return
        t_p1_st = ("xlnd", kb.cnt["xlnd"])

    if phase in ("p2", "all"):
        if phase == "p2":
            t1 = kb.op("sp", lambda e: e.dma_start(sb.wfull[:], io.gdi_wfull), dma="msc")
            t2 = kb.op("sp", lambda e: e.dma_start(sb.pref[:], io.gdi_pref), dma="msc")
            t3 = kb.op("sp", lambda e: e.dma_start(sb.Trow, io.gdi_trow), dma="msc")
            t_wp = t3
            t_cc = t3
            xlsrc = io.xlnd_in
            xln_tok = None
            px_free = pre_done[0]
            xtk = None
        else:
            t_wp = None
            xlsrc = io.xlnd
            xln_tok = t_p1_st
        t_off = stage_offsets(kb, io, sb, ps, t_cc, fused)
        t_gate = None
        for seq in range(SEQS):
            t_comb = stage_comb(kb, io, sb, ps, seq, t_off, t_wp)
            t_px = load_px(kb, io, sb, seq, xln_tok if fused else t_wp,
                           px_free=px_free, src=xlsrc)
            xsrc0 = io.xs if fused else io.x_in
            if is_final(4):
                tx, _, px_free = stage_moe(kb, io, sb, ps, seq, t_px, t_comb,
                                           xsrc0, io.out, False, px_free, x_tok=xtk)
                xtk = tx
                continue
            tx, t_xln, px_free = stage_moe(kb, io, sb, ps, seq, t_px, t_comb,
                                           xsrc0, io.xs, True, px_free, x_tok=xtk)
            xtk = tx
            t_px = load_px(kb, io, sb, seq, t_xln, px_free=px_free)
            t_dd, px_free = stage_attn(kb, io, sb, ps, seq, 1, t_px, wfree=px_free, pv_tok=kb.tw_done)
            if is_final(5):
                tx, _ = pass_residual_ln(kb, io, sb, ps, seq, "delta", False,
                                         io.xs, io.out, dd_tok=t_dd, x_tok=xtk)
                xtk = tx
                continue
            tx, t_xln = pass_residual_ln(kb, io, sb, ps, seq, "delta", True,
                                         io.xs, io.xs, px_free=px_free, dd_tok=t_dd,
                                         x_tok=xtk)
            xtk = tx
            t_px = load_px(kb, io, sb, seq, t_xln, px_free=px_free)
            t_gate, px_free = stage_gate(kb, io, sb, ps, seq, t_px)
        if UPTO <= 5:
            final_wait()
            return
        t_gd = dump_gate(kb, io, sb, t_gate)
        t_cc = stage_counts_out(kb, io, sb, ps, t_gate, fused)
        if phase == "p2":
            final_wait()
            return
        t_p2_st = ("xlnd", kb.cnt["xlnd"])

    if phase == "p3":
        t1 = kb.op("sp", lambda e: e.dma_start(sb.wfull[:], io.gdi_wfull), dma="msc")
        t2 = kb.op("sp", lambda e: e.dma_start(sb.pref[:], io.gdi_pref), dma="msc")
        t3 = kb.op("sp", lambda e: e.dma_start(sb.Trow, io.gdi_trow), dma="msc")
        t_wp = t3
        t_cc = t3
        xlsrc = io.xlnd_in
        xsrc = io.x_in
        xln_tok = None
        px_free = pre_done[0]
        xtk = None
    else:
        t_wp = None
        xlsrc = io.xlnd
        xsrc = io.xs
        xln_tok = t_p2_st
    t_off = stage_offsets(kb, io, sb, ps, t_cc, fused)
    for seq in range(SEQS):
        t_comb = stage_comb(kb, io, sb, ps, seq, t_off, t_wp)
        t_px = load_px(kb, io, sb, seq, xln_tok if fused else t_wp,
                       px_free=px_free, src=xlsrc)
        tx, _, px_free = stage_moe(kb, io, sb, ps, seq, t_px, t_comb,
                                   xsrc, io.out, False, px_free, x_tok=xtk)
        xtk = tx
    final_wait()



def host_consts(core):
    k = np.arange(128)[:, None]
    m = np.arange(128)[None, :]
    return {
        "u128": (k < m).astype(ml_dtypes.bfloat16),
        "cmask": np.where(k <= m, 0.0, -BIG).astype(ml_dtypes.bfloat16),
        "rankmask": (np.arange(8) < core).astype(np.float32)[:, None],
        "rowsel": (np.arange(8) == core).astype(np.float32)[:, None],
    }


def kernel(**inputs):
    x_full = np.ascontiguousarray(np.asarray(inputs["L0_attn_h"], dtype=np.float32))
    shards = [np.ascontiguousarray(x_full[4 * c:4 * (c + 1)].reshape(TOK, D))
              for c in range(NCORES)]
    weights = {nm: np.ascontiguousarray(np.asarray(inputs[nm], dtype=np.float32))
               for nm, _ in W_NAMES if nm != "L0_attn_h"}
    weights.pop("L0_attn_h", None)

    def maps(extra_per_core):
        out = []
        for c in range(NCORES):
            m = dict(weights)
            m.update(host_consts(c))
            m.update(extra_per_core[c])
            out.append(m)
        return out

    if FUSED:
        nc = build_program("all")
        in_maps = maps([{"x": shards[c]} for c in range(NCORES)])
        res = run_bass_kernel_spmd(nc, in_maps, core_ids=list(range(NCORES)))
        outs = [np.asarray(r["out"]) for r in res.results]
    else:
        nc1 = build_program("p1")
        in_maps = maps([{"x": shards[c]} for c in range(NCORES)])
        r1 = run_bass_kernel_spmd(nc1, in_maps, core_ids=list(range(NCORES)))
        if UPTO <= 3:
            outs = [np.asarray(r["out"]) for r in r1.results]
        else:
            cc = np.concatenate([np.asarray(r["ccrow"]) for r in r1.results], axis=0)
            nc2 = build_program("p2")
            in_maps = maps([{
                "xs_in": np.asarray(r1.results[c]["xs_out"]),
                "xlnd_in": np.asarray(r1.results[c]["xlnd_out"]),
                "gdi_wfull": np.asarray(r1.results[c]["gdo_wfull"]),
                "gdi_pref": np.asarray(r1.results[c]["gdo_pref"]),
                "gdi_trow": np.asarray(r1.results[c]["gdo_trow"]),
                "cc": cc,
            } for c in range(NCORES)])
            r2 = run_bass_kernel_spmd(nc2, in_maps, core_ids=list(range(NCORES)))
            if UPTO <= 5:
                outs = [np.asarray(r["out"]) for r in r2.results]
            else:
                cc2 = np.concatenate([np.asarray(r["ccrow"]) for r in r2.results], axis=0)
                nc3 = build_program("p3")
                in_maps = maps([{
                    "xs_in": np.asarray(r2.results[c]["xs_out"]),
                    "xlnd_in": np.asarray(r2.results[c]["xlnd_out"]),
                    "gdi_wfull": np.asarray(r2.results[c]["gdo_wfull"]),
                    "gdi_pref": np.asarray(r2.results[c]["gdo_pref"]),
                    "gdi_trow": np.asarray(r2.results[c]["gdo_trow"]),
                    "cc": cc2,
                } for c in range(NCORES)])
                r3 = run_bass_kernel_spmd(nc3, in_maps, core_ids=list(range(NCORES)))
                outs = [np.asarray(r["out"]) for r in r3.results]

    full = np.concatenate([o.reshape(4, S, D) for o in outs], axis=0)
    return full.astype(np.float32)
